# version 01
# baseline (speedup 1.0000x reference)
import contextlib
import types
import numpy as np
import ml_dtypes
import concourse.bass as bass
import concourse.mybir as mybir
from concourse.bass_utils import run_bass_kernel_spmd

F32 = mybir.dt.float32
BF16 = mybir.dt.bfloat16
AF = mybir.ActivationFunctionType
ALU = mybir.AluOpType
AX = mybir.AxisListType
BF = ml_dtypes.bfloat16
f32 = np.float32

NCORES = 8
S = 8192
DM = 2048
TLOC = S // NCORES
C0_SHIFT = 16.0
ENGINES = ("sync", "gpsimd", "tensor", "vector", "scalar")
INV_SQRT_HD = 1.0 / np.sqrt(128.0)


class B:

    def __init__(self, nc, eng, q, sems):
        self.nc, self.eng, self.q, self.sems = nc, eng, q, sems
        self.cnt = {}
        self.nbar = 0

    def _sem(self, name):
        if name not in self.sems:
            assert self.eng == "__dry__", f"sem {name} not in dry pass"
            self.sems[name] = None

    def c(self, name):
        return self.cnt.get(name, 0)

    def inc(self, name, by=1):
        self._sem(name)
        self.cnt[name] = self.cnt.get(name, 0) + by
        return self.cnt[name]

    def do(self, eng, fn, sem=None, by=1):
        v = self.inc(sem, by) if sem else None
        if self.eng == eng:
            ins = fn(self.q)
            if sem:
                ins.then_inc(self.sems[sem], by)
        return v

    def wait(self, eng, sem, val):
        self._sem(sem)
        if self.eng == eng and val > 0:
            self.q.wait_ge(self.sems[sem], val)

    def fdo(self, eng, fn, sem=None):
        sem = sem or ("f_" + eng)
        v = self.do(eng, fn, sem=sem)
        self.wait(eng, sem, v)
        return v

    def dma(self, eng, out, in_, sem=None, **kw):
        sem = sem or ("ld_" + eng)
        return self.do(eng, lambda q: q.dma_start(out=out, in_=in_, **kw),
                       sem=sem, by=16)

    def barrier(self):
        self.nbar += 1
        self._sem("bar")
        tgt = 5 * self.nbar
        self.cnt["bar"] = tgt
        if self.eng != "__dry__":
            self.q.sem_inc(self.sems["bar"], 1)
            self.q.wait_ge(self.sems["bar"], tgt)


def build(make_tensors, emit):
    nc = bass.Bass("TRN2", target_bir_lowering=False)
    es = contextlib.ExitStack()
    with es:
        t = make_tensors(nc, es)
        sems = {}
        emit(B(nc, "__dry__", None, sems), t)
        for name in list(sems):
            sems[name] = es.enter_context(nc.semaphore(name))
        with nc.Block() as block:
            for eng in ENGINES:
                getattr(block, eng)(
                    lambda q, eng=eng: emit(B(nc, eng, q, sems), t))
    return nc



def emit_consts(b, t):
    b.do("vector", lambda v: v.memset(t.zeros1[:, :], 0.0), sem="const")
    b.do("vector", lambda v: v.memset(t.ones1[:, :], 1.0), sem="const")
    b.dma("gpsimd", t.ident[:, :], t.ident_c, sem="ld_const")


def emit_ln_stream(b, t, x_dram, n_groups, on_group, rms_groups=None):
    st = t.stats
    for g in range(n_groups):
        buf = g % 2
        xsl = slice(buf * DM, (buf + 1) * DM)
        b.wait("sync", "ln_done", b.c("ln_done") - 1)
        b.wait("sync", "st_s", b.c("st_s") - 1)
        vld = b.dma("sync", t.x_tm2[:, xsl], x_dram[g * 128:(g + 1) * 128, :])
        b.wait("vector", "ld_sync", vld)
        b.fdo("vector", lambda v, xsl=xsl: v.tensor_reduce(
            st[:, 0:1], t.x_tm2[:, xsl], axis=AX.X, op=ALU.add))
        b.do("vector", lambda v: v.tensor_scalar_mul(
            st[:, 1:2], st[:, 0:1], -1.0 / DM), sem="st_v")
        b.wait("vector", "st_v", b.c("st_v"))
        b.wait("scalar", "ld_sync", vld)
        b.wait("scalar", "st_v", b.c("st_v"))
        sqv = b.do("scalar", lambda q, xsl=xsl: q.activation(
            out=t.sq_scr[:, :], in_=t.x_tm2[:, xsl], func=AF.Square,
            accum_out=st[:, 2:3]), sem="st_s")
        b.wait("vector", "st_s", sqv)
        b.fdo("vector", lambda v: v.tensor_scalar(
            st[:, 3:4], st[:, 2:3], 1.0 / DM, 1e-5, op0=ALU.mult, op1=ALU.add))
        b.fdo("vector", lambda v: v.tensor_tensor(
            st[:, 4:5], st[:, 1:2], st[:, 1:2], op=ALU.mult))
        s2v = b.do("vector", lambda v: v.tensor_sub(
            st[:, 3:4], st[:, 3:4], st[:, 4:5]), sem="st_v2")
        b.wait("scalar", "st_v2", s2v)
        s3v = b.do("scalar", lambda q: q.sqrt(st[:, 4:5], st[:, 3:4]),
                   sem="st_s2")
        b.wait("vector", "st_s2", s3v)
        b.fdo("vector", lambda v: v.reciprocal(st[:, 5:6], st[:, 4:5]))
        b.wait("vector", "tp_done", b.c("tp_done") - 1)
        if rms_groups and (g - 2) in rms_groups:
            b.wait("vector", "rms_done",
                   b.c("rms_done") - (0 if g in rms_groups else 1))
        lnv = b.do("vector", lambda v, xsl=xsl: v.tensor_scalar(
            t.xn_tm2[:, xsl], t.x_tm2[:, xsl], st[:, 1:2], st[:, 5:6],
            op0=ALU.add, op1=ALU.mult), sem="ln_done")
        if rms_groups and g in rms_groups:
            lg = rms_groups.index(g)
            b.wait("scalar", "ln_done", lnv)
            b.do("scalar", lambda q, xsl=xsl, lg=lg: q.activation(
                out=t.sq_scr[:, :], in_=t.xn_tm2[:, xsl], func=AF.Square,
                accum_out=t.rms_sumsq[:, lg:lg + 1]), sem="rms_done")
        on_group(g, lnv, xsl)


def emit_transposes(b, t, lnv, xsl, g):
    pa = t.ps_tr0 if g % 2 == 0 else t.ps_tr1
    pb = t.ps_tr1 if g % 2 == 0 else t.ps_tr0
    b.wait("tensor", "ln_done", lnv)
    b.wait("tensor", "xp_evac", b.c("xp_evac") - 1)

    def t_tr(q, xsl=xsl, pa=pa, pb=pb):
        mm = None
        for j in range(16):
            dst = pa if j < 8 else pb
            mm = q.transpose(
                dst[:, (j % 8) * 128:(j % 8) * 128 + 128],
                t.xn_tm2[:, xsl.start + j * 128:xsl.start + j * 128 + 128],
                t.ident[:, :])
        return mm
    tpv = b.do("tensor", t_tr, sem="tp_done")
    return pa, pb, tpv



def make_tensors_A(nc, es):
    t = types.SimpleNamespace()
    t.x = nc.dram_tensor("x", [S, DM], F32, kind="ExternalInput").ap()
    t.xloc = nc.dram_tensor("xloc", [TLOC, DM], F32, kind="ExternalInput").ap()
    t.wqkv = nc.dram_tensor("wqkv", [DM, 768], F32, kind="ExternalInput").ap()
    t.rw = nc.dram_tensor("rw", [DM, 8], F32, kind="ExternalInput").ap()
    t.nw = nc.dram_tensor("nw", [128, 16], F32, kind="ExternalInput").ap()
    t.gw = nc.dram_tensor("gw", [DM, 8192], F32, kind="ExternalInput").ap()
    t.uw = nc.dram_tensor("uw", [DM, 8192], F32, kind="ExternalInput").ap()
    t.dwn = nc.dram_tensor("dwn", [8192, DM], F32, kind="ExternalInput").ap()
    t.ident_c = nc.dram_tensor("ident_c", [128, 128], BF16,
                               kind="ExternalInput").ap()
    t.val_o = nc.dram_tensor("valout", [256, S], BF16,
                             kind="ExternalOutput").ap()
    t.g0_o = nc.dram_tensor("g0out", [DM, TLOC], BF16,
                            kind="ExternalOutput").ap()
    t.a_dram = nc.dram_tensor("a_stage", [8192, TLOC], BF16).ap()

    sb = es.enter_context
    t.xn_loc = sb(nc.sbuf_tensor("xn_loc", [128, 16 * TLOC], BF16))
    t.zeros1 = sb(nc.sbuf_tensor("zeros1", [128, 1], BF16))
    t.ones1 = sb(nc.sbuf_tensor("ones1", [1, 128], BF16))
    t.ident = sb(nc.sbuf_tensor("ident", [128, 128], BF16))
    t.rms_sumsq = sb(nc.sbuf_tensor("rms_sumsq", [128, 8], F32))
    t.stats = sb(nc.sbuf_tensor("stats", [128, 8], F32))
    t.sq_scr = sb(nc.sbuf_tensor("sq_scr", [128, DM], BF16))

    with contextlib.ExitStack() as s12:
        t.xq = s12.enter_context(nc.sbuf_tensor("xq", [128, 2 * S], BF16))
        t.xk = s12.enter_context(nc.sbuf_tensor("xk", [128, 2 * S], BF16))
        t.xv = s12.enter_context(nc.sbuf_tensor("xv", [128, 2 * S], BF16))
        with contextlib.ExitStack() as s1:
            t.wqkv_sb = s1.enter_context(
                nc.sbuf_tensor("wqkv_sb", [128, 16 * 768], BF16))
            t.x_tm2 = s1.enter_context(
                nc.sbuf_tensor("x_tm2", [128, 2 * DM], F32))
            t.xn_tm2 = s1.enter_context(
                nc.sbuf_tensor("xn_tm2", [128, 2 * DM], BF16))
            t.xn_t = s1.enter_context(
                nc.sbuf_tensor("xn_t", [128, 16 * 512], BF16))
            t.ps_tr0 = s1.enter_context(
                nc.psum_tensor("ps_tr0", [128, 1024], BF16))
            t.ps_tr1 = s1.enter_context(
                nc.psum_tensor("ps_tr1", [128, 1024], BF16))
            t.ps_qkv = s1.enter_context(
                nc.psum_tensor("ps_qkv", [128, 512], F32))
        with contextlib.ExitStack() as s2:
            t.E2 = s2.enter_context(nc.sbuf_tensor("E2", [128, 2 * S], BF16))
            t.Zp = s2.enter_context(nc.sbuf_tensor("Zp", [128, 16], F32))
            t.zwork = s2.enter_context(nc.sbuf_tensor("zwork", [128, 4], F32))
            t.rbf = s2.enter_context(nc.sbuf_tensor("rbf", [128, 2], BF16))
            t.cs_sb = s2.enter_context(nc.sbuf_tensor("cs_sb", [1, S], BF16))
            t.val_sb = s2.enter_context(nc.sbuf_tensor("val_sb", [128, S], BF16))
            t.ps_scA = s2.enter_context(
                nc.psum_tensor("ps_scA", [128, 1024], F32))
            t.ps_scB = s2.enter_context(
                nc.psum_tensor("ps_scB", [128, 1024], F32))
            t.ps_cs = [s2.enter_context(
                nc.psum_tensor(f"ps_cs{i}", [97, 512], F32)) for i in range(4)]
    with contextlib.ExitStack() as s4:
        t.rw_sb = s4.enter_context(nc.sbuf_tensor("rw_sb", [128, 128], BF16))
        t.nw_sb = s4.enter_context(nc.sbuf_tensor("nw_sb", [128, 16], F32))
        t.cwb = s4.enter_context(nc.sbuf_tensor("cwb", [128, 8 * TLOC], BF16))
        t.r_sc = s4.enter_context(nc.sbuf_tensor("r_sc", [128, 64], F32))
        t.lg_sb = s4.enter_context(nc.sbuf_tensor("lg_sb", [8, TLOC], BF16))
        t.ltm = s4.enter_context(nc.sbuf_tensor("ltm", [128, 64], F32))
        t.exps = s4.enter_context(nc.sbuf_tensor("exps", [128, 64], F32))
        t.eqs = s4.enter_context(nc.sbuf_tensor("eqs", [128, 64], F32))
        t.tts = s4.enter_context(nc.sbuf_tensor("tts", [128, 64], F32))
        t.cwtm = s4.enter_context(nc.sbuf_tensor("cwtm", [128, 64], BF16))
        t.cw8 = s4.enter_context(nc.sbuf_tensor("cw8", [8, TLOC], BF16))
        t.cw_row = s4.enter_context(nc.sbuf_tensor("cw_row", [1, 8 * TLOC], BF16))
        t.xnw2 = s4.enter_context(nc.sbuf_tensor("xnw2", [128, 2 * TLOC], BF16))
        t.gwu2 = s4.enter_context(
            nc.sbuf_tensor("gwu2", [128, 2 * 4096], BF16))
        t.gs_sb = s4.enter_context(nc.sbuf_tensor("gs_sb", [128, TLOC], BF16))
        t.aw2 = s4.enter_context(nc.sbuf_tensor("aw2", [128, 2 * TLOC], BF16))
        t.astr2 = s4.enter_context(
            nc.sbuf_tensor("astr2", [128, 2 * TLOC], BF16))
        t.dw_sb = s4.enter_context(
            nc.sbuf_tensor("dw_sb", [128, 64 * 512], BF16))
        t.g0t = s4.enter_context(nc.sbuf_tensor("g0t", [128, 2 * TLOC], BF16))
        with contextlib.ExitStack() as s4r:
            t.ps_lg = s4r.enter_context(nc.psum_tensor("ps_lg", [8, TLOC], F32))
            t.ps_t8 = s4r.enter_context(
                nc.psum_tensor("ps_t8", [128, 64], BF16))
            t.ps_ct = s4r.enter_context(
                nc.psum_tensor("ps_ct", [8, TLOC], BF16))
            t.ps_bc = s4r.enter_context(
                nc.psum_tensor("ps_bc", [128, 512], F32))
        with contextlib.ExitStack() as s4e:
            t.ps_gA = s4e.enter_context(
                nc.psum_tensor("ps_gA", [128, TLOC], F32))
            t.ps_uA = s4e.enter_context(
                nc.psum_tensor("ps_uA", [128, TLOC], F32))
            t.ps_gB = s4e.enter_context(
                nc.psum_tensor("ps_gB", [128, TLOC], F32))
            t.ps_uB = s4e.enter_context(
                nc.psum_tensor("ps_uB", [128, TLOC], F32))
        with contextlib.ExitStack() as s4d:
            t.ps_y = [s4d.enter_context(
                nc.psum_tensor(f"ps_y{i}", [128, TLOC], F32)) for i in range(4)]
    return t


def emit_A(b, t):
    emit_consts(b, t)
    wv = b.dma("gpsimd", t.wqkv_sb[:, :].rearrange("p (dc o) -> p dc o", dc=16),
               t.wqkv.rearrange("(dc p) o -> p dc o", p=128))
    b.wait("tensor", "ld_gpsimd", wv)
    b.wait("tensor", "ld_const", b.c("ld_const"))

    def on_group(g, lnv, xsl):
        tile, gi = g // 4, g % 4
        pa, pb, tpv = emit_transposes(b, t, lnv, xsl, g)
        b.wait("vector", "tp_done", tpv)
        if gi == 0:
            b.wait("vector", "qm_done", 6 * tile)

        def v_evac(q, gi=gi, pa=pa, pb=pb):
            d3 = t.xn_t.ap().rearrange("p (di tt) -> p di tt", di=16)
            q.tensor_copy(d3[:, 0:8, gi * 128:gi * 128 + 128],
                          pa.ap().rearrange("p (di tt) -> p di tt", di=8))
            return q.tensor_copy(
                d3[:, 8:16, gi * 128:gi * 128 + 128],
                pb.ap().rearrange("p (di tt) -> p di tt", di=8))
        b.do("vector", v_evac, sem="xp_evac")

        if gi == 3:
            b.wait("tensor", "xp_evac", b.c("xp_evac"))
            for oc in range(6):
                b.wait("tensor", "qe_done", b.c("qe_done"))

                def t_oc(q, oc=oc):
                    mm = None
                    for di in range(16):
                        mm = q.matmul(
                            t.ps_qkv[:, :],
                            t.wqkv_sb[:, di * 768 + oc * 128:
                                      di * 768 + oc * 128 + 128],
                            t.xn_t[:, di * 512:(di + 1) * 512],
                            start=(di == 0), stop=(di == 15))
                    return mm
                qmv = b.do("tensor", t_oc, sem="qm_done")
                b.wait("scalar", "qm_done", qmv)
                dest = (t.xq, t.xk, t.xv)[oc // 2]
                h = oc % 2
                b.do("scalar", lambda q, dest=dest, h=h, tile=tile: q.copy(
                    dest[:, h * S + tile * 512:h * S + tile * 512 + 512],
                    t.ps_qkv[:, :]), sem="qe_done")

    emit_ln_stream(b, t, t.x, 64, on_group)

    def on_group_loc(g, lnv, xsl):
        pa, pb, tpv = emit_transposes(b, t, lnv, xsl, g)
        b.wait("vector", "tp_done", tpv)

        def v_evac(q, g=g, pa=pa, pb=pb):
            d3 = t.xn_loc.ap().rearrange("p (di tt) -> p di tt", di=16)
            q.tensor_copy(d3[:, 0:8, g * 128:g * 128 + 128],
                          pa.ap().rearrange("p (di tt) -> p di tt", di=8))
            return q.tensor_copy(
                d3[:, 8:16, g * 128:g * 128 + 128],
                pb.ap().rearrange("p (di tt) -> p di tt", di=8))
        b.do("vector", v_evac, sem="xp_evac")

    emit_ln_stream(b, t, t.xloc, 8, on_group_loc, rms_groups=list(range(8)))
    b.barrier()

    def slot(kt):
        return t.ps_cs[kt % 4], 32 * (kt // 4)

    mz = b.do("vector", lambda v: v.memset(t.E2[:, 0:512], 0.0), sem="const2")
    for h in range(2):
        if h > 0:
            b.wait("tensor", "cs_evac", b.c("cs_evac"))
        b.wait("tensor", "const", b.c("const"))
        b.wait("tensor", "const2", mz)

        def t_zero(q):
            mm = None
            for kt in range(16):
                pt, row = slot(kt)
                tp = (0, row) if row else None
                mm = q.matmul(pt[row:row + 1, :], t.zeros1[:, :],
                              t.E2[:, 0:512], start=True, stop=False,
                              tile_position=tp)
            return mm
        b.do("tensor", t_zero, sem="zs")

        for qc in range(64):
            ebuf = qc % 2
            for sub in range(8):
                psc = t.ps_scA if sub % 2 == 0 else t.ps_scB
                b.wait("tensor", "exp_done", b.c("exp_done") - 1)

                def t_sc(q, h=h, qc=qc, sub=sub, psc=psc):
                    mm = None
                    for half in range(2):
                        kt = sub * 2 + half
                        mm = q.matmul(
                            psc[:, half * 512:half * 512 + 512],
                            t.xq[:, h * S + qc * 128:h * S + qc * 128 + 128],
                            t.xk[:, h * S + kt * 512:h * S + kt * 512 + 512],
                            start=True, stop=True)
                    return mm
                sdv = b.do("tensor", t_sc, sem="sc_done")
                b.wait("scalar", "sc_done", sdv)
                if sub == 0:
                    b.wait("scalar", "rmm_done", b.c("rmm_done") - 1)

                def s_exp(q, ebuf=ebuf, sub=sub, psc=psc):
                    return q.activation(
                        out=t.E2[:, ebuf * S + sub * 1024:
                                 ebuf * S + sub * 1024 + 1024],
                        in_=psc[:, :], func=AF.Exp,
                        scale=f32(INV_SQRT_HD), bias=f32(-C0_SHIFT),
                        accum_out=t.Zp[:, ebuf * 8 + sub:ebuf * 8 + sub + 1])
                b.do("scalar", s_exp, sem="exp_done")
            b.wait("vector", "exp_done", b.c("exp_done"))
            b.wait("vector", "rmm_done", b.c("rmm_done") - 1)

            b.fdo("vector", lambda q, ebuf=ebuf: q.tensor_reduce(
                t.zwork[:, 0:1], t.Zp[:, ebuf * 8:ebuf * 8 + 8],
                axis=AX.X, op=ALU.add))
            b.fdo("vector", lambda q: q.reciprocal(
                t.zwork[:, 1:2], t.zwork[:, 0:1]))
            zdv = b.do("vector", lambda q, ebuf=ebuf: q.tensor_copy(
                t.rbf[:, ebuf:ebuf + 1], t.zwork[:, 1:2]), sem="z_done")
            b.wait("tensor", "z_done", zdv)
            if qc == 0:
                b.wait("tensor", "zs", b.c("zs"))

            def t_rmm(q, qc=qc, ebuf=ebuf):
                mm = None
                for kt in range(16):
                    pt, row = slot(kt)
                    tp = (0, row) if row else None
                    mm = q.matmul(pt[row:row + 1, :], t.rbf[:, ebuf:ebuf + 1],
                                  t.E2[:, ebuf * S + kt * 512:
                                       ebuf * S + kt * 512 + 512],
                                  start=False, stop=(qc == 63),
                                  tile_position=tp)
                return mm
            b.do("tensor", t_rmm, sem="rmm_done")

        b.wait("scalar", "rmm_done", b.c("rmm_done"))
        b.wait("scalar", "bc_done", b.c("bc_done"))
        for kt in range(16):
            pt, row = slot(kt)
            b.do("scalar", lambda q, pt=pt, row=row, kt=kt: q.copy(
                t.cs_sb[0:1, kt * 512:kt * 512 + 512], pt[row:row + 1, :]),
                sem="cs_evac")
        b.wait("tensor", "cs_evac", b.c("cs_evac"))
        b.wait("vector", "st_out", b.c("st_out"))
        for kt in range(16):
            if kt > 0:
                b.wait("tensor", "vs_done", b.c("vs_done"))
            bcv = b.do("tensor", lambda q, kt=kt: q.matmul(
                t.ps_scA[:, 0:512], t.ones1[:, :],
                t.cs_sb[0:1, kt * 512:kt * 512 + 512], start=True, stop=True),
                sem="bc_done")
            b.wait("vector", "bc_done", bcv)
            b.do("vector", lambda q, h=h, kt=kt: q.tensor_tensor(
                t.val_sb[:, kt * 512:kt * 512 + 512],
                t.xv[:, h * S + kt * 512:h * S + kt * 512 + 512],
                t.ps_scA[:, 0:512], op=ALU.mult), sem="vs_done")
        b.wait("tensor", "vs_done", b.c("vs_done"))
        b.wait("gpsimd", "vs_done", b.c("vs_done"))
        b.dma("gpsimd", t.val_o[h * 128:(h + 1) * 128, :], t.val_sb[:, :],
              sem="st_out")
    b.barrier()

    emit_ffn_moe(b, t, n_exp=8, hid_per_exp=1024, use_rms=True, scale_out=4.0)
    b.wait("gpsimd", "st_out", b.c("st_out"))


def emit_ffn_moe(b, t, *, n_exp, hid_per_exp, use_rms, scale_out):
    H = n_exp * hid_per_exp
    n_hc = H // 128

    lw = b.dma("gpsimd", t.rw_sb[:, :].rearrange("p (dc e) -> p dc e", dc=16),
               t.rw.rearrange("(dc p) e -> p dc e", p=128))
    if use_rms:
        lw = b.dma("gpsimd", t.nw_sb[:, :], t.nw)
    b.wait("vector", "ld_gpsimd", lw)
    b.wait("tensor", "ld_gpsimd", lw)

    for di in range(16):
        b.wait("vector", "lg_mm", b.c("lg_mm") - 1)
        if use_rms:
            xwv = b.do("vector", lambda v, di=di: v.tensor_scalar(
                t.xnw2[:, (di % 2) * TLOC:(di % 2 + 1) * TLOC],
                t.xn_loc[:, di * TLOC:(di + 1) * TLOC],
                t.nw_sb[:, di:di + 1], None, op0=ALU.mult), sem="xnw_done")
        else:
            xwv = b.do("vector", lambda v, di=di: v.tensor_copy(
                t.xnw2[:, (di % 2) * TLOC:(di % 2 + 1) * TLOC],
                t.xn_loc[:, di * TLOC:(di + 1) * TLOC]), sem="xnw_done")
        b.wait("tensor", "xnw_done", xwv)

        def t_lg(q, di=di):
            mm = None
            for half in range(2):
                mm = q.matmul(t.ps_lg[:, half * 512:half * 512 + 512],
                              t.rw_sb[:, di * 8:di * 8 + 8],
                              t.xnw2[:, (di % 2) * TLOC + half * 512:
                                     (di % 2) * TLOC + half * 512 + 512],
                              start=(di == 0), stop=(di == 15))
            return mm
        b.do("tensor", t_lg, sem="lg_mm")
    b.wait("scalar", "lg_mm", b.c("lg_mm"))
    lev = b.do("scalar", lambda q: q.copy(t.lg_sb[:, :], t.ps_lg[:, :]),
               sem="lg_evac")
    b.wait("tensor", "lg_evac", lev)

    def t_ltr(q):
        mm = None
        for lg in range(8):
            mm = q.transpose(t.ps_t8[:, lg * 8:lg * 8 + 8],
                             t.lg_sb[:, lg * 128:lg * 128 + 128],
                             t.ident[0:8, 0:8])
        return mm
    ltv = b.do("tensor", t_ltr, sem="lg_tr")
    b.wait("vector", "lg_tr", ltv)
    ltev = b.do("vector", lambda v: v.tensor_copy(t.ltm[:, :], t.ps_t8[:, :]),
                sem="lg_tr_ev")

    b.wait("vector", "lg_tr_ev", ltev)
    if use_rms:
        b.wait("vector", "rms_done", b.c("rms_done"))
    for lg in range(8):
        col = (lambda j, lg=lg: t.r_sc[:, 8 * lg + j:8 * lg + j + 1])
        l8 = (lambda lg=lg: t.ltm[:, lg * 8:lg * 8 + 8])
        x8 = (lambda lg=lg: t.exps[:, lg * 8:lg * 8 + 8])
        e8 = (lambda lg=lg: t.eqs[:, lg * 8:lg * 8 + 8])
        t8 = (lambda lg=lg: t.tts[:, lg * 8:lg * 8 + 8])
        c8 = (lambda lg=lg: t.cwtm[:, lg * 8:lg * 8 + 8])
        if use_rms:
            rsv = b.do("vector", lambda v, col=col, lg=lg: v.tensor_scalar(
                col(0), t.rms_sumsq[:, lg:lg + 1], 1.0 / DM, 1e-6,
                op0=ALU.mult, op1=ALU.add), sem="rr1")
            b.wait("scalar", "rr1", rsv)
            rs2 = b.do("scalar", lambda q, col=col: q.sqrt(col(1), col(0)),
                       sem="rr2")
            b.wait("vector", "rr2", rs2)
            b.fdo("vector", lambda v, col=col: v.reciprocal(col(2), col(1)))
            b.fdo("vector", lambda v, col=col, l8=l8: v.tensor_scalar(
                l8(), l8(), col(2), None, op0=ALU.mult))

        b.fdo("vector", lambda v, col=col, l8=l8: v.tensor_reduce(
            col(3), l8(), axis=AX.X, op=ALU.max))
        mxv = b.do("vector", lambda v, col=col: v.tensor_scalar_mul(
            col(3), col(3), -1.0), sem="tq1")
        b.wait("scalar", "tq1", mxv)
        e8v = b.do("scalar", lambda q, col=col, l8=l8, x8=x8: q.activation(
            out=x8(), in_=l8(), func=AF.Exp, bias=col(3)), sem="tq2")
        b.wait("vector", "tq2", e8v)

        for step in [
            lambda v, col=col, x8=x8: v.tensor_reduce(
                col(4), x8(), axis=AX.X, op=ALU.max),
            lambda v, col=col, x8=x8, e8=e8: v.tensor_scalar(
                e8(), x8(), col(4), None, op0=ALU.is_ge),
            lambda v, x8=x8, e8=e8, t8=t8: v.tensor_tensor(
                t8(), x8(), e8(), op=ALU.mult),
            lambda v, x8=x8, t8=t8: v.tensor_sub(t8(), x8(), t8()),
            lambda v, col=col, t8=t8: v.tensor_reduce(
                col(5), t8(), axis=AX.X, op=ALU.max),
            lambda v, col=col, x8=x8, e8=e8: v.tensor_scalar(
                e8(), x8(), col(5), None, op0=ALU.is_ge),
            lambda v, col=col: v.tensor_add(col(6), col(4), col(5)),
            lambda v, col=col: v.reciprocal(col(7), col(6)),
            lambda v, x8=x8, e8=e8, t8=t8: v.tensor_tensor(
                t8(), x8(), e8(), op=ALU.mult),
        ]:
            b.fdo("vector", step)
        ctv = b.do("vector", lambda v, col=col, t8=t8, c8=c8: v.tensor_scalar(
            c8(), t8(), col(7), float(scale_out), op0=ALU.mult, op1=ALU.mult),
            sem="cw_tm_done")
        b.wait("tensor", "cw_tm_done", ctv)
        b.do("tensor", lambda q, lg=lg, c8=c8: q.transpose(
            t.ps_ct[:, lg * 128:lg * 128 + 128], c8(), t.ident[:, :]),
            sem="cw_tr")
    b.wait("vector", "cw_tr", b.c("cw_tr"))
    cfev = b.do("vector", lambda v: v.tensor_copy(t.cw8[:, :], t.ps_ct[:, :]),
                sem="cw_fm_ev")
    b.wait("gpsimd", "cw_fm_ev", cfev)
    crv = None
    for e in range(8):
        crv = b.dma("gpsimd", t.cw_row[0:1, e * TLOC:(e + 1) * TLOC],
                    t.cw8[e:e + 1, :], sem="ld_cwrow")
    b.wait("tensor", "ld_cwrow", crv)
    for e in range(n_exp):
        for half in range(2):
            b.wait("tensor", "cwb_ev", b.c("cwb_ev"))
            bcv = b.do("tensor", lambda q, e=e, half=half: q.matmul(
                t.ps_bc[:, :], t.ones1[:, :],
                t.cw_row[0:1, e * TLOC + half * 512:e * TLOC + half * 512 + 512],
                start=True, stop=True), sem="cwb_mm")
            b.wait("vector", "cwb_mm", bcv)
            b.do("vector", lambda v, e=e, half=half: v.tensor_copy(
                t.cwb[:, e * TLOC + half * 512:e * TLOC + half * 512 + 512],
                t.ps_bc[:, :]), sem="cwb_ev")

    b.wait("tensor", "cwb_ev", b.c("cwb_ev"))
    for hc in range(n_hc):
        e = hc // (hid_per_exp // 128)
        wbuf = hc % 2
        b.wait("gpsimd", "gu_mm", b.c("gu_mm") - 1)
        b.dma("gpsimd",
              t.gwu2[:, wbuf * 4096:wbuf * 4096 + 2048].rearrange(
                  "p (dc ff) -> p dc ff", dc=16),
              t.gw[:, hc * 128:(hc + 1) * 128].rearrange(
                  "(dc p) ff -> p dc ff", p=128), sem="ld_w")
        gl = b.dma("gpsimd",
                   t.gwu2[:, wbuf * 4096 + 2048:wbuf * 4096 + 4096].rearrange(
                       "p (dc ff) -> p dc ff", dc=16),
                   t.uw[:, hc * 128:(hc + 1) * 128].rearrange(
                       "(dc p) ff -> p dc ff", p=128), sem="ld_w")
        b.wait("tensor", "ld_w", gl)
        psg = t.ps_gA if hc % 2 == 0 else t.ps_gB
        psu = t.ps_uA if hc % 2 == 0 else t.ps_uB
        b.wait("tensor", "silu_done", b.c("silu_done") - 1)

        def t_g(q, wbuf=wbuf, psg=psg):
            mm = None
            for di in range(16):
                for half in range(2):
                    mm = q.matmul(
                        psg[:, half * 512:half * 512 + 512],
                        t.gwu2[:, wbuf * 4096 + di * 128:
                               wbuf * 4096 + di * 128 + 128],
                        t.xn_loc[:, di * TLOC + half * 512:
                                 di * TLOC + half * 512 + 512],
                        start=(di == 0), stop=(di == 15))
            return mm
        gmv = b.do("tensor", t_g, sem="g_mm")
        b.wait("tensor", "aprod_done", b.c("aprod_done") - 1)

        def t_u(q, wbuf=wbuf, psu=psu):
            mm = None
            for di in range(16):
                for half in range(2):
                    mm = q.matmul(
                        psu[:, half * 512:half * 512 + 512],
                        t.gwu2[:, wbuf * 4096 + 2048 + di * 128:
                               wbuf * 4096 + 2048 + di * 128 + 128],
                        t.xn_loc[:, di * TLOC + half * 512:
                                 di * TLOC + half * 512 + 512],
                        start=(di == 0), stop=(di == 15))
            return mm
        umv = b.do("tensor", t_u, sem="gu_mm")
        b.wait("scalar", "g_mm", gmv)
        slv = b.do("scalar", lambda q, psg=psg: q.activation(
            out=t.gs_sb[:, :], in_=psg[:, :], func=AF.Silu), sem="silu_done")
        b.wait("vector", "gu_mm", umv)
        b.wait("vector", "silu_done", slv)
        abuf = hc % 2
        b.wait("vector", "a_st", b.c("a_st") - 16)

        def v_a(q, e=e, abuf=abuf, psu=psu):
            q.tensor_tensor(t.aw2[:, abuf * TLOC:(abuf + 1) * TLOC],
                            t.gs_sb[:, :], psu[:, :], op=ALU.mult)
            return q.tensor_tensor(
                t.aw2[:, abuf * TLOC:(abuf + 1) * TLOC],
                t.aw2[:, abuf * TLOC:(abuf + 1) * TLOC],
                t.cwb[:, e * TLOC:(e + 1) * TLOC], op=ALU.mult)
        apv = b.do("vector", v_a, sem="aprod_done")
        b.wait("gpsimd", "aprod_done", apv)
        b.dma("gpsimd", t.a_dram[hc * 128:(hc + 1) * 128, :],
              t.aw2[:, abuf * TLOC:(abuf + 1) * TLOC], sem="a_st")

    b.wait("sync", "a_st", b.c("a_st"))
    for fog in range(4):
        b.wait("gpsimd", "dn_rhs", b.c("dn_rhs"))
        dl = None
        for part in range(4):
            dl = b.dma(
                "gpsimd",
                t.dw_sb[:, part * 16 * 512:(part + 1) * 16 * 512].rearrange(
                    "p (hcc ff) -> p hcc ff", hcc=16),
                t.dwn.rearrange("(hcc p) o -> p hcc o", p=128)[
                    :, part * 16:(part + 1) * 16,
                    fog * 512:(fog + 1) * 512], sem="ld_dw")
        b.wait("tensor", "ld_dw", dl)
        b.wait("tensor", "y_silu", b.c("y_silu"))
        for hc in range(n_hc):
            sbuf_i = hc % 2
            b.wait("sync", "dn_rhs", b.c("dn_rhs") - 1)
            alv = b.dma("sync", t.astr2[:, sbuf_i * TLOC:(sbuf_i + 1) * TLOC],
                        t.a_dram[hc * 128:(hc + 1) * 128, :], sem="ld_a")
            b.wait("tensor", "ld_a", alv)

            def t_dn(q, hc=hc, sbuf_i=sbuf_i, n_hc=n_hc):
                mm = None
                for fo_in in range(4):
                    for half in range(2):
                        mm = q.matmul(
                            t.ps_y[fo_in][:, half * 512:half * 512 + 512],
                            t.dw_sb[:, hc * 512 + fo_in * 128:
                                    hc * 512 + fo_in * 128 + 128],
                            t.astr2[:, sbuf_i * TLOC + half * 512:
                                    sbuf_i * TLOC + half * 512 + 512],
                            start=(hc == 0), stop=(hc == n_hc - 1))
                return mm
            b.do("tensor", t_dn, sem="dn_rhs")
        b.wait("scalar", "dn_rhs", b.c("dn_rhs"))
        for fo_in in range(4):
            fo = fog * 4 + fo_in
            obuf = fo % 2
            b.wait("scalar", "g0_st", b.c("g0_st") - 16)
            ysv = b.do("scalar", lambda q, fo_in=fo_in, obuf=obuf: q.activation(
                out=t.g0t[:, obuf * TLOC:(obuf + 1) * TLOC],
                in_=t.ps_y[fo_in][:, :], func=AF.Silu), sem="y_silu")
            b.wait("gpsimd", "y_silu", ysv)
            b.dma("gpsimd", t.g0_o[fo * 128:(fo + 1) * 128, :],
                  t.g0t[:, obuf * TLOC:(obuf + 1) * TLOC], sem="g0_st")
    b.wait("gpsimd", "g0_st", b.c("g0_st"))



_IDENT = np.eye(128, dtype=np.float32).astype(BF)


def host_inputs_A(inputs):
    x = np.ascontiguousarray(inputs["L0_attn_x"].reshape(S, DM))
    wqkv = inputs["L0_attn_w_qkv"]
    gw = np.ascontiguousarray(
        inputs["L0_ffn_gate_w"].transpose(1, 0, 2).reshape(DM, 8192))
    uw = np.ascontiguousarray(
        inputs["L0_ffn_up_w"].transpose(1, 0, 2).reshape(DM, 8192))
    dwn = np.ascontiguousarray(inputs["L0_ffn_down_w"].reshape(8192, DM))
    rw = np.ascontiguousarray(inputs["L0_ffn_router_w"].T)
    nw = np.ascontiguousarray(inputs["L0_ffn_norm_w"].reshape(16, 128).T)
    maps = []
    for c in range(NCORES):
        sl = np.concatenate([
            wqkv[:, 256 * c:256 * (c + 1)],
            wqkv[:, 2048 + 256 * c:2048 + 256 * (c + 1)],
            wqkv[:, 4096 + 256 * c:4096 + 256 * (c + 1)]], axis=1)
        maps.append({
            "x": x, "xloc": x[TLOC * c:TLOC * (c + 1)],
            "wqkv": np.ascontiguousarray(sl), "rw": rw, "nw": nw,
            "gw": gw, "uw": uw, "dwn": dwn, "ident_c": _IDENT})
    return maps





def make_tensors_B(nc, es):
    t = types.SimpleNamespace()
    t.xfm = nc.dram_tensor("xfm", [DM, TLOC], F32, kind="ExternalInput").ap()
    t.val = nc.dram_tensor("val", [DM, TLOC], BF16, kind="ExternalInput").ap()
    t.g0 = nc.dram_tensor("g0", [DM, TLOC], BF16, kind="ExternalInput").ap()
    t.wo = nc.dram_tensor("wo", [DM, DM], F32, kind="ExternalInput").ap()
    t.x1_o = nc.dram_tensor("x1out", [DM, TLOC], F32, kind="ExternalOutput").ap()
    sb = es.enter_context
    t.wo_sb = sb(nc.sbuf_tensor("wo_sb", [128, 16 * DM], BF16))
    t.val_sb = sb(nc.sbuf_tensor("val_sb", [128, 16 * TLOC], BF16))
    t.g0_sb = sb(nc.sbuf_tensor("g0_sb", [128, 16 * TLOC], BF16))
    t.x_sb = sb(nc.sbuf_tensor("x_sb", [128, 16 * TLOC], F32))
    t.x1_sb = sb(nc.sbuf_tensor("x1_sb", [128, 2 * TLOC], F32))
    t.sg_sb = sb(nc.sbuf_tensor("sg_sb", [128, TLOC], F32))
    t.ps_s = [sb(nc.psum_tensor(f"ps_s{i}", [128, TLOC], F32)) for i in range(2)]
    return t


def emit_B(b, t):
    lv = b.dma("gpsimd", t.wo_sb[:, :].rearrange("p (dc o) -> p dc o", dc=16),
               t.wo.rearrange("(dc p) o -> p dc o", p=128))
    b.dma("sync", t.val_sb[:, :].rearrange("p (dc tt) -> p dc tt", dc=16),
          t.val.rearrange("(dc p) tt -> p dc tt", p=128))
    b.dma("sync", t.g0_sb[:, :].rearrange("p (dc tt) -> p dc tt", dc=16),
          t.g0.rearrange("(dc p) tt -> p dc tt", p=128))
    xv = b.dma("sync", t.x_sb[:, :].rearrange("p (dc tt) -> p dc tt", dc=16),
               t.xfm.rearrange("(dc p) tt -> p dc tt", p=128))
    b.wait("tensor", "ld_gpsimd", lv)
    b.wait("tensor", "ld_sync", xv)
    for fo in range(16):
        ps = t.ps_s[fo % 2]
        b.wait("tensor", "sg_done", b.c("sg_done") - 1)

        def t_s(q, fo=fo, ps=ps):
            mm = None
            for fi in range(16):
                for half in range(2):
                    mm = q.matmul(
                        ps[:, half * 512:half * 512 + 512],
                        t.wo_sb[:, fi * DM + fo * 128:fi * DM + fo * 128 + 128],
                        t.val_sb[:, fi * TLOC + half * 512:
                                 fi * TLOC + half * 512 + 512],
                        start=(fi == 0), stop=(fi == 15))
            return mm
        smv = b.do("tensor", t_s, sem="s_mm")
        b.wait("vector", "s_mm", smv)
        b.wait("vector", "x1_st", b.c("x1_st") - 16)

        def v_comb(q, fo=fo, ps=ps):
            q.tensor_tensor(t.sg_sb[:, :], ps[:, :],
                            t.g0_sb[:, fo * TLOC:(fo + 1) * TLOC], op=ALU.mult)
            return q.tensor_add(t.x1_sb[:, (fo % 2) * TLOC:(fo % 2 + 1) * TLOC],
                                t.x_sb[:, fo * TLOC:(fo + 1) * TLOC],
                                t.sg_sb[:, :])
        cbv = b.do("vector", v_comb, sem="sg_done")
        b.wait("gpsimd", "sg_done", cbv)
        b.dma("gpsimd", t.x1_o[fo * 128:(fo + 1) * 128, :],
              t.x1_sb[:, (fo % 2) * TLOC:(fo % 2 + 1) * TLOC], sem="x1_st")
    b.wait("gpsimd", "x1_st", b.c("x1_st"))


def host_inputs_B(inputs, val_parts, g0_parts):
    x = inputs["L0_attn_x"].reshape(S, DM)
    val_full = np.concatenate(val_parts, axis=0)
    maps = []
    for c in range(NCORES):
        xfm = np.ascontiguousarray(x[TLOC * c:TLOC * (c + 1)].T)
        maps.append({
            "xfm": xfm,
            "val": np.ascontiguousarray(val_full[:, TLOC * c:TLOC * (c + 1)]),
            "g0": g0_parts[c],
            "wo": inputs["L0_attn_w_o"]})
    return maps





N_HC1 = 88
N_HCS = 44
SCALE1 = float(1.0 / (50.0 * np.sqrt(128.0)))


def make_tensors_C(nc, es):
    t = types.SimpleNamespace()
    t.x = nc.dram_tensor("x", [S, DM], F32, kind="ExternalInput").ap()
    t.xloc = nc.dram_tensor("xloc", [TLOC, DM], F32, kind="ExternalInput").ap()
    t.wqkv = nc.dram_tensor("wqkv", [DM, 512], F32, kind="ExternalInput").ap()
    t.rw = nc.dram_tensor("rw", [DM, 8], F32, kind="ExternalInput").ap()
    t.gw = nc.dram_tensor("gw", [DM, N_HC1 * 128], F32, kind="ExternalInput").ap()
    t.uw = nc.dram_tensor("uw", [DM, N_HC1 * 128], F32, kind="ExternalInput").ap()
    t.dwn = nc.dram_tensor("dwn", [N_HC1 * 128, DM], F32,
                           kind="ExternalInput").ap()
    t.sgw_d = nc.dram_tensor("sgw", [128, 16], F32, kind="ExternalInput").ap()
    t.swg = nc.dram_tensor("swg", [DM, N_HCS * 128], F32,
                           kind="ExternalInput").ap()
    t.swu = nc.dram_tensor("swu", [DM, N_HCS * 128], F32,
                           kind="ExternalInput").ap()
    t.swd = nc.dram_tensor("swd", [N_HCS * 128, DM], F32,
                           kind="ExternalInput").ap()
    t.cos_d = nc.dram_tensor("cos_c", [128, S], F32, kind="ExternalInput").ap()
    t.sin_d = nc.dram_tensor("sin_c", [128, S], F32, kind="ExternalInput").ap()
    t.mask_d = nc.dram_tensor("mask_c", [128, 8 * 1024], BF16,
                              kind="ExternalInput").ap()
    t.ident_c = nc.dram_tensor("ident_c", [128, 128], BF16,
                               kind="ExternalInput").ap()
    t.out1_o = nc.dram_tensor("out1", [256, S], BF16, kind="ExternalOutput").ap()
    t.g1_o = nc.dram_tensor("g1out", [DM, TLOC], BF16, kind="ExternalOutput").ap()
    t.a_dram = nc.dram_tensor("a_stage", [N_HC1 * 128, TLOC], BF16).ap()
    t.sh_dram = nc.dram_tensor("sh_stage", [N_HCS * 128, TLOC], BF16).ap()

    sb = es.enter_context
    t.zeros1 = sb(nc.sbuf_tensor("zeros1", [128, 1], BF16))
    t.ones1 = sb(nc.sbuf_tensor("ones1", [1, 128], BF16))
    t.onescol = sb(nc.sbuf_tensor("onescol", [128, 1], BF16))
    t.ident = sb(nc.sbuf_tensor("ident", [128, 128], BF16))
    t.stats = sb(nc.sbuf_tensor("stats", [128, 8], F32))
    t.rms_sumsq = sb(nc.sbuf_tensor("rms_sumsq", [128, 8], F32))
    t.xq = sb(nc.sbuf_tensor("xq", [128, 2 * S], BF16))
    t.xk = sb(nc.sbuf_tensor("xk", [128, S], BF16))
    t.xv = sb(nc.sbuf_tensor("xv", [128, S], BF16))

    with contextlib.ExitStack() as s12:
        t.xn_loc = s12.enter_context(
            nc.sbuf_tensor("xn_loc", [128, 16 * TLOC], BF16))
        with contextlib.ExitStack() as s1:
            t.sq_scr = s1.enter_context(nc.sbuf_tensor("sq_scr", [128, DM], BF16))
            t.wqkv_sb = s1.enter_context(
                nc.sbuf_tensor("wqkv_sb", [128, 16 * 512], BF16))
            t.x_tm2 = s1.enter_context(
                nc.sbuf_tensor("x_tm2", [128, 2 * DM], F32))
            t.xn_tm2 = s1.enter_context(
                nc.sbuf_tensor("xn_tm2", [128, 2 * DM], BF16))
            t.xn_t = s1.enter_context(
                nc.sbuf_tensor("xn_t", [128, 16 * 512], BF16))
            t.cossin2 = s1.enter_context(
                nc.sbuf_tensor("cossin2", [128, 4 * 512], F32))
            t.rot_sb = s1.enter_context(
                nc.sbuf_tensor("rot_sb", [128, 512], F32))
            t.t1_sb = s1.enter_context(nc.sbuf_tensor("t1_sb", [128, 512], F32))
            t.ps_tr0 = s1.enter_context(
                nc.psum_tensor("ps_tr0", [128, 1024], BF16))
            t.ps_tr1 = s1.enter_context(
                nc.psum_tensor("ps_tr1", [128, 1024], BF16))
            t.ps_qkv = s1.enter_context(
                nc.psum_tensor("ps_qkv", [128, 512], F32))
            t.ps_v = s1.enter_context(nc.psum_tensor("ps_v", [128, 128], F32))
        with contextlib.ExitStack() as s2:
            t.rw_sb = s2.enter_context(nc.sbuf_tensor("rw_sb", [128, 128], BF16))
            t.sgw_sb = s2.enter_context(nc.sbuf_tensor("sgw_sb", [128, 16], BF16))
            t.cwb = s2.enter_context(nc.sbuf_tensor("cwb", [128, 8 * TLOC], BF16))
            t.sgb = s2.enter_context(nc.sbuf_tensor("sgb", [128, TLOC], BF16))
            t.sg_row = s2.enter_context(nc.sbuf_tensor("sg_row", [1, TLOC], BF16))
            with contextlib.ExitStack() as s2a:
                t.r_sc = s2a.enter_context(nc.sbuf_tensor("r_sc", [128, 64], F32))
                t.lg_sb = s2a.enter_context(nc.sbuf_tensor("lg_sb", [8, TLOC], BF16))
                t.ltm = s2a.enter_context(nc.sbuf_tensor("ltm", [128, 64], F32))
                t.exps = s2a.enter_context(nc.sbuf_tensor("exps", [128, 64], F32))
                t.eqs = s2a.enter_context(nc.sbuf_tensor("eqs", [128, 64], F32))
                t.tts = s2a.enter_context(nc.sbuf_tensor("tts", [128, 64], F32))
                t.cwtm = s2a.enter_context(nc.sbuf_tensor("cwtm", [128, 64], BF16))
                t.cw8 = s2a.enter_context(nc.sbuf_tensor("cw8", [8, TLOC], BF16))
                t.cw_row = s2a.enter_context(
                    nc.sbuf_tensor("cw_row", [1, 8 * TLOC], BF16))
                t.xnw2 = s2a.enter_context(
                    nc.sbuf_tensor("xnw2", [128, 2 * TLOC], BF16))
            t.gwu2 = s2.enter_context(
                nc.sbuf_tensor("gwu2", [128, 2 * 4096], BF16))
            t.gs_sb = s2.enter_context(nc.sbuf_tensor("gs_sb", [128, TLOC], BF16))
            t.aw2 = s2.enter_context(nc.sbuf_tensor("aw2", [128, 2 * TLOC], BF16))
            t.astr2 = s2.enter_context(
                nc.sbuf_tensor("astr2", [128, 2 * TLOC], BF16))
            t.dw_sb = s2.enter_context(
                nc.sbuf_tensor("dw_sb", [128, 22 * 512], BF16))
            t.moe_t = s2.enter_context(
                nc.sbuf_tensor("moe_t", [128, 4 * TLOC], BF16))
            t.cmb1 = s2.enter_context(nc.sbuf_tensor("cmb1", [128, TLOC], F32))
            t.cmb2 = s2.enter_context(nc.sbuf_tensor("cmb2", [128, TLOC], F32))
            t.g0t = s2.enter_context(nc.sbuf_tensor("g0t", [128, 2 * TLOC], BF16))
            with contextlib.ExitStack() as s2r:
                t.ps_lg = s2r.enter_context(
                    nc.psum_tensor("ps_lg", [8, TLOC], F32))
                t.ps_sg = s2r.enter_context(
                    nc.psum_tensor("ps_sg", [1, TLOC], F32))
                t.ps_t8 = s2r.enter_context(
                    nc.psum_tensor("ps_t8", [128, 64], BF16))
                t.ps_ct = s2r.enter_context(
                    nc.psum_tensor("ps_ct", [8, TLOC], BF16))
                t.ps_bc = s2r.enter_context(
                    nc.psum_tensor("ps_bc", [128, 512], F32))
            with contextlib.ExitStack() as s2e:
                t.ps_gA = s2e.enter_context(
                    nc.psum_tensor("ps_gA", [128, TLOC], F32))
                t.ps_uA = s2e.enter_context(
                    nc.psum_tensor("ps_uA", [128, TLOC], F32))
                t.ps_gB = s2e.enter_context(
                    nc.psum_tensor("ps_gB", [128, TLOC], F32))
                t.ps_uB = s2e.enter_context(
                    nc.psum_tensor("ps_uB", [128, TLOC], F32))
            with contextlib.ExitStack() as s2d:
                t.ps_y = [s2d.enter_context(
                    nc.psum_tensor(f"ps_y{i}", [128, TLOC], F32))
                    for i in range(4)]
    with contextlib.ExitStack() as s3:
        t.mask_sb = s3.enter_context(
            nc.sbuf_tensor("mask_sb", [128, 8 * 1024], BF16))
        t.E2 = s3.enter_context(nc.sbuf_tensor("E2c", [128, 2 * 1024], BF16))
        t.tn_sb = s3.enter_context(nc.sbuf_tensor("tn_sb", [128, 1024], BF16))
        t.S_acc = s3.enter_context(nc.sbuf_tensor("S_acc", [128, 1024], F32))
        t.S_bf = s3.enter_context(nc.sbuf_tensor("S_bf", [128, 1024], BF16))
        t.zr_f = s3.enter_context(nc.sbuf_tensor("zr_f", [1, 1024], F32))
        t.zr_f2 = s3.enter_context(nc.sbuf_tensor("zr_f2", [1, 1024], F32))
        t.zr_b = s3.enter_context(nc.sbuf_tensor("zr_b", [1, 1024], BF16))
        t.o_raw = s3.enter_context(nc.sbuf_tensor("o_raw", [128, 1024], F32))
        t.o_bf = s3.enter_context(nc.sbuf_tensor("o_bf", [128, S], BF16))
        t.ps_scA = s3.enter_context(nc.psum_tensor("ps_scA", [128, 1024], F32))
        t.ps_scB = s3.enter_context(nc.psum_tensor("ps_scB", [128, 1024], F32))
        t.ps_o = s3.enter_context(nc.psum_tensor("ps_o", [128, 1024], F32))
        t.ps_z = s3.enter_context(nc.psum_tensor("ps_z", [1, 1024], F32))
    return t


def emit_C(b, t):
    emit_consts(b, t)
    b.do("vector", lambda v: v.memset(t.onescol[:, :], 1.0), sem="const")
    wv = b.dma("gpsimd", t.wqkv_sb[:, :].rearrange("p (dc o) -> p dc o", dc=16),
               t.wqkv.rearrange("(dc p) o -> p dc o", p=128))
    b.wait("tensor", "ld_gpsimd", wv)
    b.wait("tensor", "ld_const", b.c("ld_const"))

    def on_group(g, lnv, xsl):
        tile, gi = g // 4, g % 4
        pa, pb, tpv = emit_transposes(b, t, lnv, xsl, g)
        b.wait("vector", "tp_done", tpv)
        if gi == 0:
            b.wait("vector", "qm_done", 3 * tile)
            b.wait("vector", "v_mm", 4 * tile)

        def v_evac(q, gi=gi, pa=pa, pb=pb):
            d3 = t.xn_t.ap().rearrange("p (di tt) -> p di tt", di=16)
            q.tensor_copy(d3[:, 0:8, gi * 128:gi * 128 + 128],
                          pa.ap().rearrange("p (di tt) -> p di tt", di=8))
            return q.tensor_copy(
                d3[:, 8:16, gi * 128:gi * 128 + 128],
                pb.ap().rearrange("p (di tt) -> p di tt", di=8))
        b.do("vector", v_evac, sem="xp_evac")

        if gi == 3:
            b.wait("tensor", "xp_evac", b.c("xp_evac"))
            b.wait("sync", "rope_done", b.c("rope_done") - 3)
            csv = b.dma("sync", t.cossin2[:, (tile % 2) * 1024:
                                          (tile % 2) * 1024 + 512],
                        t.cos_d[:, tile * 512:(tile + 1) * 512], sem="ld_cs")
            csv = b.dma("sync", t.cossin2[:, (tile % 2) * 1024 + 512:
                                          (tile % 2) * 1024 + 1024],
                        t.sin_d[:, tile * 512:(tile + 1) * 512], sem="ld_cs")
            for oc in range(3):
                b.wait("tensor", "rope_done", b.c("rope_done"))

                def t_oc(q, oc=oc):
                    mm = None
                    for di in range(16):
                        mm = q.matmul(
                            t.ps_qkv[:, :],
                            t.wqkv_sb[:, di * 512 + oc * 128:
                                      di * 512 + oc * 128 + 128],
                            t.xn_t[:, di * 512:(di + 1) * 512],
                            start=(di == 0), stop=(di == 15))
                    return mm
                qmv = b.do("tensor", t_oc, sem="qm_done")
                b.wait("vector", "qm_done", qmv)
                b.wait("vector", "ld_cs", csv)
                dest = t.xq if oc < 2 else t.xk
                doff = oc * S if oc < 2 else 0
                cs0 = (tile % 2) * 1024

                def v_rope(q, dest=dest, doff=doff, tile=tile, cs0=cs0):
                    q.tensor_scalar_mul(t.rot_sb[0:64, :],
                                        t.ps_qkv[64:128, :], -1.0)
                    q.tensor_copy(t.rot_sb[64:128, :], t.ps_qkv[0:64, :])
                    q.tensor_tensor(t.t1_sb[:, :], t.ps_qkv[:, :],
                                    t.cossin2[:, cs0:cs0 + 512], op=ALU.mult)
                    q.tensor_tensor(t.rot_sb[:, :], t.rot_sb[:, :],
                                    t.cossin2[:, cs0 + 512:cs0 + 1024],
                                    op=ALU.mult)
                    return q.tensor_add(
                        dest[:, doff + tile * 512:doff + tile * 512 + 512],
                        t.t1_sb[:, :], t.rot_sb[:, :])
                b.do("vector", v_rope, sem="rope_done")
            for sub in range(4):
                b.wait("tensor", "v_evac", b.c("v_evac"))

                def t_v(q, sub=sub):
                    mm = None
                    for di in range(16):
                        mm = q.matmul(
                            t.ps_v[:, :],
                            t.xn_t[:, di * 512 + sub * 128:
                                   di * 512 + sub * 128 + 128],
                            t.wqkv_sb[:, di * 512 + 384:di * 512 + 512],
                            start=(di == 0), stop=(di == 15))
                    return mm
                vmv = b.do("tensor", t_v, sem="v_mm")
                b.wait("scalar", "v_mm", vmv)
                kc = tile * 4 + sub
                b.do("scalar", lambda q, kc=kc: q.copy(
                    t.xv[:, kc * 128:kc * 128 + 128], t.ps_v[:, :]),
                    sem="v_evac")

    emit_ln_stream(b, t, t.x, 64, on_group)

    def on_group_loc(g, lnv, xsl):
        pa, pb, tpv = emit_transposes(b, t, lnv, xsl, g)
        b.wait("vector", "tp_done", tpv)

        def v_evac(q, g=g, pa=pa, pb=pb):
            d3 = t.xn_loc.ap().rearrange("p (di tt) -> p di tt", di=16)
            q.tensor_copy(d3[:, 0:8, g * 128:g * 128 + 128],
                          pa.ap().rearrange("p (di tt) -> p di tt", di=8))
            return q.tensor_copy(
                d3[:, 8:16, g * 128:g * 128 + 128],
                pb.ap().rearrange("p (di tt) -> p di tt", di=8))
        b.do("vector", v_evac, sem="xp_evac")

    emit_ln_stream(b, t, t.xloc, 8, on_group_loc)
    b.barrier()

    emit_router_top2(b, t, n_exp=8, use_rms=False, scale_out=1.0)
    sgl = b.dma("gpsimd", t.sgw_sb[:, :], t.sgw_d, sem="ld_sgw")
    b.wait("tensor", "ld_sgw", sgl)
    for di in range(16):
        def t_sg(q, di=di):
            mm = None
            for half in range(2):
                mm = q.matmul(t.ps_sg[:, half * 512:half * 512 + 512],
                              t.sgw_sb[:, di:di + 1],
                              t.xn_loc[:, di * TLOC + half * 512:
                                       di * TLOC + half * 512 + 512],
                              start=(di == 0), stop=(di == 15))
            return mm
        b.do("tensor", t_sg, sem="sg_mm")
    b.wait("scalar", "sg_mm", b.c("sg_mm"))
    sgv = b.do("scalar", lambda q: q.activation(
        out=t.sg_row[:, :], in_=t.ps_sg[:, :], func=AF.Sigmoid), sem="sg_act")
    b.wait("tensor", "sg_act", sgv)
    for half in range(2):
        bcv = b.do("tensor", lambda q, half=half: q.matmul(
            t.ps_bc[:, :], t.ones1[:, :],
            t.sg_row[0:1, half * 512:half * 512 + 512],
            start=True, stop=True), sem="sgb_mm")
        b.wait("vector", "sgb_mm", bcv)
        sgbv = b.do("vector", lambda v, half=half: v.tensor_copy(
            t.sgb[:, half * 512:half * 512 + 512], t.ps_bc[:, :]),
            sem="sgb_ev")
        if half == 0:
            b.wait("tensor", "sgb_ev", b.c("sgb_ev"))

    b.wait("tensor", "sgb_ev", b.c("sgb_ev"))
    b.wait("gpsimd", "cwb_ev", b.c("cwb_ev"))
    emit_expert_gu(b, t, t.gw, t.uw, t.a_dram, N_HC1, 11)
    emit_expert_gu(b, t, t.swg, t.swu, t.sh_dram, N_HCS, None)

    b.wait("tensor", "aprod_done", b.c("aprod_done"))
    b.wait("tensor", "silu_done", b.c("silu_done"))
    b.wait("sync", "a_st", b.c("a_st"))
    for fog in range(4):
        b.wait("tensor", "y_silu", b.c("y_silu"))
        emit_down_accum(b, t, t.dwn, t.a_dram, N_HC1, fog, 22, first=True)
        b.wait("scalar", "dn_rhs", b.c("dn_rhs"))
        for fo_in in range(4):
            b.do("scalar", lambda q, fo_in=fo_in: q.copy(
                t.moe_t[:, fo_in * TLOC:(fo_in + 1) * TLOC],
                t.ps_y[fo_in][:, :]), sem="moe_ev")
        b.wait("tensor", "moe_ev", b.c("moe_ev"))
        emit_down_accum(b, t, t.swd, t.sh_dram, N_HCS, fog, 22, first=True)
        b.wait("vector", "dn_rhs", b.c("dn_rhs"))
        b.wait("vector", "moe_ev", b.c("moe_ev"))
        for fo_in in range(4):
            fo = fog * 4 + fo_in
            obuf = fo % 2
            b.wait("vector", "g1_st", b.c("g1_st") - 16)

            def v_comb(q, fo_in=fo_in):
                q.tensor_tensor(t.cmb1[:, :], t.ps_y[fo_in][:, :],
                                t.sgb[:, :], op=ALU.mult)
                return q.tensor_add(
                    t.cmb2[:, :],
                    t.moe_t[:, fo_in * TLOC:(fo_in + 1) * TLOC],
                    t.cmb1[:, :])
            cbv = b.do("vector", v_comb, sem="ffn_comb")
            b.wait("scalar", "ffn_comb", cbv)
            ysv = b.do("scalar", lambda q, obuf=obuf: q.activation(
                out=t.g0t[:, obuf * TLOC:(obuf + 1) * TLOC],
                in_=t.cmb2[:, :], func=AF.Silu), sem="y_silu")
            b.wait("gpsimd", "y_silu", ysv)
            b.dma("gpsimd", t.g1_o[fo * 128:(fo + 1) * 128, :],
                  t.g0t[:, obuf * TLOC:(obuf + 1) * TLOC], sem="g1_st")
    b.wait("gpsimd", "g1_st", b.c("g1_st"))
    b.barrier()

    mv = b.dma("gpsimd", t.mask_sb[:, :], t.mask_d, sem="ld_mask")
    b.wait("vector", "ld_mask", mv)
    for h in range(2):
        for qt in range(8):
            nkc = 8 * (qt + 1)
            for kc in range(nkc):
                psc = t.ps_scA if kc % 2 == 0 else t.ps_scB
                ebuf = kc % 2
                b.wait("tensor", "exp_done", b.c("exp_done") - 1)

                def t_sc(q, h=h, qt=qt, kc=kc, psc=psc):
                    mm = None
                    for half in range(2):
                        mm = q.matmul(
                            psc[:, half * 512:half * 512 + 512],
                            t.xk[:, kc * 128:kc * 128 + 128],
                            t.xq[:, h * S + qt * 1024 + half * 512:
                                 h * S + qt * 1024 + half * 512 + 512],
                            start=True, stop=True)
                    return mm
                sdv = b.do("tensor", t_sc, sem="sc_done")
                b.wait("scalar", "sc_done", sdv)
                b.wait("scalar", "pv_done", b.c("pv_done") - 1)
                tnv = b.do("scalar", lambda q, psc=psc: q.activation(
                    out=t.tn_sb[:, :], in_=psc[:, :], func=AF.Tanh,
                    scale=f32(SCALE1)), sem="tanh_done")
                b.wait("scalar", "mask_done", b.c("mask_done"))
                b.wait("scalar", "sacc_done", b.c("sacc_done"))
                exv = b.do("scalar", lambda q, ebuf=ebuf: q.activation(
                    out=t.E2[:, ebuf * 1024:ebuf * 1024 + 1024],
                    in_=t.tn_sb[:, :], func=AF.Exp, scale=f32(50.0)),
                    sem="exp_done")
                diag = kc >= 8 * qt
                if diag:
                    o = kc - 8 * qt
                    b.wait("vector", "exp_done", exv)
                    mkv = b.do("vector", lambda v, ebuf=ebuf, o=o: v.tensor_tensor(
                        t.E2[:, ebuf * 1024:ebuf * 1024 + 1024],
                        t.E2[:, ebuf * 1024:ebuf * 1024 + 1024],
                        t.mask_sb[:, o * 1024:o * 1024 + 1024], op=ALU.mult),
                        sem="mask_done")
                    b.wait("vector", "mask_done", mkv)
                else:
                    b.wait("vector", "exp_done", exv)
                if kc == 0:
                    sav = b.do("vector", lambda v, ebuf=ebuf: v.tensor_copy(
                        t.S_acc[:, :], t.E2[:, ebuf * 1024:ebuf * 1024 + 1024]),
                        sem="sacc_done")
                else:
                    sav = b.do("vector", lambda v, ebuf=ebuf: v.tensor_add(
                        t.S_acc[:, :], t.S_acc[:, :],
                        t.E2[:, ebuf * 1024:ebuf * 1024 + 1024]),
                        sem="sacc_done")
                b.wait("tensor", "sacc_done" if diag else "exp_done",
                       sav if diag else exv)
                if diag:
                    pass
                b.wait("tensor", "mask_done", b.c("mask_done"))

                def t_pv(q, h=h, qt=qt, kc=kc, ebuf=ebuf, nkc=nkc):
                    mm = None
                    for half in range(2):
                        mm = q.matmul(
                            t.ps_o[:, half * 512:half * 512 + 512],
                            t.xv[:, kc * 128:kc * 128 + 128],
                            t.E2[:, ebuf * 1024 + half * 512:
                                 ebuf * 1024 + half * 512 + 512],
                            start=(kc == 0), stop=(kc == nkc - 1))
                    return mm
                b.do("tensor", t_pv, sem="pv_done")
            b.wait("vector", "sacc_done", b.c("sacc_done"))
            sbv = b.do("vector", lambda v: v.tensor_copy(t.S_bf[:, :],
                                                         t.S_acc[:, :]),
                       sem="sbf_done")
            b.wait("tensor", "sbf_done", sbv)

            def t_z(q):
                q.matmul(t.ps_z[:, 0:512], t.onescol[:, :], t.S_bf[:, 0:512],
                         start=True, stop=True)
                return q.matmul(t.ps_z[:, 512:1024], t.onescol[:, :],
                                t.S_bf[:, 512:1024], start=True, stop=True)
            zmv = b.do("tensor", t_z, sem="z_mm")
            b.wait("scalar", "z_mm", zmv)
            zev = b.do("scalar", lambda q: q.copy(t.zr_f[:, :], t.ps_z[:, :]),
                       sem="zev_done")
            b.wait("vector", "zev_done", zev)
            b.fdo("vector", lambda v: v.reciprocal(t.zr_f2[:, :], t.zr_f[:, :]))
            zbv = b.do("vector", lambda v: v.tensor_copy(t.zr_b[:, :],
                                                         t.zr_f2[:, :]),
                       sem="zb_cast")
            b.wait("scalar", "pv_done", b.c("pv_done"))
            oev = b.do("scalar", lambda q: q.copy(t.o_raw[:, :], t.ps_o[:, :]),
                       sem="oev_done")
            b.wait("tensor", "zb_cast", zbv)
            for half in range(2):
                bcv = b.do("tensor", lambda q, half=half: q.matmul(
                    t.ps_scA[:, half * 512:half * 512 + 512], t.ones1[:, :],
                    t.zr_b[0:1, half * 512:half * 512 + 512],
                    start=True, stop=True), sem="zb_mm")
            b.wait("vector", "zb_mm", b.c("zb_mm"))
            b.wait("vector", "oev_done", oev)

            def v_osc(q, qt=qt):
                q.tensor_tensor(t.o_bf[:, qt * 1024:qt * 1024 + 512],
                                t.o_raw[:, 0:512], t.ps_scA[:, 0:512],
                                op=ALU.mult)
                return q.tensor_tensor(
                    t.o_bf[:, qt * 1024 + 512:qt * 1024 + 1024],
                    t.o_raw[:, 512:1024], t.ps_scA[:, 512:1024], op=ALU.mult)
            osv = b.do("vector", v_osc, sem="o_sc")
            b.wait("tensor", "o_sc", osv)
            b.wait("tensor", "oev_done", b.c("oev_done"))
        b.wait("gpsimd", "o_sc", b.c("o_sc"))
        b.wait("vector", "o_st", b.c("o_st"))
        b.dma("gpsimd", t.out1_o[h * 128:(h + 1) * 128, :], t.o_bf[:, :],
              sem="o_st")
    b.wait("gpsimd", "o_st", b.c("o_st"))


def emit_router_top2(b, t, *, n_exp, use_rms, scale_out):
    lw = b.dma("gpsimd", t.rw_sb[:, :].rearrange("p (dc e) -> p dc e", dc=16),
               t.rw.rearrange("(dc p) e -> p dc e", p=128))
    if use_rms:
        lw = b.dma("gpsimd", t.nw_sb[:, :], t.nw)
    b.wait("vector", "ld_gpsimd", lw)
    b.wait("tensor", "ld_gpsimd", lw)
    for di in range(16):
        b.wait("vector", "lg_mm", b.c("lg_mm") - 1)
        if use_rms:
            xwv = b.do("vector", lambda v, di=di: v.tensor_scalar(
                t.xnw2[:, (di % 2) * TLOC:(di % 2 + 1) * TLOC],
                t.xn_loc[:, di * TLOC:(di + 1) * TLOC],
                t.nw_sb[:, di:di + 1], None, op0=ALU.mult), sem="xnw_done")
        else:
            xwv = b.do("vector", lambda v, di=di: v.tensor_copy(
                t.xnw2[:, (di % 2) * TLOC:(di % 2 + 1) * TLOC],
                t.xn_loc[:, di * TLOC:(di + 1) * TLOC]), sem="xnw_done")
        b.wait("tensor", "xnw_done", xwv)

        def t_lg(q, di=di):
            mm = None
            for half in range(2):
                mm = q.matmul(t.ps_lg[:, half * 512:half * 512 + 512],
                              t.rw_sb[:, di * 8:di * 8 + 8],
                              t.xnw2[:, (di % 2) * TLOC + half * 512:
                                     (di % 2) * TLOC + half * 512 + 512],
                              start=(di == 0), stop=(di == 15))
            return mm
        b.do("tensor", t_lg, sem="lg_mm")
    b.wait("scalar", "lg_mm", b.c("lg_mm"))
    lev = b.do("scalar", lambda q: q.copy(t.lg_sb[:, :], t.ps_lg[:, :]),
               sem="lg_evac")
    b.wait("tensor", "lg_evac", lev)

    def t_ltr(q):
        mm = None
        for lg in range(8):
            mm = q.transpose(t.ps_t8[:, lg * 8:lg * 8 + 8],
                             t.lg_sb[:, lg * 128:lg * 128 + 128],
                             t.ident[0:8, 0:8])
        return mm
    ltv = b.do("tensor", t_ltr, sem="lg_tr")
    b.wait("vector", "lg_tr", ltv)
    ltev = b.do("vector", lambda v: v.tensor_copy(t.ltm[:, :], t.ps_t8[:, :]),
                sem="lg_tr_ev")
    b.wait("vector", "lg_tr_ev", ltev)
    if use_rms:
        b.wait("vector", "rms_done", b.c("rms_done"))
    for lg in range(8):
        col = (lambda j, lg=lg: t.r_sc[:, 8 * lg + j:8 * lg + j + 1])
        l8 = (lambda lg=lg: t.ltm[:, lg * 8:lg * 8 + 8])
        x8 = (lambda lg=lg: t.exps[:, lg * 8:lg * 8 + 8])
        e8 = (lambda lg=lg: t.eqs[:, lg * 8:lg * 8 + 8])
        t8 = (lambda lg=lg: t.tts[:, lg * 8:lg * 8 + 8])
        c8 = (lambda lg=lg: t.cwtm[:, lg * 8:lg * 8 + 8])
        if use_rms:
            rsv = b.do("vector", lambda v, col=col, lg=lg: v.tensor_scalar(
                col(0), t.rms_sumsq[:, lg:lg + 1], 1.0 / DM, 1e-6,
                op0=ALU.mult, op1=ALU.add), sem="rr1")
            b.wait("scalar", "rr1", rsv)
            rs2 = b.do("scalar", lambda q, col=col: q.sqrt(col(1), col(0)),
                       sem="rr2")
            b.wait("vector", "rr2", rs2)
            b.fdo("vector", lambda v, col=col: v.reciprocal(col(2), col(1)))
            b.fdo("vector", lambda v, col=col, l8=l8: v.tensor_scalar(
                l8(), l8(), col(2), None, op0=ALU.mult))
        b.fdo("vector", lambda v, col=col, l8=l8: v.tensor_reduce(
            col(3), l8(), axis=AX.X, op=ALU.max))
        mxv = b.do("vector", lambda v, col=col: v.tensor_scalar_mul(
            col(3), col(3), -1.0), sem="tq1")
        b.wait("scalar", "tq1", mxv)
        e8v = b.do("scalar", lambda q, col=col, l8=l8, x8=x8: q.activation(
            out=x8(), in_=l8(), func=AF.Exp, bias=col(3)), sem="tq2")
        b.wait("vector", "tq2", e8v)
        for step in [
            lambda v, col=col, x8=x8: v.tensor_reduce(
                col(4), x8(), axis=AX.X, op=ALU.max),
            lambda v, col=col, x8=x8, e8=e8: v.tensor_scalar(
                e8(), x8(), col(4), None, op0=ALU.is_ge),
            lambda v, x8=x8, e8=e8, t8=t8: v.tensor_tensor(
                t8(), x8(), e8(), op=ALU.mult),
            lambda v, x8=x8, t8=t8: v.tensor_sub(t8(), x8(), t8()),
            lambda v, col=col, t8=t8: v.tensor_reduce(
                col(5), t8(), axis=AX.X, op=ALU.max),
            lambda v, col=col, x8=x8, e8=e8: v.tensor_scalar(
                e8(), x8(), col(5), None, op0=ALU.is_ge),
            lambda v, col=col: v.tensor_add(col(6), col(4), col(5)),
            lambda v, col=col: v.reciprocal(col(7), col(6)),
            lambda v, x8=x8, e8=e8, t8=t8: v.tensor_tensor(
                t8(), x8(), e8(), op=ALU.mult),
        ]:
            b.fdo("vector", step)
        ctv = b.do("vector", lambda v, col=col, t8=t8, c8=c8: v.tensor_scalar(
            c8(), t8(), col(7), float(scale_out), op0=ALU.mult, op1=ALU.mult),
            sem="cw_tm_done")
        b.wait("tensor", "cw_tm_done", ctv)
        b.do("tensor", lambda q, lg=lg, c8=c8: q.transpose(
            t.ps_ct[:, lg * 128:lg * 128 + 128], c8(), t.ident[:, :]),
            sem="cw_tr")
    b.wait("vector", "cw_tr", b.c("cw_tr"))
    cfev = b.do("vector", lambda v: v.tensor_copy(t.cw8[:, :], t.ps_ct[:, :]),
                sem="cw_fm_ev")
    b.wait("gpsimd", "cw_fm_ev", cfev)
    crv = None
    for e in range(8):
        crv = b.dma("gpsimd", t.cw_row[0:1, e * TLOC:(e + 1) * TLOC],
                    t.cw8[e:e + 1, :], sem="ld_cwrow")
    b.wait("tensor", "ld_cwrow", crv)
    for e in range(n_exp):
        for half in range(2):
            b.wait("tensor", "cwb_ev", b.c("cwb_ev"))
            bcv = b.do("tensor", lambda q, e=e, half=half: q.matmul(
                t.ps_bc[:, :], t.ones1[:, :],
                t.cw_row[0:1, e * TLOC + half * 512:
                         e * TLOC + half * 512 + 512],
                start=True, stop=True), sem="cwb_mm")
            b.wait("vector", "cwb_mm", bcv)
            b.do("vector", lambda v, e=e, half=half: v.tensor_copy(
                t.cwb[:, e * TLOC + half * 512:e * TLOC + half * 512 + 512],
                t.ps_bc[:, :]), sem="cwb_ev")


def emit_expert_gu(b, t, gw_d, uw_d, a_dram, n_hc, hc_per_exp):
    for hc in range(n_hc):
        wbuf = b.c("gu_mm") % 2
        b.wait("gpsimd", "gu_mm", b.c("gu_mm") - 1)
        b.dma("gpsimd",
              t.gwu2[:, wbuf * 4096:wbuf * 4096 + 2048].rearrange(
                  "p (dc ff) -> p dc ff", dc=16),
              gw_d[:, hc * 128:(hc + 1) * 128].rearrange(
                  "(dc p) ff -> p dc ff", p=128), sem="ld_w")
        gl = b.dma("gpsimd",
                   t.gwu2[:, wbuf * 4096 + 2048:wbuf * 4096 + 4096].rearrange(
                       "p (dc ff) -> p dc ff", dc=16),
                   uw_d[:, hc * 128:(hc + 1) * 128].rearrange(
                       "(dc p) ff -> p dc ff", p=128), sem="ld_w")
        b.wait("tensor", "ld_w", gl)
        psg = t.ps_gA if b.c("gu_mm") % 2 == 0 else t.ps_gB
        psu = t.ps_uA if b.c("gu_mm") % 2 == 0 else t.ps_uB
        b.wait("tensor", "silu_done", b.c("silu_done") - 1)

        def t_g(q, wbuf=wbuf, psg=psg):
            mm = None
            for di in range(16):
                for half in range(2):
                    mm = q.matmul(
                        psg[:, half * 512:half * 512 + 512],
                        t.gwu2[:, wbuf * 4096 + di * 128:
                               wbuf * 4096 + di * 128 + 128],
                        t.xn_loc[:, di * TLOC + half * 512:
                                 di * TLOC + half * 512 + 512],
                        start=(di == 0), stop=(di == 15))
            return mm
        gmv = b.do("tensor", t_g, sem="g_mm")
        b.wait("tensor", "aprod_done", b.c("aprod_done") - 1)

        def t_u(q, wbuf=wbuf, psu=psu):
            mm = None
            for di in range(16):
                for half in range(2):
                    mm = q.matmul(
                        psu[:, half * 512:half * 512 + 512],
                        t.gwu2[:, wbuf * 4096 + 2048 + di * 128:
                               wbuf * 4096 + 2048 + di * 128 + 128],
                        t.xn_loc[:, di * TLOC + half * 512:
                                 di * TLOC + half * 512 + 512],
                        start=(di == 0), stop=(di == 15))
            return mm
        umv = b.do("tensor", t_u, sem="gu_mm")
        b.wait("scalar", "g_mm", gmv)
        slv = b.do("scalar", lambda q, psg=psg: q.activation(
            out=t.gs_sb[:, :], in_=psg[:, :], func=AF.Silu), sem="silu_done")
        b.wait("vector", "gu_mm", umv)
        b.wait("vector", "silu_done", slv)
        abuf = b.c("a_st") // 16 % 2
        b.wait("vector", "a_st", b.c("a_st") - 16)

        if hc_per_exp:
            e = hc // hc_per_exp

            def v_a(q, e=e, abuf=abuf, psu=psu):
                q.tensor_tensor(t.aw2[:, abuf * TLOC:(abuf + 1) * TLOC],
                                t.gs_sb[:, :], psu[:, :], op=ALU.mult)
                return q.tensor_tensor(
                    t.aw2[:, abuf * TLOC:(abuf + 1) * TLOC],
                    t.aw2[:, abuf * TLOC:(abuf + 1) * TLOC],
                    t.cwb[:, e * TLOC:(e + 1) * TLOC], op=ALU.mult)
        else:
            def v_a(q, abuf=abuf, psu=psu):
                return q.tensor_tensor(
                    t.aw2[:, abuf * TLOC:(abuf + 1) * TLOC],
                    t.gs_sb[:, :], psu[:, :], op=ALU.mult)
        apv = b.do("vector", v_a, sem="aprod_done")
        b.wait("gpsimd", "aprod_done", apv)
        b.dma("gpsimd", a_dram[hc * 128:(hc + 1) * 128, :],
              t.aw2[:, abuf * TLOC:(abuf + 1) * TLOC], sem="a_st")


def emit_down_accum(b, t, dwn_d, a_dram, n_hc, fog, hc_chunk, first):
    for cs in range(0, n_hc, hc_chunk):
        ce = min(cs + hc_chunk, n_hc)
        nch = ce - cs
        b.wait("gpsimd", "dn_rhs", b.c("dn_rhs"))
        dl = None
        for part in range(0, nch, 16):
            pe = min(part + 16, nch)
            dl = b.dma(
                "gpsimd",
                t.dw_sb[:, part * 512:pe * 512].rearrange(
                    "p (hcc ff) -> p hcc ff", hcc=pe - part),
                dwn_d.rearrange("(hcc p) o -> p hcc o", p=128)[
                    :, cs + part:cs + pe,
                    fog * 512:(fog + 1) * 512], sem="ld_dw")
        b.wait("tensor", "ld_dw", dl)
        for hc in range(cs, ce):
            ci = hc - cs
            sbuf_i = b.c("dn_rhs") % 2
            b.wait("sync", "dn_rhs", b.c("dn_rhs") - 1)
            alv = b.dma("sync", t.astr2[:, sbuf_i * TLOC:(sbuf_i + 1) * TLOC],
                        a_dram[hc * 128:(hc + 1) * 128, :], sem="ld_a")
            b.wait("tensor", "ld_a", alv)
            is_first = first and hc == 0
            is_last = (not first or True) and hc == n_hc - 1

            def t_dn(q, ci=ci, sbuf_i=sbuf_i, is_first=is_first,
                     is_last=is_last):
                mm = None
                for fo_in in range(4):
                    for half in range(2):
                        mm = q.matmul(
                            t.ps_y[fo_in][:, half * 512:half * 512 + 512],
                            t.dw_sb[:, ci * 512 + fo_in * 128:
                                    ci * 512 + fo_in * 128 + 128],
                            t.astr2[:, sbuf_i * TLOC + half * 512:
                                    sbuf_i * TLOC + half * 512 + 512],
                            start=is_first, stop=is_last)
                return mm
            b.do("tensor", t_dn, sem="dn_rhs")


def host_inputs_C(inputs, x1_full):
    wq = inputs["L1_attn_wq"].T
    wk = inputs["L1_attn_wk"].T
    wv = inputs["L1_attn_wv"].T
    gw = np.ascontiguousarray(
        inputs["L1_ffn_w_gate"].transpose(1, 0, 2).reshape(DM, N_HC1 * 128))
    uw = np.ascontiguousarray(
        inputs["L1_ffn_w_up"].transpose(1, 0, 2).reshape(DM, N_HC1 * 128))
    dwn = np.ascontiguousarray(inputs["L1_ffn_w_down"].reshape(N_HC1 * 128, DM))
    rw = np.ascontiguousarray(inputs["L1_ffn_gate_w"].T)
    sgw = np.ascontiguousarray(
        inputs["L1_ffn_shared_gate_w"].reshape(16, 128).T)
    swg = inputs["L1_ffn_sw_gate"]
    swu = inputs["L1_ffn_sw_up"]
    swd = inputs["L1_ffn_sw_down"]
    inv = 1.0 / 10000.0 ** (np.arange(0, 128, 2, dtype=np.float32) / 128)
    fpos = np.arange(S, dtype=np.float32)[:, None] * inv[None, :]
    emb = np.concatenate((fpos, fpos), axis=-1)
    cos_c = np.ascontiguousarray(np.cos(emb).astype(np.float32).T)
    sin_c = np.ascontiguousarray(np.sin(emb).astype(np.float32).T)
    kp = np.arange(128)[:, None]
    qp = np.arange(1024)[None, :]
    mask = np.zeros((128, 8 * 1024), np.float32)
    for o in range(8):
        mask[:, o * 1024:(o + 1) * 1024] = (o * 128 + kp <= qp)
    mask_c = mask.astype(BF)
    maps = []
    for c in range(NCORES):
        sl = np.concatenate([
            wq[:, 256 * c:256 * (c + 1)],
            wk[:, 128 * c:128 * (c + 1)],
            wv[:, 128 * c:128 * (c + 1)]], axis=1)
        maps.append({
            "x": x1_full, "xloc": x1_full[TLOC * c:TLOC * (c + 1)],
            "wqkv": np.ascontiguousarray(sl), "rw": rw,
            "gw": gw, "uw": uw, "dwn": dwn, "sgw": sgw,
            "swg": swg, "swu": swu, "swd": swd,
            "cos_c": cos_c, "sin_c": sin_c, "mask_c": mask_c,
            "ident_c": _IDENT})
    return maps





def host_inputs_D(inputs, x1_full, out1_parts, g1_parts):
    out1_full = np.concatenate(out1_parts, axis=0)
    wo1 = np.ascontiguousarray(inputs["L1_attn_wo"].T)
    maps = []
    for c in range(NCORES):
        xfm = np.ascontiguousarray(x1_full[TLOC * c:TLOC * (c + 1)].T)
        maps.append({
            "xfm": xfm,
            "val": np.ascontiguousarray(out1_full[:, TLOC * c:TLOC * (c + 1)]),
            "g0": g1_parts[c],
            "wo": wo1})
    return maps


def kernel(**inputs):
    maps_A = host_inputs_A(inputs)
    nc_A = build(make_tensors_A, emit_A)
    res_A = run_bass_kernel_spmd(nc_A, maps_A, core_ids=list(range(NCORES)))
    val_parts = [res_A.results[c]["valout"] for c in range(NCORES)]
    g0_parts = [res_A.results[c]["g0out"] for c in range(NCORES)]

    maps_B = host_inputs_B(inputs, val_parts, g0_parts)
    nc_B = build(make_tensors_B, emit_B)
    res_B = run_bass_kernel_spmd(nc_B, maps_B, core_ids=list(range(NCORES)))
    x1_full = np.concatenate(
        [np.ascontiguousarray(res_B.results[c]["x1out"].T)
         for c in range(NCORES)], axis=0)

    maps_C = host_inputs_C(inputs, x1_full)
    nc_C = build(make_tensors_C, emit_C)
    res_C = run_bass_kernel_spmd(nc_C, maps_C, core_ids=list(range(NCORES)))
    out1_parts = [res_C.results[c]["out1"] for c in range(NCORES)]
    g1_parts = [res_C.results[c]["g1out"] for c in range(NCORES)]

    maps_D = host_inputs_D(inputs, x1_full, out1_parts, g1_parts)
    nc_D = build(make_tensors_B, emit_B)
    res_D = run_bass_kernel_spmd(nc_D, maps_D, core_ids=list(range(NCORES)))
    x2 = np.concatenate(
        [np.ascontiguousarray(res_D.results[c]["x1out"].T)
         for c in range(NCORES)], axis=0)
    return x2.reshape(1, S, DM)



# version 02
# speedup vs baseline: 1.0927x; 1.0927x over previous
import contextlib
import types
import numpy as np
import ml_dtypes
import concourse.bass as bass
import concourse.mybir as mybir
from concourse.bass_utils import run_bass_kernel_spmd

F32 = mybir.dt.float32
BF16 = mybir.dt.bfloat16
AF = mybir.ActivationFunctionType
ALU = mybir.AluOpType
AX = mybir.AxisListType
BF = ml_dtypes.bfloat16
f32 = np.float32

NCORES = 8
S = 8192
DM = 2048
TLOC = S // NCORES
C0_SHIFT = 16.0
ENGINES = ("sync", "gpsimd", "tensor", "vector", "scalar")
INV_SQRT_HD = 1.0 / np.sqrt(128.0)


class B:

    def __init__(self, nc, eng, q, sems):
        self.nc, self.eng, self.q, self.sems = nc, eng, q, sems
        self.cnt = {}
        self.nbar = 0

    def _sem(self, name):
        if name not in self.sems:
            assert self.eng == "__dry__", f"sem {name} not in dry pass"
            self.sems[name] = None

    def c(self, name):
        return self.cnt.get(name, 0)

    def inc(self, name, by=1):
        self._sem(name)
        self.cnt[name] = self.cnt.get(name, 0) + by
        return self.cnt[name]

    def do(self, eng, fn, sem=None, by=1):
        v = self.inc(sem, by) if sem else None
        if self.eng == eng:
            ins = fn(self.q)
            if sem:
                ins.then_inc(self.sems[sem], by)
        return v

    def wait(self, eng, sem, val):
        self._sem(sem)
        if self.eng == eng and val > 0:
            self.q.wait_ge(self.sems[sem], val)

    def fdo(self, eng, fn, sem=None):
        sem = sem or ("f_" + eng)
        v = self.do(eng, fn, sem=sem)
        self.wait(eng, sem, v)
        return v

    def dma(self, eng, out, in_, sem=None, **kw):
        sem = sem or ("ld_" + eng)
        return self.do(eng, lambda q: q.dma_start(out=out, in_=in_, **kw),
                       sem=sem, by=16)

    def barrier(self):
        self.nbar += 1
        self._sem("bar")
        tgt = 5 * self.nbar
        self.cnt["bar"] = tgt
        if self.eng != "__dry__":
            self.q.sem_inc(self.sems["bar"], 1)
            self.q.wait_ge(self.sems["bar"], tgt)


def build(make_tensors, emit):
    nc = bass.Bass("TRN2", target_bir_lowering=False)
    es = contextlib.ExitStack()
    with es:
        t = make_tensors(nc, es)
        sems = {}
        emit(B(nc, "__dry__", None, sems), t)
        for name in list(sems):
            sems[name] = es.enter_context(nc.semaphore(name))
        with nc.Block() as block:
            for eng in ENGINES:
                getattr(block, eng)(
                    lambda q, eng=eng: emit(B(nc, eng, q, sems), t))
    return nc



def emit_consts(b, t):
    b.do("vector", lambda v: v.memset(t.zeros1[:, :], 0.0), sem="const")
    b.do("vector", lambda v: v.memset(t.ones1[:, :], 1.0), sem="const")
    b.dma("gpsimd", t.ident[:, :], t.ident_c, sem="ld_const")


def emit_ln_stream(b, t, x_dram, n_groups, on_group, rms_groups=None):
    st = t.stats
    for g in range(n_groups):
        buf = g % 2
        xsl = slice(buf * DM, (buf + 1) * DM)
        b.wait("sync", "ln_done", b.c("ln_done") - 1)
        b.wait("sync", "st_s", b.c("st_s") - 1)
        vld = b.dma("sync", t.x_tm2[:, xsl], x_dram[g * 128:(g + 1) * 128, :])
        b.wait("vector", "ld_sync", vld)
        b.fdo("vector", lambda v, xsl=xsl: v.tensor_reduce(
            st[:, 0:1], t.x_tm2[:, xsl], axis=AX.X, op=ALU.add))
        b.do("vector", lambda v: v.tensor_scalar_mul(
            st[:, 1:2], st[:, 0:1], -1.0 / DM), sem="st_v")
        b.wait("vector", "st_v", b.c("st_v"))
        b.wait("scalar", "ld_sync", vld)
        b.wait("scalar", "st_v", b.c("st_v"))
        sqv = b.do("scalar", lambda q, xsl=xsl: q.activation(
            out=t.sq_scr[:, :], in_=t.x_tm2[:, xsl], func=AF.Square,
            accum_out=st[:, 2:3]), sem="st_s")
        b.wait("vector", "st_s", sqv)
        b.fdo("vector", lambda v: v.tensor_scalar(
            st[:, 3:4], st[:, 2:3], 1.0 / DM, 1e-5, op0=ALU.mult, op1=ALU.add))
        b.fdo("vector", lambda v: v.tensor_tensor(
            st[:, 4:5], st[:, 1:2], st[:, 1:2], op=ALU.mult))
        s2v = b.do("vector", lambda v: v.tensor_sub(
            st[:, 3:4], st[:, 3:4], st[:, 4:5]), sem="st_v2")
        b.wait("scalar", "st_v2", s2v)
        s3v = b.do("scalar", lambda q: q.sqrt(st[:, 4:5], st[:, 3:4]),
                   sem="st_s2")
        b.wait("vector", "st_s2", s3v)
        b.fdo("vector", lambda v: v.reciprocal(st[:, 5:6], st[:, 4:5]))
        b.wait("vector", "tp_done", b.c("tp_done") - 1)
        if rms_groups and (g - 2) in rms_groups:
            b.wait("vector", "rms_done",
                   b.c("rms_done") - (0 if g in rms_groups else 1))
        lnv = b.do("vector", lambda v, xsl=xsl: v.tensor_scalar(
            t.xn_tm2[:, xsl], t.x_tm2[:, xsl], st[:, 1:2], st[:, 5:6],
            op0=ALU.add, op1=ALU.mult), sem="ln_done")
        if rms_groups and g in rms_groups:
            lg = rms_groups.index(g)
            b.wait("scalar", "ln_done", lnv)
            b.do("scalar", lambda q, xsl=xsl, lg=lg: q.activation(
                out=t.sq_scr[:, :], in_=t.xn_tm2[:, xsl], func=AF.Square,
                accum_out=t.rms_sumsq[:, lg:lg + 1]), sem="rms_done")
        on_group(g, lnv, xsl)


def emit_transposes(b, t, lnv, xsl, g):
    pa = t.ps_tr0 if g % 2 == 0 else t.ps_tr1
    pb = t.ps_tr1 if g % 2 == 0 else t.ps_tr0
    b.wait("tensor", "ln_done", lnv)
    b.wait("tensor", "xp_evac", b.c("xp_evac") - 1)

    def t_tr(q, xsl=xsl, pa=pa, pb=pb):
        mm = None
        for j in range(16):
            dst = pa if j < 8 else pb
            mm = q.transpose(
                dst[:, (j % 8) * 128:(j % 8) * 128 + 128],
                t.xn_tm2[:, xsl.start + j * 128:xsl.start + j * 128 + 128],
                t.ident[:, :])
        return mm
    tpv = b.do("tensor", t_tr, sem="tp_done")
    return pa, pb, tpv



def make_tensors_A(nc, es):
    t = types.SimpleNamespace()
    t.x = nc.dram_tensor("x", [S, DM], F32, kind="ExternalInput").ap()
    t.xloc = nc.dram_tensor("xloc", [TLOC, DM], F32, kind="ExternalInput").ap()
    t.wqkv = nc.dram_tensor("wqkv", [DM, 768], F32, kind="ExternalInput").ap()
    t.rw = nc.dram_tensor("rw", [DM, 8], F32, kind="ExternalInput").ap()
    t.nw = nc.dram_tensor("nw", [128, 16], F32, kind="ExternalInput").ap()
    t.gw = nc.dram_tensor("gw", [DM, 8192], F32, kind="ExternalInput").ap()
    t.uw = nc.dram_tensor("uw", [DM, 8192], F32, kind="ExternalInput").ap()
    t.dwn = nc.dram_tensor("dwn", [8192, DM], F32, kind="ExternalInput").ap()
    t.ident_c = nc.dram_tensor("ident_c", [128, 128], BF16,
                               kind="ExternalInput").ap()
    t.val_o = nc.dram_tensor("valout", [256, S], BF16,
                             kind="ExternalOutput").ap()
    t.g0_o = nc.dram_tensor("g0out", [DM, TLOC], BF16,
                            kind="ExternalOutput").ap()
    t.a_dram = nc.dram_tensor("a_stage", [8192, TLOC], BF16).ap()

    sb = es.enter_context
    t.xn_loc = sb(nc.sbuf_tensor("xn_loc", [128, 16 * TLOC], BF16))
    t.zeros1 = sb(nc.sbuf_tensor("zeros1", [128, 1], BF16))
    t.ones1 = sb(nc.sbuf_tensor("ones1", [1, 128], BF16))
    t.ident = sb(nc.sbuf_tensor("ident", [128, 128], BF16))
    t.rms_sumsq = sb(nc.sbuf_tensor("rms_sumsq", [128, 8], F32))
    t.stats = sb(nc.sbuf_tensor("stats", [128, 8], F32))
    t.sq_scr = sb(nc.sbuf_tensor("sq_scr", [128, DM], BF16))

    with contextlib.ExitStack() as s12:
        t.xq = s12.enter_context(nc.sbuf_tensor("xq", [128, 2 * S], BF16))
        t.xk = s12.enter_context(nc.sbuf_tensor("xk", [128, 2 * S], BF16))
        t.xv = s12.enter_context(nc.sbuf_tensor("xv", [128, 2 * S], BF16))
        with contextlib.ExitStack() as s1:
            t.wqkv_sb = s1.enter_context(
                nc.sbuf_tensor("wqkv_sb", [128, 16 * 768], BF16))
            t.x_tm2 = s1.enter_context(
                nc.sbuf_tensor("x_tm2", [128, 2 * DM], F32))
            t.xn_tm2 = s1.enter_context(
                nc.sbuf_tensor("xn_tm2", [128, 2 * DM], BF16))
            t.xn_t = s1.enter_context(
                nc.sbuf_tensor("xn_t", [128, 16 * 512], BF16))
            t.ps_tr0 = s1.enter_context(
                nc.psum_tensor("ps_tr0", [128, 1024], BF16))
            t.ps_tr1 = s1.enter_context(
                nc.psum_tensor("ps_tr1", [128, 1024], BF16))
            t.ps_qkv = s1.enter_context(
                nc.psum_tensor("ps_qkv", [128, 512], F32))
        with contextlib.ExitStack() as s2:
            t.E2 = s2.enter_context(nc.sbuf_tensor("E2", [128, 2 * S], BF16))
            t.Zp = s2.enter_context(nc.sbuf_tensor("Zp", [128, 16], F32))
            t.zwork = s2.enter_context(nc.sbuf_tensor("zwork", [128, 4], F32))
            t.rbf = s2.enter_context(nc.sbuf_tensor("rbf", [128, 2], BF16))
            t.cs_sb = s2.enter_context(nc.sbuf_tensor("cs_sb", [1, S], BF16))
            t.val_sb = s2.enter_context(nc.sbuf_tensor("val_sb", [128, S], BF16))
            t.ps_scA = s2.enter_context(
                nc.psum_tensor("ps_scA", [128, 1024], F32))
            t.ps_scB = s2.enter_context(
                nc.psum_tensor("ps_scB", [128, 1024], F32))
            t.ps_cs = [s2.enter_context(
                nc.psum_tensor(f"ps_cs{i}", [97, 512], F32)) for i in range(4)]
    with contextlib.ExitStack() as s4:
        t.rw_sb = s4.enter_context(nc.sbuf_tensor("rw_sb", [128, 128], BF16))
        t.nw_sb = s4.enter_context(nc.sbuf_tensor("nw_sb", [128, 16], F32))
        t.cwb = s4.enter_context(nc.sbuf_tensor("cwb", [128, 8 * TLOC], BF16))
        t.r_sc = s4.enter_context(nc.sbuf_tensor("r_sc", [128, 64], F32))
        t.lg_sb = s4.enter_context(nc.sbuf_tensor("lg_sb", [8, TLOC], BF16))
        t.ltm = s4.enter_context(nc.sbuf_tensor("ltm", [128, 64], F32))
        t.exps = s4.enter_context(nc.sbuf_tensor("exps", [128, 64], F32))
        t.eqs = s4.enter_context(nc.sbuf_tensor("eqs", [128, 64], F32))
        t.tts = s4.enter_context(nc.sbuf_tensor("tts", [128, 64], F32))
        t.cwtm = s4.enter_context(nc.sbuf_tensor("cwtm", [128, 64], BF16))
        t.cw8 = s4.enter_context(nc.sbuf_tensor("cw8", [8, TLOC], BF16))
        t.cw_row = s4.enter_context(nc.sbuf_tensor("cw_row", [1, 8 * TLOC], BF16))
        t.xnw2 = s4.enter_context(nc.sbuf_tensor("xnw2", [128, 2 * TLOC], BF16))
        t.gwu2 = s4.enter_context(
            nc.sbuf_tensor("gwu2", [128, 2 * 4096], BF16))
        t.gs_sb = s4.enter_context(nc.sbuf_tensor("gs_sb", [128, TLOC], BF16))
        t.aw2 = s4.enter_context(nc.sbuf_tensor("aw2", [128, 2 * TLOC], BF16))
        t.astr4 = s4.enter_context(
            nc.sbuf_tensor("astr4", [128, 2 * 2 * TLOC], BF16))
        t.dw_sb = s4.enter_context(
            nc.sbuf_tensor("dw_sb", [128, 2 * 32 * 512], BF16))
        t.g0t = s4.enter_context(nc.sbuf_tensor("g0t", [128, 2 * TLOC], BF16))
        with contextlib.ExitStack() as s4r:
            t.ps_lg = s4r.enter_context(nc.psum_tensor("ps_lg", [8, TLOC], F32))
            t.ps_t8 = s4r.enter_context(
                nc.psum_tensor("ps_t8", [128, 64], BF16))
            t.ps_ct = s4r.enter_context(
                nc.psum_tensor("ps_ct", [8, TLOC], BF16))
            t.ps_bc = s4r.enter_context(
                nc.psum_tensor("ps_bc", [128, 512], F32))
        with contextlib.ExitStack() as s4e:
            t.ps_gA = s4e.enter_context(
                nc.psum_tensor("ps_gA", [128, TLOC], F32))
            t.ps_uA = s4e.enter_context(
                nc.psum_tensor("ps_uA", [128, TLOC], F32))
            t.ps_gB = s4e.enter_context(
                nc.psum_tensor("ps_gB", [128, TLOC], F32))
            t.ps_uB = s4e.enter_context(
                nc.psum_tensor("ps_uB", [128, TLOC], F32))
        with contextlib.ExitStack() as s4d:
            t.ps_y = [s4d.enter_context(
                nc.psum_tensor(f"ps_y{i}", [128, TLOC], F32)) for i in range(4)]
    return t


def emit_A(b, t):
    emit_consts(b, t)
    wv = b.dma("gpsimd", t.wqkv_sb[:, :].rearrange("p (dc o) -> p dc o", dc=16),
               t.wqkv.rearrange("(dc p) o -> p dc o", p=128))
    b.wait("tensor", "ld_gpsimd", wv)
    b.wait("tensor", "ld_const", b.c("ld_const"))

    def on_group(g, lnv, xsl):
        tile, gi = g // 4, g % 4
        pa, pb, tpv = emit_transposes(b, t, lnv, xsl, g)
        b.wait("vector", "tp_done", tpv)
        if gi == 0:
            b.wait("vector", "qm_done", 6 * tile)

        def v_evac(q, gi=gi, pa=pa, pb=pb):
            d3 = t.xn_t.ap().rearrange("p (di tt) -> p di tt", di=16)
            q.tensor_copy(d3[:, 0:8, gi * 128:gi * 128 + 128],
                          pa.ap().rearrange("p (di tt) -> p di tt", di=8))
            return q.tensor_copy(
                d3[:, 8:16, gi * 128:gi * 128 + 128],
                pb.ap().rearrange("p (di tt) -> p di tt", di=8))
        b.do("vector", v_evac, sem="xp_evac")

        if gi == 3:
            b.wait("tensor", "xp_evac", b.c("xp_evac"))
            for oc in range(6):
                b.wait("tensor", "qe_done", b.c("qe_done"))

                def t_oc(q, oc=oc):
                    mm = None
                    for di in range(16):
                        mm = q.matmul(
                            t.ps_qkv[:, :],
                            t.wqkv_sb[:, di * 768 + oc * 128:
                                      di * 768 + oc * 128 + 128],
                            t.xn_t[:, di * 512:(di + 1) * 512],
                            start=(di == 0), stop=(di == 15))
                    return mm
                qmv = b.do("tensor", t_oc, sem="qm_done")
                b.wait("scalar", "qm_done", qmv)
                dest = (t.xq, t.xk, t.xv)[oc // 2]
                h = oc % 2
                b.do("scalar", lambda q, dest=dest, h=h, tile=tile: q.copy(
                    dest[:, h * S + tile * 512:h * S + tile * 512 + 512],
                    t.ps_qkv[:, :]), sem="qe_done")

    emit_ln_stream(b, t, t.x, 64, on_group)

    def on_group_loc(g, lnv, xsl):
        pa, pb, tpv = emit_transposes(b, t, lnv, xsl, g)
        b.wait("vector", "tp_done", tpv)

        def v_evac(q, g=g, pa=pa, pb=pb):
            d3 = t.xn_loc.ap().rearrange("p (di tt) -> p di tt", di=16)
            q.tensor_copy(d3[:, 0:8, g * 128:g * 128 + 128],
                          pa.ap().rearrange("p (di tt) -> p di tt", di=8))
            return q.tensor_copy(
                d3[:, 8:16, g * 128:g * 128 + 128],
                pb.ap().rearrange("p (di tt) -> p di tt", di=8))
        b.do("vector", v_evac, sem="xp_evac")

    emit_ln_stream(b, t, t.xloc, 8, on_group_loc, rms_groups=list(range(8)))
    b.barrier()

    def slot(kt):
        return t.ps_cs[kt % 4], 32 * (kt // 4)

    mz = b.do("vector", lambda v: v.memset(t.E2[:, 0:512], 0.0), sem="const2")
    for h in range(2):
        if h > 0:
            b.wait("tensor", "cs_evac", b.c("cs_evac"))
        b.wait("tensor", "const", b.c("const"))
        b.wait("tensor", "const2", mz)

        def t_zero(q):
            mm = None
            for kt in range(16):
                pt, row = slot(kt)
                tp = (0, row) if row else None
                mm = q.matmul(pt[row:row + 1, :], t.zeros1[:, :],
                              t.E2[:, 0:512], start=True, stop=False,
                              tile_position=tp)
            return mm
        b.do("tensor", t_zero, sem="zs")

        for qc in range(64):
            ebuf = qc % 2
            for sub in range(8):
                psc = t.ps_scA if sub % 2 == 0 else t.ps_scB
                b.wait("tensor", "exp_done", b.c("exp_done") - 1)

                def t_sc(q, h=h, qc=qc, sub=sub, psc=psc):
                    mm = None
                    for half in range(2):
                        kt = sub * 2 + half
                        mm = q.matmul(
                            psc[:, half * 512:half * 512 + 512],
                            t.xq[:, h * S + qc * 128:h * S + qc * 128 + 128],
                            t.xk[:, h * S + kt * 512:h * S + kt * 512 + 512],
                            start=True, stop=True)
                    return mm
                sdv = b.do("tensor", t_sc, sem="sc_done")
                b.wait("scalar", "sc_done", sdv)
                if sub == 0:
                    b.wait("scalar", "rmm_done", b.c("rmm_done") - 1)

                def s_exp(q, ebuf=ebuf, sub=sub, psc=psc):
                    return q.activation(
                        out=t.E2[:, ebuf * S + sub * 1024:
                                 ebuf * S + sub * 1024 + 1024],
                        in_=psc[:, :], func=AF.Exp,
                        scale=f32(INV_SQRT_HD), bias=f32(-C0_SHIFT),
                        accum_out=t.Zp[:, ebuf * 8 + sub:ebuf * 8 + sub + 1])
                b.do("scalar", s_exp, sem="exp_done")
            b.wait("vector", "exp_done", b.c("exp_done"))
            b.wait("vector", "rmm_done", b.c("rmm_done") - 1)

            b.fdo("vector", lambda q, ebuf=ebuf: q.tensor_reduce(
                t.zwork[:, 0:1], t.Zp[:, ebuf * 8:ebuf * 8 + 8],
                axis=AX.X, op=ALU.add))
            b.fdo("vector", lambda q: q.reciprocal(
                t.zwork[:, 1:2], t.zwork[:, 0:1]))
            zdv = b.do("vector", lambda q, ebuf=ebuf: q.tensor_copy(
                t.rbf[:, ebuf:ebuf + 1], t.zwork[:, 1:2]), sem="z_done")
            b.wait("tensor", "z_done", zdv)
            if qc == 0:
                b.wait("tensor", "zs", b.c("zs"))

            def t_rmm(q, qc=qc, ebuf=ebuf):
                mm = None
                for kt in range(16):
                    pt, row = slot(kt)
                    tp = (0, row) if row else None
                    mm = q.matmul(pt[row:row + 1, :], t.rbf[:, ebuf:ebuf + 1],
                                  t.E2[:, ebuf * S + kt * 512:
                                       ebuf * S + kt * 512 + 512],
                                  start=False, stop=(qc == 63),
                                  tile_position=tp)
                return mm
            b.do("tensor", t_rmm, sem="rmm_done")

        b.wait("scalar", "rmm_done", b.c("rmm_done"))
        b.wait("scalar", "bc_done", b.c("bc_done"))
        for kt in range(16):
            pt, row = slot(kt)
            b.do("scalar", lambda q, pt=pt, row=row, kt=kt: q.copy(
                t.cs_sb[0:1, kt * 512:kt * 512 + 512], pt[row:row + 1, :]),
                sem="cs_evac")
        b.wait("tensor", "cs_evac", b.c("cs_evac"))
        b.wait("vector", "st_out", b.c("st_out"))
        for kt in range(16):
            if kt > 0:
                b.wait("tensor", "vs_done", b.c("vs_done"))
            bcv = b.do("tensor", lambda q, kt=kt: q.matmul(
                t.ps_scA[:, 0:512], t.ones1[:, :],
                t.cs_sb[0:1, kt * 512:kt * 512 + 512], start=True, stop=True),
                sem="bc_done")
            b.wait("vector", "bc_done", bcv)
            b.do("vector", lambda q, h=h, kt=kt: q.tensor_tensor(
                t.val_sb[:, kt * 512:kt * 512 + 512],
                t.xv[:, h * S + kt * 512:h * S + kt * 512 + 512],
                t.ps_scA[:, 0:512], op=ALU.mult), sem="vs_done")
        b.wait("tensor", "vs_done", b.c("vs_done"))
        b.wait("gpsimd", "vs_done", b.c("vs_done"))
        b.dma("gpsimd", t.val_o[h * 128:(h + 1) * 128, :], t.val_sb[:, :],
              sem="st_out")
    b.barrier()

    emit_ffn_moe(b, t, n_exp=8, hid_per_exp=1024, use_rms=True, scale_out=4.0)
    b.wait("gpsimd", "st_out", b.c("st_out"))


def emit_ffn_moe(b, t, *, n_exp, hid_per_exp, use_rms, scale_out):
    H = n_exp * hid_per_exp
    n_hc = H // 128

    lw = b.dma("gpsimd", t.rw_sb[:, :].rearrange("p (dc e) -> p dc e", dc=16),
               t.rw.rearrange("(dc p) e -> p dc e", p=128))
    if use_rms:
        lw = b.dma("gpsimd", t.nw_sb[:, :], t.nw)
    b.wait("vector", "ld_gpsimd", lw)
    b.wait("tensor", "ld_gpsimd", lw)

    for di in range(16):
        b.wait("vector", "lg_mm", b.c("lg_mm") - 1)
        if use_rms:
            xwv = b.do("vector", lambda v, di=di: v.tensor_scalar(
                t.xnw2[:, (di % 2) * TLOC:(di % 2 + 1) * TLOC],
                t.xn_loc[:, di * TLOC:(di + 1) * TLOC],
                t.nw_sb[:, di:di + 1], None, op0=ALU.mult), sem="xnw_done")
        else:
            xwv = b.do("vector", lambda v, di=di: v.tensor_copy(
                t.xnw2[:, (di % 2) * TLOC:(di % 2 + 1) * TLOC],
                t.xn_loc[:, di * TLOC:(di + 1) * TLOC]), sem="xnw_done")
        b.wait("tensor", "xnw_done", xwv)

        def t_lg(q, di=di):
            mm = None
            for half in range(2):
                mm = q.matmul(t.ps_lg[:, half * 512:half * 512 + 512],
                              t.rw_sb[:, di * 8:di * 8 + 8],
                              t.xnw2[:, (di % 2) * TLOC + half * 512:
                                     (di % 2) * TLOC + half * 512 + 512],
                              start=(di == 0), stop=(di == 15))
            return mm
        b.do("tensor", t_lg, sem="lg_mm")
    b.wait("scalar", "lg_mm", b.c("lg_mm"))
    lev = b.do("scalar", lambda q: q.copy(t.lg_sb[:, :], t.ps_lg[:, :]),
               sem="lg_evac")
    b.wait("tensor", "lg_evac", lev)

    def t_ltr(q):
        mm = None
        for lg in range(8):
            mm = q.transpose(t.ps_t8[:, lg * 8:lg * 8 + 8],
                             t.lg_sb[:, lg * 128:lg * 128 + 128],
                             t.ident[0:8, 0:8])
        return mm
    ltv = b.do("tensor", t_ltr, sem="lg_tr")
    b.wait("vector", "lg_tr", ltv)
    ltev = b.do("vector", lambda v: v.tensor_copy(t.ltm[:, :], t.ps_t8[:, :]),
                sem="lg_tr_ev")

    b.wait("vector", "lg_tr_ev", ltev)
    if use_rms:
        b.wait("vector", "rms_done", b.c("rms_done"))
    for lg in range(8):
        col = (lambda j, lg=lg: t.r_sc[:, 8 * lg + j:8 * lg + j + 1])
        l8 = (lambda lg=lg: t.ltm[:, lg * 8:lg * 8 + 8])
        x8 = (lambda lg=lg: t.exps[:, lg * 8:lg * 8 + 8])
        e8 = (lambda lg=lg: t.eqs[:, lg * 8:lg * 8 + 8])
        t8 = (lambda lg=lg: t.tts[:, lg * 8:lg * 8 + 8])
        c8 = (lambda lg=lg: t.cwtm[:, lg * 8:lg * 8 + 8])
        if use_rms:
            rsv = b.do("vector", lambda v, col=col, lg=lg: v.tensor_scalar(
                col(0), t.rms_sumsq[:, lg:lg + 1], 1.0 / DM, 1e-6,
                op0=ALU.mult, op1=ALU.add), sem="rr1")
            b.wait("scalar", "rr1", rsv)
            rs2 = b.do("scalar", lambda q, col=col: q.sqrt(col(1), col(0)),
                       sem="rr2")
            b.wait("vector", "rr2", rs2)
            b.fdo("vector", lambda v, col=col: v.reciprocal(col(2), col(1)))
            b.fdo("vector", lambda v, col=col, l8=l8: v.tensor_scalar(
                l8(), l8(), col(2), None, op0=ALU.mult))

        b.fdo("vector", lambda v, col=col, l8=l8: v.tensor_reduce(
            col(3), l8(), axis=AX.X, op=ALU.max))
        mxv = b.do("vector", lambda v, col=col: v.tensor_scalar_mul(
            col(3), col(3), -1.0), sem="tq1")
        b.wait("scalar", "tq1", mxv)
        e8v = b.do("scalar", lambda q, col=col, l8=l8, x8=x8: q.activation(
            out=x8(), in_=l8(), func=AF.Exp, bias=col(3)), sem="tq2")
        b.wait("vector", "tq2", e8v)

        for step in [
            lambda v, col=col, x8=x8: v.tensor_reduce(
                col(4), x8(), axis=AX.X, op=ALU.max),
            lambda v, col=col, x8=x8, e8=e8: v.tensor_scalar(
                e8(), x8(), col(4), None, op0=ALU.is_ge),
            lambda v, x8=x8, e8=e8, t8=t8: v.tensor_tensor(
                t8(), x8(), e8(), op=ALU.mult),
            lambda v, x8=x8, t8=t8: v.tensor_sub(t8(), x8(), t8()),
            lambda v, col=col, t8=t8: v.tensor_reduce(
                col(5), t8(), axis=AX.X, op=ALU.max),
            lambda v, col=col, x8=x8, e8=e8: v.tensor_scalar(
                e8(), x8(), col(5), None, op0=ALU.is_ge),
            lambda v, col=col: v.tensor_add(col(6), col(4), col(5)),
            lambda v, col=col: v.reciprocal(col(7), col(6)),
            lambda v, x8=x8, e8=e8, t8=t8: v.tensor_tensor(
                t8(), x8(), e8(), op=ALU.mult),
        ]:
            b.fdo("vector", step)
        ctv = b.do("vector", lambda v, col=col, t8=t8, c8=c8: v.tensor_scalar(
            c8(), t8(), col(7), float(scale_out), op0=ALU.mult, op1=ALU.mult),
            sem="cw_tm_done")
        b.wait("tensor", "cw_tm_done", ctv)
        b.do("tensor", lambda q, lg=lg, c8=c8: q.transpose(
            t.ps_ct[:, lg * 128:lg * 128 + 128], c8(), t.ident[:, :]),
            sem="cw_tr")
    b.wait("vector", "cw_tr", b.c("cw_tr"))
    cfev = b.do("vector", lambda v: v.tensor_copy(t.cw8[:, :], t.ps_ct[:, :]),
                sem="cw_fm_ev")
    b.wait("gpsimd", "cw_fm_ev", cfev)
    crv = None
    for e in range(8):
        crv = b.dma("gpsimd", t.cw_row[0:1, e * TLOC:(e + 1) * TLOC],
                    t.cw8[e:e + 1, :], sem="ld_cwrow")
    b.wait("tensor", "ld_cwrow", crv)
    for e in range(n_exp):
        for half in range(2):
            b.wait("tensor", "cwb_ev", b.c("cwb_ev"))
            bcv = b.do("tensor", lambda q, e=e, half=half: q.matmul(
                t.ps_bc[:, :], t.ones1[:, :],
                t.cw_row[0:1, e * TLOC + half * 512:e * TLOC + half * 512 + 512],
                start=True, stop=True), sem="cwb_mm")
            b.wait("vector", "cwb_mm", bcv)
            b.do("vector", lambda v, e=e, half=half: v.tensor_copy(
                t.cwb[:, e * TLOC + half * 512:e * TLOC + half * 512 + 512],
                t.ps_bc[:, :]), sem="cwb_ev")

    b.wait("tensor", "cwb_ev", b.c("cwb_ev"))
    for hc in range(n_hc):
        e = hc // (hid_per_exp // 128)
        wbuf = hc % 2
        b.wait("gpsimd", "gu_mm", b.c("gu_mm") - 1)
        b.dma("gpsimd",
              t.gwu2[:, wbuf * 4096:wbuf * 4096 + 2048].rearrange(
                  "p (dc ff) -> p dc ff", dc=16),
              t.gw[:, hc * 128:(hc + 1) * 128].rearrange(
                  "(dc p) ff -> p dc ff", p=128), sem="ld_w")
        gl = b.dma("gpsimd",
                   t.gwu2[:, wbuf * 4096 + 2048:wbuf * 4096 + 4096].rearrange(
                       "p (dc ff) -> p dc ff", dc=16),
                   t.uw[:, hc * 128:(hc + 1) * 128].rearrange(
                       "(dc p) ff -> p dc ff", p=128), sem="ld_w")
        b.wait("tensor", "ld_w", gl)
        psg = t.ps_gA if hc % 2 == 0 else t.ps_gB
        psu = t.ps_uA if hc % 2 == 0 else t.ps_uB
        b.wait("tensor", "silu_done", b.c("silu_done") - 1)

        def t_g(q, wbuf=wbuf, psg=psg):
            mm = None
            for di in range(16):
                for half in range(2):
                    mm = q.matmul(
                        psg[:, half * 512:half * 512 + 512],
                        t.gwu2[:, wbuf * 4096 + di * 128:
                               wbuf * 4096 + di * 128 + 128],
                        t.xn_loc[:, di * TLOC + half * 512:
                                 di * TLOC + half * 512 + 512],
                        start=(di == 0), stop=(di == 15))
            return mm
        gmv = b.do("tensor", t_g, sem="g_mm")
        b.wait("tensor", "aprod_done", b.c("aprod_done") - 1)

        def t_u(q, wbuf=wbuf, psu=psu):
            mm = None
            for di in range(16):
                for half in range(2):
                    mm = q.matmul(
                        psu[:, half * 512:half * 512 + 512],
                        t.gwu2[:, wbuf * 4096 + 2048 + di * 128:
                               wbuf * 4096 + 2048 + di * 128 + 128],
                        t.xn_loc[:, di * TLOC + half * 512:
                                 di * TLOC + half * 512 + 512],
                        start=(di == 0), stop=(di == 15))
            return mm
        umv = b.do("tensor", t_u, sem="gu_mm")
        b.wait("scalar", "g_mm", gmv)
        slv = b.do("scalar", lambda q, psg=psg: q.activation(
            out=t.gs_sb[:, :], in_=psg[:, :], func=AF.Silu), sem="silu_done")
        b.wait("vector", "gu_mm", umv)
        b.wait("vector", "silu_done", slv)
        abuf = hc % 2
        b.wait("vector", "a_st", b.c("a_st") - 16)

        def v_a(q, e=e, abuf=abuf, psu=psu):
            q.tensor_tensor(t.aw2[:, abuf * TLOC:(abuf + 1) * TLOC],
                            t.gs_sb[:, :], psu[:, :], op=ALU.mult)
            return q.tensor_tensor(
                t.aw2[:, abuf * TLOC:(abuf + 1) * TLOC],
                t.aw2[:, abuf * TLOC:(abuf + 1) * TLOC],
                t.cwb[:, e * TLOC:(e + 1) * TLOC], op=ALU.mult)
        apv = b.do("vector", v_a, sem="aprod_done")
        b.wait("gpsimd", "aprod_done", apv)
        b.dma("gpsimd", t.a_dram[hc * 128:(hc + 1) * 128, :],
              t.aw2[:, abuf * TLOC:(abuf + 1) * TLOC], sem="a_st")

    b.wait("sync", "a_st", b.c("a_st"))
    b.wait("tensor", "aprod_done", b.c("aprod_done"))
    b.wait("tensor", "silu_done", b.c("silu_done"))
    for fog in range(4):
        b.wait("tensor", "y_silu", b.c("y_silu"))
        emit_down_accum(b, t, t.dwn, t.a_dram, n_hc, fog, 32, first=True)
        b.wait("scalar", "dn_pair", b.c("dn_pair"))
        for fo_in in range(4):
            fo = fog * 4 + fo_in
            obuf = fo % 2
            b.wait("scalar", "g0_st", b.c("g0_st") - 16)
            ysv = b.do("scalar", lambda q, fo_in=fo_in, obuf=obuf: q.activation(
                out=t.g0t[:, obuf * TLOC:(obuf + 1) * TLOC],
                in_=t.ps_y[fo_in][:, :], func=AF.Silu), sem="y_silu")
            b.wait("gpsimd", "y_silu", ysv)
            b.dma("gpsimd", t.g0_o[fo * 128:(fo + 1) * 128, :],
                  t.g0t[:, obuf * TLOC:(obuf + 1) * TLOC], sem="g0_st")
    b.wait("gpsimd", "g0_st", b.c("g0_st"))



_IDENT = np.eye(128, dtype=np.float32).astype(BF)


def host_inputs_A(inputs):
    x = np.ascontiguousarray(inputs["L0_attn_x"].reshape(S, DM))
    wqkv = inputs["L0_attn_w_qkv"]
    gw = np.ascontiguousarray(
        inputs["L0_ffn_gate_w"].transpose(1, 0, 2).reshape(DM, 8192))
    uw = np.ascontiguousarray(
        inputs["L0_ffn_up_w"].transpose(1, 0, 2).reshape(DM, 8192))
    dwn = np.ascontiguousarray(inputs["L0_ffn_down_w"].reshape(8192, DM))
    rw = np.ascontiguousarray(inputs["L0_ffn_router_w"].T)
    nw = np.ascontiguousarray(inputs["L0_ffn_norm_w"].reshape(16, 128).T)
    maps = []
    for c in range(NCORES):
        sl = np.concatenate([
            wqkv[:, 256 * c:256 * (c + 1)],
            wqkv[:, 2048 + 256 * c:2048 + 256 * (c + 1)],
            wqkv[:, 4096 + 256 * c:4096 + 256 * (c + 1)]], axis=1)
        maps.append({
            "x": x, "xloc": x[TLOC * c:TLOC * (c + 1)],
            "wqkv": np.ascontiguousarray(sl), "rw": rw, "nw": nw,
            "gw": gw, "uw": uw, "dwn": dwn, "ident_c": _IDENT})
    return maps





def make_tensors_B(nc, es):
    t = types.SimpleNamespace()
    t.xfm = nc.dram_tensor("xfm", [DM, TLOC], F32, kind="ExternalInput").ap()
    t.val = nc.dram_tensor("val", [DM, TLOC], BF16, kind="ExternalInput").ap()
    t.g0 = nc.dram_tensor("g0", [DM, TLOC], BF16, kind="ExternalInput").ap()
    t.wo = nc.dram_tensor("wo", [DM, DM], F32, kind="ExternalInput").ap()
    t.x1_o = nc.dram_tensor("x1out", [DM, TLOC], F32, kind="ExternalOutput").ap()
    sb = es.enter_context
    t.wo_sb = sb(nc.sbuf_tensor("wo_sb", [128, 16 * DM], BF16))
    t.val_sb = sb(nc.sbuf_tensor("val_sb", [128, 16 * TLOC], BF16))
    t.g0_sb = sb(nc.sbuf_tensor("g0_sb", [128, 16 * TLOC], BF16))
    t.x_sb = sb(nc.sbuf_tensor("x_sb", [128, 16 * TLOC], F32))
    t.x1_sb = sb(nc.sbuf_tensor("x1_sb", [128, 2 * TLOC], F32))
    t.sg_sb = sb(nc.sbuf_tensor("sg_sb", [128, TLOC], F32))
    t.ps_s = [sb(nc.psum_tensor(f"ps_s{i}", [128, TLOC], F32)) for i in range(2)]
    return t


def emit_B(b, t):
    lv = b.dma("gpsimd", t.wo_sb[:, :].rearrange("p (dc o) -> p dc o", dc=16),
               t.wo.rearrange("(dc p) o -> p dc o", p=128))
    b.dma("sync", t.val_sb[:, :].rearrange("p (dc tt) -> p dc tt", dc=16),
          t.val.rearrange("(dc p) tt -> p dc tt", p=128))
    b.dma("sync", t.g0_sb[:, :].rearrange("p (dc tt) -> p dc tt", dc=16),
          t.g0.rearrange("(dc p) tt -> p dc tt", p=128))
    xv = b.dma("sync", t.x_sb[:, :].rearrange("p (dc tt) -> p dc tt", dc=16),
               t.xfm.rearrange("(dc p) tt -> p dc tt", p=128))
    b.wait("tensor", "ld_gpsimd", lv)
    b.wait("tensor", "ld_sync", xv)
    for fo in range(16):
        ps = t.ps_s[fo % 2]
        b.wait("tensor", "sg_done", b.c("sg_done") - 1)

        def t_s(q, fo=fo, ps=ps):
            mm = None
            for fi in range(16):
                for half in range(2):
                    mm = q.matmul(
                        ps[:, half * 512:half * 512 + 512],
                        t.wo_sb[:, fi * DM + fo * 128:fi * DM + fo * 128 + 128],
                        t.val_sb[:, fi * TLOC + half * 512:
                                 fi * TLOC + half * 512 + 512],
                        start=(fi == 0), stop=(fi == 15))
            return mm
        smv = b.do("tensor", t_s, sem="s_mm")
        b.wait("vector", "s_mm", smv)
        b.wait("vector", "x1_st", b.c("x1_st") - 16)

        def v_comb(q, fo=fo, ps=ps):
            q.tensor_tensor(t.sg_sb[:, :], ps[:, :],
                            t.g0_sb[:, fo * TLOC:(fo + 1) * TLOC], op=ALU.mult)
            return q.tensor_add(t.x1_sb[:, (fo % 2) * TLOC:(fo % 2 + 1) * TLOC],
                                t.x_sb[:, fo * TLOC:(fo + 1) * TLOC],
                                t.sg_sb[:, :])
        cbv = b.do("vector", v_comb, sem="sg_done")
        b.wait("gpsimd", "sg_done", cbv)
        b.dma("gpsimd", t.x1_o[fo * 128:(fo + 1) * 128, :],
              t.x1_sb[:, (fo % 2) * TLOC:(fo % 2 + 1) * TLOC], sem="x1_st")
    b.wait("gpsimd", "x1_st", b.c("x1_st"))


def host_inputs_B(inputs, val_parts, g0_parts):
    x = inputs["L0_attn_x"].reshape(S, DM)
    val_full = np.concatenate(val_parts, axis=0)
    maps = []
    for c in range(NCORES):
        xfm = np.ascontiguousarray(x[TLOC * c:TLOC * (c + 1)].T)
        maps.append({
            "xfm": xfm,
            "val": np.ascontiguousarray(val_full[:, TLOC * c:TLOC * (c + 1)]),
            "g0": g0_parts[c],
            "wo": inputs["L0_attn_w_o"]})
    return maps





N_HC1 = 88
N_HCS = 44
SCALE1 = float(1.0 / (50.0 * np.sqrt(128.0)))


def make_tensors_C(nc, es):
    t = types.SimpleNamespace()
    t.x = nc.dram_tensor("x", [S, DM], F32, kind="ExternalInput").ap()
    t.xloc = nc.dram_tensor("xloc", [TLOC, DM], F32, kind="ExternalInput").ap()
    t.wqkv = nc.dram_tensor("wqkv", [DM, 512], F32, kind="ExternalInput").ap()
    t.rw = nc.dram_tensor("rw", [DM, 8], F32, kind="ExternalInput").ap()
    t.gw = nc.dram_tensor("gw", [DM, N_HC1 * 128], F32, kind="ExternalInput").ap()
    t.uw = nc.dram_tensor("uw", [DM, N_HC1 * 128], F32, kind="ExternalInput").ap()
    t.dwn = nc.dram_tensor("dwn", [N_HC1 * 128, DM], F32,
                           kind="ExternalInput").ap()
    t.sgw_d = nc.dram_tensor("sgw", [128, 16], F32, kind="ExternalInput").ap()
    t.swg = nc.dram_tensor("swg", [DM, N_HCS * 128], F32,
                           kind="ExternalInput").ap()
    t.swu = nc.dram_tensor("swu", [DM, N_HCS * 128], F32,
                           kind="ExternalInput").ap()
    t.swd = nc.dram_tensor("swd", [N_HCS * 128, DM], F32,
                           kind="ExternalInput").ap()
    t.cos_d = nc.dram_tensor("cos_c", [128, S], F32, kind="ExternalInput").ap()
    t.sin_d = nc.dram_tensor("sin_c", [128, S], F32, kind="ExternalInput").ap()
    t.mask_d = nc.dram_tensor("mask_c", [128, 8 * 1024], BF16,
                              kind="ExternalInput").ap()
    t.ident_c = nc.dram_tensor("ident_c", [128, 128], BF16,
                               kind="ExternalInput").ap()
    t.out1_o = nc.dram_tensor("out1", [256, S], BF16, kind="ExternalOutput").ap()
    t.g1_o = nc.dram_tensor("g1out", [DM, TLOC], BF16, kind="ExternalOutput").ap()
    t.a_dram = nc.dram_tensor("a_stage", [N_HC1 * 128, TLOC], BF16).ap()
    t.sh_dram = nc.dram_tensor("sh_stage", [N_HCS * 128, TLOC], BF16).ap()

    sb = es.enter_context
    t.zeros1 = sb(nc.sbuf_tensor("zeros1", [128, 1], BF16))
    t.ones1 = sb(nc.sbuf_tensor("ones1", [1, 128], BF16))
    t.onescol = sb(nc.sbuf_tensor("onescol", [128, 1], BF16))
    t.ident = sb(nc.sbuf_tensor("ident", [128, 128], BF16))
    t.stats = sb(nc.sbuf_tensor("stats", [128, 8], F32))
    t.rms_sumsq = sb(nc.sbuf_tensor("rms_sumsq", [128, 8], F32))
    t.xq = sb(nc.sbuf_tensor("xq", [128, 2 * S], BF16))
    t.xk = sb(nc.sbuf_tensor("xk", [128, S], BF16))
    t.xv = sb(nc.sbuf_tensor("xv", [128, S], BF16))

    with contextlib.ExitStack() as s12:
        t.xn_loc = s12.enter_context(
            nc.sbuf_tensor("xn_loc", [128, 16 * TLOC], BF16))
        with contextlib.ExitStack() as s1:
            t.sq_scr = s1.enter_context(nc.sbuf_tensor("sq_scr", [128, DM], BF16))
            t.wqkv_sb = s1.enter_context(
                nc.sbuf_tensor("wqkv_sb", [128, 16 * 512], BF16))
            t.x_tm2 = s1.enter_context(
                nc.sbuf_tensor("x_tm2", [128, 2 * DM], F32))
            t.xn_tm2 = s1.enter_context(
                nc.sbuf_tensor("xn_tm2", [128, 2 * DM], BF16))
            t.xn_t = s1.enter_context(
                nc.sbuf_tensor("xn_t", [128, 16 * 512], BF16))
            t.cossin2 = s1.enter_context(
                nc.sbuf_tensor("cossin2", [128, 4 * 512], F32))
            t.rot_sb = s1.enter_context(
                nc.sbuf_tensor("rot_sb", [128, 512], F32))
            t.t1_sb = s1.enter_context(nc.sbuf_tensor("t1_sb", [128, 512], F32))
            t.ps_tr0 = s1.enter_context(
                nc.psum_tensor("ps_tr0", [128, 1024], BF16))
            t.ps_tr1 = s1.enter_context(
                nc.psum_tensor("ps_tr1", [128, 1024], BF16))
            t.ps_qkv = s1.enter_context(
                nc.psum_tensor("ps_qkv", [128, 512], F32))
            t.ps_v = s1.enter_context(nc.psum_tensor("ps_v", [128, 128], F32))
        with contextlib.ExitStack() as s2:
            t.rw_sb = s2.enter_context(nc.sbuf_tensor("rw_sb", [128, 128], BF16))
            t.sgw_sb = s2.enter_context(nc.sbuf_tensor("sgw_sb", [128, 16], BF16))
            t.cwb = s2.enter_context(nc.sbuf_tensor("cwb", [128, 8 * TLOC], BF16))
            t.sgb = s2.enter_context(nc.sbuf_tensor("sgb", [128, TLOC], BF16))
            t.sg_row = s2.enter_context(nc.sbuf_tensor("sg_row", [1, TLOC], BF16))
            with contextlib.ExitStack() as s2a:
                t.r_sc = s2a.enter_context(nc.sbuf_tensor("r_sc", [128, 64], F32))
                t.lg_sb = s2a.enter_context(nc.sbuf_tensor("lg_sb", [8, TLOC], BF16))
                t.ltm = s2a.enter_context(nc.sbuf_tensor("ltm", [128, 64], F32))
                t.exps = s2a.enter_context(nc.sbuf_tensor("exps", [128, 64], F32))
                t.eqs = s2a.enter_context(nc.sbuf_tensor("eqs", [128, 64], F32))
                t.tts = s2a.enter_context(nc.sbuf_tensor("tts", [128, 64], F32))
                t.cwtm = s2a.enter_context(nc.sbuf_tensor("cwtm", [128, 64], BF16))
                t.cw8 = s2a.enter_context(nc.sbuf_tensor("cw8", [8, TLOC], BF16))
                t.cw_row = s2a.enter_context(
                    nc.sbuf_tensor("cw_row", [1, 8 * TLOC], BF16))
                t.xnw2 = s2a.enter_context(
                    nc.sbuf_tensor("xnw2", [128, 2 * TLOC], BF16))
            t.gwu2 = s2.enter_context(
                nc.sbuf_tensor("gwu2", [128, 2 * 4096], BF16))
            t.aw2 = s2.enter_context(nc.sbuf_tensor("aw2", [128, 2 * TLOC], BF16))
            t.astr4 = s2.enter_context(
                nc.sbuf_tensor("astr4", [128, 2 * 2 * TLOC], BF16))
            t.dw_sb = s2.enter_context(
                nc.sbuf_tensor("dw_sb", [128, 2 * 20 * 512], BF16))
            t.moe_t = s2.enter_context(
                nc.sbuf_tensor("moe_t", [128, 4 * TLOC], BF16))
            t.cmb1 = s2.enter_context(nc.sbuf_tensor("cmb1", [128, TLOC], F32))
            t.cmb2 = s2.enter_context(nc.sbuf_tensor("cmb2", [128, TLOC], F32))
            t.g0t = s2.enter_context(nc.sbuf_tensor("g0t", [128, 2 * TLOC], BF16))
            with contextlib.ExitStack() as s2r:
                t.ps_lg = s2r.enter_context(
                    nc.psum_tensor("ps_lg", [8, TLOC], F32))
                t.ps_sg = s2r.enter_context(
                    nc.psum_tensor("ps_sg", [1, TLOC], F32))
                t.ps_t8 = s2r.enter_context(
                    nc.psum_tensor("ps_t8", [128, 64], BF16))
                t.ps_ct = s2r.enter_context(
                    nc.psum_tensor("ps_ct", [8, TLOC], BF16))
                t.ps_bc = s2r.enter_context(
                    nc.psum_tensor("ps_bc", [128, 512], F32))
            with contextlib.ExitStack() as s2e:
                t.ps_gA = s2e.enter_context(
                    nc.psum_tensor("ps_gA", [128, TLOC], F32))
                t.ps_uA = s2e.enter_context(
                    nc.psum_tensor("ps_uA", [128, TLOC], F32))
                t.ps_gB = s2e.enter_context(
                    nc.psum_tensor("ps_gB", [128, TLOC], F32))
                t.ps_uB = s2e.enter_context(
                    nc.psum_tensor("ps_uB", [128, TLOC], F32))
            with contextlib.ExitStack() as s2d:
                t.ps_y = [s2d.enter_context(
                    nc.psum_tensor(f"ps_y{i}", [128, TLOC], F32))
                    for i in range(4)]
    with contextlib.ExitStack() as s3:
        t.mask_sb = s3.enter_context(
            nc.sbuf_tensor("mask_sb", [128, 8 * 1024], BF16))
        t.E2 = s3.enter_context(nc.sbuf_tensor("E2c", [128, 2 * 2048], BF16))
        t.tn_sb = s3.enter_context(nc.sbuf_tensor("tn_sb", [128, 2048], BF16))
        t.S_acc = s3.enter_context(nc.sbuf_tensor("S_acc", [128, 1024], F32))
        t.S_bf = s3.enter_context(nc.sbuf_tensor("S_bf", [128, 1024], BF16))
        t.zr_f = s3.enter_context(nc.sbuf_tensor("zr_f", [1, 1024], F32))
        t.zr_f2 = s3.enter_context(nc.sbuf_tensor("zr_f2", [1, 1024], F32))
        t.zr_b = s3.enter_context(nc.sbuf_tensor("zr_b", [1, 1024], BF16))
        t.o_raw = s3.enter_context(nc.sbuf_tensor("o_raw", [128, 1024], F32))
        t.o_bf = s3.enter_context(nc.sbuf_tensor("o_bf", [128, S], BF16))
        t.ps_sc2 = s3.enter_context(nc.psum_tensor("ps_sc2", [128, 2048], F32))
        t.ps_o = s3.enter_context(nc.psum_tensor("ps_o", [128, 1024], F32))
        t.ps_z = s3.enter_context(nc.psum_tensor("ps_z", [1, 1024], F32))
    return t


def emit_C(b, t):
    emit_consts(b, t)
    b.do("vector", lambda v: v.memset(t.onescol[:, :], 1.0), sem="const")
    wv = b.dma("gpsimd", t.wqkv_sb[:, :].rearrange("p (dc o) -> p dc o", dc=16),
               t.wqkv.rearrange("(dc p) o -> p dc o", p=128))
    b.wait("tensor", "ld_gpsimd", wv)
    b.wait("tensor", "ld_const", b.c("ld_const"))

    def on_group(g, lnv, xsl):
        tile, gi = g // 4, g % 4
        pa, pb, tpv = emit_transposes(b, t, lnv, xsl, g)
        b.wait("vector", "tp_done", tpv)
        if gi == 0:
            b.wait("vector", "qm_done", 3 * tile)
            b.wait("vector", "v_mm", 4 * tile)

        def v_evac(q, gi=gi, pa=pa, pb=pb):
            d3 = t.xn_t.ap().rearrange("p (di tt) -> p di tt", di=16)
            q.tensor_copy(d3[:, 0:8, gi * 128:gi * 128 + 128],
                          pa.ap().rearrange("p (di tt) -> p di tt", di=8))
            return q.tensor_copy(
                d3[:, 8:16, gi * 128:gi * 128 + 128],
                pb.ap().rearrange("p (di tt) -> p di tt", di=8))
        b.do("vector", v_evac, sem="xp_evac")

        if gi == 3:
            b.wait("tensor", "xp_evac", b.c("xp_evac"))
            b.wait("sync", "rope_done", b.c("rope_done") - 3)
            csv = b.dma("sync", t.cossin2[:, (tile % 2) * 1024:
                                          (tile % 2) * 1024 + 512],
                        t.cos_d[:, tile * 512:(tile + 1) * 512], sem="ld_cs")
            csv = b.dma("sync", t.cossin2[:, (tile % 2) * 1024 + 512:
                                          (tile % 2) * 1024 + 1024],
                        t.sin_d[:, tile * 512:(tile + 1) * 512], sem="ld_cs")
            for oc in range(3):
                b.wait("tensor", "rope_done", b.c("rope_done"))

                def t_oc(q, oc=oc):
                    mm = None
                    for di in range(16):
                        mm = q.matmul(
                            t.ps_qkv[:, :],
                            t.wqkv_sb[:, di * 512 + oc * 128:
                                      di * 512 + oc * 128 + 128],
                            t.xn_t[:, di * 512:(di + 1) * 512],
                            start=(di == 0), stop=(di == 15))
                    return mm
                qmv = b.do("tensor", t_oc, sem="qm_done")
                b.wait("vector", "qm_done", qmv)
                b.wait("vector", "ld_cs", csv)
                dest = t.xq if oc < 2 else t.xk
                doff = oc * S if oc < 2 else 0
                cs0 = (tile % 2) * 1024

                def v_rope(q, dest=dest, doff=doff, tile=tile, cs0=cs0):
                    q.tensor_scalar_mul(t.rot_sb[0:64, :],
                                        t.ps_qkv[64:128, :], -1.0)
                    q.tensor_copy(t.rot_sb[64:128, :], t.ps_qkv[0:64, :])
                    q.tensor_tensor(t.t1_sb[:, :], t.ps_qkv[:, :],
                                    t.cossin2[:, cs0:cs0 + 512], op=ALU.mult)
                    q.tensor_tensor(t.rot_sb[:, :], t.rot_sb[:, :],
                                    t.cossin2[:, cs0 + 512:cs0 + 1024],
                                    op=ALU.mult)
                    return q.tensor_add(
                        dest[:, doff + tile * 512:doff + tile * 512 + 512],
                        t.t1_sb[:, :], t.rot_sb[:, :])
                b.do("vector", v_rope, sem="rope_done")
            for sub in range(4):
                b.wait("tensor", "v_evac", b.c("v_evac"))

                def t_v(q, sub=sub):
                    mm = None
                    for di in range(16):
                        mm = q.matmul(
                            t.ps_v[:, :],
                            t.xn_t[:, di * 512 + sub * 128:
                                   di * 512 + sub * 128 + 128],
                            t.wqkv_sb[:, di * 512 + 384:di * 512 + 512],
                            start=(di == 0), stop=(di == 15))
                    return mm
                vmv = b.do("tensor", t_v, sem="v_mm")
                b.wait("scalar", "v_mm", vmv)
                kc = tile * 4 + sub
                b.do("scalar", lambda q, kc=kc: q.copy(
                    t.xv[:, kc * 128:kc * 128 + 128], t.ps_v[:, :]),
                    sem="v_evac")

    emit_ln_stream(b, t, t.x, 64, on_group)

    def on_group_loc(g, lnv, xsl):
        pa, pb, tpv = emit_transposes(b, t, lnv, xsl, g)
        b.wait("vector", "tp_done", tpv)

        def v_evac(q, g=g, pa=pa, pb=pb):
            d3 = t.xn_loc.ap().rearrange("p (di tt) -> p di tt", di=16)
            q.tensor_copy(d3[:, 0:8, g * 128:g * 128 + 128],
                          pa.ap().rearrange("p (di tt) -> p di tt", di=8))
            return q.tensor_copy(
                d3[:, 8:16, g * 128:g * 128 + 128],
                pb.ap().rearrange("p (di tt) -> p di tt", di=8))
        b.do("vector", v_evac, sem="xp_evac")

    emit_ln_stream(b, t, t.xloc, 8, on_group_loc)
    b.barrier()

    emit_router_top2(b, t, n_exp=8, use_rms=False, scale_out=1.0)
    sgl = b.dma("gpsimd", t.sgw_sb[:, :], t.sgw_d, sem="ld_sgw")
    b.wait("tensor", "ld_sgw", sgl)
    for di in range(16):
        def t_sg(q, di=di):
            mm = None
            for half in range(2):
                mm = q.matmul(t.ps_sg[:, half * 512:half * 512 + 512],
                              t.sgw_sb[:, di:di + 1],
                              t.xn_loc[:, di * TLOC + half * 512:
                                       di * TLOC + half * 512 + 512],
                              start=(di == 0), stop=(di == 15))
            return mm
        b.do("tensor", t_sg, sem="sg_mm")
    b.wait("scalar", "sg_mm", b.c("sg_mm"))
    sgv = b.do("scalar", lambda q: q.activation(
        out=t.sg_row[:, :], in_=t.ps_sg[:, :], func=AF.Sigmoid), sem="sg_act")
    b.wait("tensor", "sg_act", sgv)
    for half in range(2):
        bcv = b.do("tensor", lambda q, half=half: q.matmul(
            t.ps_bc[:, :], t.ones1[:, :],
            t.sg_row[0:1, half * 512:half * 512 + 512],
            start=True, stop=True), sem="sgb_mm")
        b.wait("vector", "sgb_mm", bcv)
        sgbv = b.do("vector", lambda v, half=half: v.tensor_copy(
            t.sgb[:, half * 512:half * 512 + 512], t.ps_bc[:, :]),
            sem="sgb_ev")
        if half == 0:
            b.wait("tensor", "sgb_ev", b.c("sgb_ev"))

    b.wait("tensor", "sgb_ev", b.c("sgb_ev"))
    b.wait("gpsimd", "cwb_ev", b.c("cwb_ev"))
    emit_expert_gu(b, t, t.gw, t.uw, t.a_dram, N_HC1, 11)
    emit_expert_gu(b, t, t.swg, t.swu, t.sh_dram, N_HCS, None)

    b.wait("tensor", "aprod_done", b.c("aprod_done"))
    b.wait("tensor", "silu_done", b.c("silu_done"))
    b.wait("sync", "a_st", b.c("a_st"))
    for fog in range(4):
        b.wait("tensor", "y_silu", b.c("y_silu"))
        emit_down_accum(b, t, t.dwn, t.a_dram, N_HC1, fog, 20, first=True)
        b.wait("scalar", "dn_pair", b.c("dn_pair"))
        for fo_in in range(4):
            b.do("scalar", lambda q, fo_in=fo_in: q.copy(
                t.moe_t[:, fo_in * TLOC:(fo_in + 1) * TLOC],
                t.ps_y[fo_in][:, :]), sem="moe_ev")
        b.wait("tensor", "moe_ev", b.c("moe_ev"))
        emit_down_accum(b, t, t.swd, t.sh_dram, N_HCS, fog, 20, first=True)
        b.wait("vector", "dn_pair", b.c("dn_pair"))
        b.wait("vector", "moe_ev", b.c("moe_ev"))
        for fo_in in range(4):
            fo = fog * 4 + fo_in
            obuf = fo % 2
            b.wait("vector", "g1_st", b.c("g1_st") - 16)

            def v_comb(q, fo_in=fo_in):
                q.tensor_tensor(t.cmb1[:, :], t.ps_y[fo_in][:, :],
                                t.sgb[:, :], op=ALU.mult)
                return q.tensor_add(
                    t.cmb2[:, :],
                    t.moe_t[:, fo_in * TLOC:(fo_in + 1) * TLOC],
                    t.cmb1[:, :])
            cbv = b.do("vector", v_comb, sem="ffn_comb")
            b.wait("scalar", "ffn_comb", cbv)
            ysv = b.do("scalar", lambda q, obuf=obuf: q.activation(
                out=t.g0t[:, obuf * TLOC:(obuf + 1) * TLOC],
                in_=t.cmb2[:, :], func=AF.Silu), sem="y_silu")
            b.wait("gpsimd", "y_silu", ysv)
            b.dma("gpsimd", t.g1_o[fo * 128:(fo + 1) * 128, :],
                  t.g0t[:, obuf * TLOC:(obuf + 1) * TLOC], sem="g1_st")
    b.wait("gpsimd", "g1_st", b.c("g1_st"))
    b.barrier()

    mv = b.dma("gpsimd", t.mask_sb[:, :], t.mask_d, sem="ld_mask")
    b.wait("vector", "ld_mask", mv)
    for h in range(2):
        for qt in range(8):
            nkc = 8 * (qt + 1)
            for kp in range(nkc // 2):
                kc0 = 2 * kp
                ebuf = kp % 2
                b.wait("tensor", "tanh_done", b.c("tanh_done"))

                def t_sc(q, h=h, qt=qt, kc0=kc0):
                    mm = None
                    for ki in range(2):
                        for half in range(2):
                            mm = q.matmul(
                                t.ps_sc2[:, ki * 1024 + half * 512:
                                         ki * 1024 + half * 512 + 512],
                                t.xk[:, (kc0 + ki) * 128:(kc0 + ki) * 128 + 128],
                                t.xq[:, h * S + qt * 1024 + half * 512:
                                     h * S + qt * 1024 + half * 512 + 512],
                                start=True, stop=True)
                    return mm
                sdv = b.do("tensor", t_sc, sem="sc_done")
                b.wait("scalar", "sc_done", sdv)
                tnv = b.do("scalar", lambda q: q.activation(
                    out=t.tn_sb[:, :], in_=t.ps_sc2[:, :], func=AF.Tanh,
                    scale=f32(SCALE1)), sem="tanh_done")
                b.wait("scalar", "mask_done", b.c("mask_done"))
                b.wait("scalar", "sacc_done", b.c("sacc_done") - 1)
                b.wait("scalar", "pv_done", b.c("pv_done") - 1)
                exv = b.do("scalar", lambda q, ebuf=ebuf: q.activation(
                    out=t.E2[:, ebuf * 2048:ebuf * 2048 + 2048],
                    in_=t.tn_sb[:, :], func=AF.Exp, scale=f32(50.0)),
                    sem="exp_done")
                b.wait("vector", "exp_done", exv)
                for ki in range(2):
                    kc = kc0 + ki
                    if kc >= 8 * qt:
                        o = kc - 8 * qt
                        b.do("vector", lambda v, ebuf=ebuf, ki=ki, o=o:
                             v.tensor_tensor(
                                 t.E2[:, ebuf * 2048 + ki * 1024:
                                      ebuf * 2048 + ki * 1024 + 1024],
                                 t.E2[:, ebuf * 2048 + ki * 1024:
                                      ebuf * 2048 + ki * 1024 + 1024],
                                 t.mask_sb[:, o * 1024:o * 1024 + 1024],
                                 op=ALU.mult), sem="mask_done")
                b.wait("vector", "mask_done", b.c("mask_done"))
                if kp == 0:
                    b.do("vector", lambda v, ebuf=ebuf: v.tensor_copy(
                        t.S_acc[:, :], t.E2[:, ebuf * 2048:ebuf * 2048 + 1024]),
                        sem="sacc_pre")
                else:
                    b.do("vector", lambda v, ebuf=ebuf: v.tensor_add(
                        t.S_acc[:, :], t.S_acc[:, :],
                        t.E2[:, ebuf * 2048:ebuf * 2048 + 1024]),
                        sem="sacc_pre")
                sav = b.do("vector", lambda v, ebuf=ebuf: v.tensor_add(
                    t.S_acc[:, :], t.S_acc[:, :],
                    t.E2[:, ebuf * 2048 + 1024:ebuf * 2048 + 2048]),
                    sem="sacc_done")
                b.wait("tensor", "mask_done", b.c("mask_done"))
                b.wait("tensor", "exp_done", exv)

                def t_pv(q, kc0=kc0, ebuf=ebuf, nkc=nkc):
                    mm = None
                    for ki in range(2):
                        for half in range(2):
                            mm = q.matmul(
                                t.ps_o[:, half * 512:half * 512 + 512],
                                t.xv[:, (kc0 + ki) * 128:(kc0 + ki) * 128 + 128],
                                t.E2[:, ebuf * 2048 + ki * 1024 + half * 512:
                                     ebuf * 2048 + ki * 1024 + half * 512 + 512],
                                start=(kc0 + ki == 0), stop=(kc0 + ki == nkc - 1))
                    return mm
                b.do("tensor", t_pv, sem="pv_done")
            b.wait("vector", "sacc_done", b.c("sacc_done"))
            sbv = b.do("vector", lambda v: v.tensor_copy(t.S_bf[:, :],
                                                         t.S_acc[:, :]),
                       sem="sbf_done")
            b.wait("tensor", "sbf_done", sbv)

            def t_z(q):
                q.matmul(t.ps_z[:, 0:512], t.onescol[:, :], t.S_bf[:, 0:512],
                         start=True, stop=True)
                return q.matmul(t.ps_z[:, 512:1024], t.onescol[:, :],
                                t.S_bf[:, 512:1024], start=True, stop=True)
            zmv = b.do("tensor", t_z, sem="z_mm")
            b.wait("scalar", "z_mm", zmv)
            zev = b.do("scalar", lambda q: q.copy(t.zr_f[:, :], t.ps_z[:, :]),
                       sem="zev_done")
            b.wait("vector", "zev_done", zev)
            b.fdo("vector", lambda v: v.reciprocal(t.zr_f2[:, :], t.zr_f[:, :]))
            zbv = b.do("vector", lambda v: v.tensor_copy(t.zr_b[:, :],
                                                         t.zr_f2[:, :]),
                       sem="zb_cast")
            b.wait("scalar", "pv_done", b.c("pv_done"))
            oev = b.do("scalar", lambda q: q.copy(t.o_raw[:, :], t.ps_o[:, :]),
                       sem="oev_done")
            b.wait("tensor", "zb_cast", zbv)
            b.wait("tensor", "tanh_done", b.c("tanh_done"))
            for half in range(2):
                bcv = b.do("tensor", lambda q, half=half: q.matmul(
                    t.ps_sc2[:, half * 512:half * 512 + 512], t.ones1[:, :],
                    t.zr_b[0:1, half * 512:half * 512 + 512],
                    start=True, stop=True), sem="zb_mm")
            b.wait("vector", "zb_mm", b.c("zb_mm"))
            b.wait("vector", "oev_done", oev)

            def v_osc(q, qt=qt):
                q.tensor_tensor(t.o_bf[:, qt * 1024:qt * 1024 + 512],
                                t.o_raw[:, 0:512], t.ps_sc2[:, 0:512],
                                op=ALU.mult)
                return q.tensor_tensor(
                    t.o_bf[:, qt * 1024 + 512:qt * 1024 + 1024],
                    t.o_raw[:, 512:1024], t.ps_sc2[:, 512:1024], op=ALU.mult)
            osv = b.do("vector", v_osc, sem="o_sc")
            b.wait("tensor", "o_sc", osv)
            b.wait("tensor", "oev_done", b.c("oev_done"))
        b.wait("gpsimd", "o_sc", b.c("o_sc"))
        b.wait("vector", "o_st", b.c("o_st"))
        b.dma("gpsimd", t.out1_o[h * 128:(h + 1) * 128, :], t.o_bf[:, :],
              sem="o_st")
    b.wait("gpsimd", "o_st", b.c("o_st"))


def emit_router_top2(b, t, *, n_exp, use_rms, scale_out):
    lw = b.dma("gpsimd", t.rw_sb[:, :].rearrange("p (dc e) -> p dc e", dc=16),
               t.rw.rearrange("(dc p) e -> p dc e", p=128))
    if use_rms:
        lw = b.dma("gpsimd", t.nw_sb[:, :], t.nw)
    b.wait("vector", "ld_gpsimd", lw)
    b.wait("tensor", "ld_gpsimd", lw)
    for di in range(16):
        b.wait("vector", "lg_mm", b.c("lg_mm") - 1)
        if use_rms:
            xwv = b.do("vector", lambda v, di=di: v.tensor_scalar(
                t.xnw2[:, (di % 2) * TLOC:(di % 2 + 1) * TLOC],
                t.xn_loc[:, di * TLOC:(di + 1) * TLOC],
                t.nw_sb[:, di:di + 1], None, op0=ALU.mult), sem="xnw_done")
        else:
            xwv = b.do("vector", lambda v, di=di: v.tensor_copy(
                t.xnw2[:, (di % 2) * TLOC:(di % 2 + 1) * TLOC],
                t.xn_loc[:, di * TLOC:(di + 1) * TLOC]), sem="xnw_done")
        b.wait("tensor", "xnw_done", xwv)

        def t_lg(q, di=di):
            mm = None
            for half in range(2):
                mm = q.matmul(t.ps_lg[:, half * 512:half * 512 + 512],
                              t.rw_sb[:, di * 8:di * 8 + 8],
                              t.xnw2[:, (di % 2) * TLOC + half * 512:
                                     (di % 2) * TLOC + half * 512 + 512],
                              start=(di == 0), stop=(di == 15))
            return mm
        b.do("tensor", t_lg, sem="lg_mm")
    b.wait("scalar", "lg_mm", b.c("lg_mm"))
    lev = b.do("scalar", lambda q: q.copy(t.lg_sb[:, :], t.ps_lg[:, :]),
               sem="lg_evac")
    b.wait("tensor", "lg_evac", lev)

    def t_ltr(q):
        mm = None
        for lg in range(8):
            mm = q.transpose(t.ps_t8[:, lg * 8:lg * 8 + 8],
                             t.lg_sb[:, lg * 128:lg * 128 + 128],
                             t.ident[0:8, 0:8])
        return mm
    ltv = b.do("tensor", t_ltr, sem="lg_tr")
    b.wait("vector", "lg_tr", ltv)
    ltev = b.do("vector", lambda v: v.tensor_copy(t.ltm[:, :], t.ps_t8[:, :]),
                sem="lg_tr_ev")
    b.wait("vector", "lg_tr_ev", ltev)
    if use_rms:
        b.wait("vector", "rms_done", b.c("rms_done"))
    for lg in range(8):
        col = (lambda j, lg=lg: t.r_sc[:, 8 * lg + j:8 * lg + j + 1])
        l8 = (lambda lg=lg: t.ltm[:, lg * 8:lg * 8 + 8])
        x8 = (lambda lg=lg: t.exps[:, lg * 8:lg * 8 + 8])
        e8 = (lambda lg=lg: t.eqs[:, lg * 8:lg * 8 + 8])
        t8 = (lambda lg=lg: t.tts[:, lg * 8:lg * 8 + 8])
        c8 = (lambda lg=lg: t.cwtm[:, lg * 8:lg * 8 + 8])
        if use_rms:
            rsv = b.do("vector", lambda v, col=col, lg=lg: v.tensor_scalar(
                col(0), t.rms_sumsq[:, lg:lg + 1], 1.0 / DM, 1e-6,
                op0=ALU.mult, op1=ALU.add), sem="rr1")
            b.wait("scalar", "rr1", rsv)
            rs2 = b.do("scalar", lambda q, col=col: q.sqrt(col(1), col(0)),
                       sem="rr2")
            b.wait("vector", "rr2", rs2)
            b.fdo("vector", lambda v, col=col: v.reciprocal(col(2), col(1)))
            b.fdo("vector", lambda v, col=col, l8=l8: v.tensor_scalar(
                l8(), l8(), col(2), None, op0=ALU.mult))
        b.fdo("vector", lambda v, col=col, l8=l8: v.tensor_reduce(
            col(3), l8(), axis=AX.X, op=ALU.max))
        mxv = b.do("vector", lambda v, col=col: v.tensor_scalar_mul(
            col(3), col(3), -1.0), sem="tq1")
        b.wait("scalar", "tq1", mxv)
        e8v = b.do("scalar", lambda q, col=col, l8=l8, x8=x8: q.activation(
            out=x8(), in_=l8(), func=AF.Exp, bias=col(3)), sem="tq2")
        b.wait("vector", "tq2", e8v)
        for step in [
            lambda v, col=col, x8=x8: v.tensor_reduce(
                col(4), x8(), axis=AX.X, op=ALU.max),
            lambda v, col=col, x8=x8, e8=e8: v.tensor_scalar(
                e8(), x8(), col(4), None, op0=ALU.is_ge),
            lambda v, x8=x8, e8=e8, t8=t8: v.tensor_tensor(
                t8(), x8(), e8(), op=ALU.mult),
            lambda v, x8=x8, t8=t8: v.tensor_sub(t8(), x8(), t8()),
            lambda v, col=col, t8=t8: v.tensor_reduce(
                col(5), t8(), axis=AX.X, op=ALU.max),
            lambda v, col=col, x8=x8, e8=e8: v.tensor_scalar(
                e8(), x8(), col(5), None, op0=ALU.is_ge),
            lambda v, col=col: v.tensor_add(col(6), col(4), col(5)),
            lambda v, col=col: v.reciprocal(col(7), col(6)),
            lambda v, x8=x8, e8=e8, t8=t8: v.tensor_tensor(
                t8(), x8(), e8(), op=ALU.mult),
        ]:
            b.fdo("vector", step)
        ctv = b.do("vector", lambda v, col=col, t8=t8, c8=c8: v.tensor_scalar(
            c8(), t8(), col(7), float(scale_out), op0=ALU.mult, op1=ALU.mult),
            sem="cw_tm_done")
        b.wait("tensor", "cw_tm_done", ctv)
        b.do("tensor", lambda q, lg=lg, c8=c8: q.transpose(
            t.ps_ct[:, lg * 128:lg * 128 + 128], c8(), t.ident[:, :]),
            sem="cw_tr")
    b.wait("vector", "cw_tr", b.c("cw_tr"))
    cfev = b.do("vector", lambda v: v.tensor_copy(t.cw8[:, :], t.ps_ct[:, :]),
                sem="cw_fm_ev")
    b.wait("gpsimd", "cw_fm_ev", cfev)
    crv = None
    for e in range(8):
        crv = b.dma("gpsimd", t.cw_row[0:1, e * TLOC:(e + 1) * TLOC],
                    t.cw8[e:e + 1, :], sem="ld_cwrow")
    b.wait("tensor", "ld_cwrow", crv)
    for e in range(n_exp):
        for half in range(2):
            b.wait("tensor", "cwb_ev", b.c("cwb_ev"))
            bcv = b.do("tensor", lambda q, e=e, half=half: q.matmul(
                t.ps_bc[:, :], t.ones1[:, :],
                t.cw_row[0:1, e * TLOC + half * 512:
                         e * TLOC + half * 512 + 512],
                start=True, stop=True), sem="cwb_mm")
            b.wait("vector", "cwb_mm", bcv)
            b.do("vector", lambda v, e=e, half=half: v.tensor_copy(
                t.cwb[:, e * TLOC + half * 512:e * TLOC + half * 512 + 512],
                t.ps_bc[:, :]), sem="cwb_ev")


def emit_expert_gu(b, t, gw_d, uw_d, a_dram, n_hc, hc_per_exp):
    for hc in range(n_hc):
        wbuf = b.c("gu_mm") % 2
        b.wait("gpsimd", "gu_mm", b.c("gu_mm") - 1)
        b.dma("gpsimd",
              t.gwu2[:, wbuf * 4096:wbuf * 4096 + 2048].rearrange(
                  "p (dc ff) -> p dc ff", dc=16),
              gw_d[:, hc * 128:(hc + 1) * 128].rearrange(
                  "(dc p) ff -> p dc ff", p=128), sem="ld_w")
        gl = b.dma("gpsimd",
                   t.gwu2[:, wbuf * 4096 + 2048:wbuf * 4096 + 4096].rearrange(
                       "p (dc ff) -> p dc ff", dc=16),
                   uw_d[:, hc * 128:(hc + 1) * 128].rearrange(
                       "(dc p) ff -> p dc ff", p=128), sem="ld_w")
        b.wait("tensor", "ld_w", gl)
        psg = t.ps_gA if b.c("gu_mm") % 2 == 0 else t.ps_gB
        psu = t.ps_uA if b.c("gu_mm") % 2 == 0 else t.ps_uB
        b.wait("tensor", "silu_done", b.c("silu_done") - 1)

        def t_g(q, wbuf=wbuf, psg=psg):
            mm = None
            for di in range(16):
                for half in range(2):
                    mm = q.matmul(
                        psg[:, half * 512:half * 512 + 512],
                        t.gwu2[:, wbuf * 4096 + di * 128:
                               wbuf * 4096 + di * 128 + 128],
                        t.xn_loc[:, di * TLOC + half * 512:
                                 di * TLOC + half * 512 + 512],
                        start=(di == 0), stop=(di == 15))
            return mm
        gmv = b.do("tensor", t_g, sem="g_mm")
        b.wait("tensor", "aprod_done", b.c("aprod_done") - 1)

        def t_u(q, wbuf=wbuf, psu=psu):
            mm = None
            for di in range(16):
                for half in range(2):
                    mm = q.matmul(
                        psu[:, half * 512:half * 512 + 512],
                        t.gwu2[:, wbuf * 4096 + 2048 + di * 128:
                               wbuf * 4096 + 2048 + di * 128 + 128],
                        t.xn_loc[:, di * TLOC + half * 512:
                                 di * TLOC + half * 512 + 512],
                        start=(di == 0), stop=(di == 15))
            return mm
        umv = b.do("tensor", t_u, sem="gu_mm")
        abuf = b.c("a_st") // 16 % 2
        b.wait("scalar", "g_mm", gmv)
        b.wait("scalar", "a_st", b.c("a_st") - 16)
        slv = b.do("scalar", lambda q, psg=psg, abuf=abuf: q.activation(
            out=t.aw2[:, abuf * TLOC:(abuf + 1) * TLOC], in_=psg[:, :],
            func=AF.Silu), sem="silu_done")
        b.wait("vector", "gu_mm", umv)
        b.wait("vector", "silu_done", slv)

        if hc_per_exp:
            e = hc // hc_per_exp

            def v_a(q, e=e, abuf=abuf, psu=psu):
                q.tensor_tensor(t.aw2[:, abuf * TLOC:(abuf + 1) * TLOC],
                                t.aw2[:, abuf * TLOC:(abuf + 1) * TLOC],
                                psu[:, :], op=ALU.mult)
                return q.tensor_tensor(
                    t.aw2[:, abuf * TLOC:(abuf + 1) * TLOC],
                    t.aw2[:, abuf * TLOC:(abuf + 1) * TLOC],
                    t.cwb[:, e * TLOC:(e + 1) * TLOC], op=ALU.mult)
        else:
            def v_a(q, abuf=abuf, psu=psu):
                return q.tensor_tensor(
                    t.aw2[:, abuf * TLOC:(abuf + 1) * TLOC],
                    t.aw2[:, abuf * TLOC:(abuf + 1) * TLOC],
                    psu[:, :], op=ALU.mult)
        apv = b.do("vector", v_a, sem="aprod_done")
        b.wait("gpsimd", "aprod_done", apv)
        b.dma("gpsimd", a_dram[hc * 128:(hc + 1) * 128, :],
              t.aw2[:, abuf * TLOC:(abuf + 1) * TLOC], sem="a_st")


def emit_down_accum(b, t, dwn_d, a_dram, n_hc, fog, hc_chunk, first):
    for cs in range(0, n_hc, hc_chunk):
        ce = min(cs + hc_chunk, n_hc)
        cbuf = b.c("dn_chunk") % 2
        b.wait("gpsimd", "dn_chunk", b.c("dn_chunk") - 1)
        dl = None
        for part in range(0, ce - cs, 16):
            pe = min(part + 16, ce - cs)
            dl = b.dma(
                "gpsimd",
                t.dw_sb[:, (cbuf * hc_chunk + part) * 512:
                        (cbuf * hc_chunk + pe) * 512].rearrange(
                    "p (hcc ff) -> p hcc ff", hcc=pe - part),
                dwn_d.rearrange("(hcc p) o -> p hcc o", p=128)[
                    :, cs + part:cs + pe,
                    fog * 512:(fog + 1) * 512], sem="ld_dw")
        b.wait("tensor", "ld_dw", dl)
        for hc in range(cs, ce, 2):
            npair = min(2, ce - hc)
            pbuf = b.c("dn_pair") % 2
            b.wait("sync", "dn_pair", b.c("dn_pair") - 1)
            alv = b.dma(
                "sync",
                t.astr4[:, pbuf * 2048:pbuf * 2048 + npair * TLOC].rearrange(
                    "p (hh tt) -> p hh tt", hh=npair),
                a_dram[hc * 128:(hc + npair) * 128, :].rearrange(
                    "(hh p) tt -> p hh tt", p=128), sem="ld_a")
            b.wait("tensor", "ld_a", alv)

            def t_dn(q, hc=hc, cs=cs, npair=npair, pbuf=pbuf, cbuf=cbuf,
                     n_hc=n_hc, hc_chunk=hc_chunk):
                mm = None
                for ki in range(npair):
                    ci = hc + ki - cs
                    for fo_in in range(4):
                        for half in range(2):
                            mm = q.matmul(
                                t.ps_y[fo_in][:, half * 512:half * 512 + 512],
                                t.dw_sb[:, (cbuf * hc_chunk + ci) * 512 +
                                        fo_in * 128:
                                        (cbuf * hc_chunk + ci) * 512 +
                                        fo_in * 128 + 128],
                                t.astr4[:, pbuf * 2048 + ki * TLOC + half * 512:
                                        pbuf * 2048 + ki * TLOC + half * 512
                                        + 512],
                                start=(hc + ki == 0), stop=(hc + ki == n_hc - 1))
                return mm
            b.do("tensor", t_dn, sem="dn_pair")
        b.do("tensor", lambda q: q.nop(), sem="dn_chunk")


def host_inputs_C(inputs, x1_full):
    wq = inputs["L1_attn_wq"].T
    wk = inputs["L1_attn_wk"].T
    wv = inputs["L1_attn_wv"].T
    gw = np.ascontiguousarray(
        inputs["L1_ffn_w_gate"].transpose(1, 0, 2).reshape(DM, N_HC1 * 128))
    uw = np.ascontiguousarray(
        inputs["L1_ffn_w_up"].transpose(1, 0, 2).reshape(DM, N_HC1 * 128))
    dwn = np.ascontiguousarray(inputs["L1_ffn_w_down"].reshape(N_HC1 * 128, DM))
    rw = np.ascontiguousarray(inputs["L1_ffn_gate_w"].T)
    sgw = np.ascontiguousarray(
        inputs["L1_ffn_shared_gate_w"].reshape(16, 128).T)
    swg = inputs["L1_ffn_sw_gate"]
    swu = inputs["L1_ffn_sw_up"]
    swd = inputs["L1_ffn_sw_down"]
    inv = 1.0 / 10000.0 ** (np.arange(0, 128, 2, dtype=np.float32) / 128)
    fpos = np.arange(S, dtype=np.float32)[:, None] * inv[None, :]
    emb = np.concatenate((fpos, fpos), axis=-1)
    cos_c = np.ascontiguousarray(np.cos(emb).astype(np.float32).T)
    sin_c = np.ascontiguousarray(np.sin(emb).astype(np.float32).T)
    kp = np.arange(128)[:, None]
    qp = np.arange(1024)[None, :]
    mask = np.zeros((128, 8 * 1024), np.float32)
    for o in range(8):
        mask[:, o * 1024:(o + 1) * 1024] = (o * 128 + kp <= qp)
    mask_c = mask.astype(BF)
    maps = []
    for c in range(NCORES):
        sl = np.concatenate([
            wq[:, 256 * c:256 * (c + 1)],
            wk[:, 128 * c:128 * (c + 1)],
            wv[:, 128 * c:128 * (c + 1)]], axis=1)
        maps.append({
            "x": x1_full, "xloc": x1_full[TLOC * c:TLOC * (c + 1)],
            "wqkv": np.ascontiguousarray(sl), "rw": rw,
            "gw": gw, "uw": uw, "dwn": dwn, "sgw": sgw,
            "swg": swg, "swu": swu, "swd": swd,
            "cos_c": cos_c, "sin_c": sin_c, "mask_c": mask_c,
            "ident_c": _IDENT})
    return maps





def host_inputs_D(inputs, x1_full, out1_parts, g1_parts):
    out1_full = np.concatenate(out1_parts, axis=0)
    wo1 = np.ascontiguousarray(inputs["L1_attn_wo"].T)
    maps = []
    for c in range(NCORES):
        xfm = np.ascontiguousarray(x1_full[TLOC * c:TLOC * (c + 1)].T)
        maps.append({
            "xfm": xfm,
            "val": np.ascontiguousarray(out1_full[:, TLOC * c:TLOC * (c + 1)]),
            "g0": g1_parts[c],
            "wo": wo1})
    return maps


def kernel(**inputs):
    maps_A = host_inputs_A(inputs)
    nc_A = build(make_tensors_A, emit_A)
    res_A = run_bass_kernel_spmd(nc_A, maps_A, core_ids=list(range(NCORES)))
    val_parts = [res_A.results[c]["valout"] for c in range(NCORES)]
    g0_parts = [res_A.results[c]["g0out"] for c in range(NCORES)]

    maps_B = host_inputs_B(inputs, val_parts, g0_parts)
    nc_B = build(make_tensors_B, emit_B)
    res_B = run_bass_kernel_spmd(nc_B, maps_B, core_ids=list(range(NCORES)))
    x1_full = np.concatenate(
        [np.ascontiguousarray(res_B.results[c]["x1out"].T)
         for c in range(NCORES)], axis=0)

    maps_C = host_inputs_C(inputs, x1_full)
    nc_C = build(make_tensors_C, emit_C)
    res_C = run_bass_kernel_spmd(nc_C, maps_C, core_ids=list(range(NCORES)))
    out1_parts = [res_C.results[c]["out1"] for c in range(NCORES)]
    g1_parts = [res_C.results[c]["g1out"] for c in range(NCORES)]

    maps_D = host_inputs_D(inputs, x1_full, out1_parts, g1_parts)
    nc_D = build(make_tensors_B, emit_B)
    res_D = run_bass_kernel_spmd(nc_D, maps_D, core_ids=list(range(NCORES)))
    x2 = np.concatenate(
        [np.ascontiguousarray(res_D.results[c]["x1out"].T)
         for c in range(NCORES)], axis=0)
    return x2.reshape(1, S, DM)



# version 03
# speedup vs baseline: 1.1454x; 1.0483x over previous
import contextlib
import types
import numpy as np
import ml_dtypes
import concourse.bass as bass
import concourse.mybir as mybir
from concourse.bass_utils import run_bass_kernel_spmd

F32 = mybir.dt.float32
BF16 = mybir.dt.bfloat16
AF = mybir.ActivationFunctionType
ALU = mybir.AluOpType
AX = mybir.AxisListType
BF = ml_dtypes.bfloat16
f32 = np.float32

NCORES = 8
S = 8192
DM = 2048
TLOC = S // NCORES
C0_SHIFT = 16.0
ENGINES = ("sync", "gpsimd", "tensor", "vector", "scalar")
INV_SQRT_HD = 1.0 / np.sqrt(128.0)


class B:

    def __init__(self, nc, eng, q, sems):
        self.nc, self.eng, self.q, self.sems = nc, eng, q, sems
        self.cnt = {}
        self.nbar = 0

    def _sem(self, name):
        if name not in self.sems:
            assert self.eng == "__dry__", f"sem {name} not in dry pass"
            self.sems[name] = None

    def c(self, name):
        return self.cnt.get(name, 0)

    def inc(self, name, by=1):
        self._sem(name)
        self.cnt[name] = self.cnt.get(name, 0) + by
        return self.cnt[name]

    def do(self, eng, fn, sem=None, by=1):
        v = self.inc(sem, by) if sem else None
        if self.eng == eng:
            ins = fn(self.q)
            if sem:
                ins.then_inc(self.sems[sem], by)
        return v

    def wait(self, eng, sem, val):
        self._sem(sem)
        if self.eng == eng and val > 0:
            self.q.wait_ge(self.sems[sem], val)

    def fdo(self, eng, fn, sem=None):
        sem = sem or ("f_" + eng)
        v = self.do(eng, fn, sem=sem)
        self.wait(eng, sem, v)
        return v

    def dma(self, eng, out, in_, sem=None, **kw):
        sem = sem or ("ld_" + eng)
        return self.do(eng, lambda q: q.dma_start(out=out, in_=in_, **kw),
                       sem=sem, by=16)

    def barrier(self):
        self.nbar += 1
        self._sem("bar")
        tgt = 5 * self.nbar
        self.cnt["bar"] = tgt
        if self.eng != "__dry__":
            self.q.sem_inc(self.sems["bar"], 1)
            self.q.wait_ge(self.sems["bar"], tgt)


def build(make_tensors, emit):
    nc = bass.Bass("TRN2", target_bir_lowering=False)
    es = contextlib.ExitStack()
    with es:
        t = make_tensors(nc, es)
        sems = {}
        emit(B(nc, "__dry__", None, sems), t)
        for name in list(sems):
            sems[name] = es.enter_context(nc.semaphore(name))
        with nc.Block() as block:
            for eng in ENGINES:
                getattr(block, eng)(
                    lambda q, eng=eng: emit(B(nc, eng, q, sems), t))
    return nc



def emit_consts(b, t):
    b.do("vector", lambda v: v.memset(t.zeros1[:, :], 0.0), sem="const")
    b.do("vector", lambda v: v.memset(t.ones1[:, :], 1.0), sem="const")
    b.dma("gpsimd", t.ident[:, :], t.ident_c, sem="ld_const")


def interleave(*tasks):
    active = list(tasks)
    while active:
        keep = []
        for g, w in active:
            alive = True
            for _ in range(w):
                try:
                    next(g)
                except StopIteration:
                    alive = False
                    break
            if alive:
                keep.append((g, w))
        active = keep


def emit_ln_stream(b, t, x_dram, n_groups, on_group, rms_groups=None):
    for _ in gen_ln_stream(b, t, x_dram, n_groups, on_group, rms_groups):
        pass


def gen_ln_stream(b, t, x_dram, n_groups, on_group, rms_groups=None):
    st = t.stats
    for g in range(n_groups):
        buf = g % 2
        xsl = slice(buf * DM, (buf + 1) * DM)
        b.wait("sync", "ln_done", b.c("ln_done") - 1)
        b.wait("sync", "st_s", b.c("st_s") - 1)
        vld = b.dma("sync", t.x_tm2[:, xsl], x_dram[g * 128:(g + 1) * 128, :])
        b.wait("vector", "ld_sync", vld)
        b.fdo("vector", lambda v, xsl=xsl: v.tensor_reduce(
            st[:, 0:1], t.x_tm2[:, xsl], axis=AX.X, op=ALU.add))
        b.do("vector", lambda v: v.tensor_scalar_mul(
            st[:, 1:2], st[:, 0:1], -1.0 / DM), sem="st_v")
        b.wait("vector", "st_v", b.c("st_v"))
        b.wait("scalar", "ld_sync", vld)
        b.wait("scalar", "st_v", b.c("st_v"))
        sqv = b.do("scalar", lambda q, xsl=xsl: q.activation(
            out=t.sq_scr[:, :], in_=t.x_tm2[:, xsl], func=AF.Square,
            accum_out=st[:, 2:3]), sem="st_s")
        b.wait("vector", "st_s", sqv)
        b.fdo("vector", lambda v: v.tensor_scalar(
            st[:, 3:4], st[:, 2:3], 1.0 / DM, 1e-5, op0=ALU.mult, op1=ALU.add))
        b.fdo("vector", lambda v: v.tensor_tensor(
            st[:, 4:5], st[:, 1:2], st[:, 1:2], op=ALU.mult))
        s2v = b.do("vector", lambda v: v.tensor_sub(
            st[:, 3:4], st[:, 3:4], st[:, 4:5]), sem="st_v2")
        b.wait("scalar", "st_v2", s2v)
        s3v = b.do("scalar", lambda q: q.sqrt(st[:, 4:5], st[:, 3:4]),
                   sem="st_s2")
        b.wait("vector", "st_s2", s3v)
        b.fdo("vector", lambda v: v.reciprocal(st[:, 5:6], st[:, 4:5]))
        b.wait("vector", "tp_done", b.c("tp_done") - 1)
        if rms_groups and (g - 2) in rms_groups:
            b.wait("vector", "rms_done",
                   b.c("rms_done") - (0 if g in rms_groups else 1))
        lnv = b.do("vector", lambda v, xsl=xsl: v.tensor_scalar(
            t.xn_tm2[:, xsl], t.x_tm2[:, xsl], st[:, 1:2], st[:, 5:6],
            op0=ALU.add, op1=ALU.mult), sem="ln_done")
        if rms_groups and g in rms_groups:
            lg = rms_groups.index(g)
            b.wait("scalar", "ln_done", lnv)
            b.do("scalar", lambda q, xsl=xsl, lg=lg: q.activation(
                out=t.sq_scr[:, :], in_=t.xn_tm2[:, xsl], func=AF.Square,
                accum_out=t.rms_sumsq[:, lg:lg + 1]), sem="rms_done")
        on_group(g, lnv, xsl)
        yield


def emit_transposes(b, t, lnv, xsl, g):
    pa = t.ps_tr0 if g % 2 == 0 else t.ps_tr1
    pb = t.ps_tr1 if g % 2 == 0 else t.ps_tr0
    b.wait("tensor", "ln_done", lnv)
    b.wait("tensor", "xp_evac", b.c("xp_evac") - 1)

    def t_tr(q, xsl=xsl, pa=pa, pb=pb):
        mm = None
        for j in range(16):
            dst = pa if j < 8 else pb
            mm = q.transpose(
                dst[:, (j % 8) * 128:(j % 8) * 128 + 128],
                t.xn_tm2[:, xsl.start + j * 128:xsl.start + j * 128 + 128],
                t.ident[:, :])
        return mm
    tpv = b.do("tensor", t_tr, sem="tp_done")
    return pa, pb, tpv



def make_tensors_A(nc, es):
    t = types.SimpleNamespace()
    t.x = nc.dram_tensor("x", [S, DM], F32, kind="ExternalInput").ap()
    t.xloc = nc.dram_tensor("xloc", [TLOC, DM], F32, kind="ExternalInput").ap()
    t.wqkv = nc.dram_tensor("wqkv", [128, 16 * 768], F32,
                            kind="ExternalInput").ap()
    t.rw = nc.dram_tensor("rw", [128, 128], F32, kind="ExternalInput").ap()
    t.nw = nc.dram_tensor("nw", [128, 16], F32, kind="ExternalInput").ap()
    t.gwu = nc.dram_tensor("gwu", [128, 64 * 4096], F32,
                           kind="ExternalInput").ap()
    t.dwn = nc.dram_tensor("dwn", [128, 4 * 64 * 512], F32,
                           kind="ExternalInput").ap()
    t.ident_c = nc.dram_tensor("ident_c", [128, 128], BF16,
                               kind="ExternalInput").ap()
    t.val_o = nc.dram_tensor("valout", [256, S], BF16,
                             kind="ExternalOutput").ap()
    t.g0_o = nc.dram_tensor("g0out", [DM, TLOC], BF16,
                            kind="ExternalOutput").ap()
    t.a_dram = nc.dram_tensor("a_stage", [8192, TLOC], BF16).ap()

    sb = es.enter_context
    t.xn_loc = sb(nc.sbuf_tensor("xn_loc", [128, 16 * TLOC], BF16))
    t.zeros1 = sb(nc.sbuf_tensor("zeros1", [128, 1], BF16))
    t.ones1 = sb(nc.sbuf_tensor("ones1", [1, 128], BF16))
    t.ident = sb(nc.sbuf_tensor("ident", [128, 128], BF16))
    t.rms_sumsq = sb(nc.sbuf_tensor("rms_sumsq", [128, 8], F32))
    t.stats = sb(nc.sbuf_tensor("stats", [128, 8], F32))
    t.sq_scr = sb(nc.sbuf_tensor("sq_scr", [128, DM], BF16))

    with contextlib.ExitStack() as s12:
        t.xq = s12.enter_context(nc.sbuf_tensor("xq", [128, 2 * S], BF16))
        t.xk = s12.enter_context(nc.sbuf_tensor("xk", [128, 2 * S], BF16))
        t.xv = s12.enter_context(nc.sbuf_tensor("xv", [128, 2 * S], BF16))
        with contextlib.ExitStack() as s1:
            t.wqkv_sb = s1.enter_context(
                nc.sbuf_tensor("wqkv_sb", [128, 16 * 768], BF16))
            t.x_tm2 = s1.enter_context(
                nc.sbuf_tensor("x_tm2", [128, 2 * DM], F32))
            t.xn_tm2 = s1.enter_context(
                nc.sbuf_tensor("xn_tm2", [128, 2 * DM], BF16))
            t.xn_t = s1.enter_context(
                nc.sbuf_tensor("xn_t", [128, 16 * 512], BF16))
            t.ps_tr0 = s1.enter_context(
                nc.psum_tensor("ps_tr0", [128, 1024], BF16))
            t.ps_tr1 = s1.enter_context(
                nc.psum_tensor("ps_tr1", [128, 1024], BF16))
            t.ps_qkv = s1.enter_context(
                nc.psum_tensor("ps_qkv", [128, 512], F32))
        with contextlib.ExitStack() as s2:
            t.E2 = s2.enter_context(nc.sbuf_tensor("E2", [128, 2 * S], BF16))
            t.Zp = s2.enter_context(nc.sbuf_tensor("Zp", [128, 16], F32))
            t.zwork = s2.enter_context(nc.sbuf_tensor("zwork", [128, 4], F32))
            t.rbf = s2.enter_context(nc.sbuf_tensor("rbf", [128, 2], BF16))
            t.cs_sb = s2.enter_context(nc.sbuf_tensor("cs_sb", [1, S], BF16))
            t.val_sb = s2.enter_context(nc.sbuf_tensor("val_sb", [128, S], BF16))
            t.ps_scA = s2.enter_context(
                nc.psum_tensor("ps_scA", [128, 1024], F32))
            t.ps_scB = s2.enter_context(
                nc.psum_tensor("ps_scB", [128, 1024], F32))
            t.ps_cs = [s2.enter_context(
                nc.psum_tensor(f"ps_cs{i}", [97, 512], F32)) for i in range(4)]
    with contextlib.ExitStack() as s4:
        t.rw_sb = s4.enter_context(nc.sbuf_tensor("rw_sb", [128, 128], BF16))
        t.nw_sb = s4.enter_context(nc.sbuf_tensor("nw_sb", [128, 16], F32))
        t.cwb = s4.enter_context(nc.sbuf_tensor("cwb", [128, 8 * TLOC], BF16))
        t.r_sc = s4.enter_context(nc.sbuf_tensor("r_sc", [128, 64], F32))
        t.lg_sb = s4.enter_context(nc.sbuf_tensor("lg_sb", [8, TLOC], BF16))
        t.ltm = s4.enter_context(nc.sbuf_tensor("ltm", [128, 64], F32))
        t.exps = s4.enter_context(nc.sbuf_tensor("exps", [128, 64], F32))
        t.eqs = s4.enter_context(nc.sbuf_tensor("eqs", [128, 64], F32))
        t.tts = s4.enter_context(nc.sbuf_tensor("tts", [128, 64], F32))
        t.cwtm = s4.enter_context(nc.sbuf_tensor("cwtm", [128, 64], BF16))
        t.cw8 = s4.enter_context(nc.sbuf_tensor("cw8", [8, TLOC], BF16))
        t.cw_row = s4.enter_context(nc.sbuf_tensor("cw_row", [1, 8 * TLOC], BF16))
        t.xnw2 = s4.enter_context(nc.sbuf_tensor("xnw2", [128, 2 * TLOC], BF16))
        t.gwu2 = s4.enter_context(
            nc.sbuf_tensor("gwu2", [128, 2 * 4096], BF16))
        t.gs_sb = s4.enter_context(nc.sbuf_tensor("gs_sb", [128, TLOC], BF16))
        t.aw2 = s4.enter_context(nc.sbuf_tensor("aw2", [128, 2 * TLOC], BF16))
        t.astr4 = s4.enter_context(
            nc.sbuf_tensor("astr4", [128, 2 * 2 * TLOC], BF16))
        t.dw_sb = s4.enter_context(
            nc.sbuf_tensor("dw_sb", [128, 2 * 32 * 512], BF16))
        t.g0t = s4.enter_context(nc.sbuf_tensor("g0t", [128, 2 * TLOC], BF16))
        with contextlib.ExitStack() as s4r:
            t.ps_lg = s4r.enter_context(nc.psum_tensor("ps_lg", [8, TLOC], F32))
            t.ps_t8 = s4r.enter_context(
                nc.psum_tensor("ps_t8", [128, 64], BF16))
            t.ps_ct = s4r.enter_context(
                nc.psum_tensor("ps_ct", [8, TLOC], BF16))
            t.ps_bc = s4r.enter_context(
                nc.psum_tensor("ps_bc", [128, 512], F32))
        with contextlib.ExitStack() as s4e:
            t.ps_gA = s4e.enter_context(
                nc.psum_tensor("ps_gA", [128, TLOC], F32))
            t.ps_uA = s4e.enter_context(
                nc.psum_tensor("ps_uA", [128, TLOC], F32))
            t.ps_gB = s4e.enter_context(
                nc.psum_tensor("ps_gB", [128, TLOC], F32))
            t.ps_uB = s4e.enter_context(
                nc.psum_tensor("ps_uB", [128, TLOC], F32))
        with contextlib.ExitStack() as s4d:
            t.ps_y = [s4d.enter_context(
                nc.psum_tensor(f"ps_y{i}", [128, TLOC], F32)) for i in range(4)]
    return t


def emit_A(b, t):
    emit_consts(b, t)
    wv = b.dma("gpsimd", t.wqkv_sb[:, :], t.wqkv)
    b.wait("tensor", "ld_gpsimd", wv)
    b.wait("tensor", "ld_const", b.c("ld_const"))

    def on_group(g, lnv, xsl):
        tile, gi = g // 4, g % 4
        pa, pb, tpv = emit_transposes(b, t, lnv, xsl, g)
        b.wait("vector", "tp_done", tpv)
        if gi == 0:
            b.wait("vector", "qm_done", 6 * tile)

        def v_evac(q, gi=gi, pa=pa, pb=pb):
            d3 = t.xn_t.ap().rearrange("p (di tt) -> p di tt", di=16)
            q.tensor_copy(d3[:, 0:8, gi * 128:gi * 128 + 128],
                          pa.ap().rearrange("p (di tt) -> p di tt", di=8))
            return q.tensor_copy(
                d3[:, 8:16, gi * 128:gi * 128 + 128],
                pb.ap().rearrange("p (di tt) -> p di tt", di=8))
        b.do("vector", v_evac, sem="xp_evac")

        if gi == 3:
            b.wait("tensor", "xp_evac", b.c("xp_evac"))
            for oc in range(6):
                b.wait("tensor", "qe_done", b.c("qe_done"))

                def t_oc(q, oc=oc):
                    mm = None
                    for di in range(16):
                        mm = q.matmul(
                            t.ps_qkv[:, :],
                            t.wqkv_sb[:, di * 768 + oc * 128:
                                      di * 768 + oc * 128 + 128],
                            t.xn_t[:, di * 512:(di + 1) * 512],
                            start=(di == 0), stop=(di == 15))
                    return mm
                qmv = b.do("tensor", t_oc, sem="qm_done")
                b.wait("scalar", "qm_done", qmv)
                dest = (t.xq, t.xk, t.xv)[oc // 2]
                h = oc % 2
                b.do("scalar", lambda q, dest=dest, h=h, tile=tile: q.copy(
                    dest[:, h * S + tile * 512:h * S + tile * 512 + 512],
                    t.ps_qkv[:, :]), sem="qe_done")

    emit_ln_stream(b, t, t.x, 64, on_group)

    def on_group_loc(g, lnv, xsl):
        pa, pb, tpv = emit_transposes(b, t, lnv, xsl, g)
        b.wait("vector", "tp_done", tpv)

        def v_evac(q, g=g, pa=pa, pb=pb):
            d3 = t.xn_loc.ap().rearrange("p (di tt) -> p di tt", di=16)
            q.tensor_copy(d3[:, 0:8, g * 128:g * 128 + 128],
                          pa.ap().rearrange("p (di tt) -> p di tt", di=8))
            return q.tensor_copy(
                d3[:, 8:16, g * 128:g * 128 + 128],
                pb.ap().rearrange("p (di tt) -> p di tt", di=8))
        b.do("vector", v_evac, sem="xp_evac")

    emit_ln_stream(b, t, t.xloc, 8, on_group_loc, rms_groups=list(range(8)))
    b.barrier()

    def slot(kt):
        return t.ps_cs[kt % 4], 32 * (kt // 4)

    mz = b.do("vector", lambda v: v.memset(t.E2[:, 0:512], 0.0), sem="const2")
    for h in range(2):
        if h > 0:
            b.wait("tensor", "cs_evac", b.c("cs_evac"))
        b.wait("tensor", "const", b.c("const"))
        b.wait("tensor", "const2", mz)

        def t_zero(q):
            mm = None
            for kt in range(16):
                pt, row = slot(kt)
                tp = (0, row) if row else None
                mm = q.matmul(pt[row:row + 1, :], t.zeros1[:, :],
                              t.E2[:, 0:512], start=True, stop=False,
                              tile_position=tp)
            return mm
        b.do("tensor", t_zero, sem="zs")

        for qc in range(64):
            ebuf = qc % 2
            for sub in range(8):
                psc = t.ps_scA if sub % 2 == 0 else t.ps_scB
                b.wait("tensor", "exp_done", b.c("exp_done") - 1)

                def t_sc(q, h=h, qc=qc, sub=sub, psc=psc):
                    mm = None
                    for half in range(2):
                        kt = sub * 2 + half
                        mm = q.matmul(
                            psc[:, half * 512:half * 512 + 512],
                            t.xq[:, h * S + qc * 128:h * S + qc * 128 + 128],
                            t.xk[:, h * S + kt * 512:h * S + kt * 512 + 512],
                            start=True, stop=True)
                    return mm
                sdv = b.do("tensor", t_sc, sem="sc_done")
                b.wait("scalar", "sc_done", sdv)
                if sub == 0:
                    b.wait("scalar", "rmm_done", b.c("rmm_done") - 1)

                def s_exp(q, ebuf=ebuf, sub=sub, psc=psc):
                    return q.activation(
                        out=t.E2[:, ebuf * S + sub * 1024:
                                 ebuf * S + sub * 1024 + 1024],
                        in_=psc[:, :], func=AF.Exp,
                        scale=f32(INV_SQRT_HD), bias=f32(-C0_SHIFT),
                        accum_out=t.Zp[:, ebuf * 8 + sub:ebuf * 8 + sub + 1])
                b.do("scalar", s_exp, sem="exp_done")
            b.wait("vector", "exp_done", b.c("exp_done"))
            b.wait("vector", "rmm_done", b.c("rmm_done") - 1)

            b.fdo("vector", lambda q, ebuf=ebuf: q.tensor_reduce(
                t.zwork[:, 0:1], t.Zp[:, ebuf * 8:ebuf * 8 + 8],
                axis=AX.X, op=ALU.add))
            b.fdo("vector", lambda q: q.reciprocal(
                t.zwork[:, 1:2], t.zwork[:, 0:1]))
            zdv = b.do("vector", lambda q, ebuf=ebuf: q.tensor_copy(
                t.rbf[:, ebuf:ebuf + 1], t.zwork[:, 1:2]), sem="z_done")
            b.wait("tensor", "z_done", zdv)
            if qc == 0:
                b.wait("tensor", "zs", b.c("zs"))

            def t_rmm(q, qc=qc, ebuf=ebuf):
                mm = None
                for kt in range(16):
                    pt, row = slot(kt)
                    tp = (0, row) if row else None
                    mm = q.matmul(pt[row:row + 1, :], t.rbf[:, ebuf:ebuf + 1],
                                  t.E2[:, ebuf * S + kt * 512:
                                       ebuf * S + kt * 512 + 512],
                                  start=False, stop=(qc == 63),
                                  tile_position=tp)
                return mm
            b.do("tensor", t_rmm, sem="rmm_done")

        b.wait("scalar", "rmm_done", b.c("rmm_done"))
        b.wait("scalar", "bc_done", b.c("bc_done"))
        for kt in range(16):
            pt, row = slot(kt)
            b.do("scalar", lambda q, pt=pt, row=row, kt=kt: q.copy(
                t.cs_sb[0:1, kt * 512:kt * 512 + 512], pt[row:row + 1, :]),
                sem="cs_evac")
        b.wait("tensor", "cs_evac", b.c("cs_evac"))
        b.wait("vector", "st_out", b.c("st_out"))
        for kt in range(16):
            if kt > 0:
                b.wait("tensor", "vs_done", b.c("vs_done"))
            bcv = b.do("tensor", lambda q, kt=kt: q.matmul(
                t.ps_scA[:, 0:512], t.ones1[:, :],
                t.cs_sb[0:1, kt * 512:kt * 512 + 512], start=True, stop=True),
                sem="bc_done")
            b.wait("vector", "bc_done", bcv)
            b.do("vector", lambda q, h=h, kt=kt: q.tensor_tensor(
                t.val_sb[:, kt * 512:kt * 512 + 512],
                t.xv[:, h * S + kt * 512:h * S + kt * 512 + 512],
                t.ps_scA[:, 0:512], op=ALU.mult), sem="vs_done")
        b.wait("tensor", "vs_done", b.c("vs_done"))
        b.wait("gpsimd", "vs_done", b.c("vs_done"))
        b.dma("gpsimd", t.val_o[h * 128:(h + 1) * 128, :], t.val_sb[:, :],
              sem="st_out")
    b.barrier()

    emit_ffn_moe(b, t, n_exp=8, hid_per_exp=1024, use_rms=True, scale_out=4.0)
    b.wait("gpsimd", "st_out", b.c("st_out"))


def emit_ffn_moe(b, t, *, n_exp, hid_per_exp, use_rms, scale_out):
    H = n_exp * hid_per_exp
    n_hc = H // 128

    lw = b.dma("gpsimd", t.rw_sb[:, :].rearrange("p (dc e) -> p dc e", dc=16),
               t.rw.rearrange("(dc p) e -> p dc e", p=128))
    if use_rms:
        lw = b.dma("gpsimd", t.nw_sb[:, :], t.nw)
    b.wait("vector", "ld_gpsimd", lw)
    b.wait("tensor", "ld_gpsimd", lw)

    for di in range(16):
        if use_rms:
            b.wait("vector", "lg_mm", b.c("lg_mm") - 1)
            xwv = b.do("vector", lambda v, di=di: v.tensor_scalar(
                t.xnw2[:, (di % 2) * TLOC:(di % 2 + 1) * TLOC],
                t.xn_loc[:, di * TLOC:(di + 1) * TLOC],
                t.nw_sb[:, di:di + 1], None, op0=ALU.mult), sem="xnw_done")
            b.wait("tensor", "xnw_done", xwv)
            rhs = (lambda half, di=di:
                   t.xnw2[:, (di % 2) * TLOC + half * 512:
                          (di % 2) * TLOC + half * 512 + 512])
        else:
            rhs = (lambda half, di=di:
                   t.xn_loc[:, di * TLOC + half * 512:
                            di * TLOC + half * 512 + 512])

        def t_lg(q, di=di, rhs=rhs):
            mm = None
            for half in range(2):
                mm = q.matmul(t.ps_lg[0:8, half * 512:half * 512 + 512],
                              t.rw_sb[:, di * 8:di * 8 + 8],
                              rhs(half),
                              start=(di == 0), stop=(di == 15))
            return mm
        b.do("tensor", t_lg, sem="lg_mm")
    b.wait("scalar", "lg_mm", b.c("lg_mm"))
    lev = b.do("scalar", lambda q: q.copy(t.lg_sb[:, :], t.ps_lg[0:8, :]),
               sem="lg_evac")
    b.wait("tensor", "lg_evac", lev)

    def t_ltr(q):
        mm = None
        for lg in range(8):
            mm = q.transpose(t.ps_t8[:, lg * 8:lg * 8 + 8],
                             t.lg_sb[:, lg * 128:lg * 128 + 128],
                             t.ident[0:8, 0:8])
        return mm
    ltv = b.do("tensor", t_ltr, sem="lg_tr")
    b.wait("vector", "lg_tr", ltv)
    ltev = b.do("vector", lambda v: v.tensor_copy(t.ltm[:, :], t.ps_t8[:, :]),
                sem="lg_tr_ev")

    b.wait("vector", "lg_tr_ev", ltev)
    if use_rms:
        b.wait("vector", "rms_done", b.c("rms_done"))
    for lg in range(8):
        col = (lambda j, lg=lg: t.r_sc[:, 8 * lg + j:8 * lg + j + 1])
        l8 = (lambda lg=lg: t.ltm[:, lg * 8:lg * 8 + 8])
        x8 = (lambda lg=lg: t.exps[:, lg * 8:lg * 8 + 8])
        e8 = (lambda lg=lg: t.eqs[:, lg * 8:lg * 8 + 8])
        t8 = (lambda lg=lg: t.tts[:, lg * 8:lg * 8 + 8])
        c8 = (lambda lg=lg: t.cwtm[:, lg * 8:lg * 8 + 8])
        if use_rms:
            rsv = b.do("vector", lambda v, col=col, lg=lg: v.tensor_scalar(
                col(0), t.rms_sumsq[:, lg:lg + 1], 1.0 / DM, 1e-6,
                op0=ALU.mult, op1=ALU.add), sem="rr1")
            b.wait("scalar", "rr1", rsv)
            rs2 = b.do("scalar", lambda q, col=col: q.sqrt(col(1), col(0)),
                       sem="rr2")
            b.wait("vector", "rr2", rs2)
            b.fdo("vector", lambda v, col=col: v.reciprocal(col(2), col(1)))
            b.fdo("vector", lambda v, col=col, l8=l8: v.tensor_scalar(
                l8(), l8(), col(2), None, op0=ALU.mult))

        b.fdo("vector", lambda v, col=col, l8=l8: v.tensor_reduce(
            col(3), l8(), axis=AX.X, op=ALU.max))
        mxv = b.do("vector", lambda v, col=col: v.tensor_scalar_mul(
            col(3), col(3), -1.0), sem="tq1")
        b.wait("scalar", "tq1", mxv)
        e8v = b.do("scalar", lambda q, col=col, l8=l8, x8=x8: q.activation(
            out=x8(), in_=l8(), func=AF.Exp, bias=col(3)), sem="tq2")
        b.wait("vector", "tq2", e8v)

        for step in [
            lambda v, col=col, x8=x8: v.tensor_reduce(
                col(4), x8(), axis=AX.X, op=ALU.max),
            lambda v, col=col, x8=x8, e8=e8: v.tensor_scalar(
                e8(), x8(), col(4), None, op0=ALU.is_ge),
            lambda v, x8=x8, e8=e8, t8=t8: v.tensor_tensor(
                t8(), x8(), e8(), op=ALU.mult),
            lambda v, x8=x8, t8=t8: v.tensor_sub(t8(), x8(), t8()),
            lambda v, col=col, t8=t8: v.tensor_reduce(
                col(5), t8(), axis=AX.X, op=ALU.max),
            lambda v, col=col, x8=x8, e8=e8: v.tensor_scalar(
                e8(), x8(), col(5), None, op0=ALU.is_ge),
            lambda v, col=col: v.tensor_add(col(6), col(4), col(5)),
            lambda v, col=col: v.reciprocal(col(7), col(6)),
            lambda v, x8=x8, e8=e8, t8=t8: v.tensor_tensor(
                t8(), x8(), e8(), op=ALU.mult),
        ]:
            b.fdo("vector", step)
        ctv = b.do("vector", lambda v, col=col, t8=t8, c8=c8: v.tensor_scalar(
            c8(), t8(), col(7), float(scale_out), op0=ALU.mult, op1=ALU.mult),
            sem="cw_tm_done")
        b.wait("tensor", "cw_tm_done", ctv)
        b.do("tensor", lambda q, lg=lg, c8=c8: q.transpose(
            t.ps_ct[:, lg * 128:lg * 128 + 128], c8(), t.ident[:, :]),
            sem="cw_tr")
    b.wait("vector", "cw_tr", b.c("cw_tr"))
    cfev = b.do("vector", lambda v: v.tensor_copy(t.cw8[:, :], t.ps_ct[:, :]),
                sem="cw_fm_ev")
    b.wait("gpsimd", "cw_fm_ev", cfev)
    crv = None
    for e in range(8):
        crv = b.dma("gpsimd", t.cw_row[0:1, e * TLOC:(e + 1) * TLOC],
                    t.cw8[e:e + 1, :], sem="ld_cwrow")
    b.wait("tensor", "ld_cwrow", crv)
    for e in range(n_exp):
        for half in range(2):
            b.wait("tensor", "cwb_ev", b.c("cwb_ev"))
            bcv = b.do("tensor", lambda q, e=e, half=half: q.matmul(
                t.ps_bc[:, :], t.ones1[:, :],
                t.cw_row[0:1, e * TLOC + half * 512:e * TLOC + half * 512 + 512],
                start=True, stop=True), sem="cwb_mm")
            b.wait("vector", "cwb_mm", bcv)
            b.do("vector", lambda v, e=e, half=half: v.tensor_copy(
                t.cwb[:, e * TLOC + half * 512:e * TLOC + half * 512 + 512],
                t.ps_bc[:, :]), sem="cwb_ev")

    b.wait("tensor", "cwb_ev", b.c("cwb_ev"))
    for hc in range(n_hc):
        e = hc // (hid_per_exp // 128)
        wbuf = hc % 2
        b.wait("gpsimd", "gu_mm", b.c("gu_mm") - 1)
        gl = b.dma("gpsimd", t.gwu2[:, wbuf * 4096:(wbuf + 1) * 4096],
                   t.gwu[:, hc * 4096:(hc + 1) * 4096], sem="ld_w")
        b.wait("tensor", "ld_w", gl)
        psg = t.ps_gA if hc % 2 == 0 else t.ps_gB
        psu = t.ps_uA if hc % 2 == 0 else t.ps_uB
        b.wait("tensor", "silu_done", b.c("silu_done") - 1)

        def t_g(q, wbuf=wbuf, psg=psg):
            mm = None
            for di in range(16):
                for half in range(2):
                    mm = q.matmul(
                        psg[:, half * 512:half * 512 + 512],
                        t.gwu2[:, wbuf * 4096 + di * 128:
                               wbuf * 4096 + di * 128 + 128],
                        t.xn_loc[:, di * TLOC + half * 512:
                                 di * TLOC + half * 512 + 512],
                        start=(di == 0), stop=(di == 15))
            return mm
        gmv = b.do("tensor", t_g, sem="g_mm")
        b.wait("tensor", "aprod_done", b.c("aprod_done") - 1)

        def t_u(q, wbuf=wbuf, psu=psu):
            mm = None
            for di in range(16):
                for half in range(2):
                    mm = q.matmul(
                        psu[:, half * 512:half * 512 + 512],
                        t.gwu2[:, wbuf * 4096 + 2048 + di * 128:
                               wbuf * 4096 + 2048 + di * 128 + 128],
                        t.xn_loc[:, di * TLOC + half * 512:
                                 di * TLOC + half * 512 + 512],
                        start=(di == 0), stop=(di == 15))
            return mm
        umv = b.do("tensor", t_u, sem="gu_mm")
        b.wait("scalar", "g_mm", gmv)
        slv = b.do("scalar", lambda q, psg=psg: q.activation(
            out=t.gs_sb[:, :], in_=psg[:, :], func=AF.Silu), sem="silu_done")
        b.wait("vector", "gu_mm", umv)
        b.wait("vector", "silu_done", slv)
        abuf = hc % 2
        b.wait("vector", "a_st", b.c("a_st") - 16)

        def v_a(q, e=e, abuf=abuf, psu=psu):
            q.tensor_tensor(t.aw2[:, abuf * TLOC:(abuf + 1) * TLOC],
                            t.gs_sb[:, :], psu[:, :], op=ALU.mult)
            return q.tensor_tensor(
                t.aw2[:, abuf * TLOC:(abuf + 1) * TLOC],
                t.aw2[:, abuf * TLOC:(abuf + 1) * TLOC],
                t.cwb[:, e * TLOC:(e + 1) * TLOC], op=ALU.mult)
        apv = b.do("vector", v_a, sem="aprod_done")
        b.wait("gpsimd", "aprod_done", apv)
        b.dma("gpsimd", t.a_dram[hc * 128:(hc + 1) * 128, :],
              t.aw2[:, abuf * TLOC:(abuf + 1) * TLOC], sem="a_st")

    b.wait("sync", "a_st", b.c("a_st"))
    b.wait("tensor", "aprod_done", b.c("aprod_done"))
    b.wait("tensor", "silu_done", b.c("silu_done"))
    for fog in range(4):
        b.wait("tensor", "y_silu", b.c("y_silu"))
        emit_down_accum(b, t, t.dwn, t.a_dram, n_hc, fog, 32, first=True)
        b.wait("scalar", "dn_pair", b.c("dn_pair"))
        for fo_in in range(4):
            fo = fog * 4 + fo_in
            obuf = fo % 2
            b.wait("scalar", "g0_st", b.c("g0_st") - 16)
            ysv = b.do("scalar", lambda q, fo_in=fo_in, obuf=obuf: q.activation(
                out=t.g0t[:, obuf * TLOC:(obuf + 1) * TLOC],
                in_=t.ps_y[fo_in][:, :], func=AF.Silu), sem="y_silu")
            b.wait("gpsimd", "y_silu", ysv)
            b.dma("gpsimd", t.g0_o[fo * 128:(fo + 1) * 128, :],
                  t.g0t[:, obuf * TLOC:(obuf + 1) * TLOC], sem="g0_st")
    b.wait("gpsimd", "g0_st", b.c("g0_st"))



_IDENT = np.eye(128, dtype=np.float32).astype(BF)


def _chunked(w, inner=128):
    K, N = w.shape
    return np.ascontiguousarray(
        w.reshape(K // 128, 128, N).transpose(1, 0, 2).reshape(128, -1))


def _gu_layout(w):
    K, N = w.shape
    r = w.reshape(K // 128, 128, N // 128, 128).transpose(1, 2, 0, 3)
    return np.ascontiguousarray(r.reshape(128, -1))


def _down_layout(w, n_fog, fo_w):
    H, N = w.shape
    r = w.reshape(H // 128, 128, n_fog, fo_w).transpose(1, 2, 0, 3)
    return np.ascontiguousarray(r.reshape(128, -1))


def host_inputs_A(inputs):
    x = np.ascontiguousarray(inputs["L0_attn_x"].reshape(S, DM))
    wqkv = inputs["L0_attn_w_qkv"]
    gw = _gu_layout(inputs["L0_ffn_gate_w"].transpose(1, 0, 2).reshape(DM, 8192))
    uw = _gu_layout(inputs["L0_ffn_up_w"].transpose(1, 0, 2).reshape(DM, 8192))
    gwu = np.ascontiguousarray(np.concatenate(
        [gw.reshape(128, 64, 2048), uw.reshape(128, 64, 2048)],
        axis=2).reshape(128, 64 * 4096))
    dwn = _down_layout(inputs["L0_ffn_down_w"].reshape(8192, DM), 4, 512)
    rw = _chunked(np.ascontiguousarray(inputs["L0_ffn_router_w"].T))
    nw = np.ascontiguousarray(inputs["L0_ffn_norm_w"].reshape(16, 128).T)
    maps = []
    for c in range(NCORES):
        sl = np.concatenate([
            wqkv[:, 256 * c:256 * (c + 1)],
            wqkv[:, 2048 + 256 * c:2048 + 256 * (c + 1)],
            wqkv[:, 4096 + 256 * c:4096 + 256 * (c + 1)]], axis=1)
        maps.append({
            "x": x, "xloc": x[TLOC * c:TLOC * (c + 1)],
            "wqkv": _chunked(np.ascontiguousarray(sl)), "rw": rw, "nw": nw,
            "gwu": gwu, "dwn": dwn, "ident_c": _IDENT})
    return maps





def make_tensors_B(nc, es):
    t = types.SimpleNamespace()
    t.xfm = nc.dram_tensor("xfm", [DM, TLOC], F32, kind="ExternalInput").ap()
    t.val = nc.dram_tensor("val", [DM, TLOC], BF16, kind="ExternalInput").ap()
    t.g0 = nc.dram_tensor("g0", [DM, TLOC], BF16, kind="ExternalInput").ap()
    t.wo = nc.dram_tensor("wo", [128, 16 * DM], F32, kind="ExternalInput").ap()
    t.x1_o = nc.dram_tensor("x1out", [DM, TLOC], F32, kind="ExternalOutput").ap()
    sb = es.enter_context
    t.wo_sb = sb(nc.sbuf_tensor("wo_sb", [128, 16 * DM], BF16))
    t.val_sb = sb(nc.sbuf_tensor("val_sb", [128, 16 * TLOC], BF16))
    t.g0_sb = sb(nc.sbuf_tensor("g0_sb", [128, 16 * TLOC], BF16))
    t.x_sb = sb(nc.sbuf_tensor("x_sb", [128, 16 * TLOC], F32))
    t.x1_sb = sb(nc.sbuf_tensor("x1_sb", [128, 2 * TLOC], F32))
    t.sg_sb = sb(nc.sbuf_tensor("sg_sb", [128, TLOC], F32))
    t.ps_s = [sb(nc.psum_tensor(f"ps_s{i}", [128, TLOC], F32)) for i in range(2)]
    return t


def emit_B(b, t):
    lv = b.dma("gpsimd", t.wo_sb[:, :], t.wo)
    b.dma("sync", t.val_sb[:, :].rearrange("p (dc tt) -> p dc tt", dc=16),
          t.val.rearrange("(dc p) tt -> p dc tt", p=128))
    b.dma("sync", t.g0_sb[:, :].rearrange("p (dc tt) -> p dc tt", dc=16),
          t.g0.rearrange("(dc p) tt -> p dc tt", p=128))
    xv = b.dma("sync", t.x_sb[:, :].rearrange("p (dc tt) -> p dc tt", dc=16),
               t.xfm.rearrange("(dc p) tt -> p dc tt", p=128))
    b.wait("tensor", "ld_gpsimd", lv)
    b.wait("tensor", "ld_sync", xv)
    for fo in range(16):
        ps = t.ps_s[fo % 2]
        b.wait("tensor", "sg_done", b.c("sg_done") - 1)

        def t_s(q, fo=fo, ps=ps):
            mm = None
            for fi in range(16):
                for half in range(2):
                    mm = q.matmul(
                        ps[:, half * 512:half * 512 + 512],
                        t.wo_sb[:, fi * DM + fo * 128:fi * DM + fo * 128 + 128],
                        t.val_sb[:, fi * TLOC + half * 512:
                                 fi * TLOC + half * 512 + 512],
                        start=(fi == 0), stop=(fi == 15))
            return mm
        smv = b.do("tensor", t_s, sem="s_mm")
        b.wait("vector", "s_mm", smv)
        b.wait("vector", "x1_st", b.c("x1_st") - 16)

        def v_comb(q, fo=fo, ps=ps):
            q.tensor_tensor(t.sg_sb[:, :], ps[:, :],
                            t.g0_sb[:, fo * TLOC:(fo + 1) * TLOC], op=ALU.mult)
            return q.tensor_add(t.x1_sb[:, (fo % 2) * TLOC:(fo % 2 + 1) * TLOC],
                                t.x_sb[:, fo * TLOC:(fo + 1) * TLOC],
                                t.sg_sb[:, :])
        cbv = b.do("vector", v_comb, sem="sg_done")
        b.wait("gpsimd", "sg_done", cbv)
        b.dma("gpsimd", t.x1_o[fo * 128:(fo + 1) * 128, :],
              t.x1_sb[:, (fo % 2) * TLOC:(fo % 2 + 1) * TLOC], sem="x1_st")
    b.wait("gpsimd", "x1_st", b.c("x1_st"))


def host_inputs_B(inputs, val_parts, g0_parts, wo=None):
    x = inputs["L0_attn_x"].reshape(S, DM)
    val_full = np.concatenate(val_parts, axis=0)
    wo_r = _chunked(wo if wo is not None else inputs["L0_attn_w_o"])
    maps = []
    for c in range(NCORES):
        xfm = np.ascontiguousarray(x[TLOC * c:TLOC * (c + 1)].T)
        maps.append({
            "xfm": xfm,
            "val": np.ascontiguousarray(val_full[:, TLOC * c:TLOC * (c + 1)]),
            "g0": g0_parts[c],
            "wo": wo_r})
    return maps





N_HC1 = 88
N_HCS = 44
SCALE1 = float(1.0 / (50.0 * np.sqrt(128.0)))


def make_tensors_C(nc, es):
    t = types.SimpleNamespace()
    t.x = nc.dram_tensor("x", [S, DM], F32, kind="ExternalInput").ap()
    t.xloc = nc.dram_tensor("xloc", [TLOC, DM], F32, kind="ExternalInput").ap()
    t.wqkv = nc.dram_tensor("wqkv", [128, 16 * 512], F32,
                            kind="ExternalInput").ap()
    t.rw = nc.dram_tensor("rw", [128, 128], F32, kind="ExternalInput").ap()
    t.gwu = nc.dram_tensor("gwu", [128, N_HC1 * 4096], F32,
                           kind="ExternalInput").ap()
    t.dwn = nc.dram_tensor("dwn", [128, 8 * N_HC1 * 256], F32,
                           kind="ExternalInput").ap()
    t.sgw_d = nc.dram_tensor("sgw", [128, 16], F32, kind="ExternalInput").ap()
    t.swgu = nc.dram_tensor("swgu", [128, N_HCS * 4096], F32,
                            kind="ExternalInput").ap()
    t.swd = nc.dram_tensor("swd", [128, 8 * N_HCS * 256], F32,
                           kind="ExternalInput").ap()
    t.cos_d = nc.dram_tensor("cos_c", [128, S], F32, kind="ExternalInput").ap()
    t.sin_d = nc.dram_tensor("sin_c", [128, S], F32, kind="ExternalInput").ap()
    t.mask_d = nc.dram_tensor("mask_c", [128, 8 * 1024], BF16,
                              kind="ExternalInput").ap()
    t.ident_c = nc.dram_tensor("ident_c", [128, 128], BF16,
                               kind="ExternalInput").ap()
    t.out1_o = nc.dram_tensor("out1", [256, S], BF16, kind="ExternalOutput").ap()
    t.g1_o = nc.dram_tensor("g1out", [DM, TLOC], BF16, kind="ExternalOutput").ap()
    t.a_dram = nc.dram_tensor("a_stage", [N_HC1 * 128, TLOC], BF16).ap()
    t.sh_dram = nc.dram_tensor("sh_stage", [N_HCS * 128, TLOC], BF16).ap()

    sb = es.enter_context
    t.zeros1 = sb(nc.sbuf_tensor("zeros1", [128, 1], BF16))
    t.ones1 = sb(nc.sbuf_tensor("ones1", [1, 128], BF16))
    t.onescol = sb(nc.sbuf_tensor("onescol", [128, 1], BF16))
    t.ident = sb(nc.sbuf_tensor("ident", [128, 128], BF16))
    t.stats = sb(nc.sbuf_tensor("stats", [128, 8], F32))
    t.rms_sumsq = sb(nc.sbuf_tensor("rms_sumsq", [128, 8], F32))
    t.xq = sb(nc.sbuf_tensor("xq", [128, 2 * S], BF16))
    t.xk = sb(nc.sbuf_tensor("xk", [128, S], BF16))
    t.xv = sb(nc.sbuf_tensor("xv", [128, S], BF16))
    t.sgb = sb(nc.sbuf_tensor("sgb", [128, TLOC], BF16))

    with contextlib.ExitStack() as p1:
        t.sq_scr = p1.enter_context(nc.sbuf_tensor("sq_scr", [128, DM], BF16))
        t.wqkv_sb = p1.enter_context(
            nc.sbuf_tensor("wqkv_sb", [128, 16 * 512], BF16))
        t.x_tm2 = p1.enter_context(nc.sbuf_tensor("x_tm2", [128, 2 * DM], F32))
        t.xn_tm2 = p1.enter_context(
            nc.sbuf_tensor("xn_tm2", [128, 2 * DM], BF16))
        t.xn_t = p1.enter_context(nc.sbuf_tensor("xn_t", [128, 16 * 512], BF16))
        t.cossin2 = p1.enter_context(
            nc.sbuf_tensor("cossin2", [128, 4 * 512], F32))
        t.rot_sb = p1.enter_context(nc.sbuf_tensor("rot_sb", [128, 512], F32))
        t.t1_sb = p1.enter_context(nc.sbuf_tensor("t1_sb", [128, 512], F32))
        t.xn_loc = p1.enter_context(
            nc.sbuf_tensor("xn_loc", [128, 16 * TLOC], BF16))
        t.rw_sb = p1.enter_context(nc.sbuf_tensor("rw_sb", [128, 128], BF16))
        t.sgw_sb = p1.enter_context(nc.sbuf_tensor("sgw_sb", [128, 16], BF16))
        t.cwb = p1.enter_context(nc.sbuf_tensor("cwb", [128, 8 * TLOC], BF16))
        with contextlib.ExitStack() as p1r:
            t.sg_row = p1r.enter_context(
                nc.sbuf_tensor("sg_row", [1, TLOC], BF16))
            t.r_sc = p1r.enter_context(nc.sbuf_tensor("r_sc", [128, 64], F32))
            t.lg_sb = p1r.enter_context(nc.sbuf_tensor("lg_sb", [8, TLOC], BF16))
            t.ltm = p1r.enter_context(nc.sbuf_tensor("ltm", [128, 64], F32))
            t.exps = p1r.enter_context(nc.sbuf_tensor("exps", [128, 64], F32))
            t.eqs = p1r.enter_context(nc.sbuf_tensor("eqs", [128, 64], F32))
            t.tts = p1r.enter_context(nc.sbuf_tensor("tts", [128, 64], F32))
            t.cwtm = p1r.enter_context(nc.sbuf_tensor("cwtm", [128, 64], BF16))
            t.cw8 = p1r.enter_context(nc.sbuf_tensor("cw8", [8, TLOC], BF16))
            t.cw_row = p1r.enter_context(
                nc.sbuf_tensor("cw_row", [1, 4 * TLOC], BF16))
        t.gwu2 = p1.enter_context(nc.sbuf_tensor("gwu2", [128, 2 * 4096], BF16))
        t.aw2 = p1.enter_context(nc.sbuf_tensor("aw2", [128, 2 * TLOC], BF16))
        t.ps_tr0 = p1.enter_context(nc.psum_tensor("ps_tr0", [128, 1024], BF16))
        t.ps_tr1 = p1.enter_context(nc.psum_tensor("ps_tr1", [128, 1024], BF16))
        t.ps_qkv = p1.enter_context(nc.psum_tensor("ps_qkv", [128, 512], F32))
        t.ps_v = p1.enter_context(nc.psum_tensor("ps_v", [128, 128], F32))
        with contextlib.ExitStack() as p1p:
            t.ps_lg12 = p1p.enter_context(
                nc.psum_tensor("ps_lg12", [128, 1024], F32))
            t.ps_t8 = p1p.enter_context(
                nc.psum_tensor("ps_t8", [128, 64], BF16))
            t.ps_ct = p1p.enter_context(
                nc.psum_tensor("ps_ct", [8, TLOC], BF16))
            t.ps_lg = t.ps_lg12
            t.ps_bc = None
            t.ps_sg = None
        t.ps_g = p1.enter_context(nc.psum_tensor("ps_g", [128, TLOC], F32))
        t.ps_u = p1.enter_context(nc.psum_tensor("ps_u", [128, TLOC], F32))

    with contextlib.ExitStack() as p2:
        t.mask_sb = p2.enter_context(
            nc.sbuf_tensor("mask_sb", [128, 8 * 1024], BF16))
        t.astr4 = p2.enter_context(
            nc.sbuf_tensor("astr4", [128, 2 * 2 * TLOC], BF16))
        t.dw_sb = p2.enter_context(
            nc.sbuf_tensor("dw_sb", [128, 2 * 20 * 256], BF16))
        t.moe_t = p2.enter_context(nc.sbuf_tensor("moe_t", [128, 2 * TLOC], BF16))
        t.cmb1 = p2.enter_context(nc.sbuf_tensor("cmb1", [128, TLOC], F32))
        t.cmb2 = p2.enter_context(nc.sbuf_tensor("cmb2", [128, TLOC], F32))
        t.g0t = p2.enter_context(nc.sbuf_tensor("g0t", [128, 2 * TLOC], BF16))
        t.E2 = p2.enter_context(nc.sbuf_tensor("E2c", [128, 2 * 1024], BF16))
        t.tn_sb = p2.enter_context(nc.sbuf_tensor("tn_sb", [128, 1024], BF16))
        t.S_acc = p2.enter_context(nc.sbuf_tensor("S_acc", [128, 1024], F32))
        t.S_bf = p2.enter_context(nc.sbuf_tensor("S_bf", [128, 1024], BF16))
        t.zr_f = p2.enter_context(nc.sbuf_tensor("zr_f", [1, 1024], F32))
        t.zr_f2 = p2.enter_context(nc.sbuf_tensor("zr_f2", [1, 1024], F32))
        t.zr_b = p2.enter_context(nc.sbuf_tensor("zr_b", [1, 1024], BF16))
        t.o_raw = p2.enter_context(nc.sbuf_tensor("o_raw", [128, 1024], F32))
        t.o_bf = p2.enter_context(nc.sbuf_tensor("o_bf", [128, S], BF16))
        t.ps_y = [p2.enter_context(
            nc.psum_tensor(f"ps_y{i}", [128, TLOC], F32)) for i in range(2)]
        t.ps_sc = p2.enter_context(nc.psum_tensor("ps_sc", [128, 1024], F32))
        t.ps_o = p2.enter_context(nc.psum_tensor("ps_o", [128, 1024], F32))
    return t


def gen_C1(b, t):
    def on_group(g, lnv, xsl):
        tile, gi = g // 4, g % 4
        pa, pb, tpv = emit_transposes(b, t, lnv, xsl, g)
        b.wait("vector", "tp_done", tpv)
        if gi == 0:
            b.wait("vector", "qm_done", 3 * tile)
            b.wait("vector", "v_mm", 4 * tile)

        def v_evac(q, gi=gi, pa=pa, pb=pb):
            d3 = t.xn_t.ap().rearrange("p (di tt) -> p di tt", di=16)
            q.tensor_copy(d3[:, 0:8, gi * 128:gi * 128 + 128],
                          pa.ap().rearrange("p (di tt) -> p di tt", di=8))
            return q.tensor_copy(
                d3[:, 8:16, gi * 128:gi * 128 + 128],
                pb.ap().rearrange("p (di tt) -> p di tt", di=8))
        b.do("vector", v_evac, sem="xp_evac")

        if gi == 3:
            b.wait("tensor", "xp_evac", b.c("xp_evac"))
            b.wait("sync", "rope_done", b.c("rope_done") - 3)
            csv = b.dma("sync", t.cossin2[:, (tile % 2) * 1024:
                                          (tile % 2) * 1024 + 512],
                        t.cos_d[:, tile * 512:(tile + 1) * 512], sem="ld_cs")
            csv = b.dma("sync", t.cossin2[:, (tile % 2) * 1024 + 512:
                                          (tile % 2) * 1024 + 1024],
                        t.sin_d[:, tile * 512:(tile + 1) * 512], sem="ld_cs")
            for oc in range(3):
                b.wait("tensor", "rope_done", b.c("rope_done"))

                def t_oc(q, oc=oc):
                    mm = None
                    for di in range(16):
                        mm = q.matmul(
                            t.ps_qkv[:, :],
                            t.wqkv_sb[:, di * 512 + oc * 128:
                                      di * 512 + oc * 128 + 128],
                            t.xn_t[:, di * 512:(di + 1) * 512],
                            start=(di == 0), stop=(di == 15))
                    return mm
                qmv = b.do("tensor", t_oc, sem="qm_done")
                b.wait("vector", "qm_done", qmv)
                b.wait("vector", "ld_cs", csv)
                dest = t.xq if oc < 2 else t.xk
                doff = oc * S if oc < 2 else 0
                cs0 = (tile % 2) * 1024

                def v_rope(q, dest=dest, doff=doff, tile=tile, cs0=cs0):
                    q.tensor_scalar_mul(t.rot_sb[0:64, :],
                                        t.ps_qkv[64:128, :], -1.0)
                    q.tensor_copy(t.rot_sb[64:128, :], t.ps_qkv[0:64, :])
                    q.tensor_tensor(t.t1_sb[:, :], t.ps_qkv[:, :],
                                    t.cossin2[:, cs0:cs0 + 512], op=ALU.mult)
                    q.tensor_tensor(t.rot_sb[:, :], t.rot_sb[:, :],
                                    t.cossin2[:, cs0 + 512:cs0 + 1024],
                                    op=ALU.mult)
                    return q.tensor_add(
                        dest[:, doff + tile * 512:doff + tile * 512 + 512],
                        t.t1_sb[:, :], t.rot_sb[:, :])
                b.do("vector", v_rope, sem="rope_done")
            for sub in range(4):
                b.wait("tensor", "v_evac", b.c("v_evac"))

                def t_v(q, sub=sub):
                    mm = None
                    for di in range(16):
                        mm = q.matmul(
                            t.ps_v[:, :],
                            t.xn_t[:, di * 512 + sub * 128:
                                   di * 512 + sub * 128 + 128],
                            t.wqkv_sb[:, di * 512 + 384:di * 512 + 512],
                            start=(di == 0), stop=(di == 15))
                    return mm
                vmv = b.do("tensor", t_v, sem="v_mm")
                b.wait("scalar", "v_mm", vmv)
                kc = tile * 4 + sub
                b.do("scalar", lambda q, kc=kc: q.copy(
                    t.xv[:, kc * 128:kc * 128 + 128], t.ps_v[:, :]),
                    sem="v_evac")

    yield from gen_ln_stream(b, t, t.x, 64, on_group)


def gen_gu(b, t):
    yield from gen_expert_gu(b, t, t.gwu, t.a_dram, N_HC1, 11)
    yield from gen_expert_gu(b, t, t.swgu, t.sh_dram, N_HCS, None)


def gen_expert_gu(b, t, gwu_d, a_dram, n_hc, hc_per_exp):
    for hc in range(n_hc):
        wbuf = b.c("gu_mm") % 2
        b.wait("gpsimd", "gu_mm", b.c("gu_mm") - 1)
        gl = b.dma("gpsimd", t.gwu2[:, wbuf * 4096:(wbuf + 1) * 4096],
                   gwu_d[:, hc * 4096:(hc + 1) * 4096], sem="ld_w")
        b.wait("tensor", "ld_w", gl)
        b.wait("tensor", "silu_done", b.c("silu_done"))

        def t_g(q, wbuf=wbuf):
            mm = None
            for di in range(16):
                for half in range(2):
                    mm = q.matmul(
                        t.ps_g[:, half * 512:half * 512 + 512],
                        t.gwu2[:, wbuf * 4096 + di * 128:
                               wbuf * 4096 + di * 128 + 128],
                        t.xn_loc[:, di * TLOC + half * 512:
                                 di * TLOC + half * 512 + 512],
                        start=(di == 0), stop=(di == 15))
            return mm
        gmv = b.do("tensor", t_g, sem="g_mm")
        b.wait("tensor", "aprod_done", b.c("aprod_done"))

        def t_u(q, wbuf=wbuf):
            mm = None
            for di in range(16):
                for half in range(2):
                    mm = q.matmul(
                        t.ps_u[:, half * 512:half * 512 + 512],
                        t.gwu2[:, wbuf * 4096 + 2048 + di * 128:
                               wbuf * 4096 + 2048 + di * 128 + 128],
                        t.xn_loc[:, di * TLOC + half * 512:
                                 di * TLOC + half * 512 + 512],
                        start=(di == 0), stop=(di == 15))
            return mm
        umv = b.do("tensor", t_u, sem="gu_mm")
        abuf = b.c("a_st") // 16 % 2
        b.wait("scalar", "g_mm", gmv)
        b.wait("scalar", "a_st", b.c("a_st") - 16)
        slv = b.do("scalar", lambda q, abuf=abuf: q.activation(
            out=t.aw2[:, abuf * TLOC:(abuf + 1) * TLOC], in_=t.ps_g[:, :],
            func=AF.Silu), sem="silu_done")
        b.wait("vector", "gu_mm", umv)
        b.wait("vector", "silu_done", slv)

        if hc_per_exp:
            e = hc // hc_per_exp

            def v_a(q, e=e, abuf=abuf):
                q.tensor_tensor(t.aw2[:, abuf * TLOC:(abuf + 1) * TLOC],
                                t.aw2[:, abuf * TLOC:(abuf + 1) * TLOC],
                                t.ps_u[:, :], op=ALU.mult)
                return q.tensor_tensor(
                    t.aw2[:, abuf * TLOC:(abuf + 1) * TLOC],
                    t.aw2[:, abuf * TLOC:(abuf + 1) * TLOC],
                    t.cwb[:, e * TLOC:(e + 1) * TLOC], op=ALU.mult)
        else:
            def v_a(q, abuf=abuf):
                return q.tensor_tensor(
                    t.aw2[:, abuf * TLOC:(abuf + 1) * TLOC],
                    t.aw2[:, abuf * TLOC:(abuf + 1) * TLOC],
                    t.ps_u[:, :], op=ALU.mult)
        apv = b.do("vector", v_a, sem="aprod_done")
        b.wait("gpsimd", "aprod_done", apv)
        b.dma("gpsimd", a_dram[hc * 128:(hc + 1) * 128, :],
              t.aw2[:, abuf * TLOC:(abuf + 1) * TLOC], sem="a_st")
        yield


def gen_down_C(b, t):
    for fog in range(8):
        b.wait("tensor", "y_silu", b.c("y_silu"))
        yield from gen_down_accum(
            b, t, t.dwn, t.a_dram, N_HC1, fog, 20, 2, 256)
        b.wait("scalar", "dn_pair", b.c("dn_pair"))
        b.wait("scalar", "ffn_comb", b.c("ffn_comb"))
        for fo_in in range(2):
            b.do("scalar", lambda q, fo_in=fo_in: q.copy(
                t.moe_t[:, fo_in * TLOC:(fo_in + 1) * TLOC],
                t.ps_y[fo_in][:, :]), sem="moe_ev")
        b.wait("tensor", "moe_ev", b.c("moe_ev"))
        yield from gen_down_accum(
            b, t, t.swd, t.sh_dram, N_HCS, fog, 20, 2, 256)
        b.wait("vector", "dn_pair", b.c("dn_pair"))
        b.wait("vector", "moe_ev", b.c("moe_ev"))
        for fo_in in range(2):
            fo = fog * 2 + fo_in
            obuf = fo % 2
            b.wait("vector", "g1_st", b.c("g1_st") - 16)
            b.wait("vector", "y_silu", b.c("y_silu"))

            def v_comb(q, fo_in=fo_in):
                q.tensor_tensor(t.cmb1[:, :], t.ps_y[fo_in][:, :],
                                t.sgb[:, :], op=ALU.mult)
                return q.tensor_add(
                    t.cmb2[:, :],
                    t.moe_t[:, fo_in * TLOC:(fo_in + 1) * TLOC],
                    t.cmb1[:, :])
            cbv = b.do("vector", v_comb, sem="ffn_comb")
            b.wait("scalar", "ffn_comb", cbv)
            ysv = b.do("scalar", lambda q, obuf=obuf: q.activation(
                out=t.g0t[:, obuf * TLOC:(obuf + 1) * TLOC],
                in_=t.cmb2[:, :], func=AF.Silu), sem="y_silu")
            b.wait("gpsimd", "y_silu", ysv)
            b.dma("gpsimd", t.g1_o[fo * 128:(fo + 1) * 128, :],
                  t.g0t[:, obuf * TLOC:(obuf + 1) * TLOC], sem="g1_st")
        yield


def gen_down_accum(b, t, dwn_d, a_dram, n_hc, fog, hc_chunk, n_fo, fo_w):
    for cs in range(0, n_hc, hc_chunk):
        ce = min(cs + hc_chunk, n_hc)
        cbuf = b.c("dn_chunk") % 2
        b.wait("gpsimd", "dn_chunk", b.c("dn_chunk") - 1)
        dl = b.dma(
            "gpsimd",
            t.dw_sb[:, cbuf * hc_chunk * fo_w:
                    (cbuf * hc_chunk + (ce - cs)) * fo_w],
            dwn_d[:, (fog * n_hc + cs) * fo_w:(fog * n_hc + ce) * fo_w],
            sem="ld_dw")
        b.wait("tensor", "ld_dw", dl)
        for hc in range(cs, ce, 2):
            npair = min(2, ce - hc)
            pbuf = b.c("dn_pair") % 2
            b.wait("sync", "dn_pair", b.c("dn_pair") - 1)
            alv = b.dma(
                "sync",
                t.astr4[:, pbuf * 2048:pbuf * 2048 + npair * TLOC].rearrange(
                    "p (hh tt) -> p hh tt", hh=npair),
                a_dram[hc * 128:(hc + npair) * 128, :].rearrange(
                    "(hh p) tt -> p hh tt", p=128), sem="ld_a")
            b.wait("tensor", "ld_a", alv)

            def t_dn(q, hc=hc, cs=cs, npair=npair, pbuf=pbuf, cbuf=cbuf,
                     n_hc=n_hc, hc_chunk=hc_chunk, n_fo=n_fo, fo_w=fo_w):
                mm = None
                for ki in range(npair):
                    ci = hc + ki - cs
                    for fo_in in range(n_fo):
                        for half in range(2):
                            mm = q.matmul(
                                t.ps_y[fo_in][:, half * 512:half * 512 + 512],
                                t.dw_sb[:, (cbuf * hc_chunk + ci) * fo_w +
                                        fo_in * 128:
                                        (cbuf * hc_chunk + ci) * fo_w +
                                        fo_in * 128 + 128],
                                t.astr4[:, pbuf * 2048 + ki * TLOC + half * 512:
                                        pbuf * 2048 + ki * TLOC + half * 512
                                        + 512],
                                start=(hc + ki == 0), stop=(hc + ki == n_hc - 1))
                return mm
            b.do("tensor", t_dn, sem="dn_pair")
            yield
        b.do("tensor", lambda q: q.nop(), sem="dn_chunk")


def gen_C3(b, t):
    for h in range(2):
        for qt in range(8):
            nkc = 8 * (qt + 1)
            for kc in range(nkc):
                ebuf = kc % 2
                b.wait("tensor", "tanh_done", b.c("tanh_done"))

                def t_sc(q, h=h, qt=qt, kc=kc):
                    mm = None
                    for half in range(2):
                        mm = q.matmul(
                            t.ps_sc[:, half * 512:half * 512 + 512],
                            t.xk[:, kc * 128:kc * 128 + 128],
                            t.xq[:, h * S + qt * 1024 + half * 512:
                                 h * S + qt * 1024 + half * 512 + 512],
                            start=True, stop=True)
                    return mm
                sdv = b.do("tensor", t_sc, sem="sc_done")
                b.wait("scalar", "sc_done", sdv)
                tnv = b.do("scalar", lambda q: q.activation(
                    out=t.tn_sb[:, :], in_=t.ps_sc[:, :], func=AF.Tanh,
                    scale=f32(SCALE1)), sem="tanh_done")
                b.wait("scalar", "mask_done", b.c("mask_done"))
                b.wait("scalar", "sacc_done", b.c("sacc_done") - 1)
                b.wait("scalar", "pv_done", b.c("pv_done") - 1)
                exv = b.do("scalar", lambda q, ebuf=ebuf: q.activation(
                    out=t.E2[:, ebuf * 1024:ebuf * 1024 + 1024],
                    in_=t.tn_sb[:, :], func=AF.Exp, scale=f32(50.0)),
                    sem="exp_done")
                b.wait("vector", "exp_done", exv)
                if kc >= 8 * qt:
                    o = kc - 8 * qt
                    b.do("vector", lambda v, ebuf=ebuf, o=o: v.tensor_tensor(
                        t.E2[:, ebuf * 1024:ebuf * 1024 + 1024],
                        t.E2[:, ebuf * 1024:ebuf * 1024 + 1024],
                        t.mask_sb[:, o * 1024:o * 1024 + 1024], op=ALU.mult),
                        sem="mask_done")
                b.wait("vector", "mask_done", b.c("mask_done"))
                if kc == 0:
                    sav = b.do("vector", lambda v, ebuf=ebuf: v.tensor_copy(
                        t.S_acc[:, :], t.E2[:, ebuf * 1024:ebuf * 1024 + 1024]),
                        sem="sacc_done")
                else:
                    sav = b.do("vector", lambda v, ebuf=ebuf: v.tensor_add(
                        t.S_acc[:, :], t.S_acc[:, :],
                        t.E2[:, ebuf * 1024:ebuf * 1024 + 1024]),
                        sem="sacc_done")
                b.wait("tensor", "mask_done", b.c("mask_done"))
                b.wait("tensor", "exp_done", exv)

                def t_pv(q, kc=kc, ebuf=ebuf, nkc=nkc):
                    mm = None
                    for half in range(2):
                        mm = q.matmul(
                            t.ps_o[:, half * 512:half * 512 + 512],
                            t.xv[:, kc * 128:kc * 128 + 128],
                            t.E2[:, ebuf * 1024 + half * 512:
                                 ebuf * 1024 + half * 512 + 512],
                            start=(kc == 0), stop=(kc == nkc - 1))
                    return mm
                b.do("tensor", t_pv, sem="pv_done")
                yield
            b.wait("vector", "sacc_done", b.c("sacc_done"))
            b.wait("vector", "z_mm", b.c("z_mm"))
            sbv = b.do("vector", lambda v: v.tensor_copy(t.S_bf[:, :],
                                                         t.S_acc[:, :]),
                       sem="sbf_done")
            b.wait("tensor", "sbf_done", sbv)
            b.wait("tensor", "tanh_done", b.c("tanh_done"))

            def t_z(q):
                q.matmul(t.ps_sc[0:1, 0:512], t.onescol[:, :], t.S_bf[:, 0:512],
                         start=True, stop=True)
                return q.matmul(t.ps_sc[0:1, 512:1024], t.onescol[:, :],
                                t.S_bf[:, 512:1024], start=True, stop=True)
            zmv = b.do("tensor", t_z, sem="z_mm")
            b.wait("scalar", "z_mm", zmv)
            b.wait("scalar", "zb_cast", b.c("zb_cast"))
            zev = b.do("scalar", lambda q: q.copy(t.zr_f[:, :],
                                                  t.ps_sc[0:1, :]),
                       sem="zev_done")
            b.wait("vector", "zev_done", zev)
            b.wait("vector", "zb_mm", b.c("zb_mm"))
            b.fdo("vector", lambda v: v.reciprocal(t.zr_f2[:, :], t.zr_f[:, :]))
            zbv = b.do("vector", lambda v: v.tensor_copy(t.zr_b[:, :],
                                                         t.zr_f2[:, :]),
                       sem="zb_cast")
            b.wait("scalar", "pv_done", b.c("pv_done"))
            b.wait("scalar", "o_sc", b.c("o_sc"))
            oev = b.do("scalar", lambda q: q.copy(t.o_raw[:, :], t.ps_o[:, :]),
                       sem="oev_done")
            b.wait("tensor", "zb_cast", zbv)
            b.wait("tensor", "zev_done", zev)
            for half in range(2):
                bcv = b.do("tensor", lambda q, half=half: q.matmul(
                    t.ps_sc[:, half * 512:half * 512 + 512], t.ones1[:, :],
                    t.zr_b[0:1, half * 512:half * 512 + 512],
                    start=True, stop=True), sem="zb_mm")
            b.wait("vector", "zb_mm", b.c("zb_mm"))
            b.wait("vector", "oev_done", oev)

            def v_osc(q, qt=qt):
                q.tensor_tensor(t.o_bf[:, qt * 1024:qt * 1024 + 512],
                                t.o_raw[:, 0:512], t.ps_sc[:, 0:512],
                                op=ALU.mult)
                return q.tensor_tensor(
                    t.o_bf[:, qt * 1024 + 512:qt * 1024 + 1024],
                    t.o_raw[:, 512:1024], t.ps_sc[:, 512:1024], op=ALU.mult)
            osv = b.do("vector", v_osc, sem="o_sc")
            b.wait("tensor", "o_sc", osv)
            b.wait("tensor", "oev_done", b.c("oev_done"))
            yield
        b.wait("gpsimd", "o_sc", b.c("o_sc"))
        b.wait("vector", "o_st", b.c("o_st"))
        b.dma("gpsimd", t.out1_o[h * 128:(h + 1) * 128, :], t.o_bf[:, :],
              sem="o_st")
    b.wait("gpsimd", "o_st", b.c("o_st"))


def emit_C(b, t):
    emit_consts(b, t)
    b.do("vector", lambda v: v.memset(t.onescol[:, :], 1.0), sem="const")
    wv = b.dma("gpsimd", t.wqkv_sb[:, :], t.wqkv)
    b.wait("tensor", "ld_gpsimd", wv)
    b.wait("tensor", "ld_const", b.c("ld_const"))

    def on_group_loc(g, lnv, xsl):
        pa, pb, tpv = emit_transposes(b, t, lnv, xsl, g)
        b.wait("vector", "tp_done", tpv)

        def v_evac(q, g=g, pa=pa, pb=pb):
            d3 = t.xn_loc.ap().rearrange("p (di tt) -> p di tt", di=16)
            q.tensor_copy(d3[:, 0:8, g * 128:g * 128 + 128],
                          pa.ap().rearrange("p (di tt) -> p di tt", di=8))
            return q.tensor_copy(
                d3[:, 8:16, g * 128:g * 128 + 128],
                pb.ap().rearrange("p (di tt) -> p di tt", di=8))
        b.do("vector", v_evac, sem="xp_evac")

    emit_ln_stream(b, t, t.xloc, 8, on_group_loc)

    emit_router_top2(b, t, n_exp=8, use_rms=False, scale_out=1.0)
    sgl = b.dma("gpsimd", t.sgw_sb[:, :], t.sgw_d, sem="ld_sgw")
    b.wait("tensor", "ld_sgw", sgl)
    b.wait("tensor", "cwb_ev", b.c("cwb_ev"))
    for di in range(16):
        def t_sg(q, di=di):
            mm = None
            for half in range(2):
                mm = q.matmul(t.ps_lg12[0:1, half * 512:half * 512 + 512],
                              t.sgw_sb[:, di:di + 1],
                              t.xn_loc[:, di * TLOC + half * 512:
                                       di * TLOC + half * 512 + 512],
                              start=(di == 0), stop=(di == 15))
            return mm
        b.do("tensor", t_sg, sem="sg_mm")
    b.wait("scalar", "sg_mm", b.c("sg_mm"))
    sgv = b.do("scalar", lambda q: q.activation(
        out=t.sg_row[:, :], in_=t.ps_lg12[0:1, :], func=AF.Sigmoid),
        sem="sg_act")
    b.wait("tensor", "sg_act", sgv)
    for half in range(2):
        bcv = b.do("tensor", lambda q, half=half: q.matmul(
            t.ps_lg12[:, half * 512:half * 512 + 512], t.ones1[:, :],
            t.sg_row[0:1, half * 512:half * 512 + 512],
            start=True, stop=True), sem="sgb_mm")
        b.wait("vector", "sgb_mm", bcv)
        sgbv = b.do("vector", lambda v, half=half: v.tensor_copy(
            t.sgb[:, half * 512:half * 512 + 512],
            t.ps_lg12[:, half * 512:half * 512 + 512]), sem="sgb_ev")
    b.wait("tensor", "sgb_ev", b.c("sgb_ev"))
    b.wait("gpsimd", "cwb_ev", b.c("cwb_ev"))
    b.wait("gpsimd", "sgb_ev", b.c("sgb_ev"))

    interleave((gen_C1(b, t), 1), (gen_gu(b, t), 2))
    b.barrier()

    mv = b.dma("gpsimd", t.mask_sb[:, :], t.mask_d, sem="ld_mask")
    b.wait("vector", "ld_mask", mv)
    b.wait("sync", "a_st", b.c("a_st"))
    interleave((gen_down_C(b, t), 1), (gen_C3(b, t), 1))
    b.wait("gpsimd", "g1_st", b.c("g1_st"))
    b.wait("gpsimd", "o_st", b.c("o_st"))


def emit_router_top2(b, t, *, n_exp, use_rms, scale_out):
    lw = b.dma("gpsimd", t.rw_sb[:, :], t.rw)
    if use_rms:
        lw = b.dma("gpsimd", t.nw_sb[:, :], t.nw)
    b.wait("vector", "ld_gpsimd", lw)
    b.wait("tensor", "ld_gpsimd", lw)
    for di in range(16):
        if use_rms:
            b.wait("vector", "lg_mm", b.c("lg_mm") - 1)
            xwv = b.do("vector", lambda v, di=di: v.tensor_scalar(
                t.xnw2[:, (di % 2) * TLOC:(di % 2 + 1) * TLOC],
                t.xn_loc[:, di * TLOC:(di + 1) * TLOC],
                t.nw_sb[:, di:di + 1], None, op0=ALU.mult), sem="xnw_done")
            b.wait("tensor", "xnw_done", xwv)
            rhs = (lambda half, di=di:
                   t.xnw2[:, (di % 2) * TLOC + half * 512:
                          (di % 2) * TLOC + half * 512 + 512])
        else:
            rhs = (lambda half, di=di:
                   t.xn_loc[:, di * TLOC + half * 512:
                            di * TLOC + half * 512 + 512])

        def t_lg(q, di=di, rhs=rhs):
            mm = None
            for half in range(2):
                mm = q.matmul(t.ps_lg[0:8, half * 512:half * 512 + 512],
                              t.rw_sb[:, di * 8:di * 8 + 8],
                              rhs(half),
                              start=(di == 0), stop=(di == 15))
            return mm
        b.do("tensor", t_lg, sem="lg_mm")
    b.wait("scalar", "lg_mm", b.c("lg_mm"))
    lev = b.do("scalar", lambda q: q.copy(t.lg_sb[:, :], t.ps_lg[0:8, :]),
               sem="lg_evac")
    b.wait("tensor", "lg_evac", lev)

    def t_ltr(q):
        mm = None
        for lg in range(8):
            mm = q.transpose(t.ps_t8[:, lg * 8:lg * 8 + 8],
                             t.lg_sb[:, lg * 128:lg * 128 + 128],
                             t.ident[0:8, 0:8])
        return mm
    ltv = b.do("tensor", t_ltr, sem="lg_tr")
    b.wait("vector", "lg_tr", ltv)
    ltev = b.do("vector", lambda v: v.tensor_copy(t.ltm[:, :], t.ps_t8[:, :]),
                sem="lg_tr_ev")
    b.wait("vector", "lg_tr_ev", ltev)
    if use_rms:
        b.wait("vector", "rms_done", b.c("rms_done"))
    for lg in range(8):
        col = (lambda j, lg=lg: t.r_sc[:, 8 * lg + j:8 * lg + j + 1])
        l8 = (lambda lg=lg: t.ltm[:, lg * 8:lg * 8 + 8])
        x8 = (lambda lg=lg: t.exps[:, lg * 8:lg * 8 + 8])
        e8 = (lambda lg=lg: t.eqs[:, lg * 8:lg * 8 + 8])
        t8 = (lambda lg=lg: t.tts[:, lg * 8:lg * 8 + 8])
        c8 = (lambda lg=lg: t.cwtm[:, lg * 8:lg * 8 + 8])
        if use_rms:
            rsv = b.do("vector", lambda v, col=col, lg=lg: v.tensor_scalar(
                col(0), t.rms_sumsq[:, lg:lg + 1], 1.0 / DM, 1e-6,
                op0=ALU.mult, op1=ALU.add), sem="rr1")
            b.wait("scalar", "rr1", rsv)
            rs2 = b.do("scalar", lambda q, col=col: q.sqrt(col(1), col(0)),
                       sem="rr2")
            b.wait("vector", "rr2", rs2)
            b.fdo("vector", lambda v, col=col: v.reciprocal(col(2), col(1)))
            b.fdo("vector", lambda v, col=col, l8=l8: v.tensor_scalar(
                l8(), l8(), col(2), None, op0=ALU.mult))
        b.fdo("vector", lambda v, col=col, l8=l8: v.tensor_reduce(
            col(3), l8(), axis=AX.X, op=ALU.max))
        mxv = b.do("vector", lambda v, col=col: v.tensor_scalar_mul(
            col(3), col(3), -1.0), sem="tq1")
        b.wait("scalar", "tq1", mxv)
        e8v = b.do("scalar", lambda q, col=col, l8=l8, x8=x8: q.activation(
            out=x8(), in_=l8(), func=AF.Exp, bias=col(3)), sem="tq2")
        b.wait("vector", "tq2", e8v)
        for step in [
            lambda v, col=col, x8=x8: v.tensor_reduce(
                col(4), x8(), axis=AX.X, op=ALU.max),
            lambda v, col=col, x8=x8, e8=e8: v.tensor_scalar(
                e8(), x8(), col(4), None, op0=ALU.is_ge),
            lambda v, x8=x8, e8=e8, t8=t8: v.tensor_tensor(
                t8(), x8(), e8(), op=ALU.mult),
            lambda v, x8=x8, t8=t8: v.tensor_sub(t8(), x8(), t8()),
            lambda v, col=col, t8=t8: v.tensor_reduce(
                col(5), t8(), axis=AX.X, op=ALU.max),
            lambda v, col=col, x8=x8, e8=e8: v.tensor_scalar(
                e8(), x8(), col(5), None, op0=ALU.is_ge),
            lambda v, col=col: v.tensor_add(col(6), col(4), col(5)),
            lambda v, col=col: v.reciprocal(col(7), col(6)),
            lambda v, x8=x8, e8=e8, t8=t8: v.tensor_tensor(
                t8(), x8(), e8(), op=ALU.mult),
        ]:
            b.fdo("vector", step)
        ctv = b.do("vector", lambda v, col=col, t8=t8, c8=c8: v.tensor_scalar(
            c8(), t8(), col(7), float(scale_out), op0=ALU.mult, op1=ALU.mult),
            sem="cw_tm_done")
        b.wait("tensor", "cw_tm_done", ctv)
        b.do("tensor", lambda q, lg=lg, c8=c8: q.transpose(
            t.ps_ct[:, lg * 128:lg * 128 + 128], c8(), t.ident[:, :]),
            sem="cw_tr")
    b.wait("vector", "cw_tr", b.c("cw_tr"))
    cfev = b.do("vector", lambda v: v.tensor_copy(t.cw8[:, :], t.ps_ct[:, :]),
                sem="cw_fm_ev")
    b.wait("gpsimd", "cw_fm_ev", cfev)
    bct = t.ps_bc if getattr(t, "ps_bc", None) is not None else t.ps_lg12
    for eh in range(2):
        b.wait("gpsimd", "cwb_mm", b.c("cwb_mm"))
        crv = None
        for ei in range(4):
            e = eh * 4 + ei
            crv = b.dma("gpsimd", t.cw_row[0:1, ei * TLOC:(ei + 1) * TLOC],
                        t.cw8[e:e + 1, :], sem="ld_cwrow")
        b.wait("tensor", "ld_cwrow", crv)
        for ei in range(4):
            e = eh * 4 + ei
            if e >= n_exp:
                continue
            for half in range(2):
                b.wait("tensor", "cwb_ev", b.c("cwb_ev"))
                bcv = b.do("tensor", lambda q, ei=ei, half=half: q.matmul(
                    bct[:, 0:512], t.ones1[:, :],
                    t.cw_row[0:1, ei * TLOC + half * 512:
                             ei * TLOC + half * 512 + 512],
                    start=True, stop=True), sem="cwb_mm")
                b.wait("vector", "cwb_mm", bcv)
                b.do("vector", lambda v, e=e, half=half: v.tensor_copy(
                    t.cwb[:, e * TLOC + half * 512:e * TLOC + half * 512 + 512],
                    bct[:, 0:512]), sem="cwb_ev")


def emit_expert_gu(b, t, gw_d, uw_d, a_dram, n_hc, hc_per_exp):
    for hc in range(n_hc):
        wbuf = b.c("gu_mm") % 2
        b.wait("gpsimd", "gu_mm", b.c("gu_mm") - 1)
        b.wait("vector", "gu_mm", b.c("gu_mm") - 1)
        b.dma("gpsimd", t.gwu2[:, wbuf * 4096:wbuf * 4096 + 2048],
              gw_d[:, hc * 2048:(hc + 1) * 2048], sem="ld_w")
        gl = b.dma("vector", t.gwu2[:, wbuf * 4096 + 2048:wbuf * 4096 + 4096],
                   uw_d[:, hc * 2048:(hc + 1) * 2048], sem="ld_w2")
        b.wait("tensor", "ld_w", b.c("ld_w"))
        b.wait("tensor", "ld_w2", gl)
        psg = t.ps_gA if b.c("gu_mm") % 2 == 0 else t.ps_gB
        psu = t.ps_uA if b.c("gu_mm") % 2 == 0 else t.ps_uB
        b.wait("tensor", "silu_done", b.c("silu_done") - 1)

        def t_g(q, wbuf=wbuf, psg=psg):
            mm = None
            for di in range(16):
                for half in range(2):
                    mm = q.matmul(
                        psg[:, half * 512:half * 512 + 512],
                        t.gwu2[:, wbuf * 4096 + di * 128:
                               wbuf * 4096 + di * 128 + 128],
                        t.xn_loc[:, di * TLOC + half * 512:
                                 di * TLOC + half * 512 + 512],
                        start=(di == 0), stop=(di == 15))
            return mm
        gmv = b.do("tensor", t_g, sem="g_mm")
        b.wait("tensor", "aprod_done", b.c("aprod_done") - 1)

        def t_u(q, wbuf=wbuf, psu=psu):
            mm = None
            for di in range(16):
                for half in range(2):
                    mm = q.matmul(
                        psu[:, half * 512:half * 512 + 512],
                        t.gwu2[:, wbuf * 4096 + 2048 + di * 128:
                               wbuf * 4096 + 2048 + di * 128 + 128],
                        t.xn_loc[:, di * TLOC + half * 512:
                                 di * TLOC + half * 512 + 512],
                        start=(di == 0), stop=(di == 15))
            return mm
        umv = b.do("tensor", t_u, sem="gu_mm")
        abuf = b.c("a_st") // 16 % 2
        b.wait("scalar", "g_mm", gmv)
        b.wait("scalar", "a_st", b.c("a_st") - 16)
        slv = b.do("scalar", lambda q, psg=psg, abuf=abuf: q.activation(
            out=t.aw2[:, abuf * TLOC:(abuf + 1) * TLOC], in_=psg[:, :],
            func=AF.Silu), sem="silu_done")
        b.wait("vector", "gu_mm", umv)
        b.wait("vector", "silu_done", slv)

        if hc_per_exp:
            e = hc // hc_per_exp

            def v_a(q, e=e, abuf=abuf, psu=psu):
                q.tensor_tensor(t.aw2[:, abuf * TLOC:(abuf + 1) * TLOC],
                                t.aw2[:, abuf * TLOC:(abuf + 1) * TLOC],
                                psu[:, :], op=ALU.mult)
                return q.tensor_tensor(
                    t.aw2[:, abuf * TLOC:(abuf + 1) * TLOC],
                    t.aw2[:, abuf * TLOC:(abuf + 1) * TLOC],
                    t.cwb[:, e * TLOC:(e + 1) * TLOC], op=ALU.mult)
        else:
            def v_a(q, abuf=abuf, psu=psu):
                return q.tensor_tensor(
                    t.aw2[:, abuf * TLOC:(abuf + 1) * TLOC],
                    t.aw2[:, abuf * TLOC:(abuf + 1) * TLOC],
                    psu[:, :], op=ALU.mult)
        apv = b.do("vector", v_a, sem="aprod_done")
        b.wait("gpsimd", "aprod_done", apv)
        b.dma("gpsimd", a_dram[hc * 128:(hc + 1) * 128, :],
              t.aw2[:, abuf * TLOC:(abuf + 1) * TLOC], sem="a_st")


def emit_down_accum(b, t, dwn_d, a_dram, n_hc, fog, hc_chunk, first):
    for cs in range(0, n_hc, hc_chunk):
        ce = min(cs + hc_chunk, n_hc)
        cbuf = b.c("dn_chunk") % 2
        b.wait("gpsimd", "dn_chunk", b.c("dn_chunk") - 1)
        dl = b.dma(
            "gpsimd",
            t.dw_sb[:, cbuf * hc_chunk * 512:
                    (cbuf * hc_chunk + (ce - cs)) * 512],
            dwn_d[:, (fog * n_hc + cs) * 512:(fog * n_hc + ce) * 512],
            sem="ld_dw")
        b.wait("tensor", "ld_dw", dl)
        for hc in range(cs, ce, 2):
            npair = min(2, ce - hc)
            pbuf = b.c("dn_pair") % 2
            b.wait("sync", "dn_pair", b.c("dn_pair") - 1)
            alv = b.dma(
                "sync",
                t.astr4[:, pbuf * 2048:pbuf * 2048 + npair * TLOC].rearrange(
                    "p (hh tt) -> p hh tt", hh=npair),
                a_dram[hc * 128:(hc + npair) * 128, :].rearrange(
                    "(hh p) tt -> p hh tt", p=128), sem="ld_a")
            b.wait("tensor", "ld_a", alv)

            def t_dn(q, hc=hc, cs=cs, npair=npair, pbuf=pbuf, cbuf=cbuf,
                     n_hc=n_hc, hc_chunk=hc_chunk):
                mm = None
                for ki in range(npair):
                    ci = hc + ki - cs
                    for fo_in in range(4):
                        for half in range(2):
                            mm = q.matmul(
                                t.ps_y[fo_in][:, half * 512:half * 512 + 512],
                                t.dw_sb[:, (cbuf * hc_chunk + ci) * 512 +
                                        fo_in * 128:
                                        (cbuf * hc_chunk + ci) * 512 +
                                        fo_in * 128 + 128],
                                t.astr4[:, pbuf * 2048 + ki * TLOC + half * 512:
                                        pbuf * 2048 + ki * TLOC + half * 512
                                        + 512],
                                start=(hc + ki == 0), stop=(hc + ki == n_hc - 1))
                return mm
            b.do("tensor", t_dn, sem="dn_pair")
        b.do("tensor", lambda q: q.nop(), sem="dn_chunk")


def host_inputs_C(inputs, x1_full):
    wq = inputs["L1_attn_wq"].T
    wk = inputs["L1_attn_wk"].T
    wv = inputs["L1_attn_wv"].T
    gw = _gu_layout(
        inputs["L1_ffn_w_gate"].transpose(1, 0, 2).reshape(DM, N_HC1 * 128))
    uw = _gu_layout(
        inputs["L1_ffn_w_up"].transpose(1, 0, 2).reshape(DM, N_HC1 * 128))
    gwu = np.ascontiguousarray(np.concatenate(
        [gw.reshape(128, N_HC1, 2048), uw.reshape(128, N_HC1, 2048)],
        axis=2).reshape(128, N_HC1 * 4096))
    dwn = _down_layout(inputs["L1_ffn_w_down"].reshape(N_HC1 * 128, DM), 8, 256)
    rw = _chunked(np.ascontiguousarray(inputs["L1_ffn_gate_w"].T))
    sgw = np.ascontiguousarray(
        inputs["L1_ffn_shared_gate_w"].reshape(16, 128).T)
    swg = _gu_layout(inputs["L1_ffn_sw_gate"])
    swu = _gu_layout(inputs["L1_ffn_sw_up"])
    swgu = np.ascontiguousarray(np.concatenate(
        [swg.reshape(128, N_HCS, 2048), swu.reshape(128, N_HCS, 2048)],
        axis=2).reshape(128, N_HCS * 4096))
    swd = _down_layout(inputs["L1_ffn_sw_down"], 8, 256)
    inv = 1.0 / 10000.0 ** (np.arange(0, 128, 2, dtype=np.float32) / 128)
    fpos = np.arange(S, dtype=np.float32)[:, None] * inv[None, :]
    emb = np.concatenate((fpos, fpos), axis=-1)
    cos_c = np.ascontiguousarray(np.cos(emb).astype(np.float32).T)
    sin_c = np.ascontiguousarray(np.sin(emb).astype(np.float32).T)
    kp = np.arange(128)[:, None]
    qp = np.arange(1024)[None, :]
    mask = np.zeros((128, 8 * 1024), np.float32)
    for o in range(8):
        mask[:, o * 1024:(o + 1) * 1024] = (o * 128 + kp <= qp)
    mask_c = mask.astype(BF)
    maps = []
    for c in range(NCORES):
        sl = np.concatenate([
            wq[:, 256 * c:256 * (c + 1)],
            wk[:, 128 * c:128 * (c + 1)],
            wv[:, 128 * c:128 * (c + 1)]], axis=1)
        maps.append({
            "x": x1_full, "xloc": x1_full[TLOC * c:TLOC * (c + 1)],
            "wqkv": _chunked(np.ascontiguousarray(sl)), "rw": rw,
            "gwu": gwu, "dwn": dwn, "sgw": sgw, "swgu": swgu, "swd": swd,
            "cos_c": cos_c, "sin_c": sin_c, "mask_c": mask_c,
            "ident_c": _IDENT})
    return maps





def host_inputs_D(inputs, x1_full, out1_parts, g1_parts):
    out1_full = np.concatenate(out1_parts, axis=0)
    wo1 = _chunked(np.ascontiguousarray(inputs["L1_attn_wo"].T))
    maps = []
    for c in range(NCORES):
        xfm = np.ascontiguousarray(x1_full[TLOC * c:TLOC * (c + 1)].T)
        maps.append({
            "xfm": xfm,
            "val": np.ascontiguousarray(out1_full[:, TLOC * c:TLOC * (c + 1)]),
            "g0": g1_parts[c],
            "wo": wo1})
    return maps


def kernel(**inputs):
    maps_A = host_inputs_A(inputs)
    nc_A = build(make_tensors_A, emit_A)
    res_A = run_bass_kernel_spmd(nc_A, maps_A, core_ids=list(range(NCORES)))
    val_parts = [res_A.results[c]["valout"] for c in range(NCORES)]
    g0_parts = [res_A.results[c]["g0out"] for c in range(NCORES)]

    maps_B = host_inputs_B(inputs, val_parts, g0_parts)
    nc_B = build(make_tensors_B, emit_B)
    res_B = run_bass_kernel_spmd(nc_B, maps_B, core_ids=list(range(NCORES)))
    x1_full = np.concatenate(
        [np.ascontiguousarray(res_B.results[c]["x1out"].T)
         for c in range(NCORES)], axis=0)

    maps_C = host_inputs_C(inputs, x1_full)
    nc_C = build(make_tensors_C, emit_C)
    res_C = run_bass_kernel_spmd(nc_C, maps_C, core_ids=list(range(NCORES)))
    out1_parts = [res_C.results[c]["out1"] for c in range(NCORES)]
    g1_parts = [res_C.results[c]["g1out"] for c in range(NCORES)]

    maps_D = host_inputs_D(inputs, x1_full, out1_parts, g1_parts)
    nc_D = build(make_tensors_B, emit_B)
    res_D = run_bass_kernel_spmd(nc_D, maps_D, core_ids=list(range(NCORES)))
    x2 = np.concatenate(
        [np.ascontiguousarray(res_D.results[c]["x1out"].T)
         for c in range(NCORES)], axis=0)
    return x2.reshape(1, S, DM)



# version 04
# speedup vs baseline: 1.1873x; 1.0365x over previous
import contextlib
import types
import numpy as np
import ml_dtypes
import concourse.bass as bass
import concourse.mybir as mybir
from concourse.bass_utils import run_bass_kernel_spmd

F32 = mybir.dt.float32
BF16 = mybir.dt.bfloat16
AF = mybir.ActivationFunctionType
ALU = mybir.AluOpType
AX = mybir.AxisListType
BF = ml_dtypes.bfloat16
f32 = np.float32

NCORES = 8
S = 8192
DM = 2048
TLOC = S // NCORES
C0_SHIFT = 16.0
ENGINES = ("sync", "gpsimd", "tensor", "vector", "scalar")
INV_SQRT_HD = 1.0 / np.sqrt(128.0)


class B:

    def __init__(self, nc, eng, q, sems):
        self.nc, self.eng, self.q, self.sems = nc, eng, q, sems
        self.cnt = {}
        self.nbar = 0

    def _sem(self, name):
        if name not in self.sems:
            assert self.eng == "__dry__", f"sem {name} not in dry pass"
            self.sems[name] = None

    def c(self, name):
        return self.cnt.get(name, 0)

    def inc(self, name, by=1):
        self._sem(name)
        self.cnt[name] = self.cnt.get(name, 0) + by
        return self.cnt[name]

    def do(self, eng, fn, sem=None, by=1):
        v = self.inc(sem, by) if sem else None
        if self.eng == eng:
            ins = fn(self.q)
            if sem:
                ins.then_inc(self.sems[sem], by)
        return v

    def wait(self, eng, sem, val):
        self._sem(sem)
        if self.eng == eng and val > 0:
            self.q.wait_ge(self.sems[sem], val)

    def fdo(self, eng, fn, sem=None):
        sem = sem or ("f_" + eng)
        v = self.do(eng, fn, sem=sem)
        self.wait(eng, sem, v)
        return v

    def dma(self, eng, out, in_, sem=None, **kw):
        sem = sem or ("ld_" + eng)
        return self.do(eng, lambda q: q.dma_start(out=out, in_=in_, **kw),
                       sem=sem, by=16)

    def barrier(self):
        self.nbar += 1
        self._sem("bar")
        tgt = 5 * self.nbar
        self.cnt["bar"] = tgt
        if self.eng != "__dry__":
            self.q.sem_inc(self.sems["bar"], 1)
            self.q.wait_ge(self.sems["bar"], tgt)


def build(make_tensors, emit):
    nc = bass.Bass("TRN2", target_bir_lowering=False)
    es = contextlib.ExitStack()
    with es:
        t = make_tensors(nc, es)
        sems = {}
        emit(B(nc, "__dry__", None, sems), t)
        for name in list(sems):
            sems[name] = es.enter_context(nc.semaphore(name))
        with nc.Block() as block:
            for eng in ENGINES:
                getattr(block, eng)(
                    lambda q, eng=eng: emit(B(nc, eng, q, sems), t))
    return nc



def emit_consts(b, t):
    b.do("vector", lambda v: v.memset(t.zeros1[:, :], 0.0), sem="const")
    b.do("vector", lambda v: v.memset(t.ones1[:, :], 1.0), sem="const")
    b.dma("gpsimd", t.ident[:, :], t.ident_c, sem="ld_const")


def interleave(*tasks):
    active = list(tasks)
    while active:
        keep = []
        for g, w in active:
            alive = True
            for _ in range(w):
                try:
                    next(g)
                except StopIteration:
                    alive = False
                    break
            if alive:
                keep.append((g, w))
        active = keep


def emit_ln_stream(b, t, x_dram, n_groups, on_group, rms_groups=None):
    for _ in gen_ln_stream(b, t, x_dram, n_groups, on_group, rms_groups):
        pass


def gen_ln_stream(b, t, x_dram, n_groups, on_group, rms_groups=None):
    st = t.stats
    for g in range(n_groups):
        buf = g % 2
        xsl = slice(buf * DM, (buf + 1) * DM)
        b.wait("sync", "ln_done", b.c("ln_done") - 1)
        b.wait("sync", "st_s", b.c("st_s") - 1)
        vld = b.dma("sync", t.x_tm2[:, xsl], x_dram[g * 128:(g + 1) * 128, :])
        b.wait("vector", "ld_sync", vld)
        b.fdo("vector", lambda v, xsl=xsl: v.tensor_reduce(
            st[:, 0:1], t.x_tm2[:, xsl], axis=AX.X, op=ALU.add))
        b.do("vector", lambda v: v.tensor_scalar_mul(
            st[:, 1:2], st[:, 0:1], -1.0 / DM), sem="st_v")
        b.wait("vector", "st_v", b.c("st_v"))
        b.wait("scalar", "ld_sync", vld)
        b.wait("scalar", "st_v", b.c("st_v"))
        sqv = b.do("scalar", lambda q, xsl=xsl: q.activation(
            out=t.sq_scr[:, :], in_=t.x_tm2[:, xsl], func=AF.Square,
            accum_out=st[:, 2:3]), sem="st_s")
        b.wait("vector", "st_s", sqv)
        b.fdo("vector", lambda v: v.tensor_scalar(
            st[:, 3:4], st[:, 2:3], 1.0 / DM, 1e-5, op0=ALU.mult, op1=ALU.add))
        b.fdo("vector", lambda v: v.tensor_tensor(
            st[:, 4:5], st[:, 1:2], st[:, 1:2], op=ALU.mult))
        s2v = b.do("vector", lambda v: v.tensor_sub(
            st[:, 3:4], st[:, 3:4], st[:, 4:5]), sem="st_v2")
        b.wait("scalar", "st_v2", s2v)
        s3v = b.do("scalar", lambda q: q.sqrt(st[:, 4:5], st[:, 3:4]),
                   sem="st_s2")
        b.wait("vector", "st_s2", s3v)
        b.fdo("vector", lambda v: v.reciprocal(st[:, 5:6], st[:, 4:5]))
        b.wait("vector", "tp_done", b.c("tp_done") - 1)
        if rms_groups and (g - 2) in rms_groups:
            b.wait("vector", "rms_done",
                   b.c("rms_done") - (0 if g in rms_groups else 1))
        lnv = b.do("vector", lambda v, xsl=xsl: v.tensor_scalar(
            t.xn_tm2[:, xsl], t.x_tm2[:, xsl], st[:, 1:2], st[:, 5:6],
            op0=ALU.add, op1=ALU.mult), sem="ln_done")
        if rms_groups and g in rms_groups:
            lg = rms_groups.index(g)
            b.wait("scalar", "ln_done", lnv)
            b.do("scalar", lambda q, xsl=xsl, lg=lg: q.activation(
                out=t.sq_scr[:, :], in_=t.xn_tm2[:, xsl], func=AF.Square,
                accum_out=t.rms_sumsq[:, lg:lg + 1]), sem="rms_done")
        on_group(g, lnv, xsl)
        yield


def emit_transposes(b, t, lnv, xsl, g):
    pa = t.ps_tr0 if g % 2 == 0 else t.ps_tr1
    pb = t.ps_tr1 if g % 2 == 0 else t.ps_tr0
    b.wait("tensor", "ln_done", lnv)
    b.wait("tensor", "xp_evac", b.c("xp_evac") - 1)

    def t_tr(q, xsl=xsl, pa=pa, pb=pb):
        mm = None
        for j in range(16):
            dst = pa if j < 8 else pb
            mm = q.transpose(
                dst[:, (j % 8) * 128:(j % 8) * 128 + 128],
                t.xn_tm2[:, xsl.start + j * 128:xsl.start + j * 128 + 128],
                t.ident[:, :])
        return mm
    tpv = b.do("tensor", t_tr, sem="tp_done")
    return pa, pb, tpv



def make_tensors_A(nc, es):
    t = types.SimpleNamespace()
    t.x = nc.dram_tensor("x", [S, DM], F32, kind="ExternalInput").ap()
    t.xloc = nc.dram_tensor("xloc", [TLOC, DM], F32, kind="ExternalInput").ap()
    t.wqkv = nc.dram_tensor("wqkv", [128, 16 * 768], F32,
                            kind="ExternalInput").ap()
    t.rw = nc.dram_tensor("rw", [128, 128], F32, kind="ExternalInput").ap()
    t.nw = nc.dram_tensor("nw", [128, 16], F32, kind="ExternalInput").ap()
    t.gwu = nc.dram_tensor("gwu", [128, 64 * 4096], F32,
                           kind="ExternalInput").ap()
    t.dwn = nc.dram_tensor("dwn", [128, 16 * 64 * 128], F32,
                           kind="ExternalInput").ap()
    t.ident_c = nc.dram_tensor("ident_c", [128, 128], BF16,
                               kind="ExternalInput").ap()
    t.val_o = nc.dram_tensor("valout", [256, S], BF16,
                             kind="ExternalOutput").ap()
    t.g0_o = nc.dram_tensor("g0out", [DM, TLOC], BF16,
                            kind="ExternalOutput").ap()
    t.a_dram = nc.dram_tensor("a_stage", [8192, TLOC], BF16).ap()
    t.q_dram = nc.dram_tensor("q_stage", [256, S], BF16).ap()

    sb = es.enter_context
    t.zeros1 = sb(nc.sbuf_tensor("zeros1", [128, 1], BF16))
    t.ones1 = sb(nc.sbuf_tensor("ones1", [1, 128], BF16))
    t.ident = sb(nc.sbuf_tensor("ident", [128, 128], BF16))
    t.stats = sb(nc.sbuf_tensor("stats", [128, 8], F32))
    t.rms_sumsq = sb(nc.sbuf_tensor("rms_sumsq", [128, 8], F32))
    t.xk = sb(nc.sbuf_tensor("xk", [128, 2 * S], BF16))
    t.xv = sb(nc.sbuf_tensor("xv", [128, 2 * S], BF16))

    with contextlib.ExitStack() as p1:
        t.sq_scr = p1.enter_context(nc.sbuf_tensor("sq_scr", [128, DM], BF16))
        t.wqkv_sb = p1.enter_context(
            nc.sbuf_tensor("wqkv_sb", [128, 16 * 768], BF16))
        t.x_tm2 = p1.enter_context(nc.sbuf_tensor("x_tm2", [128, 2 * DM], F32))
        t.xn_tm2 = p1.enter_context(
            nc.sbuf_tensor("xn_tm2", [128, 2 * DM], BF16))
        t.xn_t = p1.enter_context(nc.sbuf_tensor("xn_t", [128, 16 * 512], BF16))
        t.qstg = p1.enter_context(nc.sbuf_tensor("qstg", [128, 2 * 512], BF16))
        t.xn_loc = p1.enter_context(
            nc.sbuf_tensor("xn_loc", [128, 16 * TLOC], BF16))
        t.rw_sb = p1.enter_context(nc.sbuf_tensor("rw_sb", [128, 128], BF16))
        t.nw_sb = p1.enter_context(nc.sbuf_tensor("nw_sb", [128, 16], F32))
        t.cwb = p1.enter_context(nc.sbuf_tensor("cwb", [128, 8 * TLOC], BF16))
        with contextlib.ExitStack() as p1r:
            t.r_sc = p1r.enter_context(nc.sbuf_tensor("r_sc", [128, 64], F32))
            t.lg_sb = p1r.enter_context(nc.sbuf_tensor("lg_sb", [8, TLOC], BF16))
            t.ltm = p1r.enter_context(nc.sbuf_tensor("ltm", [128, 64], F32))
            t.exps = p1r.enter_context(nc.sbuf_tensor("exps", [128, 64], F32))
            t.eqs = p1r.enter_context(nc.sbuf_tensor("eqs", [128, 64], F32))
            t.tts = p1r.enter_context(nc.sbuf_tensor("tts", [128, 64], F32))
            t.cwtm = p1r.enter_context(nc.sbuf_tensor("cwtm", [128, 64], BF16))
            t.cw8 = p1r.enter_context(nc.sbuf_tensor("cw8", [8, TLOC], BF16))
            t.cw_row = p1r.enter_context(
                nc.sbuf_tensor("cw_row", [1, 4 * TLOC], BF16))
            t.xnw2 = p1r.enter_context(
                nc.sbuf_tensor("xnw2", [128, 2 * TLOC], BF16))
        t.gwu2 = p1.enter_context(nc.sbuf_tensor("gwu2", [128, 2 * 4096], BF16))
        t.aw2 = p1.enter_context(nc.sbuf_tensor("aw2", [128, 2 * TLOC], BF16))
        t.ps_tr0 = p1.enter_context(nc.psum_tensor("ps_tr0", [128, 1024], BF16))
        t.ps_tr1 = p1.enter_context(nc.psum_tensor("ps_tr1", [128, 1024], BF16))
        t.ps_qkvA = p1.enter_context(nc.psum_tensor("ps_qkvA", [128, 512], F32))
        t.ps_qkvB = p1.enter_context(nc.psum_tensor("ps_qkvB", [128, 512], F32))
        with contextlib.ExitStack() as p1p:
            t.ps_lg12 = p1p.enter_context(
                nc.psum_tensor("ps_lg12", [128, 1024], F32))
            t.ps_t8 = p1p.enter_context(
                nc.psum_tensor("ps_t8", [128, 64], BF16))
            t.ps_ct = p1p.enter_context(
                nc.psum_tensor("ps_ct", [8, TLOC], BF16))
            t.ps_lg = t.ps_lg12
            t.ps_bc = None
        t.ps_g = p1.enter_context(nc.psum_tensor("ps_g", [128, TLOC], F32))
        t.ps_u = p1.enter_context(nc.psum_tensor("ps_u", [128, TLOC], F32))

    with contextlib.ExitStack() as p2:
        t.E2 = p2.enter_context(nc.sbuf_tensor("E2", [128, 2 * S], BF16))
        t.Zp = p2.enter_context(nc.sbuf_tensor("Zp", [128, 16], F32))
        t.zwork = p2.enter_context(nc.sbuf_tensor("zwork", [128, 4], F32))
        t.rbf = p2.enter_context(nc.sbuf_tensor("rbf", [128, 2], BF16))
        t.cs_sb = p2.enter_context(nc.sbuf_tensor("cs_sb", [1, S], BF16))
        t.val_sb = p2.enter_context(nc.sbuf_tensor("val_sb", [128, S], BF16))
        t.qld = p2.enter_context(nc.sbuf_tensor("qld", [128, 2 * 128], BF16))
        t.astr4 = p2.enter_context(
            nc.sbuf_tensor("astr4", [128, 2 * 2 * TLOC], BF16))
        t.dw_sb = p2.enter_context(
            nc.sbuf_tensor("dw_sb", [128, 2 * 32 * 256], BF16))
        t.g0t = p2.enter_context(nc.sbuf_tensor("g0t", [128, 2 * TLOC], BF16))
        with contextlib.ExitStack() as p2a:
            t.ps_scA = p2a.enter_context(
                nc.psum_tensor("ps_scA", [128, 1024], F32))
            t.ps_scB = p2a.enter_context(
                nc.psum_tensor("ps_scB", [128, 1024], F32))
            t.ps_cs = [p2a.enter_context(
                nc.psum_tensor(f"ps_cs{i}", [97, 512], F32)) for i in range(4)]
        t.ps_y = [p2.enter_context(
            nc.psum_tensor(f"ps_y{i}", [128, TLOC], F32)) for i in range(2)]
    return t


def gen_A1(b, t):
    def on_group(g, lnv, xsl):
        tile, gi = g // 4, g % 4
        pa, pb, tpv = emit_transposes(b, t, lnv, xsl, g)
        b.wait("vector", "tp_done", tpv)
        if gi == 0:
            b.wait("vector", "qm_done", 6 * tile)

        def v_evac(q, gi=gi, pa=pa, pb=pb):
            d3 = t.xn_t.ap().rearrange("p (di tt) -> p di tt", di=16)
            q.tensor_copy(d3[:, 0:8, gi * 128:gi * 128 + 128],
                          pa.ap().rearrange("p (di tt) -> p di tt", di=8))
            return q.tensor_copy(
                d3[:, 8:16, gi * 128:gi * 128 + 128],
                pb.ap().rearrange("p (di tt) -> p di tt", di=8))
        b.do("vector", v_evac, sem="xp_evac")

        if gi == 3:
            b.wait("tensor", "xp_evac", b.c("xp_evac"))
            for oc in range(6):
                ps = t.ps_qkvA if oc % 2 == 0 else t.ps_qkvB
                b.wait("tensor", "qe_done", b.c("qe_done") - 1)

                def t_oc(q, oc=oc, ps=ps):
                    mm = None
                    for di in range(16):
                        mm = q.matmul(
                            ps[:, :],
                            t.wqkv_sb[:, di * 768 + oc * 128:
                                      di * 768 + oc * 128 + 128],
                            t.xn_t[:, di * 512:(di + 1) * 512],
                            start=(di == 0), stop=(di == 15))
                    return mm
                qmv = b.do("tensor", t_oc, sem="qm_done")
                b.wait("scalar", "qm_done", qmv)
                h = oc % 2
                if oc < 2:
                    qb = b.c("qst") // 16 % 2
                    b.wait("scalar", "qst", b.c("qst") - 16)
                    qev = b.do("scalar", lambda q, ps=ps, qb=qb: q.copy(
                        t.qstg[:, qb * 512:(qb + 1) * 512], ps[:, :]),
                        sem="qe_done")
                    b.wait("sync", "qe_done", qev)
                    b.dma("sync", t.q_dram[h * 128:(h + 1) * 128,
                                           tile * 512:(tile + 1) * 512],
                          t.qstg[:, qb * 512:(qb + 1) * 512], sem="qst")
                else:
                    dest = t.xk if oc < 4 else t.xv
                    b.do("scalar", lambda q, dest=dest, h=h, tile=tile, ps=ps:
                         q.copy(dest[:, h * S + tile * 512:
                                     h * S + tile * 512 + 512], ps[:, :]),
                         sem="qe_done")

    yield from gen_ln_stream(b, t, t.x, 64, on_group)


def gen_A2(b, t):
    def slot(kt):
        return t.ps_cs[kt % 4], 32 * (kt // 4)

    for h in range(2):
        if h > 0:
            b.wait("tensor", "cs_evac", b.c("cs_evac"))
        b.wait("tensor", "const", b.c("const"))
        b.wait("tensor", "const2", b.c("const2"))

        def t_zero(q):
            mm = None
            for kt in range(16):
                pt, row = slot(kt)
                tp = (0, row) if row else None
                mm = q.matmul(pt[row:row + 1, :], t.zeros1[:, :], t.E2[:, 0:512],
                              start=True, stop=False, tile_position=tp)
            return mm
        b.do("tensor", t_zero, sem="zs")

        for qc in range(64):
            ebuf = qc % 2
            qb = b.c("ld_q") // 16 % 2
            b.wait("sync", "sc_done", b.c("sc_done") - 8)
            qlv = b.dma("sync", t.qld[:, qb * 128:(qb + 1) * 128],
                        t.q_dram[h * 128:(h + 1) * 128, qc * 128:qc * 128 + 128],
                        sem="ld_q")
            b.wait("tensor", "ld_q", qlv)
            for sub in range(8):
                psc = t.ps_scA if sub % 2 == 0 else t.ps_scB
                b.wait("tensor", "exp_done", b.c("exp_done") - 1)

                def t_sc(q, h=h, sub=sub, qb=qb, psc=psc):
                    mm = None
                    for half in range(2):
                        kt = sub * 2 + half
                        mm = q.matmul(
                            psc[:, half * 512:half * 512 + 512],
                            t.qld[:, qb * 128:(qb + 1) * 128],
                            t.xk[:, h * S + kt * 512:h * S + kt * 512 + 512],
                            start=True, stop=True)
                    return mm
                sdv = b.do("tensor", t_sc, sem="sc_done")
                b.wait("scalar", "sc_done", sdv)
                if sub == 0:
                    b.wait("scalar", "rmm_done", b.c("rmm_done") - 1)

                def s_exp(q, ebuf=ebuf, sub=sub, psc=psc):
                    return q.activation(
                        out=t.E2[:, ebuf * S + sub * 1024:
                                 ebuf * S + sub * 1024 + 1024],
                        in_=psc[:, :], func=AF.Exp,
                        scale=f32(INV_SQRT_HD), bias=f32(-C0_SHIFT),
                        accum_out=t.Zp[:, ebuf * 8 + sub:ebuf * 8 + sub + 1])
                b.do("scalar", s_exp, sem="exp_done")
            b.wait("vector", "exp_done", b.c("exp_done"))
            b.wait("vector", "rmm_done", b.c("rmm_done") - 1)
            b.fdo("vector", lambda q, ebuf=ebuf: q.tensor_reduce(
                t.zwork[:, 0:1], t.Zp[:, ebuf * 8:ebuf * 8 + 8],
                axis=AX.X, op=ALU.add))
            b.fdo("vector", lambda q: q.reciprocal(
                t.zwork[:, 1:2], t.zwork[:, 0:1]))
            zdv = b.do("vector", lambda q, ebuf=ebuf: q.tensor_copy(
                t.rbf[:, ebuf:ebuf + 1], t.zwork[:, 1:2]), sem="z_done")
            b.wait("tensor", "z_done", zdv)
            if qc == 0:
                b.wait("tensor", "zs", b.c("zs"))

            def t_rmm(q, qc=qc, ebuf=ebuf):
                mm = None
                for kt in range(16):
                    pt, row = slot(kt)
                    tp = (0, row) if row else None
                    mm = q.matmul(pt[row:row + 1, :], t.rbf[:, ebuf:ebuf + 1],
                                  t.E2[:, ebuf * S + kt * 512:
                                       ebuf * S + kt * 512 + 512],
                                  start=False, stop=(qc == 63),
                                  tile_position=tp)
                return mm
            b.do("tensor", t_rmm, sem="rmm_done")
            yield

        b.wait("scalar", "rmm_done", b.c("rmm_done"))
        b.wait("scalar", "bc_done", b.c("bc_done"))
        for kt in range(16):
            pt, row = slot(kt)
            b.do("scalar", lambda q, pt=pt, row=row, kt=kt: q.copy(
                t.cs_sb[0:1, kt * 512:kt * 512 + 512], pt[row:row + 1, :]),
                sem="cs_evac")
        b.wait("tensor", "cs_evac", b.c("cs_evac"))
        b.wait("tensor", "exp_done", b.c("exp_done"))
        b.wait("vector", "st_out", b.c("st_out"))
        for kt in range(16):
            if kt > 0:
                b.wait("tensor", "vs_done", b.c("vs_done"))
            bcv = b.do("tensor", lambda q, kt=kt: q.matmul(
                t.ps_scA[:, 0:512], t.ones1[:, :],
                t.cs_sb[0:1, kt * 512:kt * 512 + 512], start=True, stop=True),
                sem="bc_done")
            b.wait("vector", "bc_done", bcv)
            b.do("vector", lambda q, h=h, kt=kt: q.tensor_tensor(
                t.val_sb[:, kt * 512:kt * 512 + 512],
                t.xv[:, h * S + kt * 512:h * S + kt * 512 + 512],
                t.ps_scA[:, 0:512], op=ALU.mult), sem="vs_done")
            yield
        b.wait("tensor", "vs_done", b.c("vs_done"))
        b.wait("gpsimd", "vs_done", b.c("vs_done"))
        b.dma("gpsimd", t.val_o[h * 128:(h + 1) * 128, :], t.val_sb[:, :],
              sem="st_out")
    b.wait("gpsimd", "st_out", b.c("st_out"))


def gen_down_A(b, t):
    for fog in range(8):
        b.wait("tensor", "y_silu", b.c("y_silu"))
        yield from gen_down_accum(b, t, t.dwn, t.a_dram, 64, fog, 32, 2, 256)
        b.wait("scalar", "dn_pair", b.c("dn_pair"))
        for fo_in in range(2):
            fo = fog * 2 + fo_in
            obuf = fo % 2
            b.wait("scalar", "g0_st", b.c("g0_st") - 16)
            ysv = b.do("scalar", lambda q, obuf=obuf, fo_in=fo_in: q.activation(
                out=t.g0t[:, obuf * TLOC:(obuf + 1) * TLOC],
                in_=t.ps_y[fo_in][:, :], func=AF.Silu), sem="y_silu")
            b.wait("gpsimd", "y_silu", ysv)
            b.dma("gpsimd", t.g0_o[fo * 128:(fo + 1) * 128, :],
                  t.g0t[:, obuf * TLOC:(obuf + 1) * TLOC], sem="g0_st")
        yield


A_UPTO = 9


def emit_A(b, t):
    emit_consts(b, t)
    wv = b.dma("gpsimd", t.wqkv_sb[:, :], t.wqkv)
    b.wait("tensor", "ld_gpsimd", wv)
    b.wait("tensor", "ld_const", b.c("ld_const"))

    def on_group_loc(g, lnv, xsl):
        pa, pb, tpv = emit_transposes(b, t, lnv, xsl, g)
        b.wait("vector", "tp_done", tpv)

        def v_evac(q, g=g, pa=pa, pb=pb):
            d3 = t.xn_loc.ap().rearrange("p (di tt) -> p di tt", di=16)
            q.tensor_copy(d3[:, 0:8, g * 128:g * 128 + 128],
                          pa.ap().rearrange("p (di tt) -> p di tt", di=8))
            return q.tensor_copy(
                d3[:, 8:16, g * 128:g * 128 + 128],
                pb.ap().rearrange("p (di tt) -> p di tt", di=8))
        b.do("vector", v_evac, sem="xp_evac")

    emit_ln_stream(b, t, t.xloc, 8, on_group_loc, rms_groups=list(range(8)))

    emit_router_top2(b, t, n_exp=8, use_rms=True, scale_out=4.0)
    b.wait("gpsimd", "cwb_ev", b.c("cwb_ev"))
    b.wait("tensor", "cwb_ev", b.c("cwb_ev"))
    if A_UPTO < 2:
        b.barrier()
        return
    interleave((gen_A1(b, t), 1), (gen_expert_gu(b, t, t.gwu, t.a_dram, 64, 8), 1))
    b.barrier()
    if A_UPTO < 3:
        return
    mz = b.do("vector", lambda v: v.memset(t.E2[:, 0:512], 0.0), sem="const2")
    b.wait("sync", "a_st", b.c("a_st"))
    b.wait("sync", "qst", b.c("qst"))
    interleave((gen_A2(b, t), 1))
    b.barrier()
    interleave((gen_down_A(b, t), 1))
    b.wait("gpsimd", "g0_st", b.c("g0_st"))
    b.wait("gpsimd", "st_out", b.c("st_out"))



def make_tensors_B(nc, es):
    t = types.SimpleNamespace()
    t.xfm = nc.dram_tensor("xfm", [DM, TLOC], F32, kind="ExternalInput").ap()
    t.val = nc.dram_tensor("val", [DM, TLOC], BF16, kind="ExternalInput").ap()
    t.g0 = nc.dram_tensor("g0", [DM, TLOC], BF16, kind="ExternalInput").ap()
    t.wo = nc.dram_tensor("wo", [128, 16 * DM], F32, kind="ExternalInput").ap()
    t.x1_o = nc.dram_tensor("x1out", [DM, TLOC], F32, kind="ExternalOutput").ap()
    sb = es.enter_context
    t.wo_sb = sb(nc.sbuf_tensor("wo_sb", [128, 16 * DM], BF16))
    t.val_sb = sb(nc.sbuf_tensor("val_sb", [128, 16 * TLOC], BF16))
    t.g0_sb = sb(nc.sbuf_tensor("g0_sb", [128, 16 * TLOC], BF16))
    t.x_sb = sb(nc.sbuf_tensor("x_sb", [128, 16 * TLOC], F32))
    t.x1_sb = sb(nc.sbuf_tensor("x1_sb", [128, 2 * TLOC], F32))
    t.sg_sb = sb(nc.sbuf_tensor("sg_sb", [128, TLOC], F32))
    t.ps_s = [sb(nc.psum_tensor(f"ps_s{i}", [128, TLOC], F32)) for i in range(2)]
    return t


def emit_B(b, t):
    lv = b.dma("gpsimd", t.wo_sb[:, :], t.wo)
    b.dma("sync", t.val_sb[:, :].rearrange("p (dc tt) -> p dc tt", dc=16),
          t.val.rearrange("(dc p) tt -> p dc tt", p=128))
    b.dma("sync", t.g0_sb[:, :].rearrange("p (dc tt) -> p dc tt", dc=16),
          t.g0.rearrange("(dc p) tt -> p dc tt", p=128))
    xv = b.dma("sync", t.x_sb[:, :].rearrange("p (dc tt) -> p dc tt", dc=16),
               t.xfm.rearrange("(dc p) tt -> p dc tt", p=128))
    b.wait("tensor", "ld_gpsimd", lv)
    b.wait("tensor", "ld_sync", xv)
    for fo in range(16):
        ps = t.ps_s[fo % 2]
        b.wait("tensor", "sg_done", b.c("sg_done") - 1)

        def t_s(q, fo=fo, ps=ps):
            mm = None
            for fi in range(16):
                for half in range(2):
                    mm = q.matmul(
                        ps[:, half * 512:half * 512 + 512],
                        t.wo_sb[:, fi * DM + fo * 128:fi * DM + fo * 128 + 128],
                        t.val_sb[:, fi * TLOC + half * 512:
                                 fi * TLOC + half * 512 + 512],
                        start=(fi == 0), stop=(fi == 15))
            return mm
        smv = b.do("tensor", t_s, sem="s_mm")
        b.wait("vector", "s_mm", smv)
        b.wait("vector", "x1_st", b.c("x1_st") - 16)

        def v_comb(q, fo=fo, ps=ps):
            q.tensor_tensor(t.sg_sb[:, :], ps[:, :],
                            t.g0_sb[:, fo * TLOC:(fo + 1) * TLOC], op=ALU.mult)
            return q.tensor_add(t.x1_sb[:, (fo % 2) * TLOC:(fo % 2 + 1) * TLOC],
                                t.x_sb[:, fo * TLOC:(fo + 1) * TLOC],
                                t.sg_sb[:, :])
        cbv = b.do("vector", v_comb, sem="sg_done")
        b.wait("gpsimd", "sg_done", cbv)
        b.dma("gpsimd", t.x1_o[fo * 128:(fo + 1) * 128, :],
              t.x1_sb[:, (fo % 2) * TLOC:(fo % 2 + 1) * TLOC], sem="x1_st")
    b.wait("gpsimd", "x1_st", b.c("x1_st"))


_IDENT = np.eye(128, dtype=np.float32).astype(BF)


def _chunked(w, inner=128):
    K, N = w.shape
    return np.ascontiguousarray(
        w.reshape(K // 128, 128, N).transpose(1, 0, 2).reshape(128, -1))


def _gu_layout(w):
    K, N = w.shape
    r = w.reshape(K // 128, 128, N // 128, 128).transpose(1, 2, 0, 3)
    return np.ascontiguousarray(r.reshape(128, -1))


def _down_layout(w, n_fog, fo_w):
    H, N = w.shape
    r = w.reshape(H // 128, 128, n_fog, fo_w).transpose(1, 2, 0, 3)
    return np.ascontiguousarray(r.reshape(128, -1))


def host_inputs_A(inputs):
    x = np.ascontiguousarray(inputs["L0_attn_x"].reshape(S, DM))
    wqkv = inputs["L0_attn_w_qkv"]
    gw = _gu_layout(inputs["L0_ffn_gate_w"].transpose(1, 0, 2).reshape(DM, 8192))
    uw = _gu_layout(inputs["L0_ffn_up_w"].transpose(1, 0, 2).reshape(DM, 8192))
    gwu = np.ascontiguousarray(np.concatenate(
        [gw.reshape(128, 64, 2048), uw.reshape(128, 64, 2048)],
        axis=2).reshape(128, 64 * 4096))
    dwn = _down_layout(inputs["L0_ffn_down_w"].reshape(8192, DM), 8, 256)
    rw = _chunked(np.ascontiguousarray(inputs["L0_ffn_router_w"].T))
    nw = np.ascontiguousarray(inputs["L0_ffn_norm_w"].reshape(16, 128).T)
    maps = []
    for c in range(NCORES):
        sl = np.concatenate([
            wqkv[:, 256 * c:256 * (c + 1)],
            wqkv[:, 2048 + 256 * c:2048 + 256 * (c + 1)],
            wqkv[:, 4096 + 256 * c:4096 + 256 * (c + 1)]], axis=1)
        maps.append({
            "x": x, "xloc": x[TLOC * c:TLOC * (c + 1)],
            "wqkv": _chunked(np.ascontiguousarray(sl)), "rw": rw, "nw": nw,
            "gwu": gwu, "dwn": dwn, "ident_c": _IDENT})
    return maps


def host_inputs_B(inputs, val_parts, g0_parts, wo=None):
    x = inputs["L0_attn_x"].reshape(S, DM)
    val_full = np.concatenate(val_parts, axis=0)
    wo_r = _chunked(wo if wo is not None else inputs["L0_attn_w_o"])
    maps = []
    for c in range(NCORES):
        xfm = np.ascontiguousarray(x[TLOC * c:TLOC * (c + 1)].T)
        maps.append({
            "xfm": xfm,
            "val": np.ascontiguousarray(val_full[:, TLOC * c:TLOC * (c + 1)]),
            "g0": g0_parts[c],
            "wo": wo_r})
    return maps





N_HC1 = 88
N_HCS = 44
SCALE1 = float(1.0 / (50.0 * np.sqrt(128.0)))


def make_tensors_C(nc, es):
    t = types.SimpleNamespace()
    t.x = nc.dram_tensor("x", [S, DM], F32, kind="ExternalInput").ap()
    t.xloc = nc.dram_tensor("xloc", [TLOC, DM], F32, kind="ExternalInput").ap()
    t.wqkv = nc.dram_tensor("wqkv", [128, 16 * 512], F32,
                            kind="ExternalInput").ap()
    t.rw = nc.dram_tensor("rw", [128, 128], F32, kind="ExternalInput").ap()
    t.gwu = nc.dram_tensor("gwu", [128, N_HC1 * 4096], F32,
                           kind="ExternalInput").ap()
    t.dwn = nc.dram_tensor("dwn", [128, 8 * N_HC1 * 256], F32,
                           kind="ExternalInput").ap()
    t.sgw_d = nc.dram_tensor("sgw", [128, 16], F32, kind="ExternalInput").ap()
    t.swgu = nc.dram_tensor("swgu", [128, N_HCS * 4096], F32,
                            kind="ExternalInput").ap()
    t.swd = nc.dram_tensor("swd", [128, 8 * N_HCS * 256], F32,
                           kind="ExternalInput").ap()
    t.cos_d = nc.dram_tensor("cos_c", [128, S], F32, kind="ExternalInput").ap()
    t.sin_d = nc.dram_tensor("sin_c", [128, S], F32, kind="ExternalInput").ap()
    t.mask_d = nc.dram_tensor("mask_c", [128, 8 * 1024], BF16,
                              kind="ExternalInput").ap()
    t.ident_c = nc.dram_tensor("ident_c", [128, 128], BF16,
                               kind="ExternalInput").ap()
    t.out1_o = nc.dram_tensor("out1", [256, S], BF16, kind="ExternalOutput").ap()
    t.g1_o = nc.dram_tensor("g1out", [DM, TLOC], BF16, kind="ExternalOutput").ap()
    t.a_dram = nc.dram_tensor("a_stage", [N_HC1 * 128, TLOC], BF16).ap()
    t.sh_dram = nc.dram_tensor("sh_stage", [N_HCS * 128, TLOC], BF16).ap()

    sb = es.enter_context
    t.zeros1 = sb(nc.sbuf_tensor("zeros1", [128, 1], BF16))
    t.ones1 = sb(nc.sbuf_tensor("ones1", [1, 128], BF16))
    t.onescol = sb(nc.sbuf_tensor("onescol", [128, 1], BF16))
    t.ident = sb(nc.sbuf_tensor("ident", [128, 128], BF16))
    t.stats = sb(nc.sbuf_tensor("stats", [128, 8], F32))
    t.rms_sumsq = sb(nc.sbuf_tensor("rms_sumsq", [128, 8], F32))
    t.xq = sb(nc.sbuf_tensor("xq", [128, 2 * S], BF16))
    t.xk = sb(nc.sbuf_tensor("xk", [128, S], BF16))
    t.xv = sb(nc.sbuf_tensor("xv", [128, S], BF16))
    t.sgb = sb(nc.sbuf_tensor("sgb", [128, TLOC], BF16))

    with contextlib.ExitStack() as p1:
        t.sq_scr = p1.enter_context(nc.sbuf_tensor("sq_scr", [128, DM], BF16))
        t.wqkv_sb = p1.enter_context(
            nc.sbuf_tensor("wqkv_sb", [128, 16 * 512], BF16))
        t.x_tm2 = p1.enter_context(nc.sbuf_tensor("x_tm2", [128, 2 * DM], F32))
        t.xn_tm2 = p1.enter_context(
            nc.sbuf_tensor("xn_tm2", [128, 2 * DM], BF16))
        t.xn_t = p1.enter_context(nc.sbuf_tensor("xn_t", [128, 16 * 512], BF16))
        t.cossin2 = p1.enter_context(
            nc.sbuf_tensor("cossin2", [128, 4 * 512], F32))
        t.rot_sb = p1.enter_context(nc.sbuf_tensor("rot_sb", [128, 512], F32))
        t.t1_sb = p1.enter_context(nc.sbuf_tensor("t1_sb", [128, 512], F32))
        t.xn_loc = p1.enter_context(
            nc.sbuf_tensor("xn_loc", [128, 16 * TLOC], BF16))
        t.rw_sb = p1.enter_context(nc.sbuf_tensor("rw_sb", [128, 128], BF16))
        t.sgw_sb = p1.enter_context(nc.sbuf_tensor("sgw_sb", [128, 16], BF16))
        t.cwb = p1.enter_context(nc.sbuf_tensor("cwb", [128, 8 * TLOC], BF16))
        with contextlib.ExitStack() as p1r:
            t.sg_row = p1r.enter_context(
                nc.sbuf_tensor("sg_row", [1, TLOC], BF16))
            t.r_sc = p1r.enter_context(nc.sbuf_tensor("r_sc", [128, 64], F32))
            t.lg_sb = p1r.enter_context(nc.sbuf_tensor("lg_sb", [8, TLOC], BF16))
            t.ltm = p1r.enter_context(nc.sbuf_tensor("ltm", [128, 64], F32))
            t.exps = p1r.enter_context(nc.sbuf_tensor("exps", [128, 64], F32))
            t.eqs = p1r.enter_context(nc.sbuf_tensor("eqs", [128, 64], F32))
            t.tts = p1r.enter_context(nc.sbuf_tensor("tts", [128, 64], F32))
            t.cwtm = p1r.enter_context(nc.sbuf_tensor("cwtm", [128, 64], BF16))
            t.cw8 = p1r.enter_context(nc.sbuf_tensor("cw8", [8, TLOC], BF16))
            t.cw_row = p1r.enter_context(
                nc.sbuf_tensor("cw_row", [1, 4 * TLOC], BF16))
        t.gwu2 = p1.enter_context(nc.sbuf_tensor("gwu2", [128, 2 * 4096], BF16))
        t.aw2 = p1.enter_context(nc.sbuf_tensor("aw2", [128, 2 * TLOC], BF16))
        t.ps_tr0 = p1.enter_context(nc.psum_tensor("ps_tr0", [128, 1024], BF16))
        t.ps_tr1 = p1.enter_context(nc.psum_tensor("ps_tr1", [128, 1024], BF16))
        t.ps_qkv = p1.enter_context(nc.psum_tensor("ps_qkv", [128, 512], F32))
        t.ps_v = p1.enter_context(nc.psum_tensor("ps_v", [128, 128], F32))
        with contextlib.ExitStack() as p1p:
            t.ps_lg12 = p1p.enter_context(
                nc.psum_tensor("ps_lg12", [128, 1024], F32))
            t.ps_t8 = p1p.enter_context(
                nc.psum_tensor("ps_t8", [128, 64], BF16))
            t.ps_ct = p1p.enter_context(
                nc.psum_tensor("ps_ct", [8, TLOC], BF16))
            t.ps_lg = t.ps_lg12
            t.ps_bc = None
            t.ps_sg = None
        t.ps_g = p1.enter_context(nc.psum_tensor("ps_g", [128, TLOC], F32))
        t.ps_u = p1.enter_context(nc.psum_tensor("ps_u", [128, TLOC], F32))

    with contextlib.ExitStack() as p2:
        t.mask_sb = p2.enter_context(
            nc.sbuf_tensor("mask_sb", [128, 8 * 1024], BF16))
        t.astr4 = p2.enter_context(
            nc.sbuf_tensor("astr4", [128, 2 * 2 * TLOC], BF16))
        t.dw_sb = p2.enter_context(
            nc.sbuf_tensor("dw_sb", [128, 2 * 20 * 256], BF16))
        t.moe_t = p2.enter_context(nc.sbuf_tensor("moe_t", [128, 2 * TLOC], BF16))
        t.cmb1 = p2.enter_context(nc.sbuf_tensor("cmb1", [128, TLOC], F32))
        t.cmb2 = p2.enter_context(nc.sbuf_tensor("cmb2", [128, TLOC], F32))
        t.g0t = p2.enter_context(nc.sbuf_tensor("g0t", [128, 2 * TLOC], BF16))
        t.E2 = p2.enter_context(nc.sbuf_tensor("E2c", [128, 2 * 1024], BF16))
        t.tn_sb = p2.enter_context(nc.sbuf_tensor("tn_sb", [128, 1024], BF16))
        t.S_acc = p2.enter_context(nc.sbuf_tensor("S_acc", [128, 1024], F32))
        t.S_bf = p2.enter_context(nc.sbuf_tensor("S_bf", [128, 1024], BF16))
        t.zr_f = p2.enter_context(nc.sbuf_tensor("zr_f", [1, 1024], F32))
        t.zr_f2 = p2.enter_context(nc.sbuf_tensor("zr_f2", [1, 1024], F32))
        t.zr_b = p2.enter_context(nc.sbuf_tensor("zr_b", [1, 1024], BF16))
        t.o_raw = p2.enter_context(nc.sbuf_tensor("o_raw", [128, 1024], F32))
        t.o_bf = p2.enter_context(nc.sbuf_tensor("o_bf", [128, S], BF16))
        t.ps_y = [p2.enter_context(
            nc.psum_tensor(f"ps_y{i}", [128, TLOC], F32)) for i in range(2)]
        t.ps_sc = p2.enter_context(nc.psum_tensor("ps_sc", [128, 1024], F32))
        t.ps_o = p2.enter_context(nc.psum_tensor("ps_o", [128, 1024], F32))
    return t


def gen_C1(b, t):
    def on_group(g, lnv, xsl):
        tile, gi = g // 4, g % 4
        pa, pb, tpv = emit_transposes(b, t, lnv, xsl, g)
        b.wait("vector", "tp_done", tpv)
        if gi == 0:
            b.wait("vector", "qm_done", 3 * tile)
            b.wait("vector", "v_mm", 4 * tile)

        def v_evac(q, gi=gi, pa=pa, pb=pb):
            d3 = t.xn_t.ap().rearrange("p (di tt) -> p di tt", di=16)
            q.tensor_copy(d3[:, 0:8, gi * 128:gi * 128 + 128],
                          pa.ap().rearrange("p (di tt) -> p di tt", di=8))
            return q.tensor_copy(
                d3[:, 8:16, gi * 128:gi * 128 + 128],
                pb.ap().rearrange("p (di tt) -> p di tt", di=8))
        b.do("vector", v_evac, sem="xp_evac")

        if gi == 3:
            b.wait("tensor", "xp_evac", b.c("xp_evac"))
            b.wait("sync", "rope_done", b.c("rope_done") - 3)
            csv = b.dma("sync", t.cossin2[:, (tile % 2) * 1024:
                                          (tile % 2) * 1024 + 512],
                        t.cos_d[:, tile * 512:(tile + 1) * 512], sem="ld_cs")
            csv = b.dma("sync", t.cossin2[:, (tile % 2) * 1024 + 512:
                                          (tile % 2) * 1024 + 1024],
                        t.sin_d[:, tile * 512:(tile + 1) * 512], sem="ld_cs")
            for oc in range(3):
                b.wait("tensor", "rope_done", b.c("rope_done"))

                def t_oc(q, oc=oc):
                    mm = None
                    for di in range(16):
                        mm = q.matmul(
                            t.ps_qkv[:, :],
                            t.wqkv_sb[:, di * 512 + oc * 128:
                                      di * 512 + oc * 128 + 128],
                            t.xn_t[:, di * 512:(di + 1) * 512],
                            start=(di == 0), stop=(di == 15))
                    return mm
                qmv = b.do("tensor", t_oc, sem="qm_done")
                b.wait("vector", "qm_done", qmv)
                b.wait("vector", "ld_cs", csv)
                dest = t.xq if oc < 2 else t.xk
                doff = oc * S if oc < 2 else 0
                cs0 = (tile % 2) * 1024

                def v_rope(q, dest=dest, doff=doff, tile=tile, cs0=cs0):
                    q.tensor_scalar_mul(t.rot_sb[0:64, :],
                                        t.ps_qkv[64:128, :], -1.0)
                    q.tensor_copy(t.rot_sb[64:128, :], t.ps_qkv[0:64, :])
                    q.tensor_tensor(t.t1_sb[:, :], t.ps_qkv[:, :],
                                    t.cossin2[:, cs0:cs0 + 512], op=ALU.mult)
                    q.tensor_tensor(t.rot_sb[:, :], t.rot_sb[:, :],
                                    t.cossin2[:, cs0 + 512:cs0 + 1024],
                                    op=ALU.mult)
                    return q.tensor_add(
                        dest[:, doff + tile * 512:doff + tile * 512 + 512],
                        t.t1_sb[:, :], t.rot_sb[:, :])
                b.do("vector", v_rope, sem="rope_done")
            for sub in range(4):
                b.wait("tensor", "v_evac", b.c("v_evac"))

                def t_v(q, sub=sub):
                    mm = None
                    for di in range(16):
                        mm = q.matmul(
                            t.ps_v[:, :],
                            t.xn_t[:, di * 512 + sub * 128:
                                   di * 512 + sub * 128 + 128],
                            t.wqkv_sb[:, di * 512 + 384:di * 512 + 512],
                            start=(di == 0), stop=(di == 15))
                    return mm
                vmv = b.do("tensor", t_v, sem="v_mm")
                b.wait("scalar", "v_mm", vmv)
                kc = tile * 4 + sub
                b.do("scalar", lambda q, kc=kc: q.copy(
                    t.xv[:, kc * 128:kc * 128 + 128], t.ps_v[:, :]),
                    sem="v_evac")

    yield from gen_ln_stream(b, t, t.x, 64, on_group)


def gen_gu(b, t):
    yield from gen_expert_gu(b, t, t.gwu, t.a_dram, N_HC1, 11)
    yield from gen_expert_gu(b, t, t.swgu, t.sh_dram, N_HCS, None)


def gen_expert_gu(b, t, gwu_d, a_dram, n_hc, hc_per_exp):
    for hc in range(n_hc):
        wbuf = b.c("gu_mm") % 2
        b.wait("gpsimd", "gu_mm", b.c("gu_mm") - 1)
        gl = b.dma("gpsimd", t.gwu2[:, wbuf * 4096:(wbuf + 1) * 4096],
                   gwu_d[:, hc * 4096:(hc + 1) * 4096], sem="ld_w")
        b.wait("tensor", "ld_w", gl)
        b.wait("tensor", "silu_done", b.c("silu_done"))

        def t_g(q, wbuf=wbuf):
            mm = None
            for di in range(16):
                for half in range(2):
                    mm = q.matmul(
                        t.ps_g[:, half * 512:half * 512 + 512],
                        t.gwu2[:, wbuf * 4096 + di * 128:
                               wbuf * 4096 + di * 128 + 128],
                        t.xn_loc[:, di * TLOC + half * 512:
                                 di * TLOC + half * 512 + 512],
                        start=(di == 0), stop=(di == 15))
            return mm
        gmv = b.do("tensor", t_g, sem="g_mm")
        b.wait("tensor", "aprod_done", b.c("aprod_done"))

        def t_u(q, wbuf=wbuf):
            mm = None
            for di in range(16):
                for half in range(2):
                    mm = q.matmul(
                        t.ps_u[:, half * 512:half * 512 + 512],
                        t.gwu2[:, wbuf * 4096 + 2048 + di * 128:
                               wbuf * 4096 + 2048 + di * 128 + 128],
                        t.xn_loc[:, di * TLOC + half * 512:
                                 di * TLOC + half * 512 + 512],
                        start=(di == 0), stop=(di == 15))
            return mm
        umv = b.do("tensor", t_u, sem="gu_mm")
        abuf = b.c("a_st") // 16 % 2
        b.wait("scalar", "g_mm", gmv)
        b.wait("scalar", "a_st", b.c("a_st") - 16)
        slv = b.do("scalar", lambda q, abuf=abuf: q.activation(
            out=t.aw2[:, abuf * TLOC:(abuf + 1) * TLOC], in_=t.ps_g[:, :],
            func=AF.Silu), sem="silu_done")
        b.wait("vector", "gu_mm", umv)
        b.wait("vector", "silu_done", slv)

        if hc_per_exp:
            e = hc // hc_per_exp

            def v_a(q, e=e, abuf=abuf):
                q.tensor_tensor(t.aw2[:, abuf * TLOC:(abuf + 1) * TLOC],
                                t.aw2[:, abuf * TLOC:(abuf + 1) * TLOC],
                                t.ps_u[:, :], op=ALU.mult)
                return q.tensor_tensor(
                    t.aw2[:, abuf * TLOC:(abuf + 1) * TLOC],
                    t.aw2[:, abuf * TLOC:(abuf + 1) * TLOC],
                    t.cwb[:, e * TLOC:(e + 1) * TLOC], op=ALU.mult)
        else:
            def v_a(q, abuf=abuf):
                return q.tensor_tensor(
                    t.aw2[:, abuf * TLOC:(abuf + 1) * TLOC],
                    t.aw2[:, abuf * TLOC:(abuf + 1) * TLOC],
                    t.ps_u[:, :], op=ALU.mult)
        apv = b.do("vector", v_a, sem="aprod_done")
        b.wait("gpsimd", "aprod_done", apv)
        b.dma("gpsimd", a_dram[hc * 128:(hc + 1) * 128, :],
              t.aw2[:, abuf * TLOC:(abuf + 1) * TLOC], sem="a_st")
        yield


def gen_down_C(b, t):
    for fog in range(8):
        b.wait("tensor", "y_silu", b.c("y_silu"))
        yield from gen_down_accum(
            b, t, t.dwn, t.a_dram, N_HC1, fog, 20, 2, 256)
        b.wait("scalar", "dn_pair", b.c("dn_pair"))
        b.wait("scalar", "ffn_comb", b.c("ffn_comb"))
        for fo_in in range(2):
            b.do("scalar", lambda q, fo_in=fo_in: q.copy(
                t.moe_t[:, fo_in * TLOC:(fo_in + 1) * TLOC],
                t.ps_y[fo_in][:, :]), sem="moe_ev")
        b.wait("tensor", "moe_ev", b.c("moe_ev"))
        yield from gen_down_accum(
            b, t, t.swd, t.sh_dram, N_HCS, fog, 20, 2, 256)
        b.wait("vector", "dn_pair", b.c("dn_pair"))
        b.wait("vector", "moe_ev", b.c("moe_ev"))
        for fo_in in range(2):
            fo = fog * 2 + fo_in
            obuf = fo % 2
            b.wait("vector", "g1_st", b.c("g1_st") - 16)
            b.wait("vector", "y_silu", b.c("y_silu"))

            def v_comb(q, fo_in=fo_in):
                q.tensor_tensor(t.cmb1[:, :], t.ps_y[fo_in][:, :],
                                t.sgb[:, :], op=ALU.mult)
                return q.tensor_add(
                    t.cmb2[:, :],
                    t.moe_t[:, fo_in * TLOC:(fo_in + 1) * TLOC],
                    t.cmb1[:, :])
            cbv = b.do("vector", v_comb, sem="ffn_comb")
            b.wait("scalar", "ffn_comb", cbv)
            ysv = b.do("scalar", lambda q, obuf=obuf: q.activation(
                out=t.g0t[:, obuf * TLOC:(obuf + 1) * TLOC],
                in_=t.cmb2[:, :], func=AF.Silu), sem="y_silu")
            b.wait("gpsimd", "y_silu", ysv)
            b.dma("gpsimd", t.g1_o[fo * 128:(fo + 1) * 128, :],
                  t.g0t[:, obuf * TLOC:(obuf + 1) * TLOC], sem="g1_st")
        yield


def gen_down_accum(b, t, dwn_d, a_dram, n_hc, fog, hc_chunk, n_fo, fo_w):
    for cs in range(0, n_hc, hc_chunk):
        ce = min(cs + hc_chunk, n_hc)
        cbuf = b.c("dn_chunk") % 2
        b.wait("gpsimd", "dn_chunk", b.c("dn_chunk") - 1)
        dl = b.dma(
            "gpsimd",
            t.dw_sb[:, cbuf * hc_chunk * fo_w:
                    (cbuf * hc_chunk + (ce - cs)) * fo_w],
            dwn_d[:, (fog * n_hc + cs) * fo_w:(fog * n_hc + ce) * fo_w],
            sem="ld_dw")
        b.wait("tensor", "ld_dw", dl)
        for hc in range(cs, ce, 2):
            npair = min(2, ce - hc)
            pbuf = b.c("dn_pair") % 2
            b.wait("sync", "dn_pair", b.c("dn_pair") - 1)
            alv = b.dma(
                "sync",
                t.astr4[:, pbuf * 2048:pbuf * 2048 + npair * TLOC].rearrange(
                    "p (hh tt) -> p hh tt", hh=npair),
                a_dram[hc * 128:(hc + npair) * 128, :].rearrange(
                    "(hh p) tt -> p hh tt", p=128), sem="ld_a")
            b.wait("tensor", "ld_a", alv)

            def t_dn(q, hc=hc, cs=cs, npair=npair, pbuf=pbuf, cbuf=cbuf,
                     n_hc=n_hc, hc_chunk=hc_chunk, n_fo=n_fo, fo_w=fo_w):
                mm = None
                for ki in range(npair):
                    ci = hc + ki - cs
                    for fo_in in range(n_fo):
                        for half in range(2):
                            mm = q.matmul(
                                t.ps_y[fo_in][:, half * 512:half * 512 + 512],
                                t.dw_sb[:, (cbuf * hc_chunk + ci) * fo_w +
                                        fo_in * 128:
                                        (cbuf * hc_chunk + ci) * fo_w +
                                        fo_in * 128 + 128],
                                t.astr4[:, pbuf * 2048 + ki * TLOC + half * 512:
                                        pbuf * 2048 + ki * TLOC + half * 512
                                        + 512],
                                start=(hc + ki == 0), stop=(hc + ki == n_hc - 1))
                return mm
            b.do("tensor", t_dn, sem="dn_pair")
            yield
        b.do("tensor", lambda q: q.nop(), sem="dn_chunk")


def gen_C3(b, t):
    for h in range(2):
        for qt in range(8):
            nkc = 8 * (qt + 1)
            for kc in range(nkc):
                ebuf = kc % 2
                b.wait("tensor", "tanh_done", b.c("tanh_done"))

                def t_sc(q, h=h, qt=qt, kc=kc):
                    mm = None
                    for half in range(2):
                        mm = q.matmul(
                            t.ps_sc[:, half * 512:half * 512 + 512],
                            t.xk[:, kc * 128:kc * 128 + 128],
                            t.xq[:, h * S + qt * 1024 + half * 512:
                                 h * S + qt * 1024 + half * 512 + 512],
                            start=True, stop=True)
                    return mm
                sdv = b.do("tensor", t_sc, sem="sc_done")
                b.wait("scalar", "sc_done", sdv)
                tnv = b.do("scalar", lambda q: q.activation(
                    out=t.tn_sb[:, :], in_=t.ps_sc[:, :], func=AF.Tanh,
                    scale=f32(SCALE1)), sem="tanh_done")
                b.wait("scalar", "mask_done", b.c("mask_done"))
                b.wait("scalar", "sacc_done", b.c("sacc_done") - 1)
                b.wait("scalar", "pv_done", b.c("pv_done") - 1)
                exv = b.do("scalar", lambda q, ebuf=ebuf: q.activation(
                    out=t.E2[:, ebuf * 1024:ebuf * 1024 + 1024],
                    in_=t.tn_sb[:, :], func=AF.Exp, scale=f32(50.0)),
                    sem="exp_done")
                b.wait("vector", "exp_done", exv)
                if kc >= 8 * qt:
                    o = kc - 8 * qt
                    b.do("vector", lambda v, ebuf=ebuf, o=o: v.tensor_tensor(
                        t.E2[:, ebuf * 1024:ebuf * 1024 + 1024],
                        t.E2[:, ebuf * 1024:ebuf * 1024 + 1024],
                        t.mask_sb[:, o * 1024:o * 1024 + 1024], op=ALU.mult),
                        sem="mask_done")
                b.wait("vector", "mask_done", b.c("mask_done"))
                if kc == 0:
                    sav = b.do("vector", lambda v, ebuf=ebuf: v.tensor_copy(
                        t.S_acc[:, :], t.E2[:, ebuf * 1024:ebuf * 1024 + 1024]),
                        sem="sacc_done")
                else:
                    sav = b.do("vector", lambda v, ebuf=ebuf: v.tensor_add(
                        t.S_acc[:, :], t.S_acc[:, :],
                        t.E2[:, ebuf * 1024:ebuf * 1024 + 1024]),
                        sem="sacc_done")
                b.wait("tensor", "mask_done", b.c("mask_done"))
                b.wait("tensor", "exp_done", exv)

                def t_pv(q, kc=kc, ebuf=ebuf, nkc=nkc):
                    mm = None
                    for half in range(2):
                        mm = q.matmul(
                            t.ps_o[:, half * 512:half * 512 + 512],
                            t.xv[:, kc * 128:kc * 128 + 128],
                            t.E2[:, ebuf * 1024 + half * 512:
                                 ebuf * 1024 + half * 512 + 512],
                            start=(kc == 0), stop=(kc == nkc - 1))
                    return mm
                b.do("tensor", t_pv, sem="pv_done")
                yield
            b.wait("vector", "sacc_done", b.c("sacc_done"))
            b.wait("vector", "z_mm", b.c("z_mm"))
            sbv = b.do("vector", lambda v: v.tensor_copy(t.S_bf[:, :],
                                                         t.S_acc[:, :]),
                       sem="sbf_done")
            b.wait("tensor", "sbf_done", sbv)
            b.wait("tensor", "tanh_done", b.c("tanh_done"))

            def t_z(q):
                q.matmul(t.ps_sc[0:1, 0:512], t.onescol[:, :], t.S_bf[:, 0:512],
                         start=True, stop=True)
                return q.matmul(t.ps_sc[0:1, 512:1024], t.onescol[:, :],
                                t.S_bf[:, 512:1024], start=True, stop=True)
            zmv = b.do("tensor", t_z, sem="z_mm")
            b.wait("scalar", "z_mm", zmv)
            b.wait("scalar", "zb_cast", b.c("zb_cast"))
            zev = b.do("scalar", lambda q: q.copy(t.zr_f[:, :],
                                                  t.ps_sc[0:1, :]),
                       sem="zev_done")
            b.wait("vector", "zev_done", zev)
            b.wait("vector", "zb_mm", b.c("zb_mm"))
            b.fdo("vector", lambda v: v.reciprocal(t.zr_f2[:, :], t.zr_f[:, :]))
            zbv = b.do("vector", lambda v: v.tensor_copy(t.zr_b[:, :],
                                                         t.zr_f2[:, :]),
                       sem="zb_cast")
            b.wait("scalar", "pv_done", b.c("pv_done"))
            b.wait("scalar", "o_sc", b.c("o_sc"))
            oev = b.do("scalar", lambda q: q.copy(t.o_raw[:, :], t.ps_o[:, :]),
                       sem="oev_done")
            b.wait("tensor", "zb_cast", zbv)
            b.wait("tensor", "zev_done", zev)
            for half in range(2):
                bcv = b.do("tensor", lambda q, half=half: q.matmul(
                    t.ps_sc[:, half * 512:half * 512 + 512], t.ones1[:, :],
                    t.zr_b[0:1, half * 512:half * 512 + 512],
                    start=True, stop=True), sem="zb_mm")
            b.wait("vector", "zb_mm", b.c("zb_mm"))
            b.wait("vector", "oev_done", oev)

            def v_osc(q, qt=qt):
                q.tensor_tensor(t.o_bf[:, qt * 1024:qt * 1024 + 512],
                                t.o_raw[:, 0:512], t.ps_sc[:, 0:512],
                                op=ALU.mult)
                return q.tensor_tensor(
                    t.o_bf[:, qt * 1024 + 512:qt * 1024 + 1024],
                    t.o_raw[:, 512:1024], t.ps_sc[:, 512:1024], op=ALU.mult)
            osv = b.do("vector", v_osc, sem="o_sc")
            b.wait("tensor", "o_sc", osv)
            b.wait("tensor", "oev_done", b.c("oev_done"))
            yield
        b.wait("gpsimd", "o_sc", b.c("o_sc"))
        b.wait("vector", "o_st", b.c("o_st"))
        b.dma("gpsimd", t.out1_o[h * 128:(h + 1) * 128, :], t.o_bf[:, :],
              sem="o_st")
    b.wait("gpsimd", "o_st", b.c("o_st"))


def emit_C(b, t):
    emit_consts(b, t)
    b.do("vector", lambda v: v.memset(t.onescol[:, :], 1.0), sem="const")
    wv = b.dma("gpsimd", t.wqkv_sb[:, :], t.wqkv)
    b.wait("tensor", "ld_gpsimd", wv)
    b.wait("tensor", "ld_const", b.c("ld_const"))

    def on_group_loc(g, lnv, xsl):
        pa, pb, tpv = emit_transposes(b, t, lnv, xsl, g)
        b.wait("vector", "tp_done", tpv)

        def v_evac(q, g=g, pa=pa, pb=pb):
            d3 = t.xn_loc.ap().rearrange("p (di tt) -> p di tt", di=16)
            q.tensor_copy(d3[:, 0:8, g * 128:g * 128 + 128],
                          pa.ap().rearrange("p (di tt) -> p di tt", di=8))
            return q.tensor_copy(
                d3[:, 8:16, g * 128:g * 128 + 128],
                pb.ap().rearrange("p (di tt) -> p di tt", di=8))
        b.do("vector", v_evac, sem="xp_evac")

    emit_ln_stream(b, t, t.xloc, 8, on_group_loc)

    emit_router_top2(b, t, n_exp=8, use_rms=False, scale_out=1.0)
    sgl = b.dma("gpsimd", t.sgw_sb[:, :], t.sgw_d, sem="ld_sgw")
    b.wait("tensor", "ld_sgw", sgl)
    b.wait("tensor", "cwb_ev", b.c("cwb_ev"))
    for di in range(16):
        def t_sg(q, di=di):
            mm = None
            for half in range(2):
                mm = q.matmul(t.ps_lg12[0:1, half * 512:half * 512 + 512],
                              t.sgw_sb[:, di:di + 1],
                              t.xn_loc[:, di * TLOC + half * 512:
                                       di * TLOC + half * 512 + 512],
                              start=(di == 0), stop=(di == 15))
            return mm
        b.do("tensor", t_sg, sem="sg_mm")
    b.wait("scalar", "sg_mm", b.c("sg_mm"))
    sgv = b.do("scalar", lambda q: q.activation(
        out=t.sg_row[:, :], in_=t.ps_lg12[0:1, :], func=AF.Sigmoid),
        sem="sg_act")
    b.wait("tensor", "sg_act", sgv)
    for half in range(2):
        bcv = b.do("tensor", lambda q, half=half: q.matmul(
            t.ps_lg12[:, half * 512:half * 512 + 512], t.ones1[:, :],
            t.sg_row[0:1, half * 512:half * 512 + 512],
            start=True, stop=True), sem="sgb_mm")
        b.wait("vector", "sgb_mm", bcv)
        sgbv = b.do("vector", lambda v, half=half: v.tensor_copy(
            t.sgb[:, half * 512:half * 512 + 512],
            t.ps_lg12[:, half * 512:half * 512 + 512]), sem="sgb_ev")
    b.wait("tensor", "sgb_ev", b.c("sgb_ev"))
    b.wait("gpsimd", "cwb_ev", b.c("cwb_ev"))
    b.wait("gpsimd", "sgb_ev", b.c("sgb_ev"))

    interleave((gen_C1(b, t), 1), (gen_gu(b, t), 2))
    b.barrier()

    mv = b.dma("gpsimd", t.mask_sb[:, :], t.mask_d, sem="ld_mask")
    b.wait("vector", "ld_mask", mv)
    b.wait("sync", "a_st", b.c("a_st"))
    interleave((gen_down_C(b, t), 1), (gen_C3(b, t), 1))
    b.wait("gpsimd", "g1_st", b.c("g1_st"))
    b.wait("gpsimd", "o_st", b.c("o_st"))


def emit_router_top2(b, t, *, n_exp, use_rms, scale_out):
    lw = b.dma("gpsimd", t.rw_sb[:, :], t.rw)
    if use_rms:
        lw = b.dma("gpsimd", t.nw_sb[:, :], t.nw)
    b.wait("vector", "ld_gpsimd", lw)
    b.wait("tensor", "ld_gpsimd", lw)
    for di in range(16):
        if use_rms:
            b.wait("vector", "lg_mm", b.c("lg_mm") - 1)
            xwv = b.do("vector", lambda v, di=di: v.tensor_scalar(
                t.xnw2[:, (di % 2) * TLOC:(di % 2 + 1) * TLOC],
                t.xn_loc[:, di * TLOC:(di + 1) * TLOC],
                t.nw_sb[:, di:di + 1], None, op0=ALU.mult), sem="xnw_done")
            b.wait("tensor", "xnw_done", xwv)
            rhs = (lambda half, di=di:
                   t.xnw2[:, (di % 2) * TLOC + half * 512:
                          (di % 2) * TLOC + half * 512 + 512])
        else:
            rhs = (lambda half, di=di:
                   t.xn_loc[:, di * TLOC + half * 512:
                            di * TLOC + half * 512 + 512])

        def t_lg(q, di=di, rhs=rhs):
            mm = None
            for half in range(2):
                mm = q.matmul(t.ps_lg[0:8, half * 512:half * 512 + 512],
                              t.rw_sb[:, di * 8:di * 8 + 8],
                              rhs(half),
                              start=(di == 0), stop=(di == 15))
            return mm
        b.do("tensor", t_lg, sem="lg_mm")
    b.wait("scalar", "lg_mm", b.c("lg_mm"))
    lev = b.do("scalar", lambda q: q.copy(t.lg_sb[:, :], t.ps_lg[0:8, :]),
               sem="lg_evac")
    b.wait("tensor", "lg_evac", lev)

    def t_ltr(q):
        mm = None
        for lg in range(8):
            mm = q.transpose(t.ps_t8[:, lg * 8:lg * 8 + 8],
                             t.lg_sb[:, lg * 128:lg * 128 + 128],
                             t.ident[0:8, 0:8])
        return mm
    ltv = b.do("tensor", t_ltr, sem="lg_tr")
    b.wait("vector", "lg_tr", ltv)
    ltev = b.do("vector", lambda v: v.tensor_copy(t.ltm[:, :], t.ps_t8[:, :]),
                sem="lg_tr_ev")
    b.wait("vector", "lg_tr_ev", ltev)
    if use_rms:
        b.wait("vector", "rms_done", b.c("rms_done"))
    for lg in range(8):
        col = (lambda j, lg=lg: t.r_sc[:, 8 * lg + j:8 * lg + j + 1])
        l8 = (lambda lg=lg: t.ltm[:, lg * 8:lg * 8 + 8])
        x8 = (lambda lg=lg: t.exps[:, lg * 8:lg * 8 + 8])
        e8 = (lambda lg=lg: t.eqs[:, lg * 8:lg * 8 + 8])
        t8 = (lambda lg=lg: t.tts[:, lg * 8:lg * 8 + 8])
        c8 = (lambda lg=lg: t.cwtm[:, lg * 8:lg * 8 + 8])
        if use_rms:
            rsv = b.do("vector", lambda v, col=col, lg=lg: v.tensor_scalar(
                col(0), t.rms_sumsq[:, lg:lg + 1], 1.0 / DM, 1e-6,
                op0=ALU.mult, op1=ALU.add), sem="rr1")
            b.wait("scalar", "rr1", rsv)
            rs2 = b.do("scalar", lambda q, col=col: q.sqrt(col(1), col(0)),
                       sem="rr2")
            b.wait("vector", "rr2", rs2)
            b.fdo("vector", lambda v, col=col: v.reciprocal(col(2), col(1)))
            b.fdo("vector", lambda v, col=col, l8=l8: v.tensor_scalar(
                l8(), l8(), col(2), None, op0=ALU.mult))
        b.fdo("vector", lambda v, col=col, l8=l8: v.tensor_reduce(
            col(3), l8(), axis=AX.X, op=ALU.max))
        mxv = b.do("vector", lambda v, col=col: v.tensor_scalar_mul(
            col(3), col(3), -1.0), sem="tq1")
        b.wait("scalar", "tq1", mxv)
        e8v = b.do("scalar", lambda q, col=col, l8=l8, x8=x8: q.activation(
            out=x8(), in_=l8(), func=AF.Exp, bias=col(3)), sem="tq2")
        b.wait("vector", "tq2", e8v)
        for step in [
            lambda v, col=col, x8=x8: v.tensor_reduce(
                col(4), x8(), axis=AX.X, op=ALU.max),
            lambda v, col=col, x8=x8, e8=e8: v.tensor_scalar(
                e8(), x8(), col(4), None, op0=ALU.is_ge),
            lambda v, x8=x8, e8=e8, t8=t8: v.tensor_tensor(
                t8(), x8(), e8(), op=ALU.mult),
            lambda v, x8=x8, t8=t8: v.tensor_sub(t8(), x8(), t8()),
            lambda v, col=col, t8=t8: v.tensor_reduce(
                col(5), t8(), axis=AX.X, op=ALU.max),
            lambda v, col=col, x8=x8, e8=e8: v.tensor_scalar(
                e8(), x8(), col(5), None, op0=ALU.is_ge),
            lambda v, col=col: v.tensor_add(col(6), col(4), col(5)),
            lambda v, col=col: v.reciprocal(col(7), col(6)),
            lambda v, x8=x8, e8=e8, t8=t8: v.tensor_tensor(
                t8(), x8(), e8(), op=ALU.mult),
        ]:
            b.fdo("vector", step)
        ctv = b.do("vector", lambda v, col=col, t8=t8, c8=c8: v.tensor_scalar(
            c8(), t8(), col(7), float(scale_out), op0=ALU.mult, op1=ALU.mult),
            sem="cw_tm_done")
        b.wait("tensor", "cw_tm_done", ctv)
        b.do("tensor", lambda q, lg=lg, c8=c8: q.transpose(
            t.ps_ct[:, lg * 128:lg * 128 + 128], c8(), t.ident[:, :]),
            sem="cw_tr")
    b.wait("vector", "cw_tr", b.c("cw_tr"))
    cfev = b.do("vector", lambda v: v.tensor_copy(t.cw8[:, :], t.ps_ct[:, :]),
                sem="cw_fm_ev")
    b.wait("gpsimd", "cw_fm_ev", cfev)
    bct = t.ps_bc if getattr(t, "ps_bc", None) is not None else t.ps_lg12
    for eh in range(2):
        b.wait("gpsimd", "cwb_mm", b.c("cwb_mm"))
        crv = None
        for ei in range(4):
            e = eh * 4 + ei
            crv = b.dma("gpsimd", t.cw_row[0:1, ei * TLOC:(ei + 1) * TLOC],
                        t.cw8[e:e + 1, :], sem="ld_cwrow")
        b.wait("tensor", "ld_cwrow", crv)
        for ei in range(4):
            e = eh * 4 + ei
            if e >= n_exp:
                continue
            for half in range(2):
                b.wait("tensor", "cwb_ev", b.c("cwb_ev"))
                bcv = b.do("tensor", lambda q, ei=ei, half=half: q.matmul(
                    bct[:, 0:512], t.ones1[:, :],
                    t.cw_row[0:1, ei * TLOC + half * 512:
                             ei * TLOC + half * 512 + 512],
                    start=True, stop=True), sem="cwb_mm")
                b.wait("vector", "cwb_mm", bcv)
                b.do("vector", lambda v, e=e, half=half: v.tensor_copy(
                    t.cwb[:, e * TLOC + half * 512:e * TLOC + half * 512 + 512],
                    bct[:, 0:512]), sem="cwb_ev")


def emit_expert_gu(b, t, gw_d, uw_d, a_dram, n_hc, hc_per_exp):
    for hc in range(n_hc):
        wbuf = b.c("gu_mm") % 2
        b.wait("gpsimd", "gu_mm", b.c("gu_mm") - 1)
        b.wait("vector", "gu_mm", b.c("gu_mm") - 1)
        b.dma("gpsimd", t.gwu2[:, wbuf * 4096:wbuf * 4096 + 2048],
              gw_d[:, hc * 2048:(hc + 1) * 2048], sem="ld_w")
        gl = b.dma("vector", t.gwu2[:, wbuf * 4096 + 2048:wbuf * 4096 + 4096],
                   uw_d[:, hc * 2048:(hc + 1) * 2048], sem="ld_w2")
        b.wait("tensor", "ld_w", b.c("ld_w"))
        b.wait("tensor", "ld_w2", gl)
        psg = t.ps_gA if b.c("gu_mm") % 2 == 0 else t.ps_gB
        psu = t.ps_uA if b.c("gu_mm") % 2 == 0 else t.ps_uB
        b.wait("tensor", "silu_done", b.c("silu_done") - 1)

        def t_g(q, wbuf=wbuf, psg=psg):
            mm = None
            for di in range(16):
                for half in range(2):
                    mm = q.matmul(
                        psg[:, half * 512:half * 512 + 512],
                        t.gwu2[:, wbuf * 4096 + di * 128:
                               wbuf * 4096 + di * 128 + 128],
                        t.xn_loc[:, di * TLOC + half * 512:
                                 di * TLOC + half * 512 + 512],
                        start=(di == 0), stop=(di == 15))
            return mm
        gmv = b.do("tensor", t_g, sem="g_mm")
        b.wait("tensor", "aprod_done", b.c("aprod_done") - 1)

        def t_u(q, wbuf=wbuf, psu=psu):
            mm = None
            for di in range(16):
                for half in range(2):
                    mm = q.matmul(
                        psu[:, half * 512:half * 512 + 512],
                        t.gwu2[:, wbuf * 4096 + 2048 + di * 128:
                               wbuf * 4096 + 2048 + di * 128 + 128],
                        t.xn_loc[:, di * TLOC + half * 512:
                                 di * TLOC + half * 512 + 512],
                        start=(di == 0), stop=(di == 15))
            return mm
        umv = b.do("tensor", t_u, sem="gu_mm")
        abuf = b.c("a_st") // 16 % 2
        b.wait("scalar", "g_mm", gmv)
        b.wait("scalar", "a_st", b.c("a_st") - 16)
        slv = b.do("scalar", lambda q, psg=psg, abuf=abuf: q.activation(
            out=t.aw2[:, abuf * TLOC:(abuf + 1) * TLOC], in_=psg[:, :],
            func=AF.Silu), sem="silu_done")
        b.wait("vector", "gu_mm", umv)
        b.wait("vector", "silu_done", slv)

        if hc_per_exp:
            e = hc // hc_per_exp

            def v_a(q, e=e, abuf=abuf, psu=psu):
                q.tensor_tensor(t.aw2[:, abuf * TLOC:(abuf + 1) * TLOC],
                                t.aw2[:, abuf * TLOC:(abuf + 1) * TLOC],
                                psu[:, :], op=ALU.mult)
                return q.tensor_tensor(
                    t.aw2[:, abuf * TLOC:(abuf + 1) * TLOC],
                    t.aw2[:, abuf * TLOC:(abuf + 1) * TLOC],
                    t.cwb[:, e * TLOC:(e + 1) * TLOC], op=ALU.mult)
        else:
            def v_a(q, abuf=abuf, psu=psu):
                return q.tensor_tensor(
                    t.aw2[:, abuf * TLOC:(abuf + 1) * TLOC],
                    t.aw2[:, abuf * TLOC:(abuf + 1) * TLOC],
                    psu[:, :], op=ALU.mult)
        apv = b.do("vector", v_a, sem="aprod_done")
        b.wait("gpsimd", "aprod_done", apv)
        b.dma("gpsimd", a_dram[hc * 128:(hc + 1) * 128, :],
              t.aw2[:, abuf * TLOC:(abuf + 1) * TLOC], sem="a_st")


def emit_down_accum(b, t, dwn_d, a_dram, n_hc, fog, hc_chunk, first):
    for cs in range(0, n_hc, hc_chunk):
        ce = min(cs + hc_chunk, n_hc)
        cbuf = b.c("dn_chunk") % 2
        b.wait("gpsimd", "dn_chunk", b.c("dn_chunk") - 1)
        dl = b.dma(
            "gpsimd",
            t.dw_sb[:, cbuf * hc_chunk * 512:
                    (cbuf * hc_chunk + (ce - cs)) * 512],
            dwn_d[:, (fog * n_hc + cs) * 512:(fog * n_hc + ce) * 512],
            sem="ld_dw")
        b.wait("tensor", "ld_dw", dl)
        for hc in range(cs, ce, 2):
            npair = min(2, ce - hc)
            pbuf = b.c("dn_pair") % 2
            b.wait("sync", "dn_pair", b.c("dn_pair") - 1)
            alv = b.dma(
                "sync",
                t.astr4[:, pbuf * 2048:pbuf * 2048 + npair * TLOC].rearrange(
                    "p (hh tt) -> p hh tt", hh=npair),
                a_dram[hc * 128:(hc + npair) * 128, :].rearrange(
                    "(hh p) tt -> p hh tt", p=128), sem="ld_a")
            b.wait("tensor", "ld_a", alv)

            def t_dn(q, hc=hc, cs=cs, npair=npair, pbuf=pbuf, cbuf=cbuf,
                     n_hc=n_hc, hc_chunk=hc_chunk):
                mm = None
                for ki in range(npair):
                    ci = hc + ki - cs
                    for fo_in in range(4):
                        for half in range(2):
                            mm = q.matmul(
                                t.ps_y[fo_in][:, half * 512:half * 512 + 512],
                                t.dw_sb[:, (cbuf * hc_chunk + ci) * 512 +
                                        fo_in * 128:
                                        (cbuf * hc_chunk + ci) * 512 +
                                        fo_in * 128 + 128],
                                t.astr4[:, pbuf * 2048 + ki * TLOC + half * 512:
                                        pbuf * 2048 + ki * TLOC + half * 512
                                        + 512],
                                start=(hc + ki == 0), stop=(hc + ki == n_hc - 1))
                return mm
            b.do("tensor", t_dn, sem="dn_pair")
        b.do("tensor", lambda q: q.nop(), sem="dn_chunk")


def host_inputs_C(inputs, x1_full):
    wq = inputs["L1_attn_wq"].T
    wk = inputs["L1_attn_wk"].T
    wv = inputs["L1_attn_wv"].T
    gw = _gu_layout(
        inputs["L1_ffn_w_gate"].transpose(1, 0, 2).reshape(DM, N_HC1 * 128))
    uw = _gu_layout(
        inputs["L1_ffn_w_up"].transpose(1, 0, 2).reshape(DM, N_HC1 * 128))
    gwu = np.ascontiguousarray(np.concatenate(
        [gw.reshape(128, N_HC1, 2048), uw.reshape(128, N_HC1, 2048)],
        axis=2).reshape(128, N_HC1 * 4096))
    dwn = _down_layout(inputs["L1_ffn_w_down"].reshape(N_HC1 * 128, DM), 8, 256)
    rw = _chunked(np.ascontiguousarray(inputs["L1_ffn_gate_w"].T))
    sgw = np.ascontiguousarray(
        inputs["L1_ffn_shared_gate_w"].reshape(16, 128).T)
    swg = _gu_layout(inputs["L1_ffn_sw_gate"])
    swu = _gu_layout(inputs["L1_ffn_sw_up"])
    swgu = np.ascontiguousarray(np.concatenate(
        [swg.reshape(128, N_HCS, 2048), swu.reshape(128, N_HCS, 2048)],
        axis=2).reshape(128, N_HCS * 4096))
    swd = _down_layout(inputs["L1_ffn_sw_down"], 8, 256)
    inv = 1.0 / 10000.0 ** (np.arange(0, 128, 2, dtype=np.float32) / 128)
    fpos = np.arange(S, dtype=np.float32)[:, None] * inv[None, :]
    emb = np.concatenate((fpos, fpos), axis=-1)
    cos_c = np.ascontiguousarray(np.cos(emb).astype(np.float32).T)
    sin_c = np.ascontiguousarray(np.sin(emb).astype(np.float32).T)
    kp = np.arange(128)[:, None]
    qp = np.arange(1024)[None, :]
    mask = np.zeros((128, 8 * 1024), np.float32)
    for o in range(8):
        mask[:, o * 1024:(o + 1) * 1024] = (o * 128 + kp <= qp)
    mask_c = mask.astype(BF)
    maps = []
    for c in range(NCORES):
        sl = np.concatenate([
            wq[:, 256 * c:256 * (c + 1)],
            wk[:, 128 * c:128 * (c + 1)],
            wv[:, 128 * c:128 * (c + 1)]], axis=1)
        maps.append({
            "x": x1_full, "xloc": x1_full[TLOC * c:TLOC * (c + 1)],
            "wqkv": _chunked(np.ascontiguousarray(sl)), "rw": rw,
            "gwu": gwu, "dwn": dwn, "sgw": sgw, "swgu": swgu, "swd": swd,
            "cos_c": cos_c, "sin_c": sin_c, "mask_c": mask_c,
            "ident_c": _IDENT})
    return maps





def host_inputs_D(inputs, x1_full, out1_parts, g1_parts):
    out1_full = np.concatenate(out1_parts, axis=0)
    wo1 = _chunked(np.ascontiguousarray(inputs["L1_attn_wo"].T))
    maps = []
    for c in range(NCORES):
        xfm = np.ascontiguousarray(x1_full[TLOC * c:TLOC * (c + 1)].T)
        maps.append({
            "xfm": xfm,
            "val": np.ascontiguousarray(out1_full[:, TLOC * c:TLOC * (c + 1)]),
            "g0": g1_parts[c],
            "wo": wo1})
    return maps


def kernel(**inputs):
    maps_A = host_inputs_A(inputs)
    nc_A = build(make_tensors_A, emit_A)
    res_A = run_bass_kernel_spmd(nc_A, maps_A, core_ids=list(range(NCORES)))
    val_parts = [res_A.results[c]["valout"] for c in range(NCORES)]
    g0_parts = [res_A.results[c]["g0out"] for c in range(NCORES)]

    maps_B = host_inputs_B(inputs, val_parts, g0_parts)
    nc_B = build(make_tensors_B, emit_B)
    res_B = run_bass_kernel_spmd(nc_B, maps_B, core_ids=list(range(NCORES)))
    x1_full = np.concatenate(
        [np.ascontiguousarray(res_B.results[c]["x1out"].T)
         for c in range(NCORES)], axis=0)

    maps_C = host_inputs_C(inputs, x1_full)
    nc_C = build(make_tensors_C, emit_C)
    res_C = run_bass_kernel_spmd(nc_C, maps_C, core_ids=list(range(NCORES)))
    out1_parts = [res_C.results[c]["out1"] for c in range(NCORES)]
    g1_parts = [res_C.results[c]["g1out"] for c in range(NCORES)]

    maps_D = host_inputs_D(inputs, x1_full, out1_parts, g1_parts)
    nc_D = build(make_tensors_B, emit_B)
    res_D = run_bass_kernel_spmd(nc_D, maps_D, core_ids=list(range(NCORES)))
    x2 = np.concatenate(
        [np.ascontiguousarray(res_D.results[c]["x1out"].T)
         for c in range(NCORES)], axis=0)
    return x2.reshape(1, S, DM)



# version 05
# speedup vs baseline: 1.1936x; 1.0054x over previous
import contextlib
import types
import numpy as np
import ml_dtypes
import concourse.bass as bass
import concourse.mybir as mybir
from concourse.bass_utils import run_bass_kernel_spmd

F32 = mybir.dt.float32
BF16 = mybir.dt.bfloat16
AF = mybir.ActivationFunctionType
ALU = mybir.AluOpType
AX = mybir.AxisListType
BF = ml_dtypes.bfloat16
f32 = np.float32

NCORES = 8
S = 8192
DM = 2048
TLOC = S // NCORES
C0_SHIFT = 16.0
ENGINES = ("sync", "gpsimd", "tensor", "vector", "scalar")
INV_SQRT_HD = 1.0 / np.sqrt(128.0)


class B:

    def __init__(self, nc, eng, q, sems):
        self.nc, self.eng, self.q, self.sems = nc, eng, q, sems
        self.cnt = {}
        self.nbar = 0

    def _sem(self, name):
        if name not in self.sems:
            assert self.eng == "__dry__", f"sem {name} not in dry pass"
            self.sems[name] = None

    def c(self, name):
        return self.cnt.get(name, 0)

    def inc(self, name, by=1):
        self._sem(name)
        self.cnt[name] = self.cnt.get(name, 0) + by
        return self.cnt[name]

    def do(self, eng, fn, sem=None, by=1):
        v = self.inc(sem, by) if sem else None
        if self.eng == eng:
            ins = fn(self.q)
            if sem:
                ins.then_inc(self.sems[sem], by)
        return v

    def wait(self, eng, sem, val):
        self._sem(sem)
        if self.eng == eng and val > 0:
            self.q.wait_ge(self.sems[sem], val)

    def fdo(self, eng, fn, sem=None):
        sem = sem or ("f_" + eng)
        v = self.do(eng, fn, sem=sem)
        self.wait(eng, sem, v)
        return v

    def dma(self, eng, out, in_, sem=None, **kw):
        sem = sem or ("ld_" + eng)
        return self.do(eng, lambda q: q.dma_start(out=out, in_=in_, **kw),
                       sem=sem, by=16)

    def barrier(self):
        self.nbar += 1
        self._sem("bar")
        tgt = 5 * self.nbar
        self.cnt["bar"] = tgt
        if self.eng != "__dry__":
            self.q.sem_inc(self.sems["bar"], 1)
            self.q.wait_ge(self.sems["bar"], tgt)


def build(make_tensors, emit):
    nc = bass.Bass("TRN2", target_bir_lowering=False)
    es = contextlib.ExitStack()
    with es:
        t = make_tensors(nc, es)
        sems = {}
        emit(B(nc, "__dry__", None, sems), t)
        for name in list(sems):
            sems[name] = es.enter_context(nc.semaphore(name))
        with nc.Block() as block:
            for eng in ENGINES:
                getattr(block, eng)(
                    lambda q, eng=eng: emit(B(nc, eng, q, sems), t))
    return nc



def emit_consts(b, t):
    b.do("vector", lambda v: v.memset(t.zeros1[:, :], 0.0), sem="const")
    b.do("vector", lambda v: v.memset(t.ones1[:, :], 1.0), sem="const")
    b.dma("gpsimd", t.ident[:, :], t.ident_c, sem="ld_const")


def interleave(*tasks):
    active = list(tasks)
    while active:
        keep = []
        for g, w in active:
            alive = True
            for _ in range(w):
                try:
                    next(g)
                except StopIteration:
                    alive = False
                    break
            if alive:
                keep.append((g, w))
        active = keep


def emit_ln_stream(b, t, x_dram, n_groups, on_group, rms_groups=None):
    for _ in gen_ln_stream(b, t, x_dram, n_groups, on_group, rms_groups):
        pass


def gen_ln_stream(b, t, x_dram, n_groups, on_group, rms_groups=None):
    st = t.stats
    for g in range(n_groups):
        buf = g % 2
        xsl = slice(buf * DM, (buf + 1) * DM)
        b.wait("sync", "ln_done", b.c("ln_done") - 1)
        b.wait("sync", "st_s", b.c("st_s") - 1)
        vld = b.dma("sync", t.x_tm2[:, xsl], x_dram[g * 128:(g + 1) * 128, :])
        b.wait("vector", "ld_sync", vld)
        b.fdo("vector", lambda v, xsl=xsl: v.tensor_reduce(
            st[:, 0:1], t.x_tm2[:, xsl], axis=AX.X, op=ALU.add))
        b.do("vector", lambda v: v.tensor_scalar_mul(
            st[:, 1:2], st[:, 0:1], -1.0 / DM), sem="st_v")
        b.wait("vector", "st_v", b.c("st_v"))
        b.wait("scalar", "ld_sync", vld)
        b.wait("scalar", "st_v", b.c("st_v"))
        sqv = b.do("scalar", lambda q, xsl=xsl: q.activation(
            out=t.sq_scr[:, :], in_=t.x_tm2[:, xsl], func=AF.Square,
            accum_out=st[:, 2:3]), sem="st_s")
        b.wait("vector", "st_s", sqv)
        b.fdo("vector", lambda v: v.tensor_scalar(
            st[:, 3:4], st[:, 2:3], 1.0 / DM, 1e-5, op0=ALU.mult, op1=ALU.add))
        b.fdo("vector", lambda v: v.tensor_tensor(
            st[:, 4:5], st[:, 1:2], st[:, 1:2], op=ALU.mult))
        s2v = b.do("vector", lambda v: v.tensor_sub(
            st[:, 3:4], st[:, 3:4], st[:, 4:5]), sem="st_v2")
        b.wait("scalar", "st_v2", s2v)
        s3v = b.do("scalar", lambda q: q.sqrt(st[:, 4:5], st[:, 3:4]),
                   sem="st_s2")
        b.wait("vector", "st_s2", s3v)
        b.fdo("vector", lambda v: v.reciprocal(st[:, 5:6], st[:, 4:5]))
        b.wait("vector", "tp_done", b.c("tp_done") - 1)
        if rms_groups and (g - 2) in rms_groups:
            b.wait("vector", "rms_done",
                   b.c("rms_done") - (0 if g in rms_groups else 1))
        lnv = b.do("vector", lambda v, xsl=xsl: v.tensor_scalar(
            t.xn_tm2[:, xsl], t.x_tm2[:, xsl], st[:, 1:2], st[:, 5:6],
            op0=ALU.add, op1=ALU.mult), sem="ln_done")
        if rms_groups and g in rms_groups:
            lg = rms_groups.index(g)
            b.wait("scalar", "ln_done", lnv)
            b.do("scalar", lambda q, xsl=xsl, lg=lg: q.activation(
                out=t.sq_scr[:, :], in_=t.xn_tm2[:, xsl], func=AF.Square,
                accum_out=t.rms_sumsq[:, lg:lg + 1]), sem="rms_done")
        on_group(g, lnv, xsl)
        yield


def emit_transposes(b, t, lnv, xsl, g):
    pa = t.ps_tr0 if g % 2 == 0 else t.ps_tr1
    pb = t.ps_tr1 if g % 2 == 0 else t.ps_tr0
    b.wait("tensor", "ln_done", lnv)
    b.wait("tensor", "xp_evac", b.c("xp_evac") - 1)

    def t_tr(q, xsl=xsl, pa=pa, pb=pb):
        mm = None
        for j in range(16):
            dst = pa if j < 8 else pb
            mm = q.transpose(
                dst[:, (j % 8) * 128:(j % 8) * 128 + 128],
                t.xn_tm2[:, xsl.start + j * 128:xsl.start + j * 128 + 128],
                t.ident[:, :])
        return mm
    tpv = b.do("tensor", t_tr, sem="tp_done")
    return pa, pb, tpv



def make_tensors_A(nc, es):
    t = types.SimpleNamespace()
    t.x = nc.dram_tensor("x", [S, DM], F32, kind="ExternalInput").ap()
    t.xloc = nc.dram_tensor("xloc", [TLOC, DM], F32, kind="ExternalInput").ap()
    t.wqkv = nc.dram_tensor("wqkv", [128, 16 * 768], F32,
                            kind="ExternalInput").ap()
    t.rw = nc.dram_tensor("rw", [128, 128], F32, kind="ExternalInput").ap()
    t.nw = nc.dram_tensor("nw", [128, 16], F32, kind="ExternalInput").ap()
    t.gwu = nc.dram_tensor("gwu", [128, 64 * 4096], F32,
                           kind="ExternalInput").ap()
    t.dwn = nc.dram_tensor("dwn", [128, 16 * 64 * 128], F32,
                           kind="ExternalInput").ap()
    t.ident_c = nc.dram_tensor("ident_c", [128, 128], BF16,
                               kind="ExternalInput").ap()
    t.val_o = nc.dram_tensor("valout", [256, S], BF16,
                             kind="ExternalOutput").ap()
    t.g0_o = nc.dram_tensor("g0out", [DM, TLOC], BF16,
                            kind="ExternalOutput").ap()
    t.a_dram = nc.dram_tensor("a_stage", [8192, TLOC], BF16).ap()
    t.q_dram = nc.dram_tensor("q_stage", [256, S], BF16).ap()

    sb = es.enter_context
    t.zeros1 = sb(nc.sbuf_tensor("zeros1", [128, 1], BF16))
    t.ones1 = sb(nc.sbuf_tensor("ones1", [1, 128], BF16))
    t.ident = sb(nc.sbuf_tensor("ident", [128, 128], BF16))
    t.stats = sb(nc.sbuf_tensor("stats", [128, 8], F32))
    t.rms_sumsq = sb(nc.sbuf_tensor("rms_sumsq", [128, 8], F32))
    t.xk = sb(nc.sbuf_tensor("xk", [128, 2 * S], BF16))
    t.xv = sb(nc.sbuf_tensor("xv", [128, 2 * S], BF16))

    with contextlib.ExitStack() as p1:
        t.sq_scr = p1.enter_context(nc.sbuf_tensor("sq_scr", [128, DM], BF16))
        t.wqkv_sb = p1.enter_context(
            nc.sbuf_tensor("wqkv_sb", [128, 16 * 768], BF16))
        t.x_tm2 = p1.enter_context(nc.sbuf_tensor("x_tm2", [128, 2 * DM], F32))
        t.xn_tm2 = p1.enter_context(
            nc.sbuf_tensor("xn_tm2", [128, 2 * DM], BF16))
        t.xn_t = p1.enter_context(nc.sbuf_tensor("xn_t", [128, 16 * 512], BF16))
        t.qstg = p1.enter_context(nc.sbuf_tensor("qstg", [128, 2 * 512], BF16))
        t.xn_loc = p1.enter_context(
            nc.sbuf_tensor("xn_loc", [128, 16 * TLOC], BF16))
        t.rw_sb = p1.enter_context(nc.sbuf_tensor("rw_sb", [128, 128], BF16))
        t.nw_sb = p1.enter_context(nc.sbuf_tensor("nw_sb", [128, 16], F32))
        t.cwb = p1.enter_context(nc.sbuf_tensor("cwb", [128, 8 * TLOC], BF16))
        with contextlib.ExitStack() as p1r:
            t.r_sc = p1r.enter_context(nc.sbuf_tensor("r_sc", [128, 64], F32))
            t.lg_sb = p1r.enter_context(nc.sbuf_tensor("lg_sb", [8, TLOC], BF16))
            t.ltm = p1r.enter_context(nc.sbuf_tensor("ltm", [128, 64], F32))
            t.exps = p1r.enter_context(nc.sbuf_tensor("exps", [128, 64], F32))
            t.eqs = p1r.enter_context(nc.sbuf_tensor("eqs", [128, 64], F32))
            t.tts = p1r.enter_context(nc.sbuf_tensor("tts", [128, 64], F32))
            t.cwtm = p1r.enter_context(nc.sbuf_tensor("cwtm", [128, 64], BF16))
            t.cw8 = p1r.enter_context(nc.sbuf_tensor("cw8", [8, TLOC], BF16))
            t.cw_row = p1r.enter_context(
                nc.sbuf_tensor("cw_row", [1, 4 * TLOC], BF16))
            t.xnw2 = p1r.enter_context(
                nc.sbuf_tensor("xnw2", [128, 2 * TLOC], BF16))
        t.gwu2 = p1.enter_context(nc.sbuf_tensor("gwu2", [128, 2 * 4096], BF16))
        t.aw2 = p1.enter_context(nc.sbuf_tensor("aw2", [128, 2 * TLOC], BF16))
        t.ps_tr0 = p1.enter_context(nc.psum_tensor("ps_tr0", [128, 1024], BF16))
        t.ps_tr1 = p1.enter_context(nc.psum_tensor("ps_tr1", [128, 1024], BF16))
        t.ps_qkvA = p1.enter_context(nc.psum_tensor("ps_qkvA", [128, 512], F32))
        t.ps_qkvB = p1.enter_context(nc.psum_tensor("ps_qkvB", [128, 512], F32))
        with contextlib.ExitStack() as p1p:
            t.ps_lg12 = p1p.enter_context(
                nc.psum_tensor("ps_lg12", [128, 1024], F32))
            t.ps_t8 = p1p.enter_context(
                nc.psum_tensor("ps_t8", [128, 64], BF16))
            t.ps_ct = p1p.enter_context(
                nc.psum_tensor("ps_ct", [8, TLOC], BF16))
            t.ps_lg = t.ps_lg12
            t.ps_bc = None
        t.ps_gh = [p1.enter_context(nc.psum_tensor(f"ps_gh{i}", [128, 512],
                                                    F32)) for i in range(2)]
        t.ps_uh = [p1.enter_context(nc.psum_tensor(f"ps_uh{i}", [128, 512],
                                                    F32)) for i in range(2)]

    with contextlib.ExitStack() as p2:
        t.E2 = p2.enter_context(nc.sbuf_tensor("E2", [128, 2 * S], BF16))
        t.Zp = p2.enter_context(nc.sbuf_tensor("Zp", [128, 16], F32))
        t.zwork = p2.enter_context(nc.sbuf_tensor("zwork", [128, 4], F32))
        t.rbf = p2.enter_context(nc.sbuf_tensor("rbf", [128, 2], BF16))
        t.cs_sb = p2.enter_context(nc.sbuf_tensor("cs_sb", [1, S], BF16))
        t.val_sb = p2.enter_context(nc.sbuf_tensor("val_sb", [128, S], BF16))
        t.qld = p2.enter_context(nc.sbuf_tensor("qld", [128, 2 * 128], BF16))
        t.astr4 = p2.enter_context(
            nc.sbuf_tensor("astr4", [128, 2 * 2 * TLOC], BF16))
        t.dw_sb = p2.enter_context(
            nc.sbuf_tensor("dw_sb", [128, 2 * 32 * 256], BF16))
        t.g0t = p2.enter_context(nc.sbuf_tensor("g0t", [128, 2 * TLOC], BF16))
        with contextlib.ExitStack() as p2a:
            t.ps_scA = p2a.enter_context(
                nc.psum_tensor("ps_scA", [128, 1024], F32))
            t.ps_scB = p2a.enter_context(
                nc.psum_tensor("ps_scB", [128, 1024], F32))
            t.ps_cs = [p2a.enter_context(
                nc.psum_tensor(f"ps_cs{i}", [97, 512], F32)) for i in range(4)]
        t.ps_y = [p2.enter_context(
            nc.psum_tensor(f"ps_y{i}", [128, TLOC], F32)) for i in range(2)]
    return t


def gen_A1(b, t):
    def on_group(g, lnv, xsl):
        tile, gi = g // 4, g % 4
        pa, pb, tpv = emit_transposes(b, t, lnv, xsl, g)
        b.wait("vector", "tp_done", tpv)
        if gi == 0:
            b.wait("vector", "qm_done", 6 * tile)

        def v_evac(q, gi=gi, pa=pa, pb=pb):
            d3 = t.xn_t.ap().rearrange("p (di tt) -> p di tt", di=16)
            q.tensor_copy(d3[:, 0:8, gi * 128:gi * 128 + 128],
                          pa.ap().rearrange("p (di tt) -> p di tt", di=8))
            return q.tensor_copy(
                d3[:, 8:16, gi * 128:gi * 128 + 128],
                pb.ap().rearrange("p (di tt) -> p di tt", di=8))
        b.do("vector", v_evac, sem="xp_evac")

        if gi == 3:
            b.wait("tensor", "xp_evac", b.c("xp_evac"))
            for oc in range(6):
                ps = t.ps_qkvA if oc % 2 == 0 else t.ps_qkvB
                b.wait("tensor", "qe_done", b.c("qe_done") - 1)

                def t_oc(q, oc=oc, ps=ps):
                    mm = None
                    for di in range(16):
                        mm = q.matmul(
                            ps[:, :],
                            t.wqkv_sb[:, di * 768 + oc * 128:
                                      di * 768 + oc * 128 + 128],
                            t.xn_t[:, di * 512:(di + 1) * 512],
                            start=(di == 0), stop=(di == 15))
                    return mm
                qmv = b.do("tensor", t_oc, sem="qm_done")
                b.wait("scalar", "qm_done", qmv)
                h = oc % 2
                if oc < 2:
                    qb = b.c("qst") // 16 % 2
                    b.wait("scalar", "qst", b.c("qst") - 16)
                    qev = b.do("scalar", lambda q, ps=ps, qb=qb: q.copy(
                        t.qstg[:, qb * 512:(qb + 1) * 512], ps[:, :]),
                        sem="qe_done")
                    b.wait("sync", "qe_done", qev)
                    b.dma("sync", t.q_dram[h * 128:(h + 1) * 128,
                                           tile * 512:(tile + 1) * 512],
                          t.qstg[:, qb * 512:(qb + 1) * 512], sem="qst")
                else:
                    dest = t.xk if oc < 4 else t.xv
                    b.do("scalar", lambda q, dest=dest, h=h, tile=tile, ps=ps:
                         q.copy(dest[:, h * S + tile * 512:
                                     h * S + tile * 512 + 512], ps[:, :]),
                         sem="qe_done")

    yield from gen_ln_stream(b, t, t.x, 64, on_group)


def gen_A2(b, t):
    def slot(kt):
        return t.ps_cs[kt % 4], 32 * (kt // 4)

    for h in range(2):
        if h > 0:
            b.wait("tensor", "cs_evac", b.c("cs_evac"))
        b.wait("tensor", "const", b.c("const"))
        b.wait("tensor", "const2", b.c("const2"))

        def t_zero(q):
            mm = None
            for kt in range(16):
                pt, row = slot(kt)
                tp = (0, row) if row else None
                mm = q.matmul(pt[row:row + 1, :], t.zeros1[:, :], t.E2[:, 0:512],
                              start=True, stop=False, tile_position=tp)
            return mm
        b.do("tensor", t_zero, sem="zs")

        for qc in range(64):
            ebuf = qc % 2
            qb = b.c("ld_q") // 16 % 2
            b.wait("sync", "sc_done", b.c("sc_done") - 8)
            qlv = b.dma("sync", t.qld[:, qb * 128:(qb + 1) * 128],
                        t.q_dram[h * 128:(h + 1) * 128, qc * 128:qc * 128 + 128],
                        sem="ld_q")
            b.wait("tensor", "ld_q", qlv)
            for sub in range(8):
                psc = t.ps_scA if sub % 2 == 0 else t.ps_scB
                b.wait("tensor", "exp_done", b.c("exp_done") - 1)

                def t_sc(q, h=h, sub=sub, qb=qb, psc=psc):
                    mm = None
                    for half in range(2):
                        kt = sub * 2 + half
                        mm = q.matmul(
                            psc[:, half * 512:half * 512 + 512],
                            t.qld[:, qb * 128:(qb + 1) * 128],
                            t.xk[:, h * S + kt * 512:h * S + kt * 512 + 512],
                            start=True, stop=True)
                    return mm
                sdv = b.do("tensor", t_sc, sem="sc_done")
                b.wait("scalar", "sc_done", sdv)
                if sub == 0:
                    b.wait("scalar", "rmm_done", b.c("rmm_done") - 1)

                def s_exp(q, ebuf=ebuf, sub=sub, psc=psc):
                    return q.activation(
                        out=t.E2[:, ebuf * S + sub * 1024:
                                 ebuf * S + sub * 1024 + 1024],
                        in_=psc[:, :], func=AF.Exp,
                        scale=f32(INV_SQRT_HD), bias=f32(-C0_SHIFT),
                        accum_out=t.Zp[:, ebuf * 8 + sub:ebuf * 8 + sub + 1])
                b.do("scalar", s_exp, sem="exp_done")
            b.wait("vector", "exp_done", b.c("exp_done"))
            b.wait("vector", "rmm_done", b.c("rmm_done") - 1)
            b.fdo("vector", lambda q, ebuf=ebuf: q.tensor_reduce(
                t.zwork[:, 0:1], t.Zp[:, ebuf * 8:ebuf * 8 + 8],
                axis=AX.X, op=ALU.add))
            b.fdo("vector", lambda q: q.reciprocal(
                t.zwork[:, 1:2], t.zwork[:, 0:1]))
            zdv = b.do("vector", lambda q, ebuf=ebuf: q.tensor_copy(
                t.rbf[:, ebuf:ebuf + 1], t.zwork[:, 1:2]), sem="z_done")
            b.wait("tensor", "z_done", zdv)
            if qc == 0:
                b.wait("tensor", "zs", b.c("zs"))

            def t_rmm(q, qc=qc, ebuf=ebuf):
                mm = None
                for kt in range(16):
                    pt, row = slot(kt)
                    tp = (0, row) if row else None
                    mm = q.matmul(pt[row:row + 1, :], t.rbf[:, ebuf:ebuf + 1],
                                  t.E2[:, ebuf * S + kt * 512:
                                       ebuf * S + kt * 512 + 512],
                                  start=False, stop=(qc == 63),
                                  tile_position=tp)
                return mm
            b.do("tensor", t_rmm, sem="rmm_done")
            yield

        b.wait("scalar", "rmm_done", b.c("rmm_done"))
        b.wait("scalar", "bc_done", b.c("bc_done"))
        for kt in range(16):
            pt, row = slot(kt)
            b.do("scalar", lambda q, pt=pt, row=row, kt=kt: q.copy(
                t.cs_sb[0:1, kt * 512:kt * 512 + 512], pt[row:row + 1, :]),
                sem="cs_evac")
        b.wait("tensor", "cs_evac", b.c("cs_evac"))
        b.wait("tensor", "exp_done", b.c("exp_done"))
        b.wait("vector", "st_out", b.c("st_out"))
        for kt in range(16):
            if kt > 0:
                b.wait("tensor", "vs_done", b.c("vs_done"))
            bcv = b.do("tensor", lambda q, kt=kt: q.matmul(
                t.ps_scA[:, 0:512], t.ones1[:, :],
                t.cs_sb[0:1, kt * 512:kt * 512 + 512], start=True, stop=True),
                sem="bc_done")
            b.wait("vector", "bc_done", bcv)
            b.do("vector", lambda q, h=h, kt=kt: q.tensor_tensor(
                t.val_sb[:, kt * 512:kt * 512 + 512],
                t.xv[:, h * S + kt * 512:h * S + kt * 512 + 512],
                t.ps_scA[:, 0:512], op=ALU.mult), sem="vs_done")
            yield
        b.wait("tensor", "vs_done", b.c("vs_done"))
        b.wait("gpsimd", "vs_done", b.c("vs_done"))
        b.dma("gpsimd", t.val_o[h * 128:(h + 1) * 128, :], t.val_sb[:, :],
              sem="st_out")
    b.wait("gpsimd", "st_out", b.c("st_out"))


def gen_down_A(b, t):
    for fog in range(8):
        b.wait("tensor", "y_silu", b.c("y_silu"))
        yield from gen_down_accum(b, t, t.dwn, t.a_dram, 64, fog, 32, 2, 256)
        b.wait("scalar", "dn_pair", b.c("dn_pair"))
        for fo_in in range(2):
            fo = fog * 2 + fo_in
            obuf = fo % 2
            b.wait("scalar", "g0_st", b.c("g0_st") - 16)
            ysv = b.do("scalar", lambda q, obuf=obuf, fo_in=fo_in: q.activation(
                out=t.g0t[:, obuf * TLOC:(obuf + 1) * TLOC],
                in_=t.ps_y[fo_in][:, :], func=AF.Silu), sem="y_silu")
            b.wait("gpsimd", "y_silu", ysv)
            b.dma("gpsimd", t.g0_o[fo * 128:(fo + 1) * 128, :],
                  t.g0t[:, obuf * TLOC:(obuf + 1) * TLOC], sem="g0_st")
        yield


A_UPTO = 9


def emit_A(b, t):
    emit_consts(b, t)
    wv = b.dma("gpsimd", t.wqkv_sb[:, :], t.wqkv)
    b.wait("tensor", "ld_gpsimd", wv)
    b.wait("tensor", "ld_const", b.c("ld_const"))

    def on_group_loc(g, lnv, xsl):
        pa, pb, tpv = emit_transposes(b, t, lnv, xsl, g)
        b.wait("vector", "tp_done", tpv)

        def v_evac(q, g=g, pa=pa, pb=pb):
            d3 = t.xn_loc.ap().rearrange("p (di tt) -> p di tt", di=16)
            q.tensor_copy(d3[:, 0:8, g * 128:g * 128 + 128],
                          pa.ap().rearrange("p (di tt) -> p di tt", di=8))
            return q.tensor_copy(
                d3[:, 8:16, g * 128:g * 128 + 128],
                pb.ap().rearrange("p (di tt) -> p di tt", di=8))
        b.do("vector", v_evac, sem="xp_evac")

    emit_ln_stream(b, t, t.xloc, 8, on_group_loc, rms_groups=list(range(8)))

    emit_router_top2(b, t, n_exp=8, use_rms=True, scale_out=4.0)
    b.wait("gpsimd", "cwb_ev", b.c("cwb_ev"))
    b.wait("tensor", "cwb_ev", b.c("cwb_ev"))
    if A_UPTO < 2:
        b.barrier()
        return
    interleave((gen_A1(b, t), 1), (gen_expert_gu(b, t, t.gwu, t.a_dram, 64, 8), 1))
    b.barrier()
    if A_UPTO < 3:
        return
    mz = b.do("vector", lambda v: v.memset(t.E2[:, 0:512], 0.0), sem="const2")
    b.wait("sync", "a_st", b.c("a_st"))
    b.wait("sync", "qst", b.c("qst"))
    interleave((gen_A2(b, t), 1))
    b.barrier()
    interleave((gen_down_A(b, t), 1))
    b.wait("gpsimd", "g0_st", b.c("g0_st"))
    b.wait("gpsimd", "st_out", b.c("st_out"))



def make_tensors_B(nc, es):
    t = types.SimpleNamespace()
    t.xfm = nc.dram_tensor("xfm", [DM, TLOC], F32, kind="ExternalInput").ap()
    t.val = nc.dram_tensor("val", [DM, TLOC], BF16, kind="ExternalInput").ap()
    t.g0 = nc.dram_tensor("g0", [DM, TLOC], BF16, kind="ExternalInput").ap()
    t.wo = nc.dram_tensor("wo", [128, 16 * DM], F32, kind="ExternalInput").ap()
    t.x1_o = nc.dram_tensor("x1out", [DM, TLOC], F32, kind="ExternalOutput").ap()
    sb = es.enter_context
    t.wo_sb = sb(nc.sbuf_tensor("wo_sb", [128, 16 * DM], BF16))
    t.val_sb = sb(nc.sbuf_tensor("val_sb", [128, 16 * TLOC], BF16))
    t.g0_sb = sb(nc.sbuf_tensor("g0_sb", [128, 16 * TLOC], BF16))
    t.x_sb = sb(nc.sbuf_tensor("x_sb", [128, 16 * TLOC], F32))
    t.x1_sb = sb(nc.sbuf_tensor("x1_sb", [128, 2 * TLOC], F32))
    t.sg_sb = sb(nc.sbuf_tensor("sg_sb", [128, TLOC], F32))
    t.ps_s = [sb(nc.psum_tensor(f"ps_s{i}", [128, TLOC], F32)) for i in range(2)]
    return t


def emit_B(b, t):
    lv = b.dma("gpsimd", t.wo_sb[:, :], t.wo)
    b.dma("sync", t.val_sb[:, :].rearrange("p (dc tt) -> p dc tt", dc=16),
          t.val.rearrange("(dc p) tt -> p dc tt", p=128))
    b.dma("sync", t.g0_sb[:, :].rearrange("p (dc tt) -> p dc tt", dc=16),
          t.g0.rearrange("(dc p) tt -> p dc tt", p=128))
    xv = b.dma("sync", t.x_sb[:, :].rearrange("p (dc tt) -> p dc tt", dc=16),
               t.xfm.rearrange("(dc p) tt -> p dc tt", p=128))
    b.wait("tensor", "ld_gpsimd", lv)
    b.wait("tensor", "ld_sync", xv)
    for fo in range(16):
        ps = t.ps_s[fo % 2]
        b.wait("tensor", "sg_done", b.c("sg_done") - 1)

        def t_s(q, fo=fo, ps=ps):
            mm = None
            for fi in range(16):
                for half in range(2):
                    mm = q.matmul(
                        ps[:, half * 512:half * 512 + 512],
                        t.wo_sb[:, fi * DM + fo * 128:fi * DM + fo * 128 + 128],
                        t.val_sb[:, fi * TLOC + half * 512:
                                 fi * TLOC + half * 512 + 512],
                        start=(fi == 0), stop=(fi == 15))
            return mm
        smv = b.do("tensor", t_s, sem="s_mm")
        b.wait("vector", "s_mm", smv)
        b.wait("vector", "x1_st", b.c("x1_st") - 16)

        def v_comb(q, fo=fo, ps=ps):
            q.tensor_tensor(t.sg_sb[:, :], ps[:, :],
                            t.g0_sb[:, fo * TLOC:(fo + 1) * TLOC], op=ALU.mult)
            return q.tensor_add(t.x1_sb[:, (fo % 2) * TLOC:(fo % 2 + 1) * TLOC],
                                t.x_sb[:, fo * TLOC:(fo + 1) * TLOC],
                                t.sg_sb[:, :])
        cbv = b.do("vector", v_comb, sem="sg_done")
        b.wait("gpsimd", "sg_done", cbv)
        b.dma("gpsimd", t.x1_o[fo * 128:(fo + 1) * 128, :],
              t.x1_sb[:, (fo % 2) * TLOC:(fo % 2 + 1) * TLOC], sem="x1_st")
    b.wait("gpsimd", "x1_st", b.c("x1_st"))


_IDENT = np.eye(128, dtype=np.float32).astype(BF)


def _chunked(w, inner=128):
    K, N = w.shape
    return np.ascontiguousarray(
        w.reshape(K // 128, 128, N).transpose(1, 0, 2).reshape(128, -1))


def _gu_layout(w):
    K, N = w.shape
    r = w.reshape(K // 128, 128, N // 128, 128).transpose(1, 2, 0, 3)
    return np.ascontiguousarray(r.reshape(128, -1))


def _down_layout(w, n_fog, fo_w):
    H, N = w.shape
    r = w.reshape(H // 128, 128, n_fog, fo_w).transpose(1, 2, 0, 3)
    return np.ascontiguousarray(r.reshape(128, -1))


def host_inputs_A(inputs):
    x = np.ascontiguousarray(inputs["L0_attn_x"].reshape(S, DM))
    wqkv = inputs["L0_attn_w_qkv"]
    gw = _gu_layout(inputs["L0_ffn_gate_w"].transpose(1, 0, 2).reshape(DM, 8192))
    uw = _gu_layout(inputs["L0_ffn_up_w"].transpose(1, 0, 2).reshape(DM, 8192))
    gwu = np.ascontiguousarray(np.concatenate(
        [gw.reshape(128, 64, 2048), uw.reshape(128, 64, 2048)],
        axis=2).reshape(128, 64 * 4096))
    dwn = _down_layout(inputs["L0_ffn_down_w"].reshape(8192, DM), 8, 256)
    rw = _chunked(np.ascontiguousarray(inputs["L0_ffn_router_w"].T))
    nw = np.ascontiguousarray(inputs["L0_ffn_norm_w"].reshape(16, 128).T)
    maps = []
    for c in range(NCORES):
        sl = np.concatenate([
            wqkv[:, 256 * c:256 * (c + 1)],
            wqkv[:, 2048 + 256 * c:2048 + 256 * (c + 1)],
            wqkv[:, 4096 + 256 * c:4096 + 256 * (c + 1)]], axis=1)
        maps.append({
            "x": x, "xloc": x[TLOC * c:TLOC * (c + 1)],
            "wqkv": _chunked(np.ascontiguousarray(sl)), "rw": rw, "nw": nw,
            "gwu": gwu, "dwn": dwn, "ident_c": _IDENT})
    return maps


def host_inputs_B(inputs, val_parts, g0_parts, wo=None):
    x = inputs["L0_attn_x"].reshape(S, DM)
    val_full = np.concatenate(val_parts, axis=0)
    wo_r = _chunked(wo if wo is not None else inputs["L0_attn_w_o"])
    maps = []
    for c in range(NCORES):
        xfm = np.ascontiguousarray(x[TLOC * c:TLOC * (c + 1)].T)
        maps.append({
            "xfm": xfm,
            "val": np.ascontiguousarray(val_full[:, TLOC * c:TLOC * (c + 1)]),
            "g0": g0_parts[c],
            "wo": wo_r})
    return maps





N_HC1 = 88
N_HCS = 44
SCALE1 = float(1.0 / (50.0 * np.sqrt(128.0)))


def make_tensors_C(nc, es):
    t = types.SimpleNamespace()
    t.x = nc.dram_tensor("x", [S, DM], F32, kind="ExternalInput").ap()
    t.xloc = nc.dram_tensor("xloc", [TLOC, DM], F32, kind="ExternalInput").ap()
    t.wqkv = nc.dram_tensor("wqkv", [128, 16 * 512], F32,
                            kind="ExternalInput").ap()
    t.rw = nc.dram_tensor("rw", [128, 128], F32, kind="ExternalInput").ap()
    t.gwu = nc.dram_tensor("gwu", [128, N_HC1 * 4096], F32,
                           kind="ExternalInput").ap()
    t.dwn = nc.dram_tensor("dwn", [128, 8 * N_HC1 * 256], F32,
                           kind="ExternalInput").ap()
    t.sgw_d = nc.dram_tensor("sgw", [128, 16], F32, kind="ExternalInput").ap()
    t.swgu = nc.dram_tensor("swgu", [128, N_HCS * 4096], F32,
                            kind="ExternalInput").ap()
    t.swd = nc.dram_tensor("swd", [128, 8 * N_HCS * 256], F32,
                           kind="ExternalInput").ap()
    t.cos_d = nc.dram_tensor("cos_c", [128, S], F32, kind="ExternalInput").ap()
    t.sin_d = nc.dram_tensor("sin_c", [128, S], F32, kind="ExternalInput").ap()
    t.mask_d = nc.dram_tensor("mask_c", [128, 8 * 1024], BF16,
                              kind="ExternalInput").ap()
    t.ident_c = nc.dram_tensor("ident_c", [128, 128], BF16,
                               kind="ExternalInput").ap()
    t.out1_o = nc.dram_tensor("out1", [256, S], BF16, kind="ExternalOutput").ap()
    t.g1_o = nc.dram_tensor("g1out", [DM, TLOC], BF16, kind="ExternalOutput").ap()
    t.a_dram = nc.dram_tensor("a_stage", [N_HC1 * 128, TLOC], BF16).ap()
    t.sh_dram = nc.dram_tensor("sh_stage", [N_HCS * 128, TLOC], BF16).ap()

    sb = es.enter_context
    t.zeros1 = sb(nc.sbuf_tensor("zeros1", [128, 1], BF16))
    t.ones1 = sb(nc.sbuf_tensor("ones1", [1, 128], BF16))
    t.onescol = sb(nc.sbuf_tensor("onescol", [128, 1], BF16))
    t.ident = sb(nc.sbuf_tensor("ident", [128, 128], BF16))
    t.stats = sb(nc.sbuf_tensor("stats", [128, 8], F32))
    t.rms_sumsq = sb(nc.sbuf_tensor("rms_sumsq", [128, 8], F32))
    t.xq = sb(nc.sbuf_tensor("xq", [128, 2 * S], BF16))
    t.xk = sb(nc.sbuf_tensor("xk", [128, S], BF16))
    t.xv = sb(nc.sbuf_tensor("xv", [128, S], BF16))
    t.sgb = sb(nc.sbuf_tensor("sgb", [128, TLOC], BF16))

    with contextlib.ExitStack() as p1:
        t.sq_scr = p1.enter_context(nc.sbuf_tensor("sq_scr", [128, DM], BF16))
        t.wqkv_sb = p1.enter_context(
            nc.sbuf_tensor("wqkv_sb", [128, 16 * 512], BF16))
        t.x_tm2 = p1.enter_context(nc.sbuf_tensor("x_tm2", [128, 2 * DM], F32))
        t.xn_tm2 = p1.enter_context(
            nc.sbuf_tensor("xn_tm2", [128, 2 * DM], BF16))
        t.xn_t = p1.enter_context(nc.sbuf_tensor("xn_t", [128, 16 * 512], BF16))
        t.cossin2 = p1.enter_context(
            nc.sbuf_tensor("cossin2", [128, 4 * 512], F32))
        t.rot_sb = p1.enter_context(nc.sbuf_tensor("rot_sb", [128, 512], F32))
        t.t1_sb = p1.enter_context(nc.sbuf_tensor("t1_sb", [128, 512], F32))
        t.xn_loc = p1.enter_context(
            nc.sbuf_tensor("xn_loc", [128, 16 * TLOC], BF16))
        t.rw_sb = p1.enter_context(nc.sbuf_tensor("rw_sb", [128, 128], BF16))
        t.sgw_sb = p1.enter_context(nc.sbuf_tensor("sgw_sb", [128, 16], BF16))
        t.cwb = p1.enter_context(nc.sbuf_tensor("cwb", [128, 8 * TLOC], BF16))
        with contextlib.ExitStack() as p1r:
            t.sg_row = p1r.enter_context(
                nc.sbuf_tensor("sg_row", [1, TLOC], BF16))
            t.r_sc = p1r.enter_context(nc.sbuf_tensor("r_sc", [128, 64], F32))
            t.lg_sb = p1r.enter_context(nc.sbuf_tensor("lg_sb", [8, TLOC], BF16))
            t.ltm = p1r.enter_context(nc.sbuf_tensor("ltm", [128, 64], F32))
            t.exps = p1r.enter_context(nc.sbuf_tensor("exps", [128, 64], F32))
            t.eqs = p1r.enter_context(nc.sbuf_tensor("eqs", [128, 64], F32))
            t.tts = p1r.enter_context(nc.sbuf_tensor("tts", [128, 64], F32))
            t.cwtm = p1r.enter_context(nc.sbuf_tensor("cwtm", [128, 64], BF16))
            t.cw8 = p1r.enter_context(nc.sbuf_tensor("cw8", [8, TLOC], BF16))
            t.cw_row = p1r.enter_context(
                nc.sbuf_tensor("cw_row", [1, 4 * TLOC], BF16))
        t.gwu2 = p1.enter_context(nc.sbuf_tensor("gwu2", [128, 2 * 4096], BF16))
        t.aw2 = p1.enter_context(nc.sbuf_tensor("aw2", [128, 2 * TLOC], BF16))
        t.ps_tr0 = p1.enter_context(nc.psum_tensor("ps_tr0", [128, 1024], BF16))
        t.ps_tr1 = p1.enter_context(nc.psum_tensor("ps_tr1", [128, 1024], BF16))
        t.ps_qkv = p1.enter_context(nc.psum_tensor("ps_qkv", [128, 512], F32))
        t.ps_v = p1.enter_context(nc.psum_tensor("ps_v", [128, 128], F32))
        with contextlib.ExitStack() as p1p:
            t.ps_lg12 = p1p.enter_context(
                nc.psum_tensor("ps_lg12", [128, 1024], F32))
            t.ps_t8 = p1p.enter_context(
                nc.psum_tensor("ps_t8", [128, 64], BF16))
            t.ps_ct = p1p.enter_context(
                nc.psum_tensor("ps_ct", [8, TLOC], BF16))
            t.ps_lg = t.ps_lg12
            t.ps_bc = None
            t.ps_sg = None
        t.ps_gh = [p1.enter_context(nc.psum_tensor(f"ps_gh{i}", [128, 512],
                                                    F32)) for i in range(2)]
        t.ps_uh = [p1.enter_context(nc.psum_tensor(f"ps_uh{i}", [128, 512],
                                                    F32)) for i in range(2)]

    with contextlib.ExitStack() as p2:
        t.mask_sb = p2.enter_context(
            nc.sbuf_tensor("mask_sb", [128, 8 * 1024], BF16))
        t.astr4 = p2.enter_context(
            nc.sbuf_tensor("astr4", [128, 2 * 2 * TLOC], BF16))
        t.dw_sb = p2.enter_context(
            nc.sbuf_tensor("dw_sb", [128, 2 * 20 * 256], BF16))
        t.moe_t = p2.enter_context(nc.sbuf_tensor("moe_t", [128, 2 * TLOC], BF16))
        t.cmb1 = p2.enter_context(nc.sbuf_tensor("cmb1", [128, TLOC], F32))
        t.cmb2 = p2.enter_context(nc.sbuf_tensor("cmb2", [128, TLOC], F32))
        t.g0t = p2.enter_context(nc.sbuf_tensor("g0t", [128, 2 * TLOC], BF16))
        t.E2 = p2.enter_context(nc.sbuf_tensor("E2c", [128, 2 * 1024], BF16))
        t.tn_sb = p2.enter_context(nc.sbuf_tensor("tn_sb", [128, 1024], BF16))
        t.S_acc = p2.enter_context(nc.sbuf_tensor("S_acc", [128, 1024], F32))
        t.S_bf = p2.enter_context(nc.sbuf_tensor("S_bf", [128, 1024], BF16))
        t.zr_f = p2.enter_context(nc.sbuf_tensor("zr_f", [1, 1024], F32))
        t.zr_f2 = p2.enter_context(nc.sbuf_tensor("zr_f2", [1, 1024], F32))
        t.zr_b = p2.enter_context(nc.sbuf_tensor("zr_b", [1, 1024], BF16))
        t.o_raw = p2.enter_context(nc.sbuf_tensor("o_raw", [128, 1024], F32))
        t.o_bf = p2.enter_context(nc.sbuf_tensor("o_bf", [128, S], BF16))
        t.ps_y = [p2.enter_context(
            nc.psum_tensor(f"ps_y{i}", [128, TLOC], F32)) for i in range(2)]
        t.ps_sc = p2.enter_context(nc.psum_tensor("ps_sc", [128, 1024], F32))
        t.ps_o = p2.enter_context(nc.psum_tensor("ps_o", [128, 1024], F32))
    return t


def gen_C1(b, t):
    def on_group(g, lnv, xsl):
        tile, gi = g // 4, g % 4
        pa, pb, tpv = emit_transposes(b, t, lnv, xsl, g)
        b.wait("vector", "tp_done", tpv)
        if gi == 0:
            b.wait("vector", "qm_done", 3 * tile)
            b.wait("vector", "v_mm", 4 * tile)

        def v_evac(q, gi=gi, pa=pa, pb=pb):
            d3 = t.xn_t.ap().rearrange("p (di tt) -> p di tt", di=16)
            q.tensor_copy(d3[:, 0:8, gi * 128:gi * 128 + 128],
                          pa.ap().rearrange("p (di tt) -> p di tt", di=8))
            return q.tensor_copy(
                d3[:, 8:16, gi * 128:gi * 128 + 128],
                pb.ap().rearrange("p (di tt) -> p di tt", di=8))
        b.do("vector", v_evac, sem="xp_evac")

        if gi == 3:
            b.wait("tensor", "xp_evac", b.c("xp_evac"))
            b.wait("sync", "rope_done", b.c("rope_done") - 3)
            csv = b.dma("sync", t.cossin2[:, (tile % 2) * 1024:
                                          (tile % 2) * 1024 + 512],
                        t.cos_d[:, tile * 512:(tile + 1) * 512], sem="ld_cs")
            csv = b.dma("sync", t.cossin2[:, (tile % 2) * 1024 + 512:
                                          (tile % 2) * 1024 + 1024],
                        t.sin_d[:, tile * 512:(tile + 1) * 512], sem="ld_cs")
            for oc in range(3):
                b.wait("tensor", "rope_done", b.c("rope_done"))

                def t_oc(q, oc=oc):
                    mm = None
                    for di in range(16):
                        mm = q.matmul(
                            t.ps_qkv[:, :],
                            t.wqkv_sb[:, di * 512 + oc * 128:
                                      di * 512 + oc * 128 + 128],
                            t.xn_t[:, di * 512:(di + 1) * 512],
                            start=(di == 0), stop=(di == 15))
                    return mm
                qmv = b.do("tensor", t_oc, sem="qm_done")
                b.wait("vector", "qm_done", qmv)
                b.wait("vector", "ld_cs", csv)
                dest = t.xq if oc < 2 else t.xk
                doff = oc * S if oc < 2 else 0
                cs0 = (tile % 2) * 1024

                def v_rope(q, dest=dest, doff=doff, tile=tile, cs0=cs0):
                    q.tensor_scalar_mul(t.rot_sb[0:64, :],
                                        t.ps_qkv[64:128, :], -1.0)
                    q.tensor_copy(t.rot_sb[64:128, :], t.ps_qkv[0:64, :])
                    q.tensor_tensor(t.t1_sb[:, :], t.ps_qkv[:, :],
                                    t.cossin2[:, cs0:cs0 + 512], op=ALU.mult)
                    q.tensor_tensor(t.rot_sb[:, :], t.rot_sb[:, :],
                                    t.cossin2[:, cs0 + 512:cs0 + 1024],
                                    op=ALU.mult)
                    return q.tensor_add(
                        dest[:, doff + tile * 512:doff + tile * 512 + 512],
                        t.t1_sb[:, :], t.rot_sb[:, :])
                b.do("vector", v_rope, sem="rope_done")
            for sub in range(4):
                b.wait("tensor", "v_evac", b.c("v_evac"))

                def t_v(q, sub=sub):
                    mm = None
                    for di in range(16):
                        mm = q.matmul(
                            t.ps_v[:, :],
                            t.xn_t[:, di * 512 + sub * 128:
                                   di * 512 + sub * 128 + 128],
                            t.wqkv_sb[:, di * 512 + 384:di * 512 + 512],
                            start=(di == 0), stop=(di == 15))
                    return mm
                vmv = b.do("tensor", t_v, sem="v_mm")
                b.wait("scalar", "v_mm", vmv)
                kc = tile * 4 + sub
                b.do("scalar", lambda q, kc=kc: q.copy(
                    t.xv[:, kc * 128:kc * 128 + 128], t.ps_v[:, :]),
                    sem="v_evac")

    yield from gen_ln_stream(b, t, t.x, 64, on_group)


def gen_gu(b, t):
    yield from gen_expert_gu(b, t, t.gwu, t.a_dram, N_HC1, 11)
    yield from gen_expert_gu(b, t, t.swgu, t.sh_dram, N_HCS, None)


def gen_expert_gu(b, t, gwu_d, a_dram, n_hc, hc_per_exp):
    for hc in range(n_hc):
        wbuf = b.c("ld_w") // 16 % 2
        b.wait("gpsimd", "gu_mm", b.c("gu_mm") - 2)
        gl = b.dma("gpsimd", t.gwu2[:, wbuf * 4096:(wbuf + 1) * 4096],
                   gwu_d[:, hc * 4096:(hc + 1) * 4096], sem="ld_w")
        b.wait("tensor", "ld_w", gl)
        abuf = b.c("a_st") // 16 % 2
        e = hc // hc_per_exp if hc_per_exp else None
        for half in range(2):
            hb = b.c("g_mm") % 2
            b.wait("tensor", "silu_done", b.c("silu_done") - 1)

            def t_g(q, wbuf=wbuf, half=half, hb=hb):
                mm = None
                for di in range(16):
                    mm = q.matmul(
                        t.ps_gh[hb][:, :],
                        t.gwu2[:, wbuf * 4096 + di * 128:
                               wbuf * 4096 + di * 128 + 128],
                        t.xn_loc[:, di * TLOC + half * 512:
                                 di * TLOC + half * 512 + 512],
                        start=(di == 0), stop=(di == 15))
                return mm
            gmv = b.do("tensor", t_g, sem="g_mm")
            b.wait("tensor", "aprod_done", b.c("aprod_done") - 1)

            def t_u(q, wbuf=wbuf, half=half, hb=hb):
                mm = None
                for di in range(16):
                    mm = q.matmul(
                        t.ps_uh[hb][:, :],
                        t.gwu2[:, wbuf * 4096 + 2048 + di * 128:
                               wbuf * 4096 + 2048 + di * 128 + 128],
                        t.xn_loc[:, di * TLOC + half * 512:
                                 di * TLOC + half * 512 + 512],
                        start=(di == 0), stop=(di == 15))
                return mm
            umv = b.do("tensor", t_u, sem="gu_mm")
            b.wait("scalar", "g_mm", gmv)
            if half == 0:
                b.wait("scalar", "a_st", b.c("a_st") - 16)
            dst = (slice(None),
                   slice(abuf * TLOC + half * 512, abuf * TLOC + half * 512 + 512))
            slv = b.do("scalar", lambda q, dst=dst, hb=hb: q.activation(
                out=t.aw2[dst[0], dst[1]], in_=t.ps_gh[hb][:, :],
                func=AF.Silu), sem="silu_done")
            b.wait("vector", "gu_mm", umv)
            b.wait("vector", "silu_done", slv)

            if hc_per_exp:
                def v_a(q, e=e, dst=dst, half=half, hb=hb):
                    q.tensor_tensor(t.aw2[dst[0], dst[1]],
                                    t.aw2[dst[0], dst[1]],
                                    t.ps_uh[hb][:, :], op=ALU.mult)
                    return q.tensor_tensor(
                        t.aw2[dst[0], dst[1]], t.aw2[dst[0], dst[1]],
                        t.cwb[:, e * TLOC + half * 512:
                              e * TLOC + half * 512 + 512], op=ALU.mult)
            else:
                def v_a(q, dst=dst, hb=hb):
                    return q.tensor_tensor(
                        t.aw2[dst[0], dst[1]], t.aw2[dst[0], dst[1]],
                        t.ps_uh[hb][:, :], op=ALU.mult)
            apv = b.do("vector", v_a, sem="aprod_done")
        b.wait("gpsimd", "aprod_done", apv)
        b.dma("gpsimd", a_dram[hc * 128:(hc + 1) * 128, :],
              t.aw2[:, abuf * TLOC:(abuf + 1) * TLOC], sem="a_st")
        yield


def gen_down_C(b, t):
    for fog in range(8):
        b.wait("tensor", "y_silu", b.c("y_silu"))
        yield from gen_down_accum(
            b, t, t.dwn, t.a_dram, N_HC1, fog, 20, 2, 256)
        b.wait("scalar", "dn_pair", b.c("dn_pair"))
        b.wait("scalar", "ffn_comb", b.c("ffn_comb"))
        for fo_in in range(2):
            b.do("scalar", lambda q, fo_in=fo_in: q.copy(
                t.moe_t[:, fo_in * TLOC:(fo_in + 1) * TLOC],
                t.ps_y[fo_in][:, :]), sem="moe_ev")
        b.wait("tensor", "moe_ev", b.c("moe_ev"))
        yield from gen_down_accum(
            b, t, t.swd, t.sh_dram, N_HCS, fog, 20, 2, 256)
        b.wait("vector", "dn_pair", b.c("dn_pair"))
        b.wait("vector", "moe_ev", b.c("moe_ev"))
        for fo_in in range(2):
            fo = fog * 2 + fo_in
            obuf = fo % 2
            b.wait("vector", "g1_st", b.c("g1_st") - 16)
            b.wait("vector", "y_silu", b.c("y_silu"))

            def v_comb(q, fo_in=fo_in):
                q.tensor_tensor(t.cmb1[:, :], t.ps_y[fo_in][:, :],
                                t.sgb[:, :], op=ALU.mult)
                return q.tensor_add(
                    t.cmb2[:, :],
                    t.moe_t[:, fo_in * TLOC:(fo_in + 1) * TLOC],
                    t.cmb1[:, :])
            cbv = b.do("vector", v_comb, sem="ffn_comb")
            b.wait("scalar", "ffn_comb", cbv)
            ysv = b.do("scalar", lambda q, obuf=obuf: q.activation(
                out=t.g0t[:, obuf * TLOC:(obuf + 1) * TLOC],
                in_=t.cmb2[:, :], func=AF.Silu), sem="y_silu")
            b.wait("gpsimd", "y_silu", ysv)
            b.dma("gpsimd", t.g1_o[fo * 128:(fo + 1) * 128, :],
                  t.g0t[:, obuf * TLOC:(obuf + 1) * TLOC], sem="g1_st")
        yield


def gen_down_accum(b, t, dwn_d, a_dram, n_hc, fog, hc_chunk, n_fo, fo_w):
    for cs in range(0, n_hc, hc_chunk):
        ce = min(cs + hc_chunk, n_hc)
        cbuf = b.c("dn_chunk") % 2
        b.wait("gpsimd", "dn_chunk", b.c("dn_chunk") - 1)
        dl = b.dma(
            "gpsimd",
            t.dw_sb[:, cbuf * hc_chunk * fo_w:
                    (cbuf * hc_chunk + (ce - cs)) * fo_w],
            dwn_d[:, (fog * n_hc + cs) * fo_w:(fog * n_hc + ce) * fo_w],
            sem="ld_dw")
        b.wait("tensor", "ld_dw", dl)
        for hc in range(cs, ce, 2):
            npair = min(2, ce - hc)
            pbuf = b.c("dn_pair") % 2
            b.wait("sync", "dn_pair", b.c("dn_pair") - 1)
            alv = b.dma(
                "sync",
                t.astr4[:, pbuf * 2048:pbuf * 2048 + npair * TLOC].rearrange(
                    "p (hh tt) -> p hh tt", hh=npair),
                a_dram[hc * 128:(hc + npair) * 128, :].rearrange(
                    "(hh p) tt -> p hh tt", p=128), sem="ld_a")
            b.wait("tensor", "ld_a", alv)

            def t_dn(q, hc=hc, cs=cs, npair=npair, pbuf=pbuf, cbuf=cbuf,
                     n_hc=n_hc, hc_chunk=hc_chunk, n_fo=n_fo, fo_w=fo_w):
                mm = None
                for ki in range(npair):
                    ci = hc + ki - cs
                    for fo_in in range(n_fo):
                        for half in range(2):
                            mm = q.matmul(
                                t.ps_y[fo_in][:, half * 512:half * 512 + 512],
                                t.dw_sb[:, (cbuf * hc_chunk + ci) * fo_w +
                                        fo_in * 128:
                                        (cbuf * hc_chunk + ci) * fo_w +
                                        fo_in * 128 + 128],
                                t.astr4[:, pbuf * 2048 + ki * TLOC + half * 512:
                                        pbuf * 2048 + ki * TLOC + half * 512
                                        + 512],
                                start=(hc + ki == 0), stop=(hc + ki == n_hc - 1))
                return mm
            b.do("tensor", t_dn, sem="dn_pair")
            yield
        b.do("tensor", lambda q: q.nop(), sem="dn_chunk")


def gen_C3(b, t):
    for h in range(2):
        for qt in range(8):
            nkc = 8 * (qt + 1)
            for kc in range(nkc):
                ebuf = kc % 2
                b.wait("tensor", "tanh_done", b.c("tanh_done"))

                def t_sc(q, h=h, qt=qt, kc=kc):
                    mm = None
                    for half in range(2):
                        mm = q.matmul(
                            t.ps_sc[:, half * 512:half * 512 + 512],
                            t.xk[:, kc * 128:kc * 128 + 128],
                            t.xq[:, h * S + qt * 1024 + half * 512:
                                 h * S + qt * 1024 + half * 512 + 512],
                            start=True, stop=True)
                    return mm
                sdv = b.do("tensor", t_sc, sem="sc_done")
                b.wait("scalar", "sc_done", sdv)
                tnv = b.do("scalar", lambda q: q.activation(
                    out=t.tn_sb[:, :], in_=t.ps_sc[:, :], func=AF.Tanh,
                    scale=f32(SCALE1)), sem="tanh_done")
                b.wait("scalar", "mask_done", b.c("mask_done"))
                b.wait("scalar", "sacc_done", b.c("sacc_done") - 1)
                b.wait("scalar", "pv_done", b.c("pv_done") - 1)
                exv = b.do("scalar", lambda q, ebuf=ebuf: q.activation(
                    out=t.E2[:, ebuf * 1024:ebuf * 1024 + 1024],
                    in_=t.tn_sb[:, :], func=AF.Exp, scale=f32(50.0)),
                    sem="exp_done")
                b.wait("vector", "exp_done", exv)
                if kc >= 8 * qt:
                    o = kc - 8 * qt
                    b.do("vector", lambda v, ebuf=ebuf, o=o: v.tensor_tensor(
                        t.E2[:, ebuf * 1024:ebuf * 1024 + 1024],
                        t.E2[:, ebuf * 1024:ebuf * 1024 + 1024],
                        t.mask_sb[:, o * 1024:o * 1024 + 1024], op=ALU.mult),
                        sem="mask_done")
                b.wait("vector", "mask_done", b.c("mask_done"))
                if kc == 0:
                    sav = b.do("vector", lambda v, ebuf=ebuf: v.tensor_copy(
                        t.S_acc[:, :], t.E2[:, ebuf * 1024:ebuf * 1024 + 1024]),
                        sem="sacc_done")
                else:
                    sav = b.do("vector", lambda v, ebuf=ebuf: v.tensor_add(
                        t.S_acc[:, :], t.S_acc[:, :],
                        t.E2[:, ebuf * 1024:ebuf * 1024 + 1024]),
                        sem="sacc_done")
                b.wait("tensor", "mask_done", b.c("mask_done"))
                b.wait("tensor", "exp_done", exv)

                def t_pv(q, kc=kc, ebuf=ebuf, nkc=nkc):
                    mm = None
                    for half in range(2):
                        mm = q.matmul(
                            t.ps_o[:, half * 512:half * 512 + 512],
                            t.xv[:, kc * 128:kc * 128 + 128],
                            t.E2[:, ebuf * 1024 + half * 512:
                                 ebuf * 1024 + half * 512 + 512],
                            start=(kc == 0), stop=(kc == nkc - 1))
                    return mm
                b.do("tensor", t_pv, sem="pv_done")
                yield
            b.wait("vector", "sacc_done", b.c("sacc_done"))
            b.wait("vector", "z_mm", b.c("z_mm"))
            sbv = b.do("vector", lambda v: v.tensor_copy(t.S_bf[:, :],
                                                         t.S_acc[:, :]),
                       sem="sbf_done")
            b.wait("tensor", "sbf_done", sbv)
            b.wait("tensor", "tanh_done", b.c("tanh_done"))

            def t_z(q):
                q.matmul(t.ps_sc[0:1, 0:512], t.onescol[:, :], t.S_bf[:, 0:512],
                         start=True, stop=True)
                return q.matmul(t.ps_sc[0:1, 512:1024], t.onescol[:, :],
                                t.S_bf[:, 512:1024], start=True, stop=True)
            zmv = b.do("tensor", t_z, sem="z_mm")
            b.wait("scalar", "z_mm", zmv)
            b.wait("scalar", "zb_cast", b.c("zb_cast"))
            zev = b.do("scalar", lambda q: q.copy(t.zr_f[:, :],
                                                  t.ps_sc[0:1, :]),
                       sem="zev_done")
            b.wait("vector", "zev_done", zev)
            b.wait("vector", "zb_mm", b.c("zb_mm"))
            b.fdo("vector", lambda v: v.reciprocal(t.zr_f2[:, :], t.zr_f[:, :]))
            zbv = b.do("vector", lambda v: v.tensor_copy(t.zr_b[:, :],
                                                         t.zr_f2[:, :]),
                       sem="zb_cast")
            b.wait("scalar", "pv_done", b.c("pv_done"))
            b.wait("scalar", "o_sc", b.c("o_sc"))
            oev = b.do("scalar", lambda q: q.copy(t.o_raw[:, :], t.ps_o[:, :]),
                       sem="oev_done")
            b.wait("tensor", "zb_cast", zbv)
            b.wait("tensor", "zev_done", zev)
            for half in range(2):
                bcv = b.do("tensor", lambda q, half=half: q.matmul(
                    t.ps_sc[:, half * 512:half * 512 + 512], t.ones1[:, :],
                    t.zr_b[0:1, half * 512:half * 512 + 512],
                    start=True, stop=True), sem="zb_mm")
            b.wait("vector", "zb_mm", b.c("zb_mm"))
            b.wait("vector", "oev_done", oev)

            def v_osc(q, qt=qt):
                q.tensor_tensor(t.o_bf[:, qt * 1024:qt * 1024 + 512],
                                t.o_raw[:, 0:512], t.ps_sc[:, 0:512],
                                op=ALU.mult)
                return q.tensor_tensor(
                    t.o_bf[:, qt * 1024 + 512:qt * 1024 + 1024],
                    t.o_raw[:, 512:1024], t.ps_sc[:, 512:1024], op=ALU.mult)
            osv = b.do("vector", v_osc, sem="o_sc")
            b.wait("tensor", "o_sc", osv)
            b.wait("tensor", "oev_done", b.c("oev_done"))
            yield
        b.wait("gpsimd", "o_sc", b.c("o_sc"))
        b.wait("vector", "o_st", b.c("o_st"))
        b.dma("gpsimd", t.out1_o[h * 128:(h + 1) * 128, :], t.o_bf[:, :],
              sem="o_st")
    b.wait("gpsimd", "o_st", b.c("o_st"))


def emit_C(b, t):
    emit_consts(b, t)
    b.do("vector", lambda v: v.memset(t.onescol[:, :], 1.0), sem="const")
    wv = b.dma("gpsimd", t.wqkv_sb[:, :], t.wqkv)
    b.wait("tensor", "ld_gpsimd", wv)
    b.wait("tensor", "ld_const", b.c("ld_const"))

    def on_group_loc(g, lnv, xsl):
        pa, pb, tpv = emit_transposes(b, t, lnv, xsl, g)
        b.wait("vector", "tp_done", tpv)

        def v_evac(q, g=g, pa=pa, pb=pb):
            d3 = t.xn_loc.ap().rearrange("p (di tt) -> p di tt", di=16)
            q.tensor_copy(d3[:, 0:8, g * 128:g * 128 + 128],
                          pa.ap().rearrange("p (di tt) -> p di tt", di=8))
            return q.tensor_copy(
                d3[:, 8:16, g * 128:g * 128 + 128],
                pb.ap().rearrange("p (di tt) -> p di tt", di=8))
        b.do("vector", v_evac, sem="xp_evac")

    emit_ln_stream(b, t, t.xloc, 8, on_group_loc)

    emit_router_top2(b, t, n_exp=8, use_rms=False, scale_out=1.0)
    sgl = b.dma("gpsimd", t.sgw_sb[:, :], t.sgw_d, sem="ld_sgw")
    b.wait("tensor", "ld_sgw", sgl)
    b.wait("tensor", "cwb_ev", b.c("cwb_ev"))
    for di in range(16):
        def t_sg(q, di=di):
            mm = None
            for half in range(2):
                mm = q.matmul(t.ps_lg12[0:1, half * 512:half * 512 + 512],
                              t.sgw_sb[:, di:di + 1],
                              t.xn_loc[:, di * TLOC + half * 512:
                                       di * TLOC + half * 512 + 512],
                              start=(di == 0), stop=(di == 15))
            return mm
        b.do("tensor", t_sg, sem="sg_mm")
    b.wait("scalar", "sg_mm", b.c("sg_mm"))
    sgv = b.do("scalar", lambda q: q.activation(
        out=t.sg_row[:, :], in_=t.ps_lg12[0:1, :], func=AF.Sigmoid),
        sem="sg_act")
    b.wait("tensor", "sg_act", sgv)
    for half in range(2):
        bcv = b.do("tensor", lambda q, half=half: q.matmul(
            t.ps_lg12[:, half * 512:half * 512 + 512], t.ones1[:, :],
            t.sg_row[0:1, half * 512:half * 512 + 512],
            start=True, stop=True), sem="sgb_mm")
        b.wait("vector", "sgb_mm", bcv)
        sgbv = b.do("vector", lambda v, half=half: v.tensor_copy(
            t.sgb[:, half * 512:half * 512 + 512],
            t.ps_lg12[:, half * 512:half * 512 + 512]), sem="sgb_ev")
    b.wait("tensor", "sgb_ev", b.c("sgb_ev"))
    b.wait("gpsimd", "cwb_ev", b.c("cwb_ev"))
    b.wait("gpsimd", "sgb_ev", b.c("sgb_ev"))

    interleave((gen_C1(b, t), 1), (gen_gu(b, t), 2))
    b.barrier()

    mv = b.dma("gpsimd", t.mask_sb[:, :], t.mask_d, sem="ld_mask")
    b.wait("vector", "ld_mask", mv)
    b.wait("sync", "a_st", b.c("a_st"))
    interleave((gen_down_C(b, t), 1), (gen_C3(b, t), 1))
    b.wait("gpsimd", "g1_st", b.c("g1_st"))
    b.wait("gpsimd", "o_st", b.c("o_st"))


def emit_router_top2(b, t, *, n_exp, use_rms, scale_out):
    lw = b.dma("gpsimd", t.rw_sb[:, :], t.rw)
    if use_rms:
        lw = b.dma("gpsimd", t.nw_sb[:, :], t.nw)
    b.wait("vector", "ld_gpsimd", lw)
    b.wait("tensor", "ld_gpsimd", lw)
    for di in range(16):
        if use_rms:
            b.wait("vector", "lg_mm", b.c("lg_mm") - 1)
            xwv = b.do("vector", lambda v, di=di: v.tensor_scalar(
                t.xnw2[:, (di % 2) * TLOC:(di % 2 + 1) * TLOC],
                t.xn_loc[:, di * TLOC:(di + 1) * TLOC],
                t.nw_sb[:, di:di + 1], None, op0=ALU.mult), sem="xnw_done")
            b.wait("tensor", "xnw_done", xwv)
            rhs = (lambda half, di=di:
                   t.xnw2[:, (di % 2) * TLOC + half * 512:
                          (di % 2) * TLOC + half * 512 + 512])
        else:
            rhs = (lambda half, di=di:
                   t.xn_loc[:, di * TLOC + half * 512:
                            di * TLOC + half * 512 + 512])

        def t_lg(q, di=di, rhs=rhs):
            mm = None
            for half in range(2):
                mm = q.matmul(t.ps_lg[0:8, half * 512:half * 512 + 512],
                              t.rw_sb[:, di * 8:di * 8 + 8],
                              rhs(half),
                              start=(di == 0), stop=(di == 15))
            return mm
        b.do("tensor", t_lg, sem="lg_mm")
    b.wait("scalar", "lg_mm", b.c("lg_mm"))
    lev = b.do("scalar", lambda q: q.copy(t.lg_sb[:, :], t.ps_lg[0:8, :]),
               sem="lg_evac")
    b.wait("tensor", "lg_evac", lev)

    def t_ltr(q):
        mm = None
        for lg in range(8):
            mm = q.transpose(t.ps_t8[:, lg * 8:lg * 8 + 8],
                             t.lg_sb[:, lg * 128:lg * 128 + 128],
                             t.ident[0:8, 0:8])
        return mm
    ltv = b.do("tensor", t_ltr, sem="lg_tr")
    b.wait("vector", "lg_tr", ltv)
    ltev = b.do("vector", lambda v: v.tensor_copy(t.ltm[:, :], t.ps_t8[:, :]),
                sem="lg_tr_ev")
    b.wait("vector", "lg_tr_ev", ltev)
    if use_rms:
        b.wait("vector", "rms_done", b.c("rms_done"))
    for lg in range(8):
        col = (lambda j, lg=lg: t.r_sc[:, 8 * lg + j:8 * lg + j + 1])
        l8 = (lambda lg=lg: t.ltm[:, lg * 8:lg * 8 + 8])
        x8 = (lambda lg=lg: t.exps[:, lg * 8:lg * 8 + 8])
        e8 = (lambda lg=lg: t.eqs[:, lg * 8:lg * 8 + 8])
        t8 = (lambda lg=lg: t.tts[:, lg * 8:lg * 8 + 8])
        c8 = (lambda lg=lg: t.cwtm[:, lg * 8:lg * 8 + 8])
        if use_rms:
            rsv = b.do("vector", lambda v, col=col, lg=lg: v.tensor_scalar(
                col(0), t.rms_sumsq[:, lg:lg + 1], 1.0 / DM, 1e-6,
                op0=ALU.mult, op1=ALU.add), sem="rr1")
            b.wait("scalar", "rr1", rsv)
            rs2 = b.do("scalar", lambda q, col=col: q.sqrt(col(1), col(0)),
                       sem="rr2")
            b.wait("vector", "rr2", rs2)
            b.fdo("vector", lambda v, col=col: v.reciprocal(col(2), col(1)))
            b.fdo("vector", lambda v, col=col, l8=l8: v.tensor_scalar(
                l8(), l8(), col(2), None, op0=ALU.mult))
        b.fdo("vector", lambda v, col=col, l8=l8: v.tensor_reduce(
            col(3), l8(), axis=AX.X, op=ALU.max))
        mxv = b.do("vector", lambda v, col=col: v.tensor_scalar_mul(
            col(3), col(3), -1.0), sem="tq1")
        b.wait("scalar", "tq1", mxv)
        e8v = b.do("scalar", lambda q, col=col, l8=l8, x8=x8: q.activation(
            out=x8(), in_=l8(), func=AF.Exp, bias=col(3)), sem="tq2")
        b.wait("vector", "tq2", e8v)
        for step in [
            lambda v, col=col, x8=x8: v.tensor_reduce(
                col(4), x8(), axis=AX.X, op=ALU.max),
            lambda v, col=col, x8=x8, e8=e8: v.tensor_scalar(
                e8(), x8(), col(4), None, op0=ALU.is_ge),
            lambda v, x8=x8, e8=e8, t8=t8: v.tensor_tensor(
                t8(), x8(), e8(), op=ALU.mult),
            lambda v, x8=x8, t8=t8: v.tensor_sub(t8(), x8(), t8()),
            lambda v, col=col, t8=t8: v.tensor_reduce(
                col(5), t8(), axis=AX.X, op=ALU.max),
            lambda v, col=col, x8=x8, e8=e8: v.tensor_scalar(
                e8(), x8(), col(5), None, op0=ALU.is_ge),
            lambda v, col=col: v.tensor_add(col(6), col(4), col(5)),
            lambda v, col=col: v.reciprocal(col(7), col(6)),
            lambda v, x8=x8, e8=e8, t8=t8: v.tensor_tensor(
                t8(), x8(), e8(), op=ALU.mult),
        ]:
            b.fdo("vector", step)
        ctv = b.do("vector", lambda v, col=col, t8=t8, c8=c8: v.tensor_scalar(
            c8(), t8(), col(7), float(scale_out), op0=ALU.mult, op1=ALU.mult),
            sem="cw_tm_done")
        b.wait("tensor", "cw_tm_done", ctv)
        b.do("tensor", lambda q, lg=lg, c8=c8: q.transpose(
            t.ps_ct[:, lg * 128:lg * 128 + 128], c8(), t.ident[:, :]),
            sem="cw_tr")
    b.wait("vector", "cw_tr", b.c("cw_tr"))
    cfev = b.do("vector", lambda v: v.tensor_copy(t.cw8[:, :], t.ps_ct[:, :]),
                sem="cw_fm_ev")
    b.wait("gpsimd", "cw_fm_ev", cfev)
    bct = t.ps_bc if getattr(t, "ps_bc", None) is not None else t.ps_lg12
    for eh in range(2):
        b.wait("gpsimd", "cwb_mm", b.c("cwb_mm"))
        crv = None
        for ei in range(4):
            e = eh * 4 + ei
            crv = b.dma("gpsimd", t.cw_row[0:1, ei * TLOC:(ei + 1) * TLOC],
                        t.cw8[e:e + 1, :], sem="ld_cwrow")
        b.wait("tensor", "ld_cwrow", crv)
        for ei in range(4):
            e = eh * 4 + ei
            if e >= n_exp:
                continue
            for half in range(2):
                b.wait("tensor", "cwb_ev", b.c("cwb_ev"))
                bcv = b.do("tensor", lambda q, ei=ei, half=half: q.matmul(
                    bct[:, 0:512], t.ones1[:, :],
                    t.cw_row[0:1, ei * TLOC + half * 512:
                             ei * TLOC + half * 512 + 512],
                    start=True, stop=True), sem="cwb_mm")
                b.wait("vector", "cwb_mm", bcv)
                b.do("vector", lambda v, e=e, half=half: v.tensor_copy(
                    t.cwb[:, e * TLOC + half * 512:e * TLOC + half * 512 + 512],
                    bct[:, 0:512]), sem="cwb_ev")


def emit_expert_gu(b, t, gw_d, uw_d, a_dram, n_hc, hc_per_exp):
    for hc in range(n_hc):
        wbuf = b.c("gu_mm") % 2
        b.wait("gpsimd", "gu_mm", b.c("gu_mm") - 1)
        b.wait("vector", "gu_mm", b.c("gu_mm") - 1)
        b.dma("gpsimd", t.gwu2[:, wbuf * 4096:wbuf * 4096 + 2048],
              gw_d[:, hc * 2048:(hc + 1) * 2048], sem="ld_w")
        gl = b.dma("vector", t.gwu2[:, wbuf * 4096 + 2048:wbuf * 4096 + 4096],
                   uw_d[:, hc * 2048:(hc + 1) * 2048], sem="ld_w2")
        b.wait("tensor", "ld_w", b.c("ld_w"))
        b.wait("tensor", "ld_w2", gl)
        psg = t.ps_gA if b.c("gu_mm") % 2 == 0 else t.ps_gB
        psu = t.ps_uA if b.c("gu_mm") % 2 == 0 else t.ps_uB
        b.wait("tensor", "silu_done", b.c("silu_done") - 1)

        def t_g(q, wbuf=wbuf, psg=psg):
            mm = None
            for di in range(16):
                for half in range(2):
                    mm = q.matmul(
                        psg[:, half * 512:half * 512 + 512],
                        t.gwu2[:, wbuf * 4096 + di * 128:
                               wbuf * 4096 + di * 128 + 128],
                        t.xn_loc[:, di * TLOC + half * 512:
                                 di * TLOC + half * 512 + 512],
                        start=(di == 0), stop=(di == 15))
            return mm
        gmv = b.do("tensor", t_g, sem="g_mm")
        b.wait("tensor", "aprod_done", b.c("aprod_done") - 1)

        def t_u(q, wbuf=wbuf, psu=psu):
            mm = None
            for di in range(16):
                for half in range(2):
                    mm = q.matmul(
                        psu[:, half * 512:half * 512 + 512],
                        t.gwu2[:, wbuf * 4096 + 2048 + di * 128:
                               wbuf * 4096 + 2048 + di * 128 + 128],
                        t.xn_loc[:, di * TLOC + half * 512:
                                 di * TLOC + half * 512 + 512],
                        start=(di == 0), stop=(di == 15))
            return mm
        umv = b.do("tensor", t_u, sem="gu_mm")
        abuf = b.c("a_st") // 16 % 2
        b.wait("scalar", "g_mm", gmv)
        b.wait("scalar", "a_st", b.c("a_st") - 16)
        slv = b.do("scalar", lambda q, psg=psg, abuf=abuf: q.activation(
            out=t.aw2[:, abuf * TLOC:(abuf + 1) * TLOC], in_=psg[:, :],
            func=AF.Silu), sem="silu_done")
        b.wait("vector", "gu_mm", umv)
        b.wait("vector", "silu_done", slv)

        if hc_per_exp:
            e = hc // hc_per_exp

            def v_a(q, e=e, abuf=abuf, psu=psu):
                q.tensor_tensor(t.aw2[:, abuf * TLOC:(abuf + 1) * TLOC],
                                t.aw2[:, abuf * TLOC:(abuf + 1) * TLOC],
                                psu[:, :], op=ALU.mult)
                return q.tensor_tensor(
                    t.aw2[:, abuf * TLOC:(abuf + 1) * TLOC],
                    t.aw2[:, abuf * TLOC:(abuf + 1) * TLOC],
                    t.cwb[:, e * TLOC:(e + 1) * TLOC], op=ALU.mult)
        else:
            def v_a(q, abuf=abuf, psu=psu):
                return q.tensor_tensor(
                    t.aw2[:, abuf * TLOC:(abuf + 1) * TLOC],
                    t.aw2[:, abuf * TLOC:(abuf + 1) * TLOC],
                    psu[:, :], op=ALU.mult)
        apv = b.do("vector", v_a, sem="aprod_done")
        b.wait("gpsimd", "aprod_done", apv)
        b.dma("gpsimd", a_dram[hc * 128:(hc + 1) * 128, :],
              t.aw2[:, abuf * TLOC:(abuf + 1) * TLOC], sem="a_st")


def emit_down_accum(b, t, dwn_d, a_dram, n_hc, fog, hc_chunk, first):
    for cs in range(0, n_hc, hc_chunk):
        ce = min(cs + hc_chunk, n_hc)
        cbuf = b.c("dn_chunk") % 2
        b.wait("gpsimd", "dn_chunk", b.c("dn_chunk") - 1)
        dl = b.dma(
            "gpsimd",
            t.dw_sb[:, cbuf * hc_chunk * 512:
                    (cbuf * hc_chunk + (ce - cs)) * 512],
            dwn_d[:, (fog * n_hc + cs) * 512:(fog * n_hc + ce) * 512],
            sem="ld_dw")
        b.wait("tensor", "ld_dw", dl)
        for hc in range(cs, ce, 2):
            npair = min(2, ce - hc)
            pbuf = b.c("dn_pair") % 2
            b.wait("sync", "dn_pair", b.c("dn_pair") - 1)
            alv = b.dma(
                "sync",
                t.astr4[:, pbuf * 2048:pbuf * 2048 + npair * TLOC].rearrange(
                    "p (hh tt) -> p hh tt", hh=npair),
                a_dram[hc * 128:(hc + npair) * 128, :].rearrange(
                    "(hh p) tt -> p hh tt", p=128), sem="ld_a")
            b.wait("tensor", "ld_a", alv)

            def t_dn(q, hc=hc, cs=cs, npair=npair, pbuf=pbuf, cbuf=cbuf,
                     n_hc=n_hc, hc_chunk=hc_chunk):
                mm = None
                for ki in range(npair):
                    ci = hc + ki - cs
                    for fo_in in range(4):
                        for half in range(2):
                            mm = q.matmul(
                                t.ps_y[fo_in][:, half * 512:half * 512 + 512],
                                t.dw_sb[:, (cbuf * hc_chunk + ci) * 512 +
                                        fo_in * 128:
                                        (cbuf * hc_chunk + ci) * 512 +
                                        fo_in * 128 + 128],
                                t.astr4[:, pbuf * 2048 + ki * TLOC + half * 512:
                                        pbuf * 2048 + ki * TLOC + half * 512
                                        + 512],
                                start=(hc + ki == 0), stop=(hc + ki == n_hc - 1))
                return mm
            b.do("tensor", t_dn, sem="dn_pair")
        b.do("tensor", lambda q: q.nop(), sem="dn_chunk")


def host_inputs_C(inputs, x1_full):
    wq = inputs["L1_attn_wq"].T
    wk = inputs["L1_attn_wk"].T
    wv = inputs["L1_attn_wv"].T
    gw = _gu_layout(
        inputs["L1_ffn_w_gate"].transpose(1, 0, 2).reshape(DM, N_HC1 * 128))
    uw = _gu_layout(
        inputs["L1_ffn_w_up"].transpose(1, 0, 2).reshape(DM, N_HC1 * 128))
    gwu = np.ascontiguousarray(np.concatenate(
        [gw.reshape(128, N_HC1, 2048), uw.reshape(128, N_HC1, 2048)],
        axis=2).reshape(128, N_HC1 * 4096))
    dwn = _down_layout(inputs["L1_ffn_w_down"].reshape(N_HC1 * 128, DM), 8, 256)
    rw = _chunked(np.ascontiguousarray(inputs["L1_ffn_gate_w"].T))
    sgw = np.ascontiguousarray(
        inputs["L1_ffn_shared_gate_w"].reshape(16, 128).T)
    swg = _gu_layout(inputs["L1_ffn_sw_gate"])
    swu = _gu_layout(inputs["L1_ffn_sw_up"])
    swgu = np.ascontiguousarray(np.concatenate(
        [swg.reshape(128, N_HCS, 2048), swu.reshape(128, N_HCS, 2048)],
        axis=2).reshape(128, N_HCS * 4096))
    swd = _down_layout(inputs["L1_ffn_sw_down"], 8, 256)
    inv = 1.0 / 10000.0 ** (np.arange(0, 128, 2, dtype=np.float32) / 128)
    fpos = np.arange(S, dtype=np.float32)[:, None] * inv[None, :]
    emb = np.concatenate((fpos, fpos), axis=-1)
    cos_c = np.ascontiguousarray(np.cos(emb).astype(np.float32).T)
    sin_c = np.ascontiguousarray(np.sin(emb).astype(np.float32).T)
    kp = np.arange(128)[:, None]
    qp = np.arange(1024)[None, :]
    mask = np.zeros((128, 8 * 1024), np.float32)
    for o in range(8):
        mask[:, o * 1024:(o + 1) * 1024] = (o * 128 + kp <= qp)
    mask_c = mask.astype(BF)
    maps = []
    for c in range(NCORES):
        sl = np.concatenate([
            wq[:, 256 * c:256 * (c + 1)],
            wk[:, 128 * c:128 * (c + 1)],
            wv[:, 128 * c:128 * (c + 1)]], axis=1)
        maps.append({
            "x": x1_full, "xloc": x1_full[TLOC * c:TLOC * (c + 1)],
            "wqkv": _chunked(np.ascontiguousarray(sl)), "rw": rw,
            "gwu": gwu, "dwn": dwn, "sgw": sgw, "swgu": swgu, "swd": swd,
            "cos_c": cos_c, "sin_c": sin_c, "mask_c": mask_c,
            "ident_c": _IDENT})
    return maps





def host_inputs_D(inputs, x1_full, out1_parts, g1_parts):
    out1_full = np.concatenate(out1_parts, axis=0)
    wo1 = _chunked(np.ascontiguousarray(inputs["L1_attn_wo"].T))
    maps = []
    for c in range(NCORES):
        xfm = np.ascontiguousarray(x1_full[TLOC * c:TLOC * (c + 1)].T)
        maps.append({
            "xfm": xfm,
            "val": np.ascontiguousarray(out1_full[:, TLOC * c:TLOC * (c + 1)]),
            "g0": g1_parts[c],
            "wo": wo1})
    return maps


def kernel(**inputs):
    maps_A = host_inputs_A(inputs)
    nc_A = build(make_tensors_A, emit_A)
    res_A = run_bass_kernel_spmd(nc_A, maps_A, core_ids=list(range(NCORES)))
    val_parts = [res_A.results[c]["valout"] for c in range(NCORES)]
    g0_parts = [res_A.results[c]["g0out"] for c in range(NCORES)]

    maps_B = host_inputs_B(inputs, val_parts, g0_parts)
    nc_B = build(make_tensors_B, emit_B)
    res_B = run_bass_kernel_spmd(nc_B, maps_B, core_ids=list(range(NCORES)))
    x1_full = np.concatenate(
        [np.ascontiguousarray(res_B.results[c]["x1out"].T)
         for c in range(NCORES)], axis=0)

    maps_C = host_inputs_C(inputs, x1_full)
    nc_C = build(make_tensors_C, emit_C)
    res_C = run_bass_kernel_spmd(nc_C, maps_C, core_ids=list(range(NCORES)))
    out1_parts = [res_C.results[c]["out1"] for c in range(NCORES)]
    g1_parts = [res_C.results[c]["g1out"] for c in range(NCORES)]

    maps_D = host_inputs_D(inputs, x1_full, out1_parts, g1_parts)
    nc_D = build(make_tensors_B, emit_B)
    res_D = run_bass_kernel_spmd(nc_D, maps_D, core_ids=list(range(NCORES)))
    x2 = np.concatenate(
        [np.ascontiguousarray(res_D.results[c]["x1out"].T)
         for c in range(NCORES)], axis=0)
    return x2.reshape(1, S, DM)

